# version 04
# baseline (speedup 1.0000x reference)
import numpy as np
from contextlib import ExitStack

import concourse.bass as bass
import concourse.mybir as mybir
from concourse.bass_utils import run_bass_kernel_spmd

F32 = mybir.dt.float32
BF16 = mybir.dt.bfloat16
AF = mybir.ActivationFunctionType
ALU = mybir.AluOpType
AX = mybir.AxisListType

NC = 8
BC = 4
S = 2048
D = 1024
NTOK = BC * S
TT = 512
NTILE = NTOK // TT
TC = TT // 128
DC = D // 128
JC = 8
F = 2048
FC = F // 128
NE = 8
EPS1 = 1e-5
EPS2 = 1e-6
BIG = 1.0e30

ENGS = ("sync", "scalar", "vector", "tensor", "gpsimd")


def build_nc():
    nc = bass.Bass("TRN2", target_bir_lowering=False)

    x_h = nc.dram_tensor("x", [NTOK, D], F32, kind="ExternalInput").ap()
    out_h = nc.dram_tensor("out", [NTOK, D], F32, kind="ExternalOutput").ap()
    wq_h = nc.dram_tensor("Wq", [D, D], F32, kind="ExternalInput").ap()
    wk_h = nc.dram_tensor("Wk", [D, D], F32, kind="ExternalInput").ap()
    wv_h = nc.dram_tensor("Wv", [D, D], F32, kind="ExternalInput").ap()
    wo_h = nc.dram_tensor("Wo", [D, D], F32, kind="ExternalInput").ap()
    bq_h = nc.dram_tensor("bq", [1, D], F32, kind="ExternalInput").ap()
    bk_h = nc.dram_tensor("bk", [1, D], F32, kind="ExternalInput").ap()
    bv_h = nc.dram_tensor("bv", [1, D], F32, kind="ExternalInput").ap()
    bo_h = nc.dram_tensor("bo", [1, D], F32, kind="ExternalInput").ap()
    gam_h = nc.dram_tensor("gamma", [1, D], F32, kind="ExternalInput").ap()
    bet_h = nc.dram_tensor("beta", [1, D], F32, kind="ExternalInput").ap()
    wg_h = nc.dram_tensor("Wg", [NE, D], F32, kind="ExternalInput").ap()
    w1_h = nc.dram_tensor("W1", [NE, F, D], F32, kind="ExternalInput").ap()
    w2_h = nc.dram_tensor("W2", [NE, F, D], F32, kind="ExternalInput").ap()
    w3_h = nc.dram_tensor("W3", [NE, D, F], F32, kind="ExternalInput").ap()
    w1t_h = nc.dram_tensor("w1t", [NE, FC // 2, 128, 2048], BF16, kind="Internal").ap()
    w2t_h = nc.dram_tensor("w2t", [NE, FC // 2, 128, 2048], BF16, kind="Internal").ap()
    w3t_h = nc.dram_tensor("w3t", [NE, FC, 128, D], BF16, kind="Internal").ap()

    ctx = ExitStack()
    with ctx:
        E = ctx.enter_context
        idf = E(nc.sbuf_tensor("idf", [128, 128], F32))
        ones_bf = E(nc.sbuf_tensor("ones_bf", [1, 512], BF16))
        ones32 = E(nc.sbuf_tensor("ones32", [1, 128], F32))
        brows = E(nc.sbuf_tensor("brows", [1, 4096], BF16))
        g_bc = E(nc.sbuf_tensor("g_bc", [128, D], BF16))
        b_bc = E(nc.sbuf_tensor("b_bc", [128, D], BF16))
        wgT = E(nc.sbuf_tensor("wgT", [128, 64], BF16))
        wA = E(nc.sbuf_tensor("wA", [128, DC * D], BF16))
        wB = E(nc.sbuf_tensor("wB", [128, DC * D], BF16))
        kv = E(nc.sbuf_tensor("kv", [128, BC * 1024], BF16))
        xn = E(nc.sbuf_tensor("xn", [128, 2 * TC * D], F32))
        xnT = E(nc.sbuf_tensor("xnT", [128, DC * TT], BF16))
        qT = E(nc.sbuf_tensor("qT", [128, JC * TT], BF16))
        s_sb = E(nc.sbuf_tensor("s_sb", [128, TC * D], F32))
        offn = E(nc.sbuf_tensor("offn", [128, TC * D], F32))
        hid = E(nc.sbuf_tensor("hid", [128, FC * TT], BF16))
        w1s = E(nc.sbuf_tensor("w1s", [128, 4096], BF16))
        w2s = E(nc.sbuf_tensor("w2s", [128, 4096], BF16))
        w3sb = E(nc.sbuf_tensor("w3sb", [128, FC * D], BF16))
        st2 = E(nc.sbuf_tensor("st2", [128, 2], F32))
        vt = E(nc.sbuf_tensor("vt", [128, 4], F32))
        mcol = E(nc.sbuf_tensor("mcol", [128, 8], F32))
        rstdc = E(nc.sbuf_tensor("rstdc", [128, 8], F32))
        rinvc = E(nc.sbuf_tensor("rinvc", [128, 8], F32))
        gsc = E(nc.sbuf_tensor("gsc", [128, 32], F32))
        gtt = E(nc.sbuf_tensor("gtt", [128, 32], F32))
        gs2 = E(nc.sbuf_tensor("gs2", [128, 32], F32))
        gex = E(nc.sbuf_tensor("gex", [128, 32], F32))
        gat = E(nc.sbuf_tensor("gat", [128, 32], F32))
        gm1 = E(nc.sbuf_tensor("gm1", [128, 4], F32))
        gm2 = E(nc.sbuf_tensor("gm2", [128, 4], F32))
        gden = E(nc.sbuf_tensor("gden", [128, 4], F32))
        grec = E(nc.sbuf_tensor("grec", [128, 4], F32))
        gmk = E(nc.sbuf_tensor("gmk", [128, 8], F32))
        kss = E(nc.sbuf_tensor("kss", [128, 16], F32))
        kls = E(nc.sbuf_tensor("kls", [128, 16], F32))
        krs = E(nc.sbuf_tensor("krs", [128, 16], F32))

        psA = E(nc.psum_tensor("psA", [128, 6 * 512], F32))
        psT = [E(nc.psum_tensor("psT1", [128, 512], F32)),
               E(nc.psum_tensor("psT2", [128, 512], F32))]

        class Ctr:
            __slots__ = ("h", "n", "name")

            def __init__(self, h, name):
                self.h = h
                self.n = 0
                self.name = name

        def sem(name):
            return Ctr(E(nc.semaphore(name)), name)

        sT = sem("sT")
        sV = sem("sV")
        sS = sem("sS")
        sG = sem("sG")
        sX = sem("sX")
        sW1 = sem("sW1")
        sW2 = sem("sW2")
        sW3 = sem("sW3")
        sWAB = sem("sWAB")
        sPL = sem("sPL")
        sPS = sem("sPS")
        sRO = sem("sRO")
        sO = sem("sO")

        ops = {k: [] for k in ENGS}

        def emit(eng, f):
            ops[eng].append(f)

        def wait(eng, ctr, v):
            assert v <= ctr.n, f"wait on future {ctr.name}: {v} > {ctr.n}"
            if v <= 0:
                return
            emit(eng, lambda e, h=ctr.h, vv=v: e.wait_ge(h, vv))

        def waitp(eng, pair):
            if pair is not None:
                wait(eng, pair[0], pair[1])

        def run(eng, maker, inc=None, k=1):
            if inc is None:
                emit(eng, maker)
                return None
            inc.n += k
            emit(eng, lambda e, m=maker, h=inc.h, kk=k: m(e).then_inc(h, kk))
            return (inc, inc.n)

        def runv(maker):
            p = run("vector", maker, inc=sV)
            wait("vector", sV, p[1])
            return p

        def runsc(maker):
            p = run("scalar", maker, inc=sS)
            wait("scalar", sS, p[1])
            return p

        bank_free = [None] * 8
        bank_ready = [None] * 8

        def bank_ap(b, off=0, width=512):
            return psA[:, b * 512 + off: b * 512 + off + width]

        def emit_transposes(slot, tc):
            xoff = slot * TC * D + tc * D
            for half in range(2):
                pb = 6 + half
                waitp("tensor", bank_free[pb])
                r = None
                for i in range(4):
                    dc = half * 4 + i
                    r = run("tensor",
                            lambda e, o=psT[half][:, i * 128:(i + 1) * 128],
                            i_=xn[:, xoff + dc * 128: xoff + (dc + 1) * 128],
                            d=idf[:]: e.transpose(o, i_, d),
                            inc=sT if i == 3 else None)
                bank_ready[pb] = r
                waitp("vector", r)
                out_ap = xnT[:].rearrange("p (dc t) -> p dc t", t=TT)[
                    :, half * 4:(half + 1) * 4, tc * 128:(tc + 1) * 128]
                in_ap = psT[half][:].rearrange("p (dc t) -> p dc t", t=128)
                bank_free[pb] = run(
                    "vector", lambda e, o=out_ap, i_=in_ap: e.tensor_copy(o, i_),
                    inc=sV)
            return (sV, sV.n)

        def emit_ln1(slot, tc, need_rinv):
            xoff = slot * TC * D + tc * D
            xi = xn[:, xoff: xoff + D]
            dump = offn[:, 0:D]
            run("scalar", lambda e, o=dump, i_=xi, a=st2[:, 0:1]:
                e.activation(o, i_, AF.Identity, accum_out=a))
            a1 = run("scalar", lambda e, o=dump, i_=xi, a=st2[:, 1:2]:
                     e.activation(o, i_, AF.Square, accum_out=a), inc=sS)
            waitp("vector", a1)
            mc = mcol[:, slot * 4 + tc: slot * 4 + tc + 1]
            runv(lambda e, o=mc, i_=st2[:, 0:1]:
                 e.tensor_scalar(o, i_, 1.0 / D, None, op0=ALU.mult))
            runv(lambda e, o=vt[:, 0:1], i_=st2[:, 1:2]:
                 e.tensor_scalar(o, i_, 1.0 / D, None, op0=ALU.mult))
            runv(lambda e, o=vt[:, 1:2], i_=mc, j=mc:
                 e.tensor_tensor(o, i_, j, op=ALU.mult))
            runv(lambda e, o=vt[:, 2:3], i_=vt[:, 0:1], j=vt[:, 1:2]:
                 e.tensor_tensor(o, i_, j, op=ALU.subtract))
            v1 = runv(lambda e, o=vt[:, 2:3], i_=vt[:, 2:3]:
                      e.tensor_scalar(o, i_, EPS1, None, op0=ALU.add))
            waitp("scalar", v1)
            runsc(lambda e, o=vt[:, 3:4], i_=vt[:, 2:3]:
                  e.activation(o, i_, AF.Ln))
            rc = rstdc[:, slot * 4 + tc: slot * 4 + tc + 1]
            s1 = runsc(lambda e, o=rc, i_=vt[:, 3:4]:
                       e.activation(o, i_, AF.Exp, scale=-0.5))
            if need_rinv:
                ric = rinvc[:, slot * 4 + tc: slot * 4 + tc + 1]
                s1 = runsc(lambda e, o=ric, i_=vt[:, 3:4]:
                           e.activation(o, i_, AF.Exp, scale=0.5))
            waitp("vector", s1)
            return run("vector", lambda e, o=xi, i_=xi, m=mc, r=rc:
                       e.tensor_scalar(o, i_, m, r, op0=ALU.subtract,
                                       op1=ALU.mult), inc=sV)

        run("gpsimd", lambda e: e.iota(idf[:], [[1, 128]], channel_multiplier=-1,
                                       allow_small_or_imprecise_dtypes=True),
            inc=sG)
        wait("vector", sG, sG.n)
        run("vector", lambda e: e.tensor_scalar(idf[:], idf[:], 0.0, None,
                                                op0=ALU.is_equal))
        run("vector", lambda e: e.memset(ones_bf[:], 1.0))
        run("vector", lambda e: e.memset(ones32[:], 1.0))
        idok = run("vector", lambda e: e.memset(kv[:], 0.0), inc=sV)

        waitp("gpsimd", idok)
        for (w_src, w_dst) in ((wk_h, wA), (wv_h, wB)):
            run("gpsimd", lambda e, s=w_src, d=w_dst:
                e.dma_start(d[:].rearrange("p (dc j) -> p dc j", j=D),
                            s.rearrange("(dc p) j -> p dc j", p=128)),
                inc=sWAB, k=16)
        wab_phA = (sWAB, sWAB.n)

        for i, bh in enumerate((bq_h, bk_h, bv_h, bo_h)):
            run("sync", lambda e, s=bh, o=s_sb[0:1, i * D:(i + 1) * D]:
                e.dma_start(o, s), inc=sRO, k=16)
        r1 = (sRO, sRO.n)
        waitp("vector", r1)
        bro = run("vector", lambda e: e.tensor_copy(brows[:], s_sb[0:1, 0:4096]),
                  inc=sV)
        waitp("sync", bro)
        run("sync", lambda e: e.dma_start(s_sb[0:1, 0:D], gam_h), inc=sRO, k=16)
        run("sync", lambda e: e.dma_start(s_sb[0:1, D:2 * D], bet_h), inc=sRO, k=16)
        run("sync", lambda e: e.dma_start(s_sb[0:8, 2 * D:3 * D], wg_h),
            inc=sRO, k=16)
        r2 = (sRO, sRO.n)
        waitp("tensor", r2)
        for j, dst in ((0, g_bc), (1, b_bc)):
            for dh in range(2):
                pb = 6 + dh
                waitp("tensor", bank_free[pb])
                r = run("tensor", lambda e, o=psT[dh][:], l=ones32[0:1, :],
                        rr=s_sb[0:1, j * D + dh * 512: j * D + dh * 512 + 512]:
                        e.matmul(o, l, rr, start=True, stop=True), inc=sT)
                bank_ready[pb] = r
                waitp("vector", r)
                bank_free[pb] = run(
                    "vector", lambda e, o=dst[:, dh * 512:(dh + 1) * 512],
                    i_=psT[dh][:]: e.tensor_copy(o, i_), inc=sV)
        for dc in range(DC):
            pb = 6 + dc % 2
            waitp("tensor", bank_free[pb])
            r = run("tensor", lambda e, o=psT[dc % 2][:, 0:8],
                    i_=s_sb[0:8, 2 * D + dc * 128: 2 * D + (dc + 1) * 128],
                    d=idf[0:8, 0:8]: e.transpose(o, i_, d), inc=sT)
            bank_ready[pb] = r
            waitp("vector", r)
            bank_free[pb] = run("vector", lambda e, o=wgT[:, dc * 8:(dc + 1) * 8],
                                i_=psT[dc % 2][:, 0:8]: e.tensor_copy(o, i_),
                                inc=sV)
        rows_done = (sV, sV.n)

        units = []
        for e_i in range(NE):
            units += [("w1", e_i, fc) for fc in range(FC)]
            units += [("w2", e_i, fc) for fc in range(FC)]
            units += [("w3", e_i, fc) for fc in range(FC)]
        NU = len(units)
        unit_tdone = [None] * NU
        pl_counts = [None] * NU

        def emit_unit_load(u):
            kind, e_i, fc = units[u]
            slot = u % 2
            dst = s_sb[:, slot * 1024:(slot + 1) * 1024]
            if u >= 2:
                waitp("sync", unit_tdone[u - 2])
            else:
                waitp("sync", rows_done)
            if kind == "w1":
                run("sync", lambda e, o=dst, s=w1_h[e_i, fc * 128:(fc + 1) * 128, :]:
                    e.dma_start(o, s), inc=sPL, k=16)
            elif kind == "w2":
                run("sync", lambda e, o=dst, s=w2_h[e_i, fc * 128:(fc + 1) * 128, :]:
                    e.dma_start(o, s), inc=sPL, k=16)
            else:
                src3 = w3_h[e_i, :, fc * 128:(fc + 1) * 128].rearrange(
                    "(oc p) f -> p oc f", p=128)
                dst3 = dst.rearrange("p (oc f) -> p oc f", f=128)
                run("sync", lambda e, o=dst3[:, 0:4, :], s=src3[:, 0:4, :]:
                    e.dma_start(o, s), inc=sPL, k=16)
                run("sync", lambda e, o=dst3[:, 4:8, :], s=src3[:, 4:8, :]:
                    e.dma_start(o, s), inc=sPL, k=16)
            pl_counts[u] = (sPL, sPL.n)

        for u in range(min(2, NU)):
            emit_unit_load(u)
        for u, (kind, e_i, fc) in enumerate(units):
            slot = u % 2
            src = s_sb[:, slot * 1024:(slot + 1) * 1024]
            stage = hid[:, slot * 1024:(slot + 1) * 1024]
            waitp("tensor", pl_counts[u])
            if u >= 2:
                wait("tensor", sPS, 16 * (u - 1))
            for half in range(2):
                pb = 6 + half
                waitp("tensor", bank_free[pb])
                r = None
                for i in range(4):
                    c = half * 4 + i
                    r = run("tensor",
                            lambda e, o=psT[half][:, i * 128:(i + 1) * 128],
                            i_=src[:, c * 128:(c + 1) * 128], d=idf[:]:
                            e.transpose(o, i_, d), inc=sT if i == 3 else None)
                bank_ready[pb] = r
                waitp("vector", r)
                bank_free[pb] = run(
                    "vector", lambda e, o=stage[:, half * 512:(half + 1) * 512],
                    i_=psT[half][:]: e.tensor_copy(o, i_), inc=sV)
            unit_tdone[u] = (sT, sT.n)
            uv = (sV, sV.n)
            waitp("scalar", uv)
            if kind == "w1":
                dsth = w1t_h[e_i, fc // 2, :,
                             (fc % 2) * 1024:(fc % 2) * 1024 + 1024]
            elif kind == "w2":
                dsth = w2t_h[e_i, fc // 2, :,
                             (fc % 2) * 1024:(fc % 2) * 1024 + 1024]
            else:
                dsth = w3t_h[e_i, fc, :, :]
            run("scalar", lambda e, o=dsth, s=stage: e.dma_start(o, s),
                inc=sPS, k=16)
            if u + 2 < NU:
                emit_unit_load(u + 2)
        pre_done_S = (sPS, sPS.n)

        slot_free = [None] * 34
        x_loaded = [None] * 32

        def emit_xload(i):
            slot = i % 2
            t = i % NTILE
            src = x_h[t * TT:(t + 1) * TT, :].rearrange("(tc p) d -> p tc d",
                                                        p=128)
            dst = xn[:, slot * TC * D:(slot + 1) * TC * D].rearrange(
                "p (tc d) -> p tc d", d=D)
            if i >= 2:
                assert slot_free[i] is not None
                waitp("sync", slot_free[i])
            x_loaded[i] = run("sync", lambda e, o=dst, s=src: e.dma_start(o, s),
                              inc=sX, k=16)

        emit_xload(0)
        emit_xload(1)

        kv_ready = [None] * BC
        k_first = True
        for t in range(NTILE):
            b = t // 4
            slot = t % 2
            li = t
            waitp("scalar", x_loaded[li])
            waitp("vector", x_loaded[li])
            if t == 0:
                waitp("vector", pre_done_S)
                waitp("scalar", pre_done_S)
            for tc in range(TC):
                ln_done = emit_ln1(slot, tc, need_rinv=False)
                waitp("tensor", ln_done)
                xnT_rdy = emit_transposes(slot, tc)
                waitp("tensor", xnT_rdy)
                tcol = tc * 128
                if k_first:
                    waitp("tensor", wab_phA)
                    k_first = False
                for (bank0, wsb, brow_i) in ((0, wA, 1), (2, wB, 2)):
                    for jh in range(2):
                        bank = bank0 + jh
                        waitp("tensor", bank_free[bank])
                        for dc in range(DC):
                            run("tensor", lambda e, o=bank_ap(bank),
                                l=xnT[:, dc * TT + tcol: dc * TT + tcol + 128],
                                rr=wsb[:, dc * D + jh * 512:
                                       dc * D + (jh + 1) * 512],
                                stf=(dc == 0):
                                e.matmul(o, l, rr, start=stf, stop=False))
                        r = run("tensor", lambda e, o=bank_ap(bank),
                                l=ones_bf[0:1, 0:128],
                                rr=brows[0:1, brow_i * D + jh * 512:
                                         brow_i * D + (jh + 1) * 512]:
                                e.matmul(o, l, rr, start=False, stop=True),
                                inc=sT)
                        bank_ready[bank] = r
                f1 = s_sb[:, 0:D]
                f2 = s_sb[:, D:2 * D]
                v1 = None
                for jh in range(2):
                    waitp("vector", bank_ready[jh])
                    v1 = run("vector", lambda e, o=f2[:, jh * 512:(jh + 1) * 512],
                             i_=bank_ap(jh):
                             e.tensor_scalar(o, i_, 0.0, None, op0=ALU.min),
                             inc=sV if jh == 1 else None)
                waitp("scalar", v1)
                s1 = run("scalar", lambda e, o=f1, i_=f2:
                         e.activation(o, i_, AF.Exp), inc=sS)
                waitp("vector", s1)
                kd = None
                for jh in range(2):
                    kd = run("vector", lambda e, o=f1[:, jh * 512:(jh + 1) * 512],
                             i_=bank_ap(jh), j=f1[:, jh * 512:(jh + 1) * 512]:
                             e.scalar_tensor_tensor(o, i_, 0.0, j, op0=ALU.max,
                                                    op1=ALU.add), inc=sV)
                bank_free[0] = bank_free[1] = kd
                wait("vector", sV, kd[1])
                runv(lambda e, o=f2, i_=f1, j=f1:
                     e.tensor_tensor(o, i_, j, op=ALU.mult))
                runv(lambda e, o=kss[:],
                     i_=f2.rearrange("p (h dd) -> p h dd", dd=64):
                     e.tensor_reduce(o, i_, axis=AX.X, op=ALU.add))
                v2 = runv(lambda e, o=kss[:], i_=kss[:]:
                          e.tensor_scalar(o, i_, EPS2, None, op0=ALU.add))
                waitp("scalar", v2)
                runsc(lambda e, o=kls[:], i_=kss[:]:
                      e.activation(o, i_, AF.Ln))
                s2 = runsc(lambda e, o=krs[:], i_=kls[:]:
                           e.activation(o, i_, AF.Exp, scale=-0.5))
                waitp("vector", s2)
                kb = None
                for h in range(16):
                    kb = run("vector", lambda e,
                             o=qT[:, tc * 1024 + h * 64: tc * 1024 + (h + 1) * 64],
                             i_=f1[:, h * 64:(h + 1) * 64], sc=krs[:, h:h + 1]:
                             e.tensor_scalar(o, i_, sc, None, op0=ALU.mult),
                             inc=sV if h == 15 else None)
                vb = None
                for jh in range(2):
                    waitp("vector", bank_ready[2 + jh])
                    vb = run("vector", lambda e,
                             o=hid[:, tc * 1024 + jh * 512:
                                   tc * 1024 + (jh + 1) * 512],
                             i_=bank_ap(2 + jh): e.tensor_copy(o, i_), inc=sV)
                bank_free[2] = bank_free[3] = vb
                waitp("tensor", kb)
                waitp("tensor", vb)
                first = (tc == 0 and t % 4 == 0)
                last = (tc == TC - 1 and t % 4 == 3)
                if first:
                    waitp("tensor", bank_free[4])
                    waitp("tensor", bank_free[5])
                r = None
                for pb in range(8):
                    st_f = first and (pb % 4 == 0)
                    sp_f = last and (pb % 4 == 3)
                    r = run("tensor", lambda e,
                            o=bank_ap(4 + pb // 4, (pb % 4) * 128, 128),
                            l=qT[:, tc * 1024 + pb * 128:
                                 tc * 1024 + (pb + 1) * 128],
                            rr=hid[:, tc * 1024 + pb * 128:
                                  tc * 1024 + (pb + 1) * 128],
                            stf=st_f, spf=sp_f:
                            e.matmul(o, l, rr, start=stf, stop=spf,
                                     skip_group_check=True),
                            inc=sT if pb == 7 else None)
                bank_ready[4] = bank_ready[5] = r
                if tc == TC - 1:
                    slot_free[li + 2] = (sT, sT.n)
            if t % 4 == 3:
                waitp("vector", bank_ready[4])
                kvd = None
                for pb in range(8):
                    base = 2048 + pb * 128
                    kvo = b * 1024 + pb * 128
                    run("vector", lambda e, o=kv[0:64, kvo:kvo + 64],
                        i_=psA[0:64, base:base + 64]: e.tensor_copy(o, i_))
                    kvd = run("vector",
                              lambda e, o=kv[64:128, kvo + 64:kvo + 128],
                              i_=psA[64:128, base + 64:base + 128]:
                              e.tensor_copy(o, i_),
                              inc=sV if pb == 7 else None)
                bank_free[4] = bank_free[5] = kvd
                kv_ready[b] = kvd
            if li + 1 < 32 and x_loaded[li + 1] is None:
                emit_xload(li + 1)
            if li + 2 < 32 and slot_free[li + 2] is not None \
                    and x_loaded[li + 2] is None:
                emit_xload(li + 2)

        phaseA_T = (sT, sT.n)
        waitp("gpsimd", phaseA_T)
        for (w_src, w_dst) in ((wq_h, wA), (wo_h, wB)):
            run("gpsimd", lambda e, s=w_src, d=w_dst:
                e.dma_start(d[:].rearrange("p (dc j) -> p dc j", j=D),
                            s.rearrange("(dc p) j -> p dc j", p=128)),
                inc=sWAB, k=16)
        wab_phB = (sWAB, sWAB.n)

        npair = NTILE * NE * (FC // 2)
        w1_pair_done = [None] * npair
        w2_pair_done = [None] * npair
        w1_loaded = [None] * npair
        w2_loaded = [None] * npair

        def emit_pair_load(g, which):
            slot = g % 2
            rem = g % (NE * 8)
            e_i = rem // 8
            p = rem % 8
            if which == 1:
                q_, sbuf, hsrc, ssem, done = "sync", w1s, w1t_h, sW1, w1_pair_done
            else:
                q_, sbuf, hsrc, ssem, done = "scalar", w2s, w2t_h, sW2, w2_pair_done
            if g >= 2:
                waitp(q_, done[g - 2])
            loaded = run(q_, lambda e, o=sbuf[:, slot * 2048:(slot + 1) * 2048],
                         s=hsrc[e_i, p]: e.dma_start(o, s), inc=ssem, k=16)
            (w1_loaded if which == 1 else w2_loaded)[g] = loaded

        w3_loaded = [None] * (NTILE * NE)
        w3_read_done = [None] * (NTILE * NE)

        def emit_w3_load(gi):
            e_i = gi % NE
            if gi >= 1:
                waitp("scalar", w3_read_done[gi - 1])
            src = w3t_h[e_i].rearrange("fc p d -> p fc d")
            dst = w3sb[:].rearrange("p (fc d) -> p fc d", d=D)
            ld = None
            for q4 in range(4):
                ld = run("scalar", lambda e, o=dst[:, q4 * 4:(q4 + 1) * 4, :],
                         s=src[:, q4 * 4:(q4 + 1) * 4, :]: e.dma_start(o, s),
                         inc=sW3, k=16)
            w3_loaded[gi] = ld

        wait("sync", sPS, sPS.n)
        wait("scalar", sPS, sPS.n)
        emit_pair_load(0, 1)
        emit_pair_load(1, 1)
        emit_pair_load(0, 2)
        emit_pair_load(1, 2)
        emit_w3_load(0)

        sg_done = None
        gse_prev = None

        for t in range(NTILE):
            b = t // 4
            slot = t % 2
            li = 16 + t
            waitp("scalar", x_loaded[li])
            waitp("vector", x_loaded[li])
            if t > 0:
                wait("vector", sO, 16 * 4 * t)
                wait("scalar", sO, 16 * 4 * t)
            for tc in range(TC):
                ln_done = emit_ln1(slot, tc, need_rinv=True)
                waitp("tensor", ln_done)
                xnT_rdy = emit_transposes(slot, tc)
                waitp("tensor", xnT_rdy)
                if tc == 0:
                    waitp("tensor", bank_free[4])
                    bank_free[4] = None
                r = None
                for dc in range(DC):
                    r = run("tensor", lambda e, o=bank_ap(4, tc * 8, 8),
                            l=xnT[:, dc * TT + tc * 128:
                                 dc * TT + tc * 128 + 128],
                            rr=wgT[:, dc * 8:(dc + 1) * 8],
                            stf=(dc == 0 and tc == 0),
                            spf=(dc == DC - 1 and tc == TC - 1):
                            e.matmul(o, l, rr, start=stf, stop=spf,
                                     skip_group_check=True),
                            inc=sT if dc == DC - 1 else None)
                bank_ready[4] = r
            waitp("vector", bank_ready[4])
            if gse_prev is not None:
                waitp("vector", gse_prev)
            runv(lambda e: e.tensor_copy(gsc[:], psA[:, 2048:2080]))
            gevac = runv(lambda e, o=gm1[:],
                         i_=gsc[:].rearrange("p (tc k) -> p tc k", k=8):
                         e.tensor_reduce(o, i_, axis=AX.X, op=ALU.max))
            bank_free[4] = gevac
            for tc in range(TC):
                sl = gsc[:, tc * 8:(tc + 1) * 8]
                m1 = gm1[:, tc:tc + 1]
                runv(lambda e, o=gtt[:, tc * 8:(tc + 1) * 8], i_=sl,
                     m=m1: e.tensor_scalar(o, i_, m, None, op0=ALU.subtract))
                runv(lambda e, o=gmk[:], i_=sl, m=m1:
                     e.tensor_scalar(o, i_, m, None, op0=ALU.is_ge))
                runv(lambda e, o=gs2[:, tc * 8:(tc + 1) * 8], i_=gmk[:],
                     j=sl: e.scalar_tensor_tensor(o, i_, -BIG, j, op0=ALU.mult,
                                                  op1=ALU.add))
            gv = runv(lambda e, o=gm2[:],
                      i_=gs2[:].rearrange("p (tc k) -> p tc k", k=8):
                      e.tensor_reduce(o, i_, axis=AX.X, op=ALU.max))
            waitp("scalar", gv)
            gse = runsc(lambda e: e.activation(gex[:], gtt[:], AF.Exp))
            gse_prev = gse
            waitp("vector", gse)
            for tc in range(TC):
                sl = gsc[:, tc * 8:(tc + 1) * 8]
                runv(lambda e, o=gmk[:], i_=sl, m=gm2[:, tc:tc + 1]:
                     e.tensor_scalar(o, i_, m, None, op0=ALU.is_ge))
                runv(lambda e, o=gtt[:, tc * 8:(tc + 1) * 8],
                     i_=gex[:, tc * 8:(tc + 1) * 8], j=gmk[:]:
                     e.tensor_tensor(o, i_, j, op=ALU.mult))
            runv(lambda e, o=gden[:],
                 i_=gtt[:].rearrange("p (tc k) -> p tc k", k=8):
                 e.tensor_reduce(o, i_, axis=AX.X, op=ALU.add))
            runv(lambda e: e.reciprocal(grec[:], gden[:]))
            gat_done = None
            for tc in range(TC):
                gat_done = runv(lambda e, o=gat[:, tc * 8:(tc + 1) * 8],
                                i_=gtt[:, tc * 8:(tc + 1) * 8],
                                m=grec[:, tc:tc + 1]:
                                e.tensor_scalar(o, i_, m, None, op0=ALU.mult))
            if t == 0:
                waitp("tensor", wab_phB)
            qT_w = [None] * JC
            for jc in range(JC):
                bank = jc % 2
                waitp("tensor", bank_free[bank])
                for dc in range(DC):
                    run("tensor", lambda e, o=bank_ap(bank),
                        l=wA[:, dc * D + jc * 128: dc * D + jc * 128 + 128],
                        rr=xnT[:, dc * TT:(dc + 1) * TT], stf=(dc == 0):
                        e.matmul(o, l, rr, start=stf, stop=False))
                r = run("tensor", lambda e, o=bank_ap(bank),
                        l=brows[0:1, jc * 128:(jc + 1) * 128],
                        rr=ones_bf[0:1, 0:512]:
                        e.matmul(o, l, rr, start=False, stop=True), inc=sT)
                bank_ready[bank] = r
                t1 = s_sb[:, 0:512]
                t2 = s_sb[:, 512:1024]
                waitp("vector", r)
                v1 = run("vector", lambda e, o=t1, i_=bank_ap(bank):
                         e.tensor_scalar(o, i_, 0.0, None, op0=ALU.min), inc=sV)
                waitp("scalar", v1)
                s1 = run("scalar", lambda e, o=t2, i_=t1:
                         e.activation(o, i_, AF.Exp), inc=sS)
                waitp("vector", s1)
                qT_w[jc] = run(
                    "vector", lambda e, o=qT[:, jc * TT:(jc + 1) * TT],
                    i_=bank_ap(bank), j=t2:
                    e.scalar_tensor_tensor(o, i_, 0.0, j, op0=ALU.max,
                                           op1=ALU.add), inc=sV)
                bank_free[bank] = qT_w[jc]
            waitp("tensor", kv_ready[b])
            for jc in range(JC):
                bank = 2 + jc % 2
                waitp("tensor", bank_free[bank])
                waitp("tensor", qT_w[jc])
                r = run("tensor", lambda e, o=bank_ap(bank),
                        l=kv[:, b * 1024 + jc * 128: b * 1024 + (jc + 1) * 128],
                        rr=qT[:, jc * TT:(jc + 1) * TT]:
                        e.matmul(o, l, rr, start=True, stop=True), inc=sT)
                bank_ready[bank] = r
                waitp("vector", r)
                bank_free[bank] = run(
                    "vector", lambda e, o=qT[:, jc * TT:(jc + 1) * TT],
                    i_=bank_ap(bank): e.tensor_copy(o, i_), inc=sV)
            attnT_ready = (sV, sV.n)
            for tc in range(TC):
                for dh in range(2):
                    bank = 4 + dh
                    waitp("tensor", bank_free[bank])
                    if tc == 0 and dh == 0:
                        waitp("tensor", attnT_ready)
                    for jc in range(JC):
                        run("tensor", lambda e, o=bank_ap(bank),
                            l=qT[:, jc * TT + tc * 128:
                                 jc * TT + tc * 128 + 128],
                            rr=wB[:, jc * D + dh * 512: jc * D + (dh + 1) * 512],
                            stf=(jc == 0):
                            e.matmul(o, l, rr, start=stf, stop=False))
                    r = run("tensor", lambda e, o=bank_ap(bank),
                            l=ones_bf[0:1, 0:128],
                            rr=brows[0:1, 3 * D + dh * 512:
                                     3 * D + (dh + 1) * 512]:
                            e.matmul(o, l, rr, start=False, stop=True), inc=sT)
                    bank_ready[bank] = r
                xoff = slot * TC * D + tc * D
                ssl = s_sb[:, tc * D:(tc + 1) * D]
                yv = None
                for dh in range(2):
                    waitp("vector", bank_ready[4 + dh])
                    yv = run("vector", lambda e,
                             o=ssl[:, dh * 512:(dh + 1) * 512],
                             i_=xn[:, xoff + dh * 512: xoff + (dh + 1) * 512],
                             j=bank_ap(4 + dh):
                             e.tensor_tensor(o, i_, j, op=ALU.add), inc=sV)
                    bank_free[4 + dh] = yv
                waitp("scalar", yv)
                run("scalar", lambda e, o=offn[:, 0:D], i_=ssl, a=st2[:, 0:1]:
                    e.activation(o, i_, AF.Identity, accum_out=a))
                a1 = run("scalar", lambda e, o=offn[:, 0:D], i_=ssl,
                         a=st2[:, 1:2]:
                         e.activation(o, i_, AF.Square, accum_out=a), inc=sS)
                waitp("vector", a1)
                runv(lambda e, o=vt[:, 0:1], i_=st2[:, 0:1]:
                     e.tensor_scalar(o, i_, 1.0 / D, None, op0=ALU.mult))
                runv(lambda e, o=vt[:, 1:2], i_=st2[:, 1:2]:
                     e.tensor_scalar(o, i_, 1.0 / D, None, op0=ALU.mult))
                runv(lambda e, o=vt[:, 2:3], i_=vt[:, 0:1],
                     j=vt[:, 0:1]: e.tensor_tensor(o, i_, j, op=ALU.mult))
                runv(lambda e, o=vt[:, 3:4], i_=vt[:, 1:2],
                     j=vt[:, 2:3]:
                     e.tensor_tensor(o, i_, j, op=ALU.subtract))
                v2 = runv(lambda e, o=vt[:, 3:4], i_=vt[:, 3:4]:
                          e.tensor_scalar(o, i_, EPS2, None, op0=ALU.add))
                waitp("scalar", v2)
                runsc(lambda e, o=vt[:, 1:2], i_=vt[:, 3:4]:
                      e.activation(o, i_, AF.Ln))
                s2 = runsc(lambda e, o=vt[:, 2:3], i_=vt[:, 1:2]:
                           e.activation(o, i_, AF.Exp, scale=-0.5))
                waitp("vector", s2)
                run("vector", lambda e, o=ssl, i_=ssl, m=vt[:, 0:1],
                    r_=vt[:, 2:3]:
                    e.tensor_scalar(o, i_, m, r_, op0=ALU.subtract,
                                    op1=ALU.mult))
                run("vector", lambda e, o=ssl, i_=ssl, j=g_bc[:]:
                    e.tensor_tensor(o, i_, j, op=ALU.mult))
                run("vector", lambda e, o=ssl, i_=ssl, j=b_bc[:]:
                    e.tensor_tensor(o, i_, j, op=ALU.add))
            for e_i in range(NE):
                gpair0 = t * NE * 8 + e_i * 8
                g3 = t * NE + e_i
                mult_last = None
                for fc in range(FC):
                    gp = gpair0 + fc // 2
                    bank1 = fc % 2
                    bank2 = 2 + fc % 2
                    waitp("tensor", w1_loaded[gp])
                    waitp("tensor", bank_free[bank1])
                    sl1 = w1s[:, (gp % 2) * 2048 + (fc % 2) * 1024:
                              (gp % 2) * 2048 + (fc % 2) * 1024 + 1024]
                    r1_ = None
                    for dc in range(DC):
                        r1_ = run("tensor", lambda e, o=bank_ap(bank1),
                                  l=sl1[:, dc * 128:(dc + 1) * 128],
                                  rr=xnT[:, dc * TT:(dc + 1) * TT],
                                  stf=(dc == 0), spf=(dc == DC - 1):
                                  e.matmul(o, l, rr, start=stf, stop=spf),
                                  inc=sT if dc == DC - 1 else None)
                    bank_ready[bank1] = r1_
                    if fc % 2 == 1:
                        w1_pair_done[gp] = r1_
                        if gp + 2 < npair:
                            emit_pair_load(gp + 2, 1)
                    waitp("tensor", w2_loaded[gp])
                    waitp("tensor", bank_free[bank2])
                    sl2 = w2s[:, (gp % 2) * 2048 + (fc % 2) * 1024:
                              (gp % 2) * 2048 + (fc % 2) * 1024 + 1024]
                    r2_ = None
                    for dc in range(DC):
                        r2_ = run("tensor", lambda e, o=bank_ap(bank2),
                                  l=sl2[:, dc * 128:(dc + 1) * 128],
                                  rr=xnT[:, dc * TT:(dc + 1) * TT],
                                  stf=(dc == 0), spf=(dc == DC - 1):
                                  e.matmul(o, l, rr, start=stf, stop=spf),
                                  inc=sT if dc == DC - 1 else None)
                    bank_ready[bank2] = r2_
                    if fc % 2 == 1:
                        w2_pair_done[gp] = r2_
                        if gp + 2 < npair:
                            emit_pair_load(gp + 2, 2)
                    waitp("scalar", bank_ready[bank1])
                    if e_i == 0 and fc == 0 and sg_done is not None:
                        waitp("scalar", sg_done)
                    hsl = hid[:, fc * TT:(fc + 1) * TT]
                    silu_l = run("scalar", lambda e, o=hsl, i_=bank_ap(bank1):
                                 e.activation(o, i_, AF.Silu), inc=sS)
                    bank_free[bank1] = silu_l
                    waitp("vector", silu_l)
                    waitp("vector", bank_ready[bank2])
                    mult_last = run("vector", lambda e, o=hsl, i_=hsl,
                                    j=bank_ap(bank2):
                                    e.tensor_tensor(o, i_, j, op=ALU.mult),
                                    inc=sV)
                    bank_free[bank2] = mult_last
                waitp("tensor", w3_loaded[g3])
                waitp("tensor", mult_last)
                eo_last = None
                for ob in range(8):
                    tc, dh = ob // 2, ob % 2
                    bank = 4 + ob % 2
                    waitp("tensor", bank_free[bank])
                    r = None
                    for fc in range(FC):
                        r = run("tensor", lambda e, o=bank_ap(bank),
                                l=hid[:, fc * TT + tc * 128:
                                     fc * TT + tc * 128 + 128],
                                rr=w3sb[:, fc * D + dh * 512:
                                       fc * D + (dh + 1) * 512],
                                stf=(fc == 0), spf=(fc == FC - 1):
                                e.matmul(o, l, rr, start=stf, stop=spf),
                                inc=sT if fc == FC - 1 else None)
                    bank_ready[bank] = r
                    eo_last = r
                    waitp("vector", r)
                    if e_i == 0 and gat_done is not None:
                        waitp("vector", gat_done)
                        gat_done = None
                    osl = offn[:, tc * D + dh * 512: tc * D + (dh + 1) * 512]
                    gcol = gat[:, tc * 8 + e_i: tc * 8 + e_i + 1]
                    if e_i == 0:
                        bf_ = run("vector", lambda e, o=osl, i_=bank_ap(bank),
                                  m=gcol: e.tensor_scalar(o, i_, m, None,
                                                          op0=ALU.mult),
                                  inc=sV)
                    else:
                        bf_ = run("vector", lambda e, o=osl, i_=bank_ap(bank),
                                  m=gcol, j=osl:
                                  e.scalar_tensor_tensor(o, i_, m, j,
                                                         op0=ALU.mult,
                                                         op1=ALU.add), inc=sV)
                    bank_free[bank] = bf_
                w3_read_done[g3] = eo_last
                if g3 + 1 < NTILE * NE:
                    emit_w3_load(g3 + 1)
            ffn_done = (sV, sV.n)
            waitp("scalar", ffn_done)
            ep_v = None
            for tc in range(TC):
                gsl = hid[:, tc * 1024:(tc + 1) * 1024]
                gs_ = run("scalar", lambda e, o=gsl,
                          i_=offn[:, tc * D:(tc + 1) * D]:
                          e.activation(o, i_, AF.Silu), inc=sS)
                waitp("vector", gs_)
                ssl = s_sb[:, tc * D:(tc + 1) * D]
                xoff = slot * TC * D + tc * D
                xsl = xn[:, xoff: xoff + D]
                sgm = run("vector", lambda e, o=ssl, i_=ssl, j=gsl:
                          e.tensor_tensor(o, i_, j, op=ALU.mult), inc=sV)
                sg_done = sgm
                run("vector", lambda e, o=xsl, i_=xsl,
                    m=rinvc[:, slot * 4 + tc: slot * 4 + tc + 1],
                    mm=mcol[:, slot * 4 + tc: slot * 4 + tc + 1]:
                    e.tensor_scalar(o, i_, m, mm, op0=ALU.mult, op1=ALU.add))
                ep_v = run("vector", lambda e, o=ssl, i_=ssl, j=xsl:
                           e.tensor_tensor(o, i_, j, op=ALU.add), inc=sV)
                waitp("sync", ep_v)
                run("sync", lambda e,
                    o=out_h[t * TT + tc * 128: t * TT + (tc + 1) * 128, :],
                    s=s_sb[:, tc * D:(tc + 1) * D]: e.dma_start(o, s),
                    inc=sO, k=16)
            slot_free[li + 2] = ep_v
            if li + 2 < 32 and x_loaded[li + 2] is None:
                emit_xload(li + 2)

        wait("sync", sO, sO.n)
        assert sO.n == 16 * NTILE * TC, sO.n

        with nc.Block() as block:
            @block.sync
            def _(e):
                for f in ops["sync"]:
                    f(e)

            @block.scalar
            def _(e):
                for f in ops["scalar"]:
                    f(e)

            @block.vector
            def _(e):
                for f in ops["vector"]:
                    f(e)

            @block.tensor
            def _(e):
                for f in ops["tensor"]:
                    f(e)

            @block.gpsimd
            def _(e):
                for f in ops["gpsimd"]:
                    f(e)

    return nc


def kernel(**inputs):
    x = np.ascontiguousarray(inputs["L0_attn_x"], dtype=np.float32)
    base = {
        "Wq": np.ascontiguousarray(inputs["L0_attn_Wq"], np.float32),
        "Wk": np.ascontiguousarray(inputs["L0_attn_Wk"], np.float32),
        "Wv": np.ascontiguousarray(inputs["L0_attn_Wv"], np.float32),
        "Wo": np.ascontiguousarray(inputs["L0_attn_Wo"], np.float32),
        "bq": np.ascontiguousarray(inputs["L0_attn_bq"], np.float32).reshape(1, D),
        "bk": np.ascontiguousarray(inputs["L0_attn_bk"], np.float32).reshape(1, D),
        "bv": np.ascontiguousarray(inputs["L0_attn_bv"], np.float32).reshape(1, D),
        "bo": np.ascontiguousarray(inputs["L0_attn_bo"], np.float32).reshape(1, D),
        "gamma": np.ascontiguousarray(
            inputs["L0_attn_gamma"], np.float32).reshape(1, D),
        "beta": np.ascontiguousarray(
            inputs["L0_attn_beta"], np.float32).reshape(1, D),
        "Wg": np.ascontiguousarray(inputs["L0_ffn_Wg"], np.float32),
        "W1": np.ascontiguousarray(inputs["L0_ffn_W1"], np.float32),
        "W2": np.ascontiguousarray(inputs["L0_ffn_W2"], np.float32),
        "W3": np.ascontiguousarray(inputs["L0_ffn_W3"], np.float32),
    }
    in_maps = []
    for c in range(NC):
        m = dict(base)
        m["x"] = np.ascontiguousarray(x[c * BC:(c + 1) * BC].reshape(NTOK, D))
        in_maps.append(m)
    nc = build_nc()
    res = run_bass_kernel_spmd(nc, in_maps, core_ids=list(range(NC)))
    out = np.concatenate(
        [np.asarray(r["out"], dtype=np.float32).reshape(BC, S, D)
         for r in res.results], axis=0)
    return out


# version 05
# speedup vs baseline: 1.6398x; 1.6398x over previous
import numpy as np
from contextlib import ExitStack

import concourse.bass as bass
import concourse.mybir as mybir
from concourse.bass_utils import run_bass_kernel_spmd
from concourse.bass import IndirectOffsetOnAxis

F32 = mybir.dt.float32
BF16 = mybir.dt.bfloat16
I32 = mybir.dt.int32
AF = mybir.ActivationFunctionType
ALU = mybir.AluOpType
AX = mybir.AxisListType

NC = 8
BC = 4
S = 2048
D = 1024
NTOK = BC * S
TT = 512
NTILE = NTOK // TT
TC = TT // 128
DC = D // 128
JC = 8
F = 2048
FC = F // 128
NE = 8
CAP = 3072
NCH = CAP // 128
CHK = CAP // 512
NCOL = NTOK // 128
EPS1 = 1e-5
EPS2 = 1e-6
BIG = 1.0e30
BIGI = float(2 ** 20)

ENGS = ("sync", "scalar", "vector", "tensor", "gpsimd")

PB_W3 = 0
PB_W1 = 16384
PB_W2 = 32768
PB_XT = 49152
PB_STG = 53248
PB_HID = 55296
PB_TOT = 63488

PF_SLOT = 0
PF_S = 8192
PF_SCR = 12288
PF_TOT = 13312
PF_META = 0
PF_WRUN = 1024
PF_TOKF = 1088
PF_SLOTA = 1152
PF_WHICH = 1664
PF_BASE = 2176
PF_TMP = 2688
PF_D12 = 3200
PF_ECAP = 3328


def build_nc():
    nc = bass.Bass("TRN2", target_bir_lowering=False)

    x_h = nc.dram_tensor("x", [NTOK, D], F32, kind="ExternalInput").ap()
    out_h = nc.dram_tensor("out", [NTOK, D], F32, kind="ExternalOutput").ap()
    wq_h = nc.dram_tensor("Wq", [D, D], F32, kind="ExternalInput").ap()
    wk_h = nc.dram_tensor("Wk", [D, D], F32, kind="ExternalInput").ap()
    wv_h = nc.dram_tensor("Wv", [D, D], F32, kind="ExternalInput").ap()
    wo_h = nc.dram_tensor("Wo", [D, D], F32, kind="ExternalInput").ap()
    bq_h = nc.dram_tensor("bq", [1, D], F32, kind="ExternalInput").ap()
    bk_h = nc.dram_tensor("bk", [1, D], F32, kind="ExternalInput").ap()
    bv_h = nc.dram_tensor("bv", [1, D], F32, kind="ExternalInput").ap()
    bo_h = nc.dram_tensor("bo", [1, D], F32, kind="ExternalInput").ap()
    gam_h = nc.dram_tensor("gamma", [1, D], F32, kind="ExternalInput").ap()
    bet_h = nc.dram_tensor("beta", [1, D], F32, kind="ExternalInput").ap()
    wg_h = nc.dram_tensor("Wg", [NE, D], F32, kind="ExternalInput").ap()
    w1_h = nc.dram_tensor("W1", [NE, F, D], F32, kind="ExternalInput").ap()
    w2_h = nc.dram_tensor("W2", [NE, F, D], F32, kind="ExternalInput").ap()
    w3_h = nc.dram_tensor("W3", [NE, D, F], F32, kind="ExternalInput").ap()
    xnbf_d = nc.dram_tensor("xnbf", [NTOK + 1, D], BF16, kind="Internal").ap()
    meta_d = nc.dram_tensor("meta", [NTOK + 1, 16], F32, kind="Internal").ap()
    idx_d = nc.dram_tensor("idxd", [NE * CAP + 1, 1], I32, kind="Internal").ap()
    out2_d = nc.dram_tensor("out2", [2 * NTOK + 1, D], BF16, kind="Internal").ap()

    ctx = ExitStack()
    with ctx:
        E = ctx.enter_context
        pf = E(nc.sbuf_tensor("pf", [128, PF_TOT], F32))
        pb = E(nc.sbuf_tensor("pb", [128, PB_TOT], BF16))
        kv = E(nc.sbuf_tensor("kv", [128, BC * 1024], BF16))
        wstg = E(nc.sbuf_tensor("wstg", [128, 2048], BF16))
        xg = E(nc.sbuf_tensor("xg", [128, 2048], BF16))
        idb = E(nc.sbuf_tensor("idb", [128, 128], BF16))
        ltb = E(nc.sbuf_tensor("ltb", [128, 128], BF16))
        oneb = E(nc.sbuf_tensor("oneb", [128, 128], BF16))
        ones_bf = E(nc.sbuf_tensor("ones_bf", [1, 512], BF16))
        ones32 = E(nc.sbuf_tensor("ones32", [1, 128], F32))
        brows = E(nc.sbuf_tensor("brows", [1, 2048], BF16))
        wgT = E(nc.sbuf_tensor("wgT", [128, 64], BF16))
        gsc = E(nc.sbuf_tensor("gsc", [128, 32], F32))
        gat_all = E(nc.sbuf_tensor("gat_all", [128, NCOL * 8], F32))
        mask_all = E(nc.sbuf_tensor("mask_all", [128, NCOL * 8], BF16))
        mecb = E(nc.sbuf_tensor("mecb", [128, 64], BF16))
        mcumA = E(nc.sbuf_tensor("mcumA", [128, 64], BF16))
        mcumB = E(nc.sbuf_tensor("mcumB", [128, 64], BF16))
        mshift = E(nc.sbuf_tensor("mshift", [128, 64], BF16))
        tokid = E(nc.sbuf_tensor("tokid", [128, NCOL], I32))
        d12i = E(nc.sbuf_tensor("d12i", [128, 128], I32))
        idxsb = E(nc.sbuf_tensor("idxsb", [128, NE * NCH], I32))
        metag = E(nc.sbuf_tensor("metag", [128, 64], F32))
        d2i = E(nc.sbuf_tensor("d2i", [128, 4], I32))
        sent = E(nc.sbuf_tensor("sent", [1, 16], F32))
        st2 = E(nc.sbuf_tensor("st2", [128, 2], F32))
        vt = E(nc.sbuf_tensor("vt", [128, 4], F32))
        mcol = E(nc.sbuf_tensor("mcol", [128, 8], F32))
        rstdc = E(nc.sbuf_tensor("rstdc", [128, 8], F32))
        rinvc = E(nc.sbuf_tensor("rinvc", [128, 8], F32))
        gtt = E(nc.sbuf_tensor("gtt", [128, 32], F32))
        gs2 = E(nc.sbuf_tensor("gs2", [128, 32], F32))
        gex = E(nc.sbuf_tensor("gex", [128, 32], F32))
        gm1 = E(nc.sbuf_tensor("gm1", [128, 4], F32))
        gm2 = E(nc.sbuf_tensor("gm2", [128, 4], F32))
        gden = E(nc.sbuf_tensor("gden", [128, 4], F32))
        grec = E(nc.sbuf_tensor("grec", [128, 4], F32))
        gmk = E(nc.sbuf_tensor("gmk", [128, 8], F32))
        kss = E(nc.sbuf_tensor("kss", [128, 16], F32))
        kls = E(nc.sbuf_tensor("kls", [128, 16], F32))
        krs = E(nc.sbuf_tensor("krs", [128, 16], F32))

        psA = E(nc.psum_tensor("psA", [128, 6 * 512], F32))
        psT = [E(nc.psum_tensor("psT1", [128, 1024], BF16)),
               E(nc.psum_tensor("psT2", [128, 1024], BF16))]

        itmp = pf[:, PF_SCR:PF_SCR + 128]
        wA = pb[:, 0:8192]
        wB = pb[:, 8192:16384]
        g_bc = pb[:, PB_W1:PB_W1 + 1024]
        b_bc = pb[:, PB_W1 + 1024:PB_W1 + 2048]
        xnT = pb[:, PB_XT:PB_XT + 4096]

        class Ctr:
            __slots__ = ("h", "n", "name")

            def __init__(self, h, name):
                self.h = h
                self.n = 0
                self.name = name

        def sem(name):
            return Ctr(E(nc.semaphore(name)), name)

        sT = sem("sT")
        sV = sem("sV")
        sS = sem("sS")
        sG = sem("sG")
        sX = sem("sX")
        sWAB = sem("sWAB")
        sRO = sem("sRO")
        sO = sem("sO")
        sXB = sem("sXB")
        sZ = sem("sZ")
        sME = sem("sME")
        sO2 = sem("sO2")

        ops = {k: [] for k in ENGS}

        def emit(eng, f):
            ops[eng].append(f)

        def wait(eng, ctr, v):
            assert v <= ctr.n, f"wait on future {ctr.name}: {v} > {ctr.n}"
            if v <= 0:
                return
            emit(eng, lambda e, h=ctr.h, vv=v: e.wait_ge(h, vv))

        def waitp(eng, pair):
            if pair is not None:
                wait(eng, pair[0], pair[1])

        def run(eng, maker, inc=None, k=1):
            if inc is None:
                emit(eng, maker)
                return None
            inc.n += k
            emit(eng, lambda e, m=maker, h=inc.h, kk=k: m(e).then_inc(h, kk))
            return (inc, inc.n)

        def runv(maker):
            p = run("vector", maker, inc=sV)
            wait("vector", sV, p[1])
            return p

        def runsc(maker):
            p = run("scalar", maker, inc=sS)
            wait("scalar", sS, p[1])
            return p

        bank_free = [None] * 8
        bank_ready = [None] * 8

        def bank_ap(b, off=0, width=512):
            return psA[:, b * 512 + off: b * 512 + off + width]

        def emit_t8(src_ap, dst_ap3, par, evac_eng="vector"):
            waitp("tensor", bank_free[6 + par])
            r = None
            for i in range(8):
                r = run("tensor",
                        lambda e, o=psT[par][:, i * 128:(i + 1) * 128],
                        i_=src_ap[:, i * 128:(i + 1) * 128], d=idb[:]:
                        e.transpose(o, i_, d),
                        inc=sT if i == 7 else None)
            bank_ready[6 + par] = r
            waitp(evac_eng, r)
            in_ap = psT[par][:].rearrange("p (i t) -> p i t", t=128)
            if evac_eng == "vector":
                ev = run("vector", lambda e, o=dst_ap3, i_=in_ap:
                         e.tensor_copy(o, i_), inc=sV)
            else:
                ev = run("scalar", lambda e, o=dst_ap3, i_=in_ap:
                         e.activation(o, i_, AF.Identity), inc=sS)
            bank_free[6 + par] = ev
            return ev

        def emit_ln1(slot, tc, need_rinv):
            xoff = PF_SLOT + slot * TC * D + tc * D
            xi = pf[:, xoff: xoff + D]
            dump = pf[:, PF_SCR:PF_SCR + D]
            run("scalar", lambda e, o=dump, i_=xi, a=st2[:, 0:1]:
                e.activation(o, i_, AF.Identity, accum_out=a))
            a1 = run("scalar", lambda e, o=dump, i_=xi, a=st2[:, 1:2]:
                     e.activation(o, i_, AF.Square, accum_out=a), inc=sS)
            waitp("vector", a1)
            mc = mcol[:, slot * 4 + tc: slot * 4 + tc + 1]
            runv(lambda e, o=mc, i_=st2[:, 0:1]:
                 e.tensor_scalar(o, i_, 1.0 / D, None, op0=ALU.mult))
            runv(lambda e, o=vt[:, 0:1], i_=st2[:, 1:2]:
                 e.tensor_scalar(o, i_, 1.0 / D, None, op0=ALU.mult))
            runv(lambda e, o=vt[:, 1:2], i_=mc, j=mc:
                 e.tensor_tensor(o, i_, j, op=ALU.mult))
            runv(lambda e, o=vt[:, 2:3], i_=vt[:, 0:1], j=vt[:, 1:2]:
                 e.tensor_tensor(o, i_, j, op=ALU.subtract))
            v1 = runv(lambda e, o=vt[:, 2:3], i_=vt[:, 2:3]:
                      e.tensor_scalar(o, i_, EPS1, None, op0=ALU.add))
            waitp("scalar", v1)
            runsc(lambda e, o=vt[:, 3:4], i_=vt[:, 2:3]:
                  e.activation(o, i_, AF.Ln))
            rc = rstdc[:, slot * 4 + tc: slot * 4 + tc + 1]
            s1 = runsc(lambda e, o=rc, i_=vt[:, 3:4]:
                       e.activation(o, i_, AF.Exp, scale=-0.5))
            if need_rinv:
                ric = rinvc[:, slot * 4 + tc: slot * 4 + tc + 1]
                s1 = runsc(lambda e, o=ric, i_=vt[:, 3:4]:
                           e.activation(o, i_, AF.Exp, scale=0.5))
            waitp("vector", s1)
            return run("vector", lambda e, o=xi, i_=xi, m=mc, r=rc:
                       e.tensor_scalar(o, i_, m, r, op0=ALU.subtract,
                                       op1=ALU.mult), inc=sV)

        def emit_cast(slot, tc, ln_done, stg_gates):
            xoff = PF_SLOT + slot * TC * D + tc * D
            stg = pb[:, PB_STG + (tc % 2) * 1024: PB_STG + (tc % 2) * 1024 + 1024]
            waitp("scalar", ln_done)
            for gp in stg_gates:
                waitp("scalar", gp)
            c = run("scalar", lambda e, o=stg, i_=pf[:, xoff:xoff + D]:
                    e.activation(o, i_, AF.Identity), inc=sS)
            return c, stg

        run("gpsimd", lambda e: e.iota(itmp[:], [[1, 128]], channel_multiplier=-1,
                                       allow_small_or_imprecise_dtypes=True),
            inc=sG)
        g2 = run("gpsimd", lambda e: e.iota(tokid[:], [[128, NCOL]],
                                            channel_multiplier=1), inc=sG)
        wait("vector", sG, sG.n)
        runv(lambda e, o=ltb[:], i_=itmp[:]:
             e.tensor_scalar(o, i_, 0.0, None, op0=ALU.is_gt))
        runv(lambda e, o=itmp[:], i_=itmp[:]:
             e.tensor_scalar(o, i_, 0.0, None, op0=ALU.is_equal))
        idb_done = runv(lambda e: e.tensor_copy(idb[:], itmp[:]))
        runv(lambda e: e.memset(oneb[:], 1.0))
        runv(lambda e: e.memset(ones_bf[:], 1.0))
        runv(lambda e: e.memset(ones32[:], 1.0))
        runv(lambda e: e.memset(xg[:], 0.0))
        ixm = runv(lambda e: e.memset(idxsb[:], NTOK))
        kvz = runv(lambda e: e.memset(kv[:], 0.0))

        waitp("scalar", kvz)
        for z in range(32):
            run("scalar", lambda e, z=z, o=out2_d[z * 512:(z + 1) * 512, :]
                .rearrange("(p r) d -> p (r d)", p=128), i_=kv[:]:
                e.dma_start(o, i_), inc=sZ, k=16)
        run("scalar", lambda e, o=xnbf_d[NTOK:NTOK + 1, :], i_=kv[0:1, 0:1024]:
            e.dma_start(o, i_), inc=sZ, k=16)
        z_done = (sZ, sZ.n)

        waitp("gpsimd", ixm)
        run("gpsimd", lambda e, o=idx_d[0:NE * CAP].rearrange(
            "(p c) one -> p (c one)", p=128),
            i_=idxsb[:]: e.dma_start(o, i_), inc=sG, k=16)
        ixinit = (sG, sG.n)

        for (w_src, w_dst) in ((wk_h, wA), (wv_h, wB)):
            run("gpsimd", lambda e, s=w_src, d=w_dst:
                e.dma_start(d.rearrange("p (dc j) -> p dc j", j=D),
                            s.rearrange("(dc p) j -> p dc j", p=128)),
                inc=sWAB, k=16)
        wab_phA = (sWAB, sWAB.n)

        for i, bh in enumerate((bk_h, bv_h)):
            run("sync", lambda e, s=bh, o=pf[0:1, PF_S + i * D:PF_S + (i + 1) * D]:
                e.dma_start(o, s), inc=sRO, k=16)
        r1 = (sRO, sRO.n)
        waitp("vector", r1)
        bro = run("vector", lambda e: e.tensor_copy(
            brows[:], pf[0:1, PF_S:PF_S + 2048]), inc=sV)
        waitp("sync", bro)
        run("sync", lambda e: e.dma_start(pf[0:8, PF_S:PF_S + D], wg_h),
            inc=sRO, k=16)
        r2 = (sRO, sRO.n)
        waitp("scalar", r2)
        wgc = run("scalar", lambda e, o=pb[0:8, PB_STG:PB_STG + D],
                  i_=pf[0:8, PF_S:PF_S + D]:
                  e.activation(o, i_, AF.Identity), inc=sS)
        waitp("tensor", wgc)
        for dc in range(DC):
            par = dc % 2
            waitp("tensor", bank_free[6 + par])
            r = run("tensor", lambda e, o=psT[par][:, 0:8],
                    i_=pb[0:8, PB_STG + dc * 128:PB_STG + (dc + 1) * 128],
                    d=idb[0:8, 0:8]: e.transpose(o, i_, d), inc=sT)
            bank_ready[6 + par] = r
            waitp("vector", r)
            bank_free[6 + par] = run(
                "vector", lambda e, o=wgT[:, dc * 8:(dc + 1) * 8],
                i_=psT[par][:, 0:8]: e.tensor_copy(o, i_), inc=sV)
        rows_done = (sV, sV.n)
        wg_t_done = (sT, sT.n)

        wu_ring = [None] * 2
        wu_counter = [0]

        def emit_wu_load(kind, e_i, u):
            slot = wu_counter[0] % 2
            wu_counter[0] += 1
            if kind == 1:
                src = w1_h[e_i, u * 128:(u + 1) * 128, :]
            elif kind == 2:
                src = w2_h[e_i, u * 128:(u + 1) * 128, :]
            else:
                dch, fcol = u // 2, u % 2
                src = w3_h[e_i, dch * 128:(dch + 1) * 128,
                           fcol * 1024:(fcol + 1) * 1024]
            if wu_ring[slot] is not None:
                waitp("gpsimd", wu_ring[slot])
            ld = run("gpsimd", lambda e, o=wstg[:, slot * 1024:(slot + 1) * 1024],
                     s=src: e.dma_start(o, s), inc=sG, k=16)
            return slot, ld

        def emit_wu_xpose(kind, e_i, u, slot, ld, region_gate, par):
            waitp("tensor", ld)
            if kind == 1:
                dst = pb[:, PB_W1:PB_W1 + 16384].rearrange(
                    "p (dc f) -> p dc f", f=F)[:, :, u * 128:(u + 1) * 128]
            elif kind == 2:
                dst = pb[:, PB_W2:PB_W2 + 16384].rearrange(
                    "p (dc f) -> p dc f", f=F)[:, :, u * 128:(u + 1) * 128]
            else:
                dch, fcol = u // 2, u % 2
                dst = pb[:, PB_W3:PB_W3 + 16384].rearrange(
                    "p (fc d) -> p fc d", d=D)[
                    :, fcol * 8:(fcol + 1) * 8, dch * 128:(dch + 1) * 128]
            waitp("scalar", region_gate)
            ev = emit_t8(wstg[:, slot * 1024:(slot + 1) * 1024], dst, par,
                         evac_eng="scalar")
            wu_ring[slot] = ev
            return ev

        w1u = [[None] * FC for _ in range(NE)]
        w2u = [[None] * FC for _ in range(NE)]
        w3done = [None] * NE

        slot_free = [None] * 34
        x_loaded = [None] * 32

        def emit_xload(i):
            slot = i % 2
            t = i % NTILE
            src = x_h[t * TT:(t + 1) * TT, :].rearrange("(tc p) d -> p tc d",
                                                        p=128)
            dst = pf[:, PF_SLOT + slot * TC * D:PF_SLOT + (slot + 1) * TC * D] \
                .rearrange("p (tc d) -> p tc d", d=D)
            if i >= 2:
                assert slot_free[i] is not None
                waitp("sync", slot_free[i])
            x_loaded[i] = run("sync", lambda e, o=dst, s=src: e.dma_start(o, s),
                              inc=sX, k=16)

        emit_xload(0)
        emit_xload(1)

        kv_ready = [None] * BC
        k_first = True
        gse_prev = None
        stg_state = {0: [wg_t_done], 1: []}
        wu_sched_A = []
        for u in range(FC):
            wu_sched_A.append((1, 0, u))
        for u in range(FC):
            wu_sched_A.append((2, 0, u))
        wuA_loaded = []

        for t in range(NTILE):
            b = t // 4
            slot = t % 2
            li = t
            waitp("scalar", x_loaded[li])
            waitp("vector", x_loaded[li])
            if t == 0:
                waitp("scalar", idb_done)
            for tc in range(TC):
                ln_done = emit_ln1(slot, tc, need_rinv=False)
                cpair, stg = emit_cast(slot, tc, ln_done, stg_state[tc % 2])
                waitp("sync", cpair)
                run("sync", lambda e,
                    o=xnbf_d[t * TT + tc * 128:t * TT + (tc + 1) * 128, :],
                    i_=stg: e.dma_start(o, i_), inc=sXB, k=16)
                waitp("tensor", cpair)
                dst3 = xnT.rearrange("p (dc tt) -> p dc tt", tt=TT)[
                    :, :, tc * 128:(tc + 1) * 128]
                ev = emit_t8(stg, dst3, tc % 2, evac_eng="vector")
                stg_state[tc % 2] = [(sT, sT.n), (sXB, sXB.n)]
                xnT_rdy = ev
                waitp("tensor", xnT_rdy)
                tcol = tc * 128
                if k_first:
                    waitp("tensor", wab_phA)
                    k_first = False
                for (bank0, wsb, brow_i) in ((0, wA, 0), (2, wB, 1)):
                    for jh in range(2):
                        bank = bank0 + jh
                        waitp("tensor", bank_free[bank])
                        for dc in range(DC):
                            run("tensor", lambda e, o=bank_ap(bank),
                                l=xnT[:, dc * TT + tcol: dc * TT + tcol + 128],
                                rr=wsb[:, dc * D + jh * 512:
                                       dc * D + (jh + 1) * 512],
                                stf=(dc == 0):
                                e.matmul(o, l, rr, start=stf, stop=False))
                        r = run("tensor", lambda e, o=bank_ap(bank),
                                l=ones_bf[0:1, 0:128],
                                rr=brows[0:1, brow_i * D + jh * 512:
                                         brow_i * D + (jh + 1) * 512]:
                                e.matmul(o, l, rr, start=False, stop=True),
                                inc=sT)
                        bank_ready[bank] = r
                f1 = pf[:, PF_S:PF_S + D]
                f2 = pf[:, PF_S + D:PF_S + 2 * D]
                v1 = None
                for jh in range(2):
                    waitp("vector", bank_ready[jh])
                    v1 = run("vector", lambda e, o=f2[:, jh * 512:(jh + 1) * 512],
                             i_=bank_ap(jh):
                             e.tensor_scalar(o, i_, 0.0, None, op0=ALU.min),
                             inc=sV if jh == 1 else None)
                waitp("scalar", v1)
                s1 = run("scalar", lambda e, o=f1, i_=f2:
                         e.activation(o, i_, AF.Exp), inc=sS)
                waitp("vector", s1)
                kd = None
                for jh in range(2):
                    kd = run("vector", lambda e, o=f1[:, jh * 512:(jh + 1) * 512],
                             i_=bank_ap(jh), j=f1[:, jh * 512:(jh + 1) * 512]:
                             e.scalar_tensor_tensor(o, i_, 0.0, j, op0=ALU.max,
                                                    op1=ALU.add), inc=sV)
                waitp("tensor", kd)
                rsc = None
                for dc in range(DC):
                    rsc = run("tensor", lambda e, o=bank_ap(0, 0, 8),
                              l=xnT[:, dc * TT + tcol: dc * TT + tcol + 128],
                              rr=wgT[:, dc * 8:(dc + 1) * 8],
                              stf=(dc == 0), spf=(dc == DC - 1):
                              e.matmul(o, l, rr, start=stf, stop=spf,
                                       skip_group_check=True),
                              inc=sT if dc == DC - 1 else None)
                waitp("vector", rsc)
                sev = runv(lambda e, o=gsc[:, tc * 8:(tc + 1) * 8],
                           i_=bank_ap(0, 0, 8): e.tensor_copy(o, i_))
                bank_free[0] = sev
                bank_free[1] = kd
                wait("vector", sV, kd[1])
                runv(lambda e, o=f2, i_=f1, j=f1:
                     e.tensor_tensor(o, i_, j, op=ALU.mult))
                runv(lambda e, o=kss[:],
                     i_=f2.rearrange("p (h dd) -> p h dd", dd=64):
                     e.tensor_reduce(o, i_, axis=AX.X, op=ALU.add))
                v2 = runv(lambda e, o=kss[:], i_=kss[:]:
                          e.tensor_scalar(o, i_, EPS2, None, op0=ALU.add))
                waitp("scalar", v2)
                runsc(lambda e, o=kls[:], i_=kss[:]:
                      e.activation(o, i_, AF.Ln))
                s2 = runsc(lambda e, o=krs[:], i_=kls[:]:
                           e.activation(o, i_, AF.Exp, scale=-0.5))
                waitp("vector", s2)
                kb = None
                for h in range(16):
                    kb = run("vector", lambda e,
                             o=pb[:, PB_HID + tc * 1024 + h * 64:
                                PB_HID + tc * 1024 + (h + 1) * 64],
                             i_=f1[:, h * 64:(h + 1) * 64], sc=krs[:, h:h + 1]:
                             e.tensor_scalar(o, i_, sc, None, op0=ALU.mult),
                             inc=sV if h == 15 else None)
                vb = None
                for jh in range(2):
                    waitp("vector", bank_ready[2 + jh])
                    vb = run("vector", lambda e,
                             o=pb[:, PB_HID + 4096 + tc * 1024 + jh * 512:
                                  PB_HID + 4096 + tc * 1024 + (jh + 1) * 512],
                             i_=bank_ap(2 + jh): e.tensor_copy(o, i_), inc=sV)
                bank_free[2] = bank_free[3] = vb
                waitp("tensor", kb)
                waitp("tensor", vb)
                first = (tc == 0 and t % 4 == 0)
                last = (tc == TC - 1 and t % 4 == 3)
                if first:
                    waitp("tensor", bank_free[4])
                    waitp("tensor", bank_free[5])
                    if b == 0:
                        waitp("vector", z_done)
                r = None
                for pbk in range(8):
                    st_f = first and (pbk % 4 == 0)
                    sp_f = last and (pbk % 4 == 3)
                    r = run("tensor", lambda e,
                            o=bank_ap(4 + pbk // 4, (pbk % 4) * 128, 128),
                            l=pb[:, PB_HID + tc * 1024 + pbk * 128:
                                 PB_HID + tc * 1024 + (pbk + 1) * 128],
                            rr=pb[:, PB_HID + 4096 + tc * 1024 + pbk * 128:
                                  PB_HID + 4096 + tc * 1024 + (pbk + 1) * 128],
                            stf=st_f, spf=sp_f:
                            e.matmul(o, l, rr, start=stf, stop=spf,
                                     skip_group_check=True),
                            inc=sT if pbk == 7 else None)
                bank_ready[4] = bank_ready[5] = r
                if tc == TC - 1:
                    slot_free[li + 2] = (sT, sT.n)
            if gse_prev is not None:
                waitp("vector", gse_prev)
            gevac = runv(lambda e, o=gm1[:],
                         i_=gsc[:].rearrange("p (tc k) -> p tc k", k=8):
                         e.tensor_reduce(o, i_, axis=AX.X, op=ALU.max))
            for tc in range(TC):
                sl = gsc[:, tc * 8:(tc + 1) * 8]
                m1 = gm1[:, tc:tc + 1]
                runv(lambda e, o=gtt[:, tc * 8:(tc + 1) * 8], i_=sl,
                     m=m1: e.tensor_scalar(o, i_, m, None, op0=ALU.subtract))
                runv(lambda e, o=gmk[:], i_=sl, m=m1:
                     e.tensor_scalar(o, i_, m, None, op0=ALU.is_ge))
                runv(lambda e, o=gs2[:, tc * 8:(tc + 1) * 8], i_=gmk[:],
                     j=sl: e.scalar_tensor_tensor(o, i_, -BIG, j, op0=ALU.mult,
                                                  op1=ALU.add))
            gv = runv(lambda e, o=gm2[:],
                      i_=gs2[:].rearrange("p (tc k) -> p tc k", k=8):
                      e.tensor_reduce(o, i_, axis=AX.X, op=ALU.max))
            waitp("scalar", gv)
            gse = runsc(lambda e: e.activation(gex[:], gtt[:], AF.Exp))
            gse_prev = gse
            waitp("vector", gse)
            for tc in range(TC):
                sl = gsc[:, tc * 8:(tc + 1) * 8]
                runv(lambda e, o=gmk[:], i_=sl, m=gm2[:, tc:tc + 1]:
                     e.tensor_scalar(o, i_, m, None, op0=ALU.is_ge))
                runv(lambda e, o=mask_all[:, t * 32 + tc * 8:t * 32 + (tc + 1) * 8],
                     i_=gmk[:]: e.tensor_copy(o, i_))
                runv(lambda e, o=gtt[:, tc * 8:(tc + 1) * 8],
                     i_=gex[:, tc * 8:(tc + 1) * 8], j=gmk[:]:
                     e.tensor_tensor(o, i_, j, op=ALU.mult))
            runv(lambda e, o=gden[:],
                 i_=gtt[:].rearrange("p (tc k) -> p tc k", k=8):
                 e.tensor_reduce(o, i_, axis=AX.X, op=ALU.add))
            runv(lambda e: e.reciprocal(grec[:], gden[:]))
            for tc in range(TC):
                runv(lambda e, o=gat_all[:, t * 32 + tc * 8:t * 32 + (tc + 1) * 8],
                     i_=gtt[:, tc * 8:(tc + 1) * 8],
                     m=grec[:, tc:tc + 1]:
                     e.tensor_scalar(o, i_, m, None, op0=ALU.mult))
            if t % 4 == 3:
                waitp("vector", bank_ready[4])
                kvd = None
                for pbk in range(8):
                    base = 2048 + pbk * 128
                    kvo = b * 1024 + pbk * 128
                    run("vector", lambda e, o=kv[0:64, kvo:kvo + 64],
                        i_=psA[0:64, base:base + 64]: e.tensor_copy(o, i_))
                    kvd = run("vector",
                              lambda e, o=kv[64:128, kvo + 64:kvo + 128],
                              i_=psA[64:128, base + 64:base + 128]:
                              e.tensor_copy(o, i_),
                              inc=sV if pbk == 7 else None)
                bank_free[4] = bank_free[5] = kvd
                kv_ready[b] = kvd
            for uu in range(2):
                idx = t * 2 + uu
                if idx < len(wu_sched_A):
                    kind, e_i, u = wu_sched_A[idx]
                    slot, ld = emit_wu_load(kind, e_i, u)
                    ev = emit_wu_xpose(kind, e_i, u, slot, ld, None, idx % 2)
                    if kind == 1:
                        w1u[e_i][u] = ev
                    else:
                        w2u[e_i][u] = ev
            if li + 1 < NTILE and x_loaded[li + 1] is None:
                emit_xload(li + 1)
            if li + 2 < NTILE and slot_free[li + 2] is not None \
                    and x_loaded[li + 2] is None:
                emit_xload(li + 2)

        A_end_T = (sT, sT.n)
        A_end_V = (sV, sV.n)
        A_end_S = (sS, sS.n)
        A_end_XB = (sXB, sXB.n)

        waitp("vector", A_end_S)
        tokf = pf[:, PF_TOKF:PF_TOKF + 64]
        runv(lambda e, o=tokf, i_=tokid[:]: e.tensor_copy(o, i_))
        ecapv = pf[:, PF_ECAP:PF_ECAP + 512].rearrange("p (c e) -> p e c", e=8)
        for e_i in range(NE):
            runv(lambda e, o=ecapv[:, e_i], v=float(e_i * CAP):
                 e.memset(o, v))
        wrun = pf[:, PF_WRUN:PF_WRUN + 64]
        runv(lambda e, o=wrun: e.memset(o, 0.0))
        whichv = pf[:, PF_WHICH:PF_WHICH + 512].rearrange("p (c e) -> p e c", e=8)
        maskv = mask_all[:].rearrange("p (c e) -> p e c", e=8)
        gatv = gat_all[:].rearrange("p (c e) -> p e c", e=8)
        metab = pf[:, PF_META:PF_META + 1024].rearrange("p (c f) -> p f c", f=16)
        for e_i in range(NE):
            runv(lambda e, o=whichv[:, e_i], i_=wrun: e.tensor_copy(o, i_))
            runv(lambda e, o=metab[:, 8 + e_i], i_=tokf, j=wrun:
                 e.scalar_tensor_tensor(o, i_, 2.0, j, op0=ALU.mult,
                                        op1=ALU.add))
            runv(lambda e, o=metab[:, e_i], i_=gatv[:, e_i]:
                 e.tensor_copy(o, i_))
            if e_i < NE - 1:
                runv(lambda e, o=wrun, i_=wrun, j=maskv[:, e_i]:
                     e.tensor_tensor(o, i_, j, op=ALU.add))
        meta_built = (sV, sV.n)
        runv(lambda e: e.memset(sent[:, 0:8], 0.0))
        sb = runv(lambda e: e.memset(sent[:, 8:16], float(2 * NTOK)))
        waitp("sync", meta_built)
        waitp("sync", sb)
        run("sync", lambda e, o=meta_d[0:NTOK, :]
            .rearrange("(c p) f -> p c f", p=128),
            i_=pf[:, PF_META:PF_META + 1024].rearrange(
                "p (c f) -> p c f", f=16): e.dma_start(o, i_), inc=sME, k=16)
        run("sync", lambda e, o=meta_d[NTOK:NTOK + 1, :], i_=sent[:]:
            e.dma_start(o, i_), inc=sME, k=16)
        meta_stored = (sME, sME.n)

        slotav = pf[:, PF_SLOTA:PF_SLOTA + 512].rearrange("p (c e) -> p e c", e=8)
        for e_i in range(NE):
            mcp = runv(lambda e, o=mecb[:], i_=maskv[:, e_i]: e.tensor_copy(o, i_))
            cur, nxt = mecb, mcumA
            for sh in (1, 2, 4, 8, 16, 32):
                runv(lambda e, o=nxt[:, 0:sh], i_=cur[:, 0:sh]:
                     e.tensor_copy(o, i_))
                runv(lambda e, o=nxt[:, sh:64], i_=cur[:, sh:64],
                     j=cur[:, 0:64 - sh]: e.tensor_tensor(o, i_, j, op=ALU.add))
                cur, nxt = (mcumA, mcumB) if cur is mecb else (
                    (mcumB, mcumA) if cur is mcumA else (mcumA, mcumB))
            runv(lambda e, o=mshift[:, 0:1]: e.memset(o, 0.0))
            msf = runv(lambda e, o=mshift[:, 1:64], i_=cur[:, 0:63]:
                       e.tensor_copy(o, i_))
            waitp("tensor", mcp)
            waitp("tensor", msf)
            waitp("tensor", bank_free[0])
            run("tensor", lambda e, o=bank_ap(0, 0, 64), l=ltb[:], rr=mecb[:]:
                e.matmul(o, l, rr, start=True, stop=False))
            mm = run("tensor", lambda e, o=bank_ap(0, 0, 64), l=oneb[:],
                     rr=mshift[:]: e.matmul(o, l, rr, start=False, stop=True),
                     inc=sT)
            bank_ready[0] = mm
            waitp("vector", mm)
            bank_free[0] = runv(lambda e, o=slotav[:, e_i], i_=bank_ap(0, 0, 64):
                                e.tensor_copy(o, i_))
        slota = pf[:, PF_SLOTA:PF_SLOTA + 512]
        basea = pf[:, PF_BASE:PF_BASE + 512]
        tmpa = pf[:, PF_TMP:PF_TMP + 512]
        runv(lambda e, o=basea, i_=slota, j=pf[:, PF_ECAP:PF_ECAP + 512]:
             e.tensor_tensor(o, i_, j, op=ALU.add))
        runv(lambda e, o=tmpa, i_=slota:
             e.tensor_scalar(o, i_, float(CAP), None, op0=ALU.is_ge))
        runv(lambda e, o=basea, i_=tmpa, j=basea:
             e.scalar_tensor_tensor(o, i_, BIGI, j, op0=ALU.mult, op1=ALU.add))
        whicha = pf[:, PF_WHICH:PF_WHICH + 512]
        for k in range(2):
            runv(lambda e, o=tmpa, i_=whicha, v=float(k):
                 e.tensor_scalar(o, i_, v, None, op0=ALU.is_equal))
            runv(lambda e, o=tmpa, i_=tmpa, j=mask_all[:]:
                 e.tensor_tensor(o, i_, j, op=ALU.mult))
            runv(lambda e, o=tmpa, i_=tmpa, j=basea:
                 e.tensor_tensor(o, i_, j, op=ALU.mult))
            runv(lambda e, o=pf[:, PF_D12 + k * 64:PF_D12 + (k + 1) * 64],
                 i_=tmpa.rearrange("p (c e) -> p c e", e=8):
                 e.tensor_reduce(o, i_, axis=AX.X, op=ALU.add))
        runv(lambda e, o=pf[:, PF_D12:PF_D12 + 128],
             i_=pf[:, PF_D12:PF_D12 + 128]:
             e.tensor_scalar(o, i_, float(NE * CAP), None, op0=ALU.min))
        d12c = runv(lambda e, o=d12i[:], i_=pf[:, PF_D12:PF_D12 + 128]:
                    e.tensor_copy(o, i_))

        waitp("gpsimd", ixinit)
        waitp("gpsimd", d12c)
        for c in range(NCOL):
            for k in range(2):
                run("gpsimd", lambda e, c=c, k=k: e.indirect_dma_start(
                    idx_d,
                    IndirectOffsetOnAxis(ap=d12i[:, k * 64 + c:k * 64 + c + 1],
                                         axis=0),
                    tokid[:, c:c + 1], None),
                    inc=sG, k=16)
        scat_done = (sG, sG.n)
        wait("gpsimd", sG, sG.n)
        idx_loaded = [None] * NE
        for e_i in range(NE):
            idx_loaded[e_i] = run(
                "gpsimd", lambda e, e_i=e_i:
                e.dma_start(idxsb[:, e_i * NCH:(e_i + 1) * NCH],
                            idx_d[e_i * CAP:(e_i + 1) * CAP].rearrange(
                                "(p c) one -> p (c one)", p=128)),
                inc=sG, k=16)
        R_end_V = (sV, sV.n)

        w1sb = pb[:, PB_W1:PB_W1 + 16384]
        w2sb = pb[:, PB_W2:PB_W2 + 16384]
        w3sb = pb[:, PB_W3:PB_W3 + 16384]
        xeT = pb[:, PB_XT:PB_XT + 4096]
        hid = pb[:, PB_HID:PB_HID + 8192]
        eobs = [pb[:, PB_STG:PB_STG + 1024],
                pb[:, PB_STG + 1024:PB_STG + 2048]]

        xg_gate = [None, None]
        eob_gate = [None, None]
        last_h1 = [None] * NE
        last_h2 = [None] * NE
        last_eo = [None] * NE
        metag_gate = None
        par_ctr = [0]

        def next_par():
            par_ctr[0] ^= 1
            return par_ctr[0]

        for u in range(FC):
            slot, ld = emit_wu_load(3, 0, u)
            emit_wu_xpose(3, 0, u, slot, ld, A_end_T, next_par())
        w3done[0] = (sS, sS.n)

        prev_xeT_gate = None

        for e_i in range(NE):
            for ch in range(CHK):
                if metag_gate is not None:
                    waitp("gpsimd", metag_gate)
                waitp("gpsimd", idx_loaded[e_i])
                waitp("gpsimd", meta_stored)
                gath = [None] * 4
                mg = [None] * 4
                for g in range(4):
                    col = e_i * NCH + ch * 4 + g
                    if xg_gate[g % 2] is not None:
                        waitp("gpsimd", xg_gate[g % 2])
                    gath[g] = run("gpsimd", lambda e, col=col, g=g:
                                  e.indirect_dma_start(
                                      xg[:, (g % 2) * 1024:(g % 2) * 1024 + 1024],
                                      None, xnbf_d,
                                      IndirectOffsetOnAxis(
                                          ap=idxsb[:, col:col + 1], axis=0)),
                                  inc=sG, k=16)
                    mg[g] = run("gpsimd", lambda e, col=col, g=g:
                                e.indirect_dma_start(
                                    metag[:, g * 16:(g + 1) * 16], None, meta_d,
                                    IndirectOffsetOnAxis(
                                        ap=idxsb[:, col:col + 1], axis=0)),
                                inc=sG, k=16)
                    waitp("tensor", gath[g])
                    if prev_xeT_gate is not None:
                        waitp("tensor", prev_xeT_gate)
                    dst3 = xeT.rearrange("p (dc tt) -> p dc tt", tt=512)[
                        :, :, g * 128:(g + 1) * 128]
                    ev = emit_t8(xg[:, (g % 2) * 1024:(g % 2) * 1024 + 1024],
                                 dst3, next_par(), evac_eng="vector")
                    xg_gate[g % 2] = ev
                prev_xeT_gate = None
                xeT_rdy = (sV, sV.n)
                mgall = mg[3]
                mult_last = None
                for fc in range(FC):
                    bank1 = fc % 2
                    bank2 = 2 + fc % 2
                    waitp("tensor", w1u[e_i][fc])
                    waitp("tensor", bank_free[bank1])
                    if fc == 0:
                        waitp("tensor", xeT_rdy)
                    r1_ = None
                    for dc in range(DC):
                        r1_ = run("tensor", lambda e, o=bank_ap(bank1),
                                  l=w1sb[:, dc * F + fc * 128:
                                         dc * F + fc * 128 + 128],
                                  rr=xeT[:, dc * 512:(dc + 1) * 512],
                                  stf=(dc == 0), spf=(dc == DC - 1):
                                  e.matmul(o, l, rr, start=stf, stop=spf),
                                  inc=sT if dc == DC - 1 else None)
                    bank_ready[bank1] = r1_
                    waitp("tensor", w2u[e_i][fc])
                    waitp("tensor", bank_free[bank2])
                    r2_ = None
                    for dc in range(DC):
                        r2_ = run("tensor", lambda e, o=bank_ap(bank2),
                                  l=w2sb[:, dc * F + fc * 128:
                                         dc * F + fc * 128 + 128],
                                  rr=xeT[:, dc * 512:(dc + 1) * 512],
                                  stf=(dc == 0), spf=(dc == DC - 1):
                                  e.matmul(o, l, rr, start=stf, stop=spf),
                                  inc=sT if dc == DC - 1 else None)
                    bank_ready[bank2] = r2_
                    if fc == FC - 1:
                        if ch == CHK - 1:
                            last_h1[e_i] = r1_
                            last_h2[e_i] = r2_
                        prev_xeT_gate = r2_
                    waitp("scalar", bank_ready[bank1])
                    hsl = hid[:, fc * 512:(fc + 1) * 512]
                    silu_l = run("scalar", lambda e, o=hsl, i_=bank_ap(bank1):
                                 e.activation(o, i_, AF.Silu), inc=sS)
                    bank_free[bank1] = silu_l
                    waitp("vector", silu_l)
                    waitp("vector", bank_ready[bank2])
                    mult_last = run("vector", lambda e, o=hsl, i_=hsl,
                                    j=bank_ap(bank2):
                                    e.tensor_tensor(o, i_, j, op=ALU.mult),
                                    inc=sV)
                    bank_free[bank2] = mult_last
                waitp("tensor", mult_last)
                if ch == 0:
                    waitp("tensor", w3done[e_i])
                waitp("vector", mgall)
                eo_evac_last = None
                for g in range(4):
                    for dh in range(2):
                        bank = 4 + dh
                        waitp("tensor", bank_free[bank])
                        r = None
                        for fc in range(FC):
                            r = run("tensor", lambda e, o=bank_ap(bank),
                                    l=hid[:, fc * 512 + g * 128:
                                         fc * 512 + g * 128 + 128],
                                    rr=w3sb[:, fc * D + dh * 512:
                                           fc * D + (dh + 1) * 512],
                                    stf=(fc == 0), spf=(fc == FC - 1):
                                    e.matmul(o, l, rr, start=stf, stop=spf),
                                    inc=sT if fc == FC - 1 else None)
                        bank_ready[bank] = r
                        if e_i == NE - 1 and ch == CHK - 1 and g == 3 and dh == 1:
                            last_eo[e_i] = r
                    es = g % 2
                    if eob_gate[es] is not None:
                        waitp("vector", eob_gate[es])
                    gcol = metag[:, g * 16 + e_i:g * 16 + e_i + 1]
                    ev = None
                    for dh in range(2):
                        waitp("vector", bank_ready[4 + dh])
                        ev = run("vector", lambda e,
                                 o=eobs[es][:, dh * 512:(dh + 1) * 512],
                                 i_=bank_ap(4 + dh), m=gcol:
                                 e.tensor_scalar(o, i_, m, None, op0=ALU.mult),
                                 inc=sV)
                        bank_free[4 + dh] = ev
                    dcast = runv(lambda e, o=d2i[:, g:g + 1],
                                 i_=metag[:, g * 16 + 8 + e_i:
                                          g * 16 + 8 + e_i + 1]:
                                 e.tensor_copy(o, i_))
                    eo_evac_last = dcast
                    waitp("gpsimd", ev)
                    waitp("gpsimd", dcast)
                    sc = run("gpsimd", lambda e, es=es, g=g:
                             e.indirect_dma_start(
                                 out2_d,
                                 IndirectOffsetOnAxis(ap=d2i[:, g:g + 1], axis=0),
                                 eobs[es][:], None),
                             inc=sG, k=16)
                    eob_gate[es] = sc
                metag_gate = eo_evac_last
                if e_i == NE - 1 and ch == CHK - 1:
                    last_eo[e_i] = bank_ready[5]
            if e_i + 1 < NE:
                for u in range(FC):
                    slot, ld = emit_wu_load(1, e_i + 1, u)
                    w1u[e_i + 1][u] = emit_wu_xpose(
                        1, e_i + 1, u, slot, ld, last_h1[e_i], next_par())
                for u in range(FC):
                    slot, ld = emit_wu_load(2, e_i + 1, u)
                    w2u[e_i + 1][u] = emit_wu_xpose(
                        2, e_i + 1, u, slot, ld, last_h2[e_i], next_par())
                gate = bank_ready[5]
                for u in range(FC):
                    slot, ld = emit_wu_load(3, e_i + 1, u)
                    emit_wu_xpose(3, e_i + 1, u, slot, ld, gate, next_par())
                w3done[e_i + 1] = (sS, sS.n)

        C_end_eo = last_eo[NE - 1]
        C_scat_all = (sG, sG.n)

        waitp("gpsimd", C_end_eo)
        for (w_src, w_dst) in ((wq_h, wA), (wo_h, wB)):
            run("gpsimd", lambda e, s=w_src, d=w_dst:
                e.dma_start(d.rearrange("p (dc j) -> p dc j", j=D),
                            s.rearrange("(dc p) j -> p dc j", p=128)),
                inc=sWAB, k=16)
        wab_phB = (sWAB, sWAB.n)

        run("sync", lambda e: e.dma_start(pf[0:1, PF_S:PF_S + D], gam_h),
            inc=sRO, k=16)
        run("sync", lambda e: e.dma_start(pf[0:1, PF_S + D:PF_S + 2 * D], bet_h),
            inc=sRO, k=16)
        r3 = (sRO, sRO.n)
        waitp("tensor", r3)
        for j, dst in ((0, g_bc), (1, b_bc)):
            for dh in range(2):
                bank = 2 + dh
                waitp("tensor", bank_free[bank])
                r = run("tensor", lambda e, o=bank_ap(bank), l=ones32[0:1, :],
                        rr=pf[0:1, PF_S + j * D + dh * 512:
                              PF_S + j * D + dh * 512 + 512]:
                        e.matmul(o, l, rr, start=True, stop=True), inc=sT)
                bank_ready[bank] = r
                waitp("vector", r)
                waitp("vector", last_h1[NE - 1])
                bank_free[bank] = run(
                    "vector", lambda e, o=dst[:, dh * 512:(dh + 1) * 512],
                    i_=bank_ap(bank): e.tensor_copy(o, i_), inc=sV)
        gb_done = (sV, sV.n)

        for i, bh in enumerate((bq_h, bo_h)):
            run("sync", lambda e, s=bh, o=pf[0:1, PF_S + 2048 + i * D:
                                            PF_S + 2048 + (i + 1) * D]:
                e.dma_start(o, s), inc=sRO, k=16)
        rb = (sRO, sRO.n)
        waitp("vector", rb)
        waitp("vector", A_end_T)
        brB = run("vector", lambda e: e.tensor_copy(
            brows[:], pf[0:1, PF_S + 2048:PF_S + 4096]), inc=sV)

        slot_free[16] = R_end_V
        slot_free[17] = R_end_V
        waitp("sync", meta_stored)
        waitp("sync", R_end_V)
        emit_xload(16)
        emit_xload(17)

        waitp("vector", C_end_eo)
        waitp("scalar", eob_gate[0])
        waitp("scalar", eob_gate[1])
        waitp("sync", C_scat_all)
        waitp("scalar", C_scat_all)

        qT = pb[:, PB_HID:PB_HID + 4096]
        gbf = pb[:, PB_HID + 4096:PB_HID + 8192]
        ssb = pf[:, PF_S:PF_S + 4096]
        scr = pf[:, PF_SCR:PF_SCR + 1024]
        o2_gate = [None, None]

        stg_stateB = {0: [], 1: []}
        for t in range(NTILE):
            b = t // 4
            slot = t % 2
            li = 16 + t
            waitp("scalar", x_loaded[li])
            waitp("vector", x_loaded[li])
            if t > 0:
                wait("vector", sO, 16 * 4 * t)
                wait("scalar", sO, 16 * 4 * t)
            for tc in range(TC):
                ln_done = emit_ln1(slot, tc, need_rinv=True)
                cpair, stg = emit_cast(slot, tc, ln_done, stg_stateB[tc % 2])
                waitp("tensor", cpair)
                dst3 = xnT.rearrange("p (dc tt) -> p dc tt", tt=TT)[
                    :, :, tc * 128:(tc + 1) * 128]
                ev = emit_t8(stg, dst3, tc % 2, evac_eng="vector")
                stg_stateB[tc % 2] = [(sT, sT.n)]
            xnT_rdy = (sV, sV.n)
            waitp("tensor", xnT_rdy)
            if t == 0:
                waitp("tensor", wab_phB)
                waitp("tensor", brB)
            qT_w = [None] * JC
            t1 = scr[:, 0:512]
            t2 = scr[:, 512:1024]
            for jc in range(JC):
                bank = jc % 2
                waitp("tensor", bank_free[bank])
                for dc in range(DC):
                    run("tensor", lambda e, o=bank_ap(bank),
                        l=wA[:, dc * D + jc * 128: dc * D + jc * 128 + 128],
                        rr=xnT[:, dc * TT:(dc + 1) * TT], stf=(dc == 0):
                        e.matmul(o, l, rr, start=stf, stop=False))
                r = run("tensor", lambda e, o=bank_ap(bank),
                        l=brows[0:1, jc * 128:(jc + 1) * 128],
                        rr=ones_bf[0:1, 0:512]:
                        e.matmul(o, l, rr, start=False, stop=True), inc=sT)
                bank_ready[bank] = r
                waitp("vector", r)
                v1 = run("vector", lambda e, o=t1, i_=bank_ap(bank):
                         e.tensor_scalar(o, i_, 0.0, None, op0=ALU.min), inc=sV)
                waitp("scalar", v1)
                s1 = run("scalar", lambda e, o=t2, i_=t1:
                         e.activation(o, i_, AF.Exp), inc=sS)
                waitp("vector", s1)
                qT_w[jc] = run(
                    "vector", lambda e, o=qT[:, jc * TT:(jc + 1) * TT],
                    i_=bank_ap(bank), j=t2:
                    e.scalar_tensor_tensor(o, i_, 0.0, j, op0=ALU.max,
                                           op1=ALU.add), inc=sV)
                bank_free[bank] = qT_w[jc]
            waitp("tensor", kv_ready[b])
            for jc in range(JC):
                bank = 2 + jc % 2
                waitp("tensor", bank_free[bank])
                waitp("tensor", qT_w[jc])
                r = run("tensor", lambda e, o=bank_ap(bank),
                        l=kv[:, b * 1024 + jc * 128: b * 1024 + (jc + 1) * 128],
                        rr=pb[:, PB_HID + jc * TT:PB_HID + (jc + 1) * TT]:
                        e.matmul(o, l, rr, start=True, stop=True), inc=sT)
                bank_ready[bank] = r
                waitp("vector", r)
                bank_free[bank] = run(
                    "vector", lambda e,
                    o=pb[:, PB_HID + jc * TT:PB_HID + (jc + 1) * TT],
                    i_=bank_ap(bank): e.tensor_copy(o, i_), inc=sV)
            attnT_ready = (sV, sV.n)
            for tc in range(TC):
                for dh in range(2):
                    bank = 4 + dh
                    waitp("tensor", bank_free[bank])
                    if tc == 0 and dh == 0:
                        waitp("tensor", attnT_ready)
                    for jc in range(JC):
                        run("tensor", lambda e, o=bank_ap(bank),
                            l=pb[:, PB_HID + jc * TT + tc * 128:
                                 PB_HID + jc * TT + tc * 128 + 128],
                            rr=wB[:, jc * D + dh * 512: jc * D + (dh + 1) * 512],
                            stf=(jc == 0):
                            e.matmul(o, l, rr, start=stf, stop=False))
                    r = run("tensor", lambda e, o=bank_ap(bank),
                            l=ones_bf[0:1, 0:128],
                            rr=brows[0:1, D + dh * 512:
                                     D + (dh + 1) * 512]:
                            e.matmul(o, l, rr, start=False, stop=True), inc=sT)
                    bank_ready[bank] = r
                xoff = PF_SLOT + slot * TC * D + tc * D
                ssl = ssb[:, tc * D:(tc + 1) * D]
                yv = None
                for dh in range(2):
                    waitp("vector", bank_ready[4 + dh])
                    yv = run("vector", lambda e,
                             o=ssl[:, dh * 512:(dh + 1) * 512],
                             i_=pf[:, xoff + dh * 512: xoff + (dh + 1) * 512],
                             j=bank_ap(4 + dh):
                             e.tensor_tensor(o, i_, j, op=ALU.add), inc=sV)
                    bank_free[4 + dh] = yv
                waitp("scalar", yv)
                run("scalar", lambda e, o=scr[:], i_=ssl, a=st2[:, 0:1]:
                    e.activation(o, i_, AF.Identity, accum_out=a))
                a1 = run("scalar", lambda e, o=scr[:], i_=ssl,
                         a=st2[:, 1:2]:
                         e.activation(o, i_, AF.Square, accum_out=a), inc=sS)
                waitp("vector", a1)
                runv(lambda e, o=vt[:, 0:1], i_=st2[:, 0:1]:
                     e.tensor_scalar(o, i_, 1.0 / D, None, op0=ALU.mult))
                runv(lambda e, o=vt[:, 1:2], i_=st2[:, 1:2]:
                     e.tensor_scalar(o, i_, 1.0 / D, None, op0=ALU.mult))
                runv(lambda e, o=vt[:, 2:3], i_=vt[:, 0:1],
                     j=vt[:, 0:1]: e.tensor_tensor(o, i_, j, op=ALU.mult))
                runv(lambda e, o=vt[:, 3:4], i_=vt[:, 1:2],
                     j=vt[:, 2:3]:
                     e.tensor_tensor(o, i_, j, op=ALU.subtract))
                v2 = runv(lambda e, o=vt[:, 3:4], i_=vt[:, 3:4]:
                          e.tensor_scalar(o, i_, EPS2, None, op0=ALU.add))
                waitp("scalar", v2)
                runsc(lambda e, o=vt[:, 1:2], i_=vt[:, 3:4]:
                      e.activation(o, i_, AF.Ln))
                s2 = runsc(lambda e, o=vt[:, 2:3], i_=vt[:, 1:2]:
                           e.activation(o, i_, AF.Exp, scale=-0.5))
                waitp("vector", s2)
                if t == 0 and tc == 0:
                    waitp("vector", gb_done)
                run("vector", lambda e, o=ssl, i_=ssl, m=vt[:, 0:1],
                    r_=vt[:, 2:3]:
                    e.tensor_scalar(o, i_, m, r_, op0=ALU.subtract,
                                    op1=ALU.mult))
                run("vector", lambda e, o=ssl, i_=ssl, j=g_bc[:]:
                    e.tensor_tensor(o, i_, j, op=ALU.mult))
                run("vector", lambda e, o=ssl, i_=ssl, j=b_bc[:]:
                    e.tensor_tensor(o, i_, j, op=ALU.add))
            waitp("vector", (sS, sS.n))
            ep_v = None
            gs_prev = None
            for tc in range(TC):
                es = tc % 2
                if o2_gate[es] is not None:
                    waitp("sync", o2_gate[es])
                r0 = t * TT + tc * 128
                o2buf = wstg[:, 0:2048] if es == 0 else xg[:, 0:2048]
                o2ld = run("sync", lambda e, o=o2buf,
                           s=out2_d[2 * r0:2 * (r0 + 128), :].rearrange(
                               "(p r) d -> p (r d)", p=128):
                           e.dma_start(o, s), inc=sO2, k=16)
                waitp("vector", o2ld)
                waitp("vector", gs_prev)
                ofn = runv(lambda e, o=scr[:], i_=o2buf[:, 0:1024],
                           j=o2buf[:, 1024:2048]:
                           e.tensor_tensor(o, i_, j, op=ALU.add))
                o2_gate[es] = ofn
                waitp("scalar", ofn)
                gsl = gbf[:, tc * 1024:(tc + 1) * 1024]
                gs_ = run("scalar", lambda e, o=gsl, i_=scr[:]:
                          e.activation(o, i_, AF.Silu), inc=sS)
                gs_prev = gs_
                waitp("vector", gs_)
                ssl = ssb[:, tc * D:(tc + 1) * D]
                xoff = PF_SLOT + slot * TC * D + tc * D
                xsl = pf[:, xoff: xoff + D]
                run("vector", lambda e, o=ssl, i_=ssl, j=gsl:
                    e.tensor_tensor(o, i_, j, op=ALU.mult))
                run("vector", lambda e, o=xsl, i_=xsl,
                    m=rinvc[:, slot * 4 + tc: slot * 4 + tc + 1],
                    mm=mcol[:, slot * 4 + tc: slot * 4 + tc + 1]:
                    e.tensor_scalar(o, i_, m, mm, op0=ALU.mult, op1=ALU.add))
                ep_v = run("vector", lambda e, o=ssl, i_=ssl, j=xsl:
                           e.tensor_tensor(o, i_, j, op=ALU.add), inc=sV)
                waitp("sync", ep_v)
                run("sync", lambda e,
                    o=out_h[t * TT + tc * 128: t * TT + (tc + 1) * 128, :],
                    s=ssb[:, tc * D:(tc + 1) * D]: e.dma_start(o, s),
                    inc=sO, k=16)
            slot_free[li + 2] = ep_v
            if li + 2 < 32 and x_loaded[li + 2] is None:
                emit_xload(li + 2)

        wait("sync", sO, sO.n)
        assert sO.n == 16 * NTILE * TC, sO.n

        with nc.Block() as block:
            @block.sync
            def _(e):
                for f_ in ops["sync"]:
                    f_(e)

            @block.scalar
            def _(e):
                for f_ in ops["scalar"]:
                    f_(e)

            @block.vector
            def _(e):
                for f_ in ops["vector"]:
                    f_(e)

            @block.tensor
            def _(e):
                for f_ in ops["tensor"]:
                    f_(e)

            @block.gpsimd
            def _(e):
                for f_ in ops["gpsimd"]:
                    f_(e)

    return nc


def kernel(**inputs):
    x = np.ascontiguousarray(inputs["L0_attn_x"], dtype=np.float32)
    base = {
        "Wq": np.ascontiguousarray(inputs["L0_attn_Wq"], np.float32),
        "Wk": np.ascontiguousarray(inputs["L0_attn_Wk"], np.float32),
        "Wv": np.ascontiguousarray(inputs["L0_attn_Wv"], np.float32),
        "Wo": np.ascontiguousarray(inputs["L0_attn_Wo"], np.float32),
        "bq": np.ascontiguousarray(inputs["L0_attn_bq"], np.float32).reshape(1, D),
        "bk": np.ascontiguousarray(inputs["L0_attn_bk"], np.float32).reshape(1, D),
        "bv": np.ascontiguousarray(inputs["L0_attn_bv"], np.float32).reshape(1, D),
        "bo": np.ascontiguousarray(inputs["L0_attn_bo"], np.float32).reshape(1, D),
        "gamma": np.ascontiguousarray(
            inputs["L0_attn_gamma"], np.float32).reshape(1, D),
        "beta": np.ascontiguousarray(
            inputs["L0_attn_beta"], np.float32).reshape(1, D),
        "Wg": np.ascontiguousarray(inputs["L0_ffn_Wg"], np.float32),
        "W1": np.ascontiguousarray(inputs["L0_ffn_W1"], np.float32),
        "W2": np.ascontiguousarray(inputs["L0_ffn_W2"], np.float32),
        "W3": np.ascontiguousarray(inputs["L0_ffn_W3"], np.float32),
    }
    in_maps = []
    for c in range(NC):
        m = dict(base)
        m["x"] = np.ascontiguousarray(x[c * BC:(c + 1) * BC].reshape(NTOK, D))
        in_maps.append(m)
    nc = build_nc()
    res = run_bass_kernel_spmd(nc, in_maps, core_ids=list(range(NC)))
    out = np.concatenate(
        [np.asarray(r["out"], dtype=np.float32).reshape(BC, S, D)
         for r in res.results], axis=0)
    return out


# version 07
# speedup vs baseline: 1.6483x; 1.0052x over previous
import numpy as np
from contextlib import ExitStack

import concourse.bass as bass
import concourse.mybir as mybir
from concourse.bass_utils import run_bass_kernel_spmd
from concourse.bass import IndirectOffsetOnAxis

F32 = mybir.dt.float32
BF16 = mybir.dt.bfloat16
I32 = mybir.dt.int32
AF = mybir.ActivationFunctionType
ALU = mybir.AluOpType
AX = mybir.AxisListType

NC = 8
BC = 4
S = 2048
D = 1024
NTOK = BC * S
TT = 512
NTILE = NTOK // TT
TC = TT // 128
DC = D // 128
JC = 8
F = 2048
FC = F // 128
NE = 8
CAP = 3072
NCH = CAP // 128
CHK = CAP // 512
NCOL = NTOK // 128
EPS1 = 1e-5
EPS2 = 1e-6
BIG = 1.0e30
BIGI = float(2 ** 20)

ENGS = ("sync", "scalar", "vector", "tensor", "gpsimd")

PB_W3 = 0
PB_W1 = 16384
PB_W2 = 32768
PB_XT = 49152
PB_STG = 53248
PB_HID = 55296
PB_TOT = 63488

PF_SLOT = 0
PF_S = 8192
PF_SCR = 12288
PF_TOT = 13312
PF_META = 0
PF_WRUN = 1024
PF_TOKF = 1088
PF_SLOTA = 1152
PF_WHICH = 1664
PF_BASE = 2176
PF_TMP = 2688
PF_D12 = 3200
PF_ECAP = 3328


def build_nc():
    nc = bass.Bass("TRN2", target_bir_lowering=False)

    x_h = nc.dram_tensor("x", [NTOK, D], F32, kind="ExternalInput").ap()
    out_h = nc.dram_tensor("out", [NTOK, D], F32, kind="ExternalOutput").ap()
    wq_h = nc.dram_tensor("Wq", [D, D], F32, kind="ExternalInput").ap()
    wk_h = nc.dram_tensor("Wk", [D, D], F32, kind="ExternalInput").ap()
    wv_h = nc.dram_tensor("Wv", [D, D], F32, kind="ExternalInput").ap()
    wo_h = nc.dram_tensor("Wo", [D, D], F32, kind="ExternalInput").ap()
    bq_h = nc.dram_tensor("bq", [1, D], F32, kind="ExternalInput").ap()
    bk_h = nc.dram_tensor("bk", [1, D], F32, kind="ExternalInput").ap()
    bv_h = nc.dram_tensor("bv", [1, D], F32, kind="ExternalInput").ap()
    bo_h = nc.dram_tensor("bo", [1, D], F32, kind="ExternalInput").ap()
    gam_h = nc.dram_tensor("gamma", [1, D], F32, kind="ExternalInput").ap()
    bet_h = nc.dram_tensor("beta", [1, D], F32, kind="ExternalInput").ap()
    wg_h = nc.dram_tensor("Wg", [NE, D], F32, kind="ExternalInput").ap()
    w1_h = nc.dram_tensor("W1", [NE, F, D], F32, kind="ExternalInput").ap()
    w2_h = nc.dram_tensor("W2", [NE, F, D], F32, kind="ExternalInput").ap()
    w3_h = nc.dram_tensor("W3", [NE, D, F], F32, kind="ExternalInput").ap()
    xnbf_d = nc.dram_tensor("xnbf", [NTOK + 1, D], BF16, kind="Internal").ap()
    xnt_d = nc.dram_tensor("xnt", [NTILE, 128, 4096], BF16, kind="Internal").ap()
    meta_d = nc.dram_tensor("meta", [NTOK + 1, 16], F32, kind="Internal").ap()
    idx_d = nc.dram_tensor("idxd", [NE * CAP + 1, 1], I32, kind="Internal").ap()
    out2_d = nc.dram_tensor("out2", [2 * NTOK + 1, D], BF16, kind="Internal").ap()

    ctx = ExitStack()
    with ctx:
        E = ctx.enter_context
        pf = E(nc.sbuf_tensor("pf", [128, PF_TOT], F32))
        pb = E(nc.sbuf_tensor("pb", [128, PB_TOT], BF16))
        kv = E(nc.sbuf_tensor("kv", [128, BC * 1024], BF16))
        wstg = E(nc.sbuf_tensor("wstg", [128, 2048], BF16))
        xg = E(nc.sbuf_tensor("xg", [128, 2048], BF16))
        idb = E(nc.sbuf_tensor("idb", [128, 128], BF16))
        ltb = E(nc.sbuf_tensor("ltb", [128, 128], BF16))
        oneb = E(nc.sbuf_tensor("oneb", [128, 128], BF16))
        ones_bf = E(nc.sbuf_tensor("ones_bf", [1, 512], BF16))
        ones32 = E(nc.sbuf_tensor("ones32", [1, 128], F32))
        brows = E(nc.sbuf_tensor("brows", [1, 2048], BF16))
        wgT = E(nc.sbuf_tensor("wgT", [128, 64], BF16))
        gsc = E(nc.sbuf_tensor("gsc", [128, 32], F32))
        gat_all = E(nc.sbuf_tensor("gat_all", [128, NCOL * 8], F32))
        mask_all = E(nc.sbuf_tensor("mask_all", [128, NCOL * 8], BF16))
        mecb = E(nc.sbuf_tensor("mecb", [128, 64], BF16))
        mcumA = E(nc.sbuf_tensor("mcumA", [128, 64], BF16))
        mcumB = E(nc.sbuf_tensor("mcumB", [128, 64], BF16))
        mshift = E(nc.sbuf_tensor("mshift", [128, 64], BF16))
        tokid = E(nc.sbuf_tensor("tokid", [128, NCOL], I32))
        d12i = E(nc.sbuf_tensor("d12i", [128, 128], I32))
        idxsb = E(nc.sbuf_tensor("idxsb", [128, NE * NCH], I32))
        metag = E(nc.sbuf_tensor("metag", [128, 64], F32))
        d2i = E(nc.sbuf_tensor("d2i", [128, 4], I32))
        sent = E(nc.sbuf_tensor("sent", [1, 16], F32))
        st2 = E(nc.sbuf_tensor("st2", [128, 2], F32))
        vt = E(nc.sbuf_tensor("vt", [128, 4], F32))
        mcol = E(nc.sbuf_tensor("mcol", [128, 64], F32))
        rstdc = E(nc.sbuf_tensor("rstdc", [128, 64], F32))
        rinvc = E(nc.sbuf_tensor("rinvc", [128, 64], F32))
        gtt = E(nc.sbuf_tensor("gtt", [128, 32], F32))
        gs2 = E(nc.sbuf_tensor("gs2", [128, 32], F32))
        gex = E(nc.sbuf_tensor("gex", [128, 32], F32))
        gm1 = E(nc.sbuf_tensor("gm1", [128, 4], F32))
        gm2 = E(nc.sbuf_tensor("gm2", [128, 4], F32))
        gden = E(nc.sbuf_tensor("gden", [128, 4], F32))
        grec = E(nc.sbuf_tensor("grec", [128, 4], F32))
        gmk = E(nc.sbuf_tensor("gmk", [128, 8], F32))
        kss = E(nc.sbuf_tensor("kss", [128, 16], F32))
        kls = E(nc.sbuf_tensor("kls", [128, 16], F32))
        krs = E(nc.sbuf_tensor("krs", [128, 16], F32))

        psA = E(nc.psum_tensor("psA", [128, 6 * 512], F32))
        psT = [E(nc.psum_tensor("psT1", [128, 1024], BF16)),
               E(nc.psum_tensor("psT2", [128, 1024], BF16))]

        itmp = pf[:, PF_SCR:PF_SCR + 128]
        wA = pb[:, 0:8192]
        wB = pb[:, 8192:16384]
        g_bc = pb[:, PB_W1:PB_W1 + 1024]
        b_bc = pb[:, PB_W1 + 1024:PB_W1 + 2048]
        xnT = pb[:, PB_XT:PB_XT + 4096]

        class Ctr:
            __slots__ = ("h", "n", "name")

            def __init__(self, h, name):
                self.h = h
                self.n = 0
                self.name = name

        def sem(name):
            return Ctr(E(nc.semaphore(name)), name)

        sT = sem("sT")
        sV = sem("sV")
        sS = sem("sS")
        sG = sem("sG")
        sX = sem("sX")
        sWAB = sem("sWAB")
        sRO = sem("sRO")
        sO = sem("sO")
        sXB = sem("sXB")
        sZ = sem("sZ")
        sME = sem("sME")
        sO2 = sem("sO2")
        sXT = sem("sXT")

        ops = {k: [] for k in ENGS}

        def emit(eng, f):
            ops[eng].append(f)

        def wait(eng, ctr, v):
            assert v <= ctr.n, f"wait on future {ctr.name}: {v} > {ctr.n}"
            if v <= 0:
                return
            emit(eng, lambda e, h=ctr.h, vv=v: e.wait_ge(h, vv))

        def waitp(eng, pair):
            if pair is not None:
                wait(eng, pair[0], pair[1])

        def run(eng, maker, inc=None, k=1):
            if inc is None:
                emit(eng, maker)
                return None
            inc.n += k
            emit(eng, lambda e, m=maker, h=inc.h, kk=k: m(e).then_inc(h, kk))
            return (inc, inc.n)

        def runv(maker):
            p = run("vector", maker, inc=sV)
            wait("vector", sV, p[1])
            return p

        def runsc(maker):
            p = run("scalar", maker, inc=sS)
            wait("scalar", sS, p[1])
            return p

        bank_free = [None] * 8
        bank_ready = [None] * 8

        def bank_ap(b, off=0, width=512):
            return psA[:, b * 512 + off: b * 512 + off + width]

        def emit_t8(src_ap, dst_ap3, par, evac_eng="vector", evac_gates=()):
            waitp("tensor", bank_free[6 + par])
            r = None
            for i in range(8):
                r = run("tensor",
                        lambda e, o=psT[par][:, i * 128:(i + 1) * 128],
                        i_=src_ap[:, i * 128:(i + 1) * 128], d=idb[:]:
                        e.transpose(o, i_, d),
                        inc=sT if i == 7 else None)
            bank_ready[6 + par] = r
            waitp(evac_eng, r)
            for gp in evac_gates:
                waitp(evac_eng, gp)
            in_ap = psT[par][:].rearrange("p (i t) -> p i t", t=128)
            if evac_eng == "vector":
                ev = run("vector", lambda e, o=dst_ap3, i_=in_ap:
                         e.tensor_copy(o, i_), inc=sV)
            else:
                ev = run("scalar", lambda e, o=dst_ap3, i_=in_ap:
                         e.activation(o, i_, AF.Identity), inc=sS)
            bank_free[6 + par] = ev
            return ev

        def emit_ln1(slot, tc, need_rinv, scol):
            xoff = PF_SLOT + slot * TC * D + tc * D
            xi = pf[:, xoff: xoff + D]
            dump = pf[:, PF_SCR:PF_SCR + D]
            run("scalar", lambda e, o=dump, i_=xi, a=st2[:, 0:1]:
                e.activation(o, i_, AF.Identity, accum_out=a))
            a1 = run("scalar", lambda e, o=dump, i_=xi, a=st2[:, 1:2]:
                     e.activation(o, i_, AF.Square, accum_out=a), inc=sS)
            waitp("vector", a1)
            mc = mcol[:, scol:scol + 1]
            runv(lambda e, o=mc, i_=st2[:, 0:1]:
                 e.tensor_scalar(o, i_, 1.0 / D, None, op0=ALU.mult))
            runv(lambda e, o=vt[:, 0:1], i_=st2[:, 1:2]:
                 e.tensor_scalar(o, i_, 1.0 / D, None, op0=ALU.mult))
            runv(lambda e, o=vt[:, 1:2], i_=mc, j=mc:
                 e.tensor_tensor(o, i_, j, op=ALU.mult))
            runv(lambda e, o=vt[:, 2:3], i_=vt[:, 0:1], j=vt[:, 1:2]:
                 e.tensor_tensor(o, i_, j, op=ALU.subtract))
            v1 = runv(lambda e, o=vt[:, 2:3], i_=vt[:, 2:3]:
                      e.tensor_scalar(o, i_, EPS1, None, op0=ALU.add))
            waitp("scalar", v1)
            runsc(lambda e, o=vt[:, 3:4], i_=vt[:, 2:3]:
                  e.activation(o, i_, AF.Ln))
            rc = rstdc[:, scol:scol + 1]
            s1 = runsc(lambda e, o=rc, i_=vt[:, 3:4]:
                       e.activation(o, i_, AF.Exp, scale=-0.5))
            if need_rinv:
                ric = rinvc[:, scol:scol + 1]
                s1 = runsc(lambda e, o=ric, i_=vt[:, 3:4]:
                           e.activation(o, i_, AF.Exp, scale=0.5))
            waitp("vector", s1)
            return run("vector", lambda e, o=xi, i_=xi, m=mc, r=rc:
                       e.tensor_scalar(o, i_, m, r, op0=ALU.subtract,
                                       op1=ALU.mult), inc=sV)

        def emit_cast(slot, tc, ln_done, stg_gates):
            xoff = PF_SLOT + slot * TC * D + tc * D
            stg = pb[:, PB_STG + (tc % 2) * 1024: PB_STG + (tc % 2) * 1024 + 1024]
            waitp("scalar", ln_done)
            for gp in stg_gates:
                waitp("scalar", gp)
            c = run("scalar", lambda e, o=stg, i_=pf[:, xoff:xoff + D]:
                    e.activation(o, i_, AF.Identity), inc=sS)
            return c, stg

        run("gpsimd", lambda e: e.iota(itmp[:], [[1, 128]], channel_multiplier=-1,
                                       allow_small_or_imprecise_dtypes=True),
            inc=sG)
        g2 = run("gpsimd", lambda e: e.iota(tokid[:], [[128, NCOL]],
                                            channel_multiplier=1), inc=sG)
        wait("vector", sG, sG.n)
        runv(lambda e, o=ltb[:], i_=itmp[:]:
             e.tensor_scalar(o, i_, 0.0, None, op0=ALU.is_gt))
        runv(lambda e, o=itmp[:], i_=itmp[:]:
             e.tensor_scalar(o, i_, 0.0, None, op0=ALU.is_equal))
        idb_done = runv(lambda e: e.tensor_copy(idb[:], itmp[:]))
        runv(lambda e: e.memset(oneb[:], 1.0))
        runv(lambda e: e.memset(ones_bf[:], 1.0))
        runv(lambda e: e.memset(ones32[:], 1.0))
        runv(lambda e: e.memset(xg[:], 0.0))
        ixm = runv(lambda e: e.memset(idxsb[:], NTOK))
        kvz = runv(lambda e: e.memset(kv[:], 0.0))

        waitp("scalar", kvz)
        for z in range(32):
            run("scalar", lambda e, z=z, o=out2_d[z * 512:(z + 1) * 512, :]
                .rearrange("(p r) d -> p (r d)", p=128), i_=kv[:]:
                e.dma_start(o, i_), inc=sZ, k=16)
        run("scalar", lambda e, o=xnbf_d[NTOK:NTOK + 1, :], i_=kv[0:1, 0:1024]:
            e.dma_start(o, i_), inc=sZ, k=16)
        z_done = (sZ, sZ.n)

        waitp("gpsimd", ixm)
        run("gpsimd", lambda e, o=idx_d[0:NE * CAP].rearrange(
            "(p c) one -> p (c one)", p=128),
            i_=idxsb[:]: e.dma_start(o, i_), inc=sG, k=16)
        ixinit = (sG, sG.n)

        for (w_src, w_dst) in ((wk_h, wA), (wv_h, wB)):
            run("gpsimd", lambda e, s=w_src, d=w_dst:
                e.dma_start(d.rearrange("p (dc j) -> p dc j", j=D),
                            s.rearrange("(dc p) j -> p dc j", p=128)),
                inc=sWAB, k=16)
        wab_phA = (sWAB, sWAB.n)

        for i, bh in enumerate((bk_h, bv_h)):
            run("sync", lambda e, s=bh, o=pf[0:1, PF_S + i * D:PF_S + (i + 1) * D]:
                e.dma_start(o, s), inc=sRO, k=16)
        r1 = (sRO, sRO.n)
        waitp("vector", r1)
        bro = run("vector", lambda e: e.tensor_copy(
            brows[:], pf[0:1, PF_S:PF_S + 2048]), inc=sV)
        waitp("sync", bro)
        run("sync", lambda e: e.dma_start(pf[0:8, PF_S:PF_S + D], wg_h),
            inc=sRO, k=16)
        r2 = (sRO, sRO.n)
        waitp("scalar", r2)
        wgc = run("scalar", lambda e, o=pb[0:8, PB_STG:PB_STG + D],
                  i_=pf[0:8, PF_S:PF_S + D]:
                  e.activation(o, i_, AF.Identity), inc=sS)
        waitp("tensor", wgc)
        for dc in range(DC):
            par = dc % 2
            waitp("tensor", bank_free[6 + par])
            r = run("tensor", lambda e, o=psT[par][:, 0:8],
                    i_=pb[0:8, PB_STG + dc * 128:PB_STG + (dc + 1) * 128],
                    d=idb[0:8, 0:8]: e.transpose(o, i_, d), inc=sT)
            bank_ready[6 + par] = r
            waitp("vector", r)
            bank_free[6 + par] = run(
                "vector", lambda e, o=wgT[:, dc * 8:(dc + 1) * 8],
                i_=psT[par][:, 0:8]: e.tensor_copy(o, i_), inc=sV)
        rows_done = (sV, sV.n)
        wg_t_done = (sT, sT.n)

        wu_ring = [None] * 2
        wu_counter = [0]

        def emit_wu_load(kind, e_i, u):
            slot = wu_counter[0] % 2
            wu_counter[0] += 1
            if kind == 1:
                src = w1_h[e_i, u * 128:(u + 1) * 128, :]
            elif kind == 2:
                src = w2_h[e_i, u * 128:(u + 1) * 128, :]
            else:
                dch, fcol = u // 2, u % 2
                src = w3_h[e_i, dch * 128:(dch + 1) * 128,
                           fcol * 1024:(fcol + 1) * 1024]
            if wu_ring[slot] is not None:
                waitp("gpsimd", wu_ring[slot])
            ld = run("gpsimd", lambda e, o=wstg[:, slot * 1024:(slot + 1) * 1024],
                     s=src: e.dma_start(o, s), inc=sG, k=16)
            return slot, ld

        def emit_wu_xpose(kind, e_i, u, slot, ld, region_gate, par):
            waitp("tensor", ld)
            if kind == 1:
                dst = pb[:, PB_W1:PB_W1 + 16384].rearrange(
                    "p (dc f) -> p dc f", f=F)[:, :, u * 128:(u + 1) * 128]
            elif kind == 2:
                dst = pb[:, PB_W2:PB_W2 + 16384].rearrange(
                    "p (dc f) -> p dc f", f=F)[:, :, u * 128:(u + 1) * 128]
            else:
                dch, fcol = u // 2, u % 2
                dst = pb[:, PB_W3:PB_W3 + 16384].rearrange(
                    "p (fc d) -> p fc d", d=D)[
                    :, fcol * 8:(fcol + 1) * 8, dch * 128:(dch + 1) * 128]
            waitp("scalar", region_gate)
            ev = emit_t8(wstg[:, slot * 1024:(slot + 1) * 1024], dst, par,
                         evac_eng="scalar")
            wu_ring[slot] = ev
            return ev

        w1u = [[None] * FC for _ in range(NE)]
        w2u = [[None] * FC for _ in range(NE)]
        w3done = [None] * NE

        slot_free = [None] * 34
        x_loaded = [None] * 32

        def emit_xload(i):
            slot = i % 2
            t = i % NTILE
            src = x_h[t * TT:(t + 1) * TT, :].rearrange("(tc p) d -> p tc d",
                                                        p=128)
            dst = pf[:, PF_SLOT + slot * TC * D:PF_SLOT + (slot + 1) * TC * D] \
                .rearrange("p (tc d) -> p tc d", d=D)
            if i >= 2:
                assert slot_free[i] is not None
                waitp("sync", slot_free[i])
            x_loaded[i] = run("sync", lambda e, o=dst, s=src: e.dma_start(o, s),
                              inc=sX, k=16)

        emit_xload(0)
        emit_xload(1)

        kv_ready = [None] * BC
        k_first = True
        gse_prev = None
        stg_state = {0: [wg_t_done], 1: []}
        xnt_store_gate = None
        wu_sched_A = []
        for u in range(FC):
            wu_sched_A.append((1, 0, u))
        for u in range(FC):
            wu_sched_A.append((2, 0, u))
        wuA_loaded = []

        for t in range(NTILE):
            b = t // 4
            slot = t % 2
            li = t
            waitp("scalar", x_loaded[li])
            waitp("vector", x_loaded[li])
            if t == 0:
                waitp("scalar", idb_done)
            for tc in range(TC):
                ln_done = emit_ln1(slot, tc, True, t * 4 + tc)
                cpair, stg = emit_cast(slot, tc, ln_done, stg_state[tc % 2])
                waitp("sync", cpair)
                run("sync", lambda e,
                    o=xnbf_d[t * TT + tc * 128:t * TT + (tc + 1) * 128, :],
                    i_=stg: e.dma_start(o, i_), inc=sXB, k=16)
                waitp("tensor", cpair)
                dst3 = xnT.rearrange("p (dc tt) -> p dc tt", tt=TT)[
                    :, :, tc * 128:(tc + 1) * 128]
                ev = emit_t8(stg, dst3, tc % 2, evac_eng="vector",
                             evac_gates=(xnt_store_gate,)
                             if xnt_store_gate else ())
                stg_state[tc % 2] = [(sT, sT.n), (sXB, sXB.n)]
                xnT_rdy = ev
                waitp("tensor", xnT_rdy)
                tcol = tc * 128
                if k_first:
                    waitp("tensor", wab_phA)
                    k_first = False
                for (bank0, wsb, brow_i) in ((0, wA, 0), (2, wB, 1)):
                    for jh in range(2):
                        bank = bank0 + jh
                        waitp("tensor", bank_free[bank])
                        for dc in range(DC):
                            run("tensor", lambda e, o=bank_ap(bank),
                                l=xnT[:, dc * TT + tcol: dc * TT + tcol + 128],
                                rr=wsb[:, dc * D + jh * 512:
                                       dc * D + (jh + 1) * 512],
                                stf=(dc == 0):
                                e.matmul(o, l, rr, start=stf, stop=False))
                        r = run("tensor", lambda e, o=bank_ap(bank),
                                l=ones_bf[0:1, 0:128],
                                rr=brows[0:1, brow_i * D + jh * 512:
                                         brow_i * D + (jh + 1) * 512]:
                                e.matmul(o, l, rr, start=False, stop=True),
                                inc=sT)
                        bank_ready[bank] = r
                f1 = pf[:, PF_S:PF_S + D]
                f2 = pf[:, PF_S + D:PF_S + 2 * D]
                v1 = None
                for jh in range(2):
                    waitp("vector", bank_ready[jh])
                    v1 = run("vector", lambda e, o=f2[:, jh * 512:(jh + 1) * 512],
                             i_=bank_ap(jh):
                             e.tensor_scalar(o, i_, 0.0, None, op0=ALU.min),
                             inc=sV if jh == 1 else None)
                waitp("scalar", v1)
                s1 = run("scalar", lambda e, o=f1, i_=f2:
                         e.activation(o, i_, AF.Exp), inc=sS)
                waitp("vector", s1)
                kd = None
                for jh in range(2):
                    kd = run("vector", lambda e, o=f1[:, jh * 512:(jh + 1) * 512],
                             i_=bank_ap(jh), j=f1[:, jh * 512:(jh + 1) * 512]:
                             e.scalar_tensor_tensor(o, i_, 0.0, j, op0=ALU.max,
                                                    op1=ALU.add), inc=sV)
                waitp("tensor", kd)
                rsc = None
                for dc in range(DC):
                    rsc = run("tensor", lambda e, o=bank_ap(0, 0, 8),
                              l=xnT[:, dc * TT + tcol: dc * TT + tcol + 128],
                              rr=wgT[:, dc * 8:(dc + 1) * 8],
                              stf=(dc == 0), spf=(dc == DC - 1):
                              e.matmul(o, l, rr, start=stf, stop=spf,
                                       skip_group_check=True),
                              inc=sT if dc == DC - 1 else None)
                waitp("vector", rsc)
                sev = runv(lambda e, o=gsc[:, tc * 8:(tc + 1) * 8],
                           i_=bank_ap(0, 0, 8): e.tensor_copy(o, i_))
                bank_free[0] = sev
                bank_free[1] = kd
                wait("vector", sV, kd[1])
                runv(lambda e, o=f2, i_=f1, j=f1:
                     e.tensor_tensor(o, i_, j, op=ALU.mult))
                runv(lambda e, o=kss[:],
                     i_=f2.rearrange("p (h dd) -> p h dd", dd=64):
                     e.tensor_reduce(o, i_, axis=AX.X, op=ALU.add))
                v2 = runv(lambda e, o=kss[:], i_=kss[:]:
                          e.tensor_scalar(o, i_, EPS2, None, op0=ALU.add))
                waitp("scalar", v2)
                runsc(lambda e, o=kls[:], i_=kss[:]:
                      e.activation(o, i_, AF.Ln))
                s2 = runsc(lambda e, o=krs[:], i_=kls[:]:
                           e.activation(o, i_, AF.Exp, scale=-0.5))
                waitp("vector", s2)
                kb = None
                for h in range(16):
                    kb = run("vector", lambda e,
                             o=pb[:, PB_HID + tc * 1024 + h * 64:
                                PB_HID + tc * 1024 + (h + 1) * 64],
                             i_=f1[:, h * 64:(h + 1) * 64], sc=krs[:, h:h + 1]:
                             e.tensor_scalar(o, i_, sc, None, op0=ALU.mult),
                             inc=sV if h == 15 else None)
                vb = None
                for jh in range(2):
                    waitp("vector", bank_ready[2 + jh])
                    vb = run("vector", lambda e,
                             o=pb[:, PB_HID + 4096 + tc * 1024 + jh * 512:
                                  PB_HID + 4096 + tc * 1024 + (jh + 1) * 512],
                             i_=bank_ap(2 + jh): e.tensor_copy(o, i_), inc=sV)
                bank_free[2] = bank_free[3] = vb
                waitp("tensor", kb)
                waitp("tensor", vb)
                first = (tc == 0 and t % 4 == 0)
                last = (tc == TC - 1 and t % 4 == 3)
                if first:
                    waitp("tensor", bank_free[4])
                    waitp("tensor", bank_free[5])
                    if b == 0:
                        waitp("vector", z_done)
                r = None
                for pbk in range(8):
                    st_f = first and (pbk % 4 == 0)
                    sp_f = last and (pbk % 4 == 3)
                    r = run("tensor", lambda e,
                            o=bank_ap(4 + pbk // 4, (pbk % 4) * 128, 128),
                            l=pb[:, PB_HID + tc * 1024 + pbk * 128:
                                 PB_HID + tc * 1024 + (pbk + 1) * 128],
                            rr=pb[:, PB_HID + 4096 + tc * 1024 + pbk * 128:
                                  PB_HID + 4096 + tc * 1024 + (pbk + 1) * 128],
                            stf=st_f, spf=sp_f:
                            e.matmul(o, l, rr, start=stf, stop=spf,
                                     skip_group_check=True),
                            inc=sT if pbk == 7 else None)
                bank_ready[4] = bank_ready[5] = r
                if tc == TC - 1:
                    slot_free[li + 2] = (sT, sT.n)
                    waitp("sync", xnT_rdy)
                    run("sync", lambda e, o=xnt_d[t], i_=xnT:
                        e.dma_start(o, i_), inc=sXT, k=16)
                    xnt_store_gate = (sXT, sXT.n)
            if gse_prev is not None:
                waitp("vector", gse_prev)
            gevac = runv(lambda e, o=gm1[:],
                         i_=gsc[:].rearrange("p (tc k) -> p tc k", k=8):
                         e.tensor_reduce(o, i_, axis=AX.X, op=ALU.max))
            for tc in range(TC):
                sl = gsc[:, tc * 8:(tc + 1) * 8]
                m1 = gm1[:, tc:tc + 1]
                runv(lambda e, o=gtt[:, tc * 8:(tc + 1) * 8], i_=sl,
                     m=m1: e.tensor_scalar(o, i_, m, None, op0=ALU.subtract))
                runv(lambda e, o=gmk[:], i_=sl, m=m1:
                     e.tensor_scalar(o, i_, m, None, op0=ALU.is_ge))
                runv(lambda e, o=gs2[:, tc * 8:(tc + 1) * 8], i_=gmk[:],
                     j=sl: e.scalar_tensor_tensor(o, i_, -BIG, j, op0=ALU.mult,
                                                  op1=ALU.add))
            gv = runv(lambda e, o=gm2[:],
                      i_=gs2[:].rearrange("p (tc k) -> p tc k", k=8):
                      e.tensor_reduce(o, i_, axis=AX.X, op=ALU.max))
            waitp("scalar", gv)
            gse = runsc(lambda e: e.activation(gex[:], gtt[:], AF.Exp))
            gse_prev = gse
            waitp("vector", gse)
            for tc in range(TC):
                sl = gsc[:, tc * 8:(tc + 1) * 8]
                runv(lambda e, o=gmk[:], i_=sl, m=gm2[:, tc:tc + 1]:
                     e.tensor_scalar(o, i_, m, None, op0=ALU.is_ge))
                runv(lambda e, o=mask_all[:, t * 32 + tc * 8:t * 32 + (tc + 1) * 8],
                     i_=gmk[:]: e.tensor_copy(o, i_))
                runv(lambda e, o=gtt[:, tc * 8:(tc + 1) * 8],
                     i_=gex[:, tc * 8:(tc + 1) * 8], j=gmk[:]:
                     e.tensor_tensor(o, i_, j, op=ALU.mult))
            runv(lambda e, o=gden[:],
                 i_=gtt[:].rearrange("p (tc k) -> p tc k", k=8):
                 e.tensor_reduce(o, i_, axis=AX.X, op=ALU.add))
            runv(lambda e: e.reciprocal(grec[:], gden[:]))
            for tc in range(TC):
                runv(lambda e, o=gat_all[:, t * 32 + tc * 8:t * 32 + (tc + 1) * 8],
                     i_=gtt[:, tc * 8:(tc + 1) * 8],
                     m=grec[:, tc:tc + 1]:
                     e.tensor_scalar(o, i_, m, None, op0=ALU.mult))
            if t % 4 == 3:
                waitp("vector", bank_ready[4])
                kvd = None
                for pbk in range(8):
                    base = 2048 + pbk * 128
                    kvo = b * 1024 + pbk * 128
                    run("vector", lambda e, o=kv[0:64, kvo:kvo + 64],
                        i_=psA[0:64, base:base + 64]: e.tensor_copy(o, i_))
                    kvd = run("vector",
                              lambda e, o=kv[64:128, kvo + 64:kvo + 128],
                              i_=psA[64:128, base + 64:base + 128]:
                              e.tensor_copy(o, i_),
                              inc=sV if pbk == 7 else None)
                bank_free[4] = bank_free[5] = kvd
                kv_ready[b] = kvd
            for uu in range(2):
                idx = t * 2 + uu
                if idx < len(wu_sched_A):
                    kind, e_i, u = wu_sched_A[idx]
                    slot, ld = emit_wu_load(kind, e_i, u)
                    ev = emit_wu_xpose(kind, e_i, u, slot, ld, None, idx % 2)
                    if kind == 1:
                        w1u[e_i][u] = ev
                    else:
                        w2u[e_i][u] = ev
            if li + 1 < NTILE and x_loaded[li + 1] is None:
                emit_xload(li + 1)
            if li + 2 < NTILE and slot_free[li + 2] is not None \
                    and x_loaded[li + 2] is None:
                emit_xload(li + 2)

        A_end_T = (sT, sT.n)
        A_end_V = (sV, sV.n)
        A_end_S = (sS, sS.n)
        A_end_XB = (sXB, sXB.n)

        waitp("vector", A_end_S)
        tokf = pf[:, PF_TOKF:PF_TOKF + 64]
        runv(lambda e, o=tokf, i_=tokid[:]: e.tensor_copy(o, i_))
        ecapv = pf[:, PF_ECAP:PF_ECAP + 512].rearrange("p (c e) -> p e c", e=8)
        for e_i in range(NE):
            runv(lambda e, o=ecapv[:, e_i], v=float(e_i * CAP):
                 e.memset(o, v))
        wrun = pf[:, PF_WRUN:PF_WRUN + 64]
        runv(lambda e, o=wrun: e.memset(o, 0.0))
        whichv = pf[:, PF_WHICH:PF_WHICH + 512].rearrange("p (c e) -> p e c", e=8)
        maskv = mask_all[:].rearrange("p (c e) -> p e c", e=8)
        gatv = gat_all[:].rearrange("p (c e) -> p e c", e=8)
        metab = pf[:, PF_META:PF_META + 1024].rearrange("p (c f) -> p f c", f=16)
        for e_i in range(NE):
            runv(lambda e, o=whichv[:, e_i], i_=wrun: e.tensor_copy(o, i_))
            runv(lambda e, o=metab[:, 8 + e_i], i_=tokf, j=wrun:
                 e.scalar_tensor_tensor(o, i_, 2.0, j, op0=ALU.mult,
                                        op1=ALU.add))
            runv(lambda e, o=metab[:, e_i], i_=gatv[:, e_i]:
                 e.tensor_copy(o, i_))
            if e_i < NE - 1:
                runv(lambda e, o=wrun, i_=wrun, j=maskv[:, e_i]:
                     e.tensor_tensor(o, i_, j, op=ALU.add))
        meta_built = (sV, sV.n)
        runv(lambda e: e.memset(sent[:, 0:8], 0.0))
        sb = runv(lambda e: e.memset(sent[:, 8:16], float(2 * NTOK)))
        waitp("sync", meta_built)
        waitp("sync", sb)
        run("sync", lambda e, o=meta_d[0:NTOK, :]
            .rearrange("(c p) f -> p c f", p=128),
            i_=pf[:, PF_META:PF_META + 1024].rearrange(
                "p (c f) -> p c f", f=16): e.dma_start(o, i_), inc=sME, k=16)
        run("sync", lambda e, o=meta_d[NTOK:NTOK + 1, :], i_=sent[:]:
            e.dma_start(o, i_), inc=sME, k=16)
        meta_stored = (sME, sME.n)

        slotav = pf[:, PF_SLOTA:PF_SLOTA + 512].rearrange("p (c e) -> p e c", e=8)
        for e_i in range(NE):
            mcp = runv(lambda e, o=mecb[:], i_=maskv[:, e_i]: e.tensor_copy(o, i_))
            cur, nxt = mecb, mcumA
            for sh in (1, 2, 4, 8, 16, 32):
                runv(lambda e, o=nxt[:, 0:sh], i_=cur[:, 0:sh]:
                     e.tensor_copy(o, i_))
                runv(lambda e, o=nxt[:, sh:64], i_=cur[:, sh:64],
                     j=cur[:, 0:64 - sh]: e.tensor_tensor(o, i_, j, op=ALU.add))
                cur, nxt = (mcumA, mcumB) if cur is mecb else (
                    (mcumB, mcumA) if cur is mcumA else (mcumA, mcumB))
            runv(lambda e, o=mshift[:, 0:1]: e.memset(o, 0.0))
            msf = runv(lambda e, o=mshift[:, 1:64], i_=cur[:, 0:63]:
                       e.tensor_copy(o, i_))
            waitp("tensor", mcp)
            waitp("tensor", msf)
            waitp("tensor", bank_free[0])
            run("tensor", lambda e, o=bank_ap(0, 0, 64), l=ltb[:], rr=mecb[:]:
                e.matmul(o, l, rr, start=True, stop=False))
            mm = run("tensor", lambda e, o=bank_ap(0, 0, 64), l=oneb[:],
                     rr=mshift[:]: e.matmul(o, l, rr, start=False, stop=True),
                     inc=sT)
            bank_ready[0] = mm
            waitp("vector", mm)
            bank_free[0] = runv(lambda e, o=slotav[:, e_i], i_=bank_ap(0, 0, 64):
                                e.tensor_copy(o, i_))
        slota = pf[:, PF_SLOTA:PF_SLOTA + 512]
        basea = pf[:, PF_BASE:PF_BASE + 512]
        tmpa = pf[:, PF_TMP:PF_TMP + 512]
        runv(lambda e, o=basea, i_=slota, j=pf[:, PF_ECAP:PF_ECAP + 512]:
             e.tensor_tensor(o, i_, j, op=ALU.add))
        runv(lambda e, o=tmpa, i_=slota:
             e.tensor_scalar(o, i_, float(CAP), None, op0=ALU.is_ge))
        runv(lambda e, o=basea, i_=tmpa, j=basea:
             e.scalar_tensor_tensor(o, i_, BIGI, j, op0=ALU.mult, op1=ALU.add))
        whicha = pf[:, PF_WHICH:PF_WHICH + 512]
        for k in range(2):
            runv(lambda e, o=tmpa, i_=whicha, v=float(k):
                 e.tensor_scalar(o, i_, v, None, op0=ALU.is_equal))
            runv(lambda e, o=tmpa, i_=tmpa, j=mask_all[:]:
                 e.tensor_tensor(o, i_, j, op=ALU.mult))
            runv(lambda e, o=tmpa, i_=tmpa, j=basea:
                 e.tensor_tensor(o, i_, j, op=ALU.mult))
            runv(lambda e, o=pf[:, PF_D12 + k * 64:PF_D12 + (k + 1) * 64],
                 i_=tmpa.rearrange("p (c e) -> p c e", e=8):
                 e.tensor_reduce(o, i_, axis=AX.X, op=ALU.add))
        runv(lambda e, o=pf[:, PF_D12:PF_D12 + 128],
             i_=pf[:, PF_D12:PF_D12 + 128]:
             e.tensor_scalar(o, i_, float(NE * CAP), None, op0=ALU.min))
        d12c = runv(lambda e, o=d12i[:], i_=pf[:, PF_D12:PF_D12 + 128]:
                    e.tensor_copy(o, i_))

        waitp("gpsimd", ixinit)
        waitp("gpsimd", d12c)
        for c in range(NCOL):
            for k in range(2):
                run("gpsimd", lambda e, c=c, k=k: e.indirect_dma_start(
                    idx_d,
                    IndirectOffsetOnAxis(ap=d12i[:, k * 64 + c:k * 64 + c + 1],
                                         axis=0),
                    tokid[:, c:c + 1], None),
                    inc=sG, k=16)
        scat_done = (sG, sG.n)
        wait("gpsimd", sG, sG.n)
        idx_loaded = [None] * NE
        for e_i in range(NE):
            idx_loaded[e_i] = run(
                "gpsimd", lambda e, e_i=e_i:
                e.dma_start(idxsb[:, e_i * NCH:(e_i + 1) * NCH],
                            idx_d[e_i * CAP:(e_i + 1) * CAP].rearrange(
                                "(p c) one -> p (c one)", p=128)),
                inc=sG, k=16)
        R_end_V = (sV, sV.n)

        w1sb = pb[:, PB_W1:PB_W1 + 16384]
        w2sb = pb[:, PB_W2:PB_W2 + 16384]
        w3sb = pb[:, PB_W3:PB_W3 + 16384]
        xeT = pb[:, PB_XT:PB_XT + 4096]
        hid = pb[:, PB_HID:PB_HID + 8192]
        eobs = [pb[:, PB_STG:PB_STG + 1024],
                pb[:, PB_STG + 1024:PB_STG + 2048]]

        xg_gate = [None, None]
        eob_gate = [None, None]
        last_h1 = [None] * NE
        last_h2 = [None] * NE
        last_eo = [None] * NE
        metag_gate = None
        par_ctr = [0]

        def next_par():
            par_ctr[0] ^= 1
            return par_ctr[0]

        for u in range(FC):
            slot, ld = emit_wu_load(3, 0, u)
            emit_wu_xpose(3, 0, u, slot, ld, A_end_T, next_par())
        w3done[0] = (sS, sS.n)

        prev_xeT_gate = None

        for e_i in range(NE):
            for ch in range(CHK):
                if metag_gate is not None:
                    waitp("gpsimd", metag_gate)
                waitp("gpsimd", idx_loaded[e_i])
                waitp("gpsimd", meta_stored)
                gath = [None] * 4
                mg = [None] * 4
                for g in range(4):
                    col = e_i * NCH + ch * 4 + g
                    if xg_gate[g % 2] is not None:
                        waitp("gpsimd", xg_gate[g % 2])
                    gath[g] = run("gpsimd", lambda e, col=col, g=g:
                                  e.indirect_dma_start(
                                      xg[:, (g % 2) * 1024:(g % 2) * 1024 + 1024],
                                      None, xnbf_d,
                                      IndirectOffsetOnAxis(
                                          ap=idxsb[:, col:col + 1], axis=0)),
                                  inc=sG, k=16)
                    mg[g] = run("gpsimd", lambda e, col=col, g=g:
                                e.indirect_dma_start(
                                    metag[:, g * 16:(g + 1) * 16], None, meta_d,
                                    IndirectOffsetOnAxis(
                                        ap=idxsb[:, col:col + 1], axis=0)),
                                inc=sG, k=16)
                    waitp("tensor", gath[g])
                    if prev_xeT_gate is not None:
                        waitp("tensor", prev_xeT_gate)
                    dst3 = xeT.rearrange("p (dc tt) -> p dc tt", tt=512)[
                        :, :, g * 128:(g + 1) * 128]
                    ev = emit_t8(xg[:, (g % 2) * 1024:(g % 2) * 1024 + 1024],
                                 dst3, next_par(), evac_eng="vector")
                    xg_gate[g % 2] = ev
                prev_xeT_gate = None
                xeT_rdy = (sV, sV.n)
                mgall = mg[3]
                mult_last = None
                for fc in range(FC):
                    bank1 = fc % 2
                    bank2 = 2 + fc % 2
                    waitp("tensor", w1u[e_i][fc])
                    waitp("tensor", bank_free[bank1])
                    if fc == 0:
                        waitp("tensor", xeT_rdy)
                    r1_ = None
                    for dc in range(DC):
                        r1_ = run("tensor", lambda e, o=bank_ap(bank1),
                                  l=w1sb[:, dc * F + fc * 128:
                                         dc * F + fc * 128 + 128],
                                  rr=xeT[:, dc * 512:(dc + 1) * 512],
                                  stf=(dc == 0), spf=(dc == DC - 1):
                                  e.matmul(o, l, rr, start=stf, stop=spf),
                                  inc=sT if dc == DC - 1 else None)
                    bank_ready[bank1] = r1_
                    waitp("tensor", w2u[e_i][fc])
                    waitp("tensor", bank_free[bank2])
                    r2_ = None
                    for dc in range(DC):
                        r2_ = run("tensor", lambda e, o=bank_ap(bank2),
                                  l=w2sb[:, dc * F + fc * 128:
                                         dc * F + fc * 128 + 128],
                                  rr=xeT[:, dc * 512:(dc + 1) * 512],
                                  stf=(dc == 0), spf=(dc == DC - 1):
                                  e.matmul(o, l, rr, start=stf, stop=spf),
                                  inc=sT if dc == DC - 1 else None)
                    bank_ready[bank2] = r2_
                    if fc == FC - 1:
                        if ch == CHK - 1:
                            last_h1[e_i] = r1_
                            last_h2[e_i] = r2_
                        prev_xeT_gate = r2_
                    waitp("scalar", bank_ready[bank1])
                    hsl = hid[:, fc * 512:(fc + 1) * 512]
                    silu_l = run("scalar", lambda e, o=hsl, i_=bank_ap(bank1):
                                 e.activation(o, i_, AF.Silu), inc=sS)
                    bank_free[bank1] = silu_l
                    waitp("vector", silu_l)
                    waitp("vector", bank_ready[bank2])
                    mult_last = run("vector", lambda e, o=hsl, i_=hsl,
                                    j=bank_ap(bank2):
                                    e.tensor_tensor(o, i_, j, op=ALU.mult),
                                    inc=sV)
                    bank_free[bank2] = mult_last
                waitp("tensor", mult_last)
                if ch == 0:
                    waitp("tensor", w3done[e_i])
                waitp("vector", mgall)
                eo_evac_last = None
                for g in range(4):
                    for dh in range(2):
                        bank = 4 + dh
                        waitp("tensor", bank_free[bank])
                        r = None
                        for fc in range(FC):
                            r = run("tensor", lambda e, o=bank_ap(bank),
                                    l=hid[:, fc * 512 + g * 128:
                                         fc * 512 + g * 128 + 128],
                                    rr=w3sb[:, fc * D + dh * 512:
                                           fc * D + (dh + 1) * 512],
                                    stf=(fc == 0), spf=(fc == FC - 1):
                                    e.matmul(o, l, rr, start=stf, stop=spf),
                                    inc=sT if fc == FC - 1 else None)
                        bank_ready[bank] = r
                        if e_i == NE - 1 and ch == CHK - 1 and g == 3 and dh == 1:
                            last_eo[e_i] = r
                    es = g % 2
                    if eob_gate[es] is not None:
                        waitp("vector", eob_gate[es])
                    gcol = metag[:, g * 16 + e_i:g * 16 + e_i + 1]
                    ev = None
                    for dh in range(2):
                        waitp("vector", bank_ready[4 + dh])
                        ev = run("vector", lambda e,
                                 o=eobs[es][:, dh * 512:(dh + 1) * 512],
                                 i_=bank_ap(4 + dh), m=gcol:
                                 e.tensor_scalar(o, i_, m, None, op0=ALU.mult),
                                 inc=sV)
                        bank_free[4 + dh] = ev
                    dcast = runv(lambda e, o=d2i[:, g:g + 1],
                                 i_=metag[:, g * 16 + 8 + e_i:
                                          g * 16 + 8 + e_i + 1]:
                                 e.tensor_copy(o, i_))
                    eo_evac_last = dcast
                    waitp("gpsimd", ev)
                    waitp("gpsimd", dcast)
                    sc = run("gpsimd", lambda e, es=es, g=g:
                             e.indirect_dma_start(
                                 out2_d,
                                 IndirectOffsetOnAxis(ap=d2i[:, g:g + 1], axis=0),
                                 eobs[es][:], None),
                             inc=sG, k=16)
                    eob_gate[es] = sc
                metag_gate = eo_evac_last
                if e_i == NE - 1 and ch == CHK - 1:
                    last_eo[e_i] = bank_ready[5]
            if e_i + 1 < NE:
                for u in range(FC):
                    slot, ld = emit_wu_load(1, e_i + 1, u)
                    w1u[e_i + 1][u] = emit_wu_xpose(
                        1, e_i + 1, u, slot, ld, last_h1[e_i], next_par())
                for u in range(FC):
                    slot, ld = emit_wu_load(2, e_i + 1, u)
                    w2u[e_i + 1][u] = emit_wu_xpose(
                        2, e_i + 1, u, slot, ld, last_h2[e_i], next_par())
                gate = bank_ready[5]
                for u in range(FC):
                    slot, ld = emit_wu_load(3, e_i + 1, u)
                    emit_wu_xpose(3, e_i + 1, u, slot, ld, gate, next_par())
                w3done[e_i + 1] = (sS, sS.n)

        C_end_eo = last_eo[NE - 1]
        C_scat_all = (sG, sG.n)

        waitp("gpsimd", C_end_eo)
        for (w_src, w_dst) in ((wq_h, wA), (wo_h, wB)):
            run("gpsimd", lambda e, s=w_src, d=w_dst:
                e.dma_start(d.rearrange("p (dc j) -> p dc j", j=D),
                            s.rearrange("(dc p) j -> p dc j", p=128)),
                inc=sWAB, k=16)
        wab_phB = (sWAB, sWAB.n)

        run("sync", lambda e: e.dma_start(pf[0:1, PF_S:PF_S + D], gam_h),
            inc=sRO, k=16)
        run("sync", lambda e: e.dma_start(pf[0:1, PF_S + D:PF_S + 2 * D], bet_h),
            inc=sRO, k=16)
        r3 = (sRO, sRO.n)
        waitp("tensor", r3)
        for j, dst in ((0, g_bc), (1, b_bc)):
            for dh in range(2):
                bank = 2 + dh
                waitp("tensor", bank_free[bank])
                r = run("tensor", lambda e, o=bank_ap(bank), l=ones32[0:1, :],
                        rr=pf[0:1, PF_S + j * D + dh * 512:
                              PF_S + j * D + dh * 512 + 512]:
                        e.matmul(o, l, rr, start=True, stop=True), inc=sT)
                bank_ready[bank] = r
                waitp("vector", r)
                waitp("vector", last_h1[NE - 1])
                bank_free[bank] = run(
                    "vector", lambda e, o=dst[:, dh * 512:(dh + 1) * 512],
                    i_=bank_ap(bank): e.tensor_copy(o, i_), inc=sV)
        gb_done = (sV, sV.n)

        for i, bh in enumerate((bq_h, bo_h)):
            run("sync", lambda e, s=bh, o=pf[0:1, PF_S + 2048 + i * D:
                                            PF_S + 2048 + (i + 1) * D]:
                e.dma_start(o, s), inc=sRO, k=16)
        rb = (sRO, sRO.n)
        waitp("vector", rb)
        waitp("vector", A_end_T)
        brB = run("vector", lambda e: e.tensor_copy(
            brows[:], pf[0:1, PF_S + 2048:PF_S + 4096]), inc=sV)

        slot_free[16] = R_end_V
        slot_free[17] = R_end_V
        waitp("sync", meta_stored)
        waitp("sync", R_end_V)
        emit_xload(16)
        emit_xload(17)

        waitp("vector", C_end_eo)
        waitp("scalar", eob_gate[0])
        waitp("scalar", eob_gate[1])
        waitp("sync", C_scat_all)
        waitp("scalar", C_scat_all)

        qT = pb[:, PB_HID:PB_HID + 4096]
        gbf = pb[:, PB_HID + 4096:PB_HID + 8192]
        ssb = pf[:, PF_S:PF_S + 4096]
        scr = pf[:, PF_SCR:PF_SCR + 1024]
        o2_gate = [None, None]

        xnt_loaded = [None] * NTILE
        xnt_used = [None] * NTILE

        XNT_OFF = (PB_XT, PB_W2)

        def emit_xnt_load(tt_):
            dst = pb[:, XNT_OFF[tt_ % 2]:XNT_OFF[tt_ % 2] + 4096]
            if tt_ >= 2:
                waitp("sync", xnt_used[tt_ - 2])
            xnt_loaded[tt_] = run(
                "sync", lambda e, o=dst, s=xnt_d[tt_]: e.dma_start(o, s),
                inc=sXT, k=16)

        emit_xnt_load(0)
        emit_xnt_load(1)
        for t in range(NTILE):
            b = t // 4
            slot = t % 2
            li = 16 + t
            xnT = pb[:, XNT_OFF[t % 2]:XNT_OFF[t % 2] + 4096]
            waitp("scalar", x_loaded[li])
            waitp("vector", x_loaded[li])
            if t > 0:
                wait("vector", sO, 16 * 4 * t)
                wait("scalar", sO, 16 * 4 * t)
            xn_norm = [None] * TC
            for tc in range(TC):
                xoff = PF_SLOT + slot * TC * D + tc * D
                xn_norm[tc] = run(
                    "vector", lambda e, o=pf[:, xoff:xoff + D],
                    i_=pf[:, xoff:xoff + D],
                    m=mcol[:, t * 4 + tc:t * 4 + tc + 1],
                    r=rstdc[:, t * 4 + tc:t * 4 + tc + 1]:
                    e.tensor_scalar(o, i_, m, r, op0=ALU.subtract,
                                    op1=ALU.mult), inc=sV)
            waitp("tensor", xnt_loaded[t])
            if t == 0:
                waitp("tensor", wab_phB)
                waitp("tensor", brB)
            qT_w = [None] * JC
            t1 = scr[:, 0:512]
            t2 = scr[:, 512:1024]
            for jc in range(JC):
                bank = jc % 2
                waitp("tensor", bank_free[bank])
                for dc in range(DC):
                    run("tensor", lambda e, o=bank_ap(bank),
                        l=wA[:, dc * D + jc * 128: dc * D + jc * 128 + 128],
                        rr=xnT[:, dc * TT:(dc + 1) * TT], stf=(dc == 0):
                        e.matmul(o, l, rr, start=stf, stop=False))
                r = run("tensor", lambda e, o=bank_ap(bank),
                        l=brows[0:1, jc * 128:(jc + 1) * 128],
                        rr=ones_bf[0:1, 0:512]:
                        e.matmul(o, l, rr, start=False, stop=True), inc=sT)
                bank_ready[bank] = r
                waitp("vector", r)
                v1 = run("vector", lambda e, o=t1, i_=bank_ap(bank):
                         e.tensor_scalar(o, i_, 0.0, None, op0=ALU.min), inc=sV)
                waitp("scalar", v1)
                s1 = run("scalar", lambda e, o=t2, i_=t1:
                         e.activation(o, i_, AF.Exp), inc=sS)
                waitp("vector", s1)
                qT_w[jc] = run(
                    "vector", lambda e, o=qT[:, jc * TT:(jc + 1) * TT],
                    i_=bank_ap(bank), j=t2:
                    e.scalar_tensor_tensor(o, i_, 0.0, j, op0=ALU.max,
                                           op1=ALU.add), inc=sV)
                bank_free[bank] = qT_w[jc]
            xnt_used[t] = (sT, sT.n)
            if t + 2 < NTILE:
                emit_xnt_load(t + 2)
            waitp("tensor", kv_ready[b])
            for jc in range(JC):
                bank = 2 + jc % 2
                waitp("tensor", bank_free[bank])
                waitp("tensor", qT_w[jc])
                r = run("tensor", lambda e, o=bank_ap(bank),
                        l=kv[:, b * 1024 + jc * 128: b * 1024 + (jc + 1) * 128],
                        rr=pb[:, PB_HID + jc * TT:PB_HID + (jc + 1) * TT]:
                        e.matmul(o, l, rr, start=True, stop=True), inc=sT)
                bank_ready[bank] = r
                waitp("vector", r)
                bank_free[bank] = run(
                    "vector", lambda e,
                    o=pb[:, PB_HID + jc * TT:PB_HID + (jc + 1) * TT],
                    i_=bank_ap(bank): e.tensor_copy(o, i_), inc=sV)
            attnT_ready = (sV, sV.n)
            for tc in range(TC):
                for dh in range(2):
                    bank = 4 + dh
                    waitp("tensor", bank_free[bank])
                    if tc == 0 and dh == 0:
                        waitp("tensor", attnT_ready)
                    for jc in range(JC):
                        run("tensor", lambda e, o=bank_ap(bank),
                            l=pb[:, PB_HID + jc * TT + tc * 128:
                                 PB_HID + jc * TT + tc * 128 + 128],
                            rr=wB[:, jc * D + dh * 512: jc * D + (dh + 1) * 512],
                            stf=(jc == 0):
                            e.matmul(o, l, rr, start=stf, stop=False))
                    r = run("tensor", lambda e, o=bank_ap(bank),
                            l=ones_bf[0:1, 0:128],
                            rr=brows[0:1, D + dh * 512:
                                     D + (dh + 1) * 512]:
                            e.matmul(o, l, rr, start=False, stop=True), inc=sT)
                    bank_ready[bank] = r
                xoff = PF_SLOT + slot * TC * D + tc * D
                ssl = ssb[:, tc * D:(tc + 1) * D]
                yv = None
                waitp("vector", xn_norm[tc])
                for dh in range(2):
                    waitp("vector", bank_ready[4 + dh])
                    yv = run("vector", lambda e,
                             o=ssl[:, dh * 512:(dh + 1) * 512],
                             i_=pf[:, xoff + dh * 512: xoff + (dh + 1) * 512],
                             j=bank_ap(4 + dh):
                             e.tensor_tensor(o, i_, j, op=ALU.add), inc=sV)
                    bank_free[4 + dh] = yv
                waitp("scalar", yv)
                run("scalar", lambda e, o=scr[:], i_=ssl, a=st2[:, 0:1]:
                    e.activation(o, i_, AF.Identity, accum_out=a))
                a1 = run("scalar", lambda e, o=scr[:], i_=ssl,
                         a=st2[:, 1:2]:
                         e.activation(o, i_, AF.Square, accum_out=a), inc=sS)
                waitp("vector", a1)
                runv(lambda e, o=vt[:, 0:1], i_=st2[:, 0:1]:
                     e.tensor_scalar(o, i_, 1.0 / D, None, op0=ALU.mult))
                runv(lambda e, o=vt[:, 1:2], i_=st2[:, 1:2]:
                     e.tensor_scalar(o, i_, 1.0 / D, None, op0=ALU.mult))
                runv(lambda e, o=vt[:, 2:3], i_=vt[:, 0:1],
                     j=vt[:, 0:1]: e.tensor_tensor(o, i_, j, op=ALU.mult))
                runv(lambda e, o=vt[:, 3:4], i_=vt[:, 1:2],
                     j=vt[:, 2:3]:
                     e.tensor_tensor(o, i_, j, op=ALU.subtract))
                v2 = runv(lambda e, o=vt[:, 3:4], i_=vt[:, 3:4]:
                          e.tensor_scalar(o, i_, EPS2, None, op0=ALU.add))
                waitp("scalar", v2)
                runsc(lambda e, o=vt[:, 1:2], i_=vt[:, 3:4]:
                      e.activation(o, i_, AF.Ln))
                s2 = runsc(lambda e, o=vt[:, 2:3], i_=vt[:, 1:2]:
                           e.activation(o, i_, AF.Exp, scale=-0.5))
                waitp("vector", s2)
                if t == 0 and tc == 0:
                    waitp("vector", gb_done)
                run("vector", lambda e, o=ssl, i_=ssl, m=vt[:, 0:1],
                    r_=vt[:, 2:3]:
                    e.tensor_scalar(o, i_, m, r_, op0=ALU.subtract,
                                    op1=ALU.mult))
                run("vector", lambda e, o=ssl, i_=ssl, j=g_bc[:]:
                    e.tensor_tensor(o, i_, j, op=ALU.mult))
                run("vector", lambda e, o=ssl, i_=ssl, j=b_bc[:]:
                    e.tensor_tensor(o, i_, j, op=ALU.add))
            waitp("vector", (sS, sS.n))
            ep_v = None
            gs_prev = None
            for tc in range(TC):
                es = tc % 2
                if o2_gate[es] is not None:
                    waitp("sync", o2_gate[es])
                r0 = t * TT + tc * 128
                o2buf = wstg[:, 0:2048] if es == 0 else xg[:, 0:2048]
                o2ld = run("sync", lambda e, o=o2buf,
                           s=out2_d[2 * r0:2 * (r0 + 128), :].rearrange(
                               "(p r) d -> p (r d)", p=128):
                           e.dma_start(o, s), inc=sO2, k=16)
                waitp("vector", o2ld)
                waitp("vector", gs_prev)
                ofn = runv(lambda e, o=scr[:], i_=o2buf[:, 0:1024],
                           j=o2buf[:, 1024:2048]:
                           e.tensor_tensor(o, i_, j, op=ALU.add))
                o2_gate[es] = ofn
                waitp("scalar", ofn)
                gsl = gbf[:, tc * 1024:(tc + 1) * 1024]
                gs_ = run("scalar", lambda e, o=gsl, i_=scr[:]:
                          e.activation(o, i_, AF.Silu), inc=sS)
                gs_prev = gs_
                waitp("vector", gs_)
                ssl = ssb[:, tc * D:(tc + 1) * D]
                xoff = PF_SLOT + slot * TC * D + tc * D
                xsl = pf[:, xoff: xoff + D]
                run("vector", lambda e, o=ssl, i_=ssl, j=gsl:
                    e.tensor_tensor(o, i_, j, op=ALU.mult))
                run("vector", lambda e, o=xsl, i_=xsl,
                    m=rinvc[:, t * 4 + tc: t * 4 + tc + 1],
                    mm=mcol[:, t * 4 + tc: t * 4 + tc + 1]:
                    e.tensor_scalar(o, i_, m, mm, op0=ALU.mult, op1=ALU.add))
                ep_v = run("vector", lambda e, o=ssl, i_=ssl, j=xsl:
                           e.tensor_tensor(o, i_, j, op=ALU.add), inc=sV)
                waitp("sync", ep_v)
                run("sync", lambda e,
                    o=out_h[t * TT + tc * 128: t * TT + (tc + 1) * 128, :],
                    s=ssb[:, tc * D:(tc + 1) * D]: e.dma_start(o, s),
                    inc=sO, k=16)
            slot_free[li + 2] = ep_v
            if li + 2 < 32 and x_loaded[li + 2] is None:
                emit_xload(li + 2)

        wait("sync", sO, sO.n)
        assert sO.n == 16 * NTILE * TC, sO.n

        with nc.Block() as block:
            @block.sync
            def _(e):
                for f_ in ops["sync"]:
                    f_(e)

            @block.scalar
            def _(e):
                for f_ in ops["scalar"]:
                    f_(e)

            @block.vector
            def _(e):
                for f_ in ops["vector"]:
                    f_(e)

            @block.tensor
            def _(e):
                for f_ in ops["tensor"]:
                    f_(e)

            @block.gpsimd
            def _(e):
                for f_ in ops["gpsimd"]:
                    f_(e)

    return nc


def kernel(**inputs):
    x = np.ascontiguousarray(inputs["L0_attn_x"], dtype=np.float32)
    base = {
        "Wq": np.ascontiguousarray(inputs["L0_attn_Wq"], np.float32),
        "Wk": np.ascontiguousarray(inputs["L0_attn_Wk"], np.float32),
        "Wv": np.ascontiguousarray(inputs["L0_attn_Wv"], np.float32),
        "Wo": np.ascontiguousarray(inputs["L0_attn_Wo"], np.float32),
        "bq": np.ascontiguousarray(inputs["L0_attn_bq"], np.float32).reshape(1, D),
        "bk": np.ascontiguousarray(inputs["L0_attn_bk"], np.float32).reshape(1, D),
        "bv": np.ascontiguousarray(inputs["L0_attn_bv"], np.float32).reshape(1, D),
        "bo": np.ascontiguousarray(inputs["L0_attn_bo"], np.float32).reshape(1, D),
        "gamma": np.ascontiguousarray(
            inputs["L0_attn_gamma"], np.float32).reshape(1, D),
        "beta": np.ascontiguousarray(
            inputs["L0_attn_beta"], np.float32).reshape(1, D),
        "Wg": np.ascontiguousarray(inputs["L0_ffn_Wg"], np.float32),
        "W1": np.ascontiguousarray(inputs["L0_ffn_W1"], np.float32),
        "W2": np.ascontiguousarray(inputs["L0_ffn_W2"], np.float32),
        "W3": np.ascontiguousarray(inputs["L0_ffn_W3"], np.float32),
    }
    in_maps = []
    for c in range(NC):
        m = dict(base)
        m["x"] = np.ascontiguousarray(x[c * BC:(c + 1) * BC].reshape(NTOK, D))
        in_maps.append(m)
    nc = build_nc()
    res = run_bass_kernel_spmd(nc, in_maps, core_ids=list(range(NC)))
    out = np.concatenate(
        [np.asarray(r["out"], dtype=np.float32).reshape(BC, S, D)
         for r in res.results], axis=0)
    return out


# version 08
# speedup vs baseline: 1.9312x; 1.1716x over previous
import numpy as np
from contextlib import ExitStack

import concourse.bass as bass
import concourse.mybir as mybir
from concourse.bass_utils import run_bass_kernel_spmd
from concourse.bass import IndirectOffsetOnAxis

F32 = mybir.dt.float32
BF16 = mybir.dt.bfloat16
I32 = mybir.dt.int32
AF = mybir.ActivationFunctionType
ALU = mybir.AluOpType
AX = mybir.AxisListType

NC = 8
BC = 4
S = 2048
D = 1024
NTOK = BC * S
TT = 512
NTILE = NTOK // TT
TC = TT // 128
DC = D // 128
JC = 8
F = 2048
FC = F // 128
NE = 8
CAP = 3072
NCH = CAP // 128
CHK = CAP // 512
NCOL = NTOK // 128
EPS1 = 1e-5
EPS2 = 1e-6
BIG = 1.0e30
BIGI = float(2 ** 20)

ENGS = ("sync", "scalar", "vector", "tensor", "gpsimd")

PB_W3 = 0
PB_W1 = 16384
PB_W2 = 32768
PB_XT = 49152
PB_STG = 53248
PB_HID = 55296
PB_TOT = 63488

PF_SLOT = 0
PF_S = 8192
PF_SCR = 12288
PF_TOT = 13312
PF_META = 0
PF_WRUN = 1024
PF_TOKF = 1088
PF_SLOTA = 1152
PF_WHICH = 1664
PF_BASE = 2176
PF_TMP = 2688
PF_D12 = 3200
PF_ECAP = 3328


def build_nc():
    nc = bass.Bass("TRN2", target_bir_lowering=False)

    x_h = nc.dram_tensor("x", [NTOK, D], F32, kind="ExternalInput").ap()
    out_h = nc.dram_tensor("out", [NTOK, D], F32, kind="ExternalOutput").ap()
    wq_h = nc.dram_tensor("Wq", [D, D], F32, kind="ExternalInput").ap()
    wk_h = nc.dram_tensor("Wk", [D, D], F32, kind="ExternalInput").ap()
    wv_h = nc.dram_tensor("Wv", [D, D], F32, kind="ExternalInput").ap()
    wo_h = nc.dram_tensor("Wo", [D, D], F32, kind="ExternalInput").ap()
    bq_h = nc.dram_tensor("bq", [1, D], F32, kind="ExternalInput").ap()
    bk_h = nc.dram_tensor("bk", [1, D], F32, kind="ExternalInput").ap()
    bv_h = nc.dram_tensor("bv", [1, D], F32, kind="ExternalInput").ap()
    bo_h = nc.dram_tensor("bo", [1, D], F32, kind="ExternalInput").ap()
    gam_h = nc.dram_tensor("gamma", [1, D], F32, kind="ExternalInput").ap()
    bet_h = nc.dram_tensor("beta", [1, D], F32, kind="ExternalInput").ap()
    wg_h = nc.dram_tensor("Wg", [NE, D], F32, kind="ExternalInput").ap()
    w1_h = nc.dram_tensor("W1", [NE, F, D], F32, kind="ExternalInput").ap()
    w2_h = nc.dram_tensor("W2", [NE, F, D], F32, kind="ExternalInput").ap()
    w3_h = nc.dram_tensor("W3", [NE, D, F], F32, kind="ExternalInput").ap()
    xnbf_d = nc.dram_tensor("xnbf", [NTOK + 1, D], BF16, kind="Internal").ap()
    xnt_d = nc.dram_tensor("xnt", [NTILE, 128, 4096], BF16, kind="Internal").ap()
    meta_d = nc.dram_tensor("meta", [NTOK + 1, 16], F32, kind="Internal").ap()
    idx_d = nc.dram_tensor("idxd", [NE * CAP + 1, 1], I32, kind="Internal").ap()
    out2_d = nc.dram_tensor("out2", [2 * NTOK + 1, D], BF16, kind="Internal").ap()

    ctx = ExitStack()
    with ctx:
        E = ctx.enter_context
        pf = E(nc.sbuf_tensor("pf", [128, PF_TOT], F32))
        pb = E(nc.sbuf_tensor("pb", [128, PB_TOT], BF16))
        kv = E(nc.sbuf_tensor("kv", [128, BC * 1024], BF16))
        wstg = E(nc.sbuf_tensor("wstg", [128, 2048], BF16))
        xg = E(nc.sbuf_tensor("xg", [128, 2048], BF16))
        idb = E(nc.sbuf_tensor("idb", [128, 128], BF16))
        ltb = E(nc.sbuf_tensor("ltb", [128, 128], BF16))
        oneb = E(nc.sbuf_tensor("oneb", [128, 128], BF16))
        ones_bf = E(nc.sbuf_tensor("ones_bf", [1, 512], BF16))
        ones32 = E(nc.sbuf_tensor("ones32", [1, 128], F32))
        brows = E(nc.sbuf_tensor("brows", [1, 2048], BF16))
        wgT = E(nc.sbuf_tensor("wgT", [128, 64], BF16))
        gsc = E(nc.sbuf_tensor("gsc", [128, 32], F32))
        gat_all = E(nc.sbuf_tensor("gat_all", [128, NCOL * 8], F32))
        mask_all = E(nc.sbuf_tensor("mask_all", [128, NCOL * 8], BF16))
        mecb = E(nc.sbuf_tensor("mecb", [128, 64], BF16))
        mcumA = E(nc.sbuf_tensor("mcumA", [128, 64], BF16))
        mcumB = E(nc.sbuf_tensor("mcumB", [128, 64], BF16))
        mshift = E(nc.sbuf_tensor("mshift", [128, 64], BF16))
        tokid = E(nc.sbuf_tensor("tokid", [128, NCOL], I32))
        d12i = E(nc.sbuf_tensor("d12i", [128, 128], I32))
        idxsb = E(nc.sbuf_tensor("idxsb", [128, NE * NCH], I32))
        metag = E(nc.sbuf_tensor("metag", [128, 64], F32))
        d2i = E(nc.sbuf_tensor("d2i", [128, 4], I32))
        sent = E(nc.sbuf_tensor("sent", [1, 16], F32))
        st2 = E(nc.sbuf_tensor("st2", [128, 2], F32))
        vt = E(nc.sbuf_tensor("vt", [128, 4], F32))
        mcol = E(nc.sbuf_tensor("mcol", [128, 64], F32))
        rstdc = E(nc.sbuf_tensor("rstdc", [128, 64], F32))
        rinvc = E(nc.sbuf_tensor("rinvc", [128, 64], F32))
        gtt = E(nc.sbuf_tensor("gtt", [128, 32], F32))
        gs2 = E(nc.sbuf_tensor("gs2", [128, 32], F32))
        gex = E(nc.sbuf_tensor("gex", [128, 32], F32))
        gm1 = E(nc.sbuf_tensor("gm1", [128, 4], F32))
        gm2 = E(nc.sbuf_tensor("gm2", [128, 4], F32))
        gden = E(nc.sbuf_tensor("gden", [128, 4], F32))
        grec = E(nc.sbuf_tensor("grec", [128, 4], F32))
        gmk = E(nc.sbuf_tensor("gmk", [128, 8], F32))
        kss = E(nc.sbuf_tensor("kss", [128, 16], F32))
        kls = E(nc.sbuf_tensor("kls", [128, 16], F32))
        krs = E(nc.sbuf_tensor("krs", [128, 16], F32))

        psA = E(nc.psum_tensor("psA", [128, 6 * 512], F32))
        psT = [E(nc.psum_tensor("psT1", [128, 1024], BF16)),
               E(nc.psum_tensor("psT2", [128, 1024], BF16))]

        itmp = pf[:, PF_SCR:PF_SCR + 128]
        wA = pb[:, 0:8192]
        wB = pb[:, 8192:16384]
        g_bc = pb[:, PB_W1:PB_W1 + 1024]
        b_bc = pb[:, PB_W1 + 1024:PB_W1 + 2048]
        xnT = pb[:, PB_XT:PB_XT + 4096]

        class Ctr:
            __slots__ = ("h", "n", "name")

            def __init__(self, h, name):
                self.h = h
                self.n = 0
                self.name = name

        def sem(name):
            return Ctr(E(nc.semaphore(name)), name)

        sT = sem("sT")
        sV = sem("sV")
        sS = sem("sS")
        sG = sem("sG")
        sX = sem("sX")
        sWAB = sem("sWAB")
        sRO = sem("sRO")
        sO = sem("sO")
        sXB = sem("sXB")
        sZ = sem("sZ")
        sME = sem("sME")
        sO2 = sem("sO2")
        sXT = sem("sXT")

        ops = {k: [] for k in ENGS}

        def emit(eng, f):
            ops[eng].append(f)

        def wait(eng, ctr, v):
            assert v <= ctr.n, f"wait on future {ctr.name}: {v} > {ctr.n}"
            if v <= 0:
                return
            emit(eng, lambda e, h=ctr.h, vv=v: e.wait_ge(h, vv))

        def waitp(eng, pair):
            if pair is not None:
                wait(eng, pair[0], pair[1])

        def run(eng, maker, inc=None, k=1):
            if inc is None:
                emit(eng, maker)
                return None
            inc.n += k
            emit(eng, lambda e, m=maker, h=inc.h, kk=k: m(e).then_inc(h, kk))
            return (inc, inc.n)

        def runv(maker):
            p = run("vector", maker, inc=sV)
            wait("vector", sV, p[1])
            return p

        def runsc(maker):
            p = run("scalar", maker, inc=sS)
            wait("scalar", sS, p[1])
            return p

        bank_free = [None] * 8
        bank_ready = [None] * 8

        def bank_ap(b, off=0, width=512):
            return psA[:, b * 512 + off: b * 512 + off + width]

        def emit_t8(src_ap, dst_ap3, par, evac_eng="vector", evac_gates=()):
            waitp("tensor", bank_free[6 + par])
            r = None
            for i in range(8):
                r = run("tensor",
                        lambda e, o=psT[par][:, i * 128:(i + 1) * 128],
                        i_=src_ap[:, i * 128:(i + 1) * 128], d=idb[:]:
                        e.transpose(o, i_, d),
                        inc=sT if i == 7 else None)
            bank_ready[6 + par] = r
            waitp(evac_eng, r)
            for gp in evac_gates:
                waitp(evac_eng, gp)
            in_ap = psT[par][:].rearrange("p (i t) -> p i t", t=128)
            if evac_eng == "vector":
                ev = run("vector", lambda e, o=dst_ap3, i_=in_ap:
                         e.tensor_copy(o, i_), inc=sV)
            else:
                ev = run("scalar", lambda e, o=dst_ap3, i_=in_ap:
                         e.activation(o, i_, AF.Identity), inc=sS)
            bank_free[6 + par] = ev
            return ev

        def emit_ln1(slot, tc, need_rinv, scol):
            xoff = PF_SLOT + slot * TC * D + tc * D
            xi = pf[:, xoff: xoff + D]
            dump = pf[:, PF_SCR:PF_SCR + D]
            run("scalar", lambda e, o=dump, i_=xi, a=st2[:, 0:1]:
                e.activation(o, i_, AF.Identity, accum_out=a))
            a1 = run("scalar", lambda e, o=dump, i_=xi, a=st2[:, 1:2]:
                     e.activation(o, i_, AF.Square, accum_out=a), inc=sS)
            waitp("vector", a1)
            mc = mcol[:, scol:scol + 1]
            runv(lambda e, o=mc, i_=st2[:, 0:1]:
                 e.tensor_scalar(o, i_, 1.0 / D, None, op0=ALU.mult))
            runv(lambda e, o=vt[:, 0:1], i_=st2[:, 1:2]:
                 e.tensor_scalar(o, i_, 1.0 / D, None, op0=ALU.mult))
            runv(lambda e, o=vt[:, 1:2], i_=mc, j=mc:
                 e.tensor_tensor(o, i_, j, op=ALU.mult))
            runv(lambda e, o=vt[:, 2:3], i_=vt[:, 0:1], j=vt[:, 1:2]:
                 e.tensor_tensor(o, i_, j, op=ALU.subtract))
            v1 = runv(lambda e, o=vt[:, 2:3], i_=vt[:, 2:3]:
                      e.tensor_scalar(o, i_, EPS1, None, op0=ALU.add))
            waitp("scalar", v1)
            runsc(lambda e, o=vt[:, 3:4], i_=vt[:, 2:3]:
                  e.activation(o, i_, AF.Ln))
            rc = rstdc[:, scol:scol + 1]
            s1 = runsc(lambda e, o=rc, i_=vt[:, 3:4]:
                       e.activation(o, i_, AF.Exp, scale=-0.5))
            if need_rinv:
                ric = rinvc[:, scol:scol + 1]
                s1 = runsc(lambda e, o=ric, i_=vt[:, 3:4]:
                           e.activation(o, i_, AF.Exp, scale=0.5))
            waitp("vector", s1)
            return run("vector", lambda e, o=xi, i_=xi, m=mc, r=rc:
                       e.tensor_scalar(o, i_, m, r, op0=ALU.subtract,
                                       op1=ALU.mult), inc=sV)

        def emit_cast(slot, tc, ln_done, stg_gates):
            xoff = PF_SLOT + slot * TC * D + tc * D
            stg = pb[:, PB_STG + (tc % 2) * 1024: PB_STG + (tc % 2) * 1024 + 1024]
            waitp("scalar", ln_done)
            for gp in stg_gates:
                waitp("scalar", gp)
            c = run("scalar", lambda e, o=stg, i_=pf[:, xoff:xoff + D]:
                    e.activation(o, i_, AF.Identity), inc=sS)
            return c, stg

        run("gpsimd", lambda e: e.iota(itmp[:], [[1, 128]], channel_multiplier=-1,
                                       allow_small_or_imprecise_dtypes=True),
            inc=sG)
        g2 = run("gpsimd", lambda e: e.iota(tokid[:], [[128, NCOL]],
                                            channel_multiplier=1), inc=sG)
        wait("vector", sG, sG.n)
        runv(lambda e, o=ltb[:], i_=itmp[:]:
             e.tensor_scalar(o, i_, 0.0, None, op0=ALU.is_gt))
        runv(lambda e, o=itmp[:], i_=itmp[:]:
             e.tensor_scalar(o, i_, 0.0, None, op0=ALU.is_equal))
        idb_done = runv(lambda e: e.tensor_copy(idb[:], itmp[:]))
        runv(lambda e: e.memset(oneb[:], 1.0))
        runv(lambda e: e.memset(ones_bf[:], 1.0))
        runv(lambda e: e.memset(ones32[:], 1.0))
        runv(lambda e: e.memset(xg[:], 0.0))
        ixm = runv(lambda e: e.memset(idxsb[:], NTOK))
        kvz = runv(lambda e: e.memset(kv[:], 0.0))

        waitp("scalar", kvz)
        for z in range(32):
            run("scalar", lambda e, z=z, o=out2_d[z * 512:(z + 1) * 512, :]
                .rearrange("(p r) d -> p (r d)", p=128), i_=kv[:]:
                e.dma_start(o, i_), inc=sZ, k=16)
        run("scalar", lambda e, o=xnbf_d[NTOK:NTOK + 1, :], i_=kv[0:1, 0:1024]:
            e.dma_start(o, i_), inc=sZ, k=16)
        z_done = (sZ, sZ.n)

        waitp("gpsimd", ixm)
        run("gpsimd", lambda e, o=idx_d[0:NE * CAP].rearrange(
            "(p c) one -> p (c one)", p=128),
            i_=idxsb[:]: e.dma_start(o, i_), inc=sG, k=16)
        ixinit = (sG, sG.n)

        for (w_src, w_dst) in ((wk_h, wA), (wv_h, wB)):
            run("gpsimd", lambda e, s=w_src, d=w_dst:
                e.dma_start(d.rearrange("p (dc j) -> p dc j", j=D),
                            s.rearrange("(dc p) j -> p dc j", p=128)),
                inc=sWAB, k=16)
        wab_phA = (sWAB, sWAB.n)

        for i, bh in enumerate((bk_h, bv_h)):
            run("sync", lambda e, s=bh, o=pf[0:1, PF_S + i * D:PF_S + (i + 1) * D]:
                e.dma_start(o, s), inc=sRO, k=16)
        r1 = (sRO, sRO.n)
        waitp("vector", r1)
        bro = run("vector", lambda e: e.tensor_copy(
            brows[:], pf[0:1, PF_S:PF_S + 2048]), inc=sV)
        waitp("sync", bro)
        run("sync", lambda e: e.dma_start(pf[0:8, PF_S:PF_S + D], wg_h),
            inc=sRO, k=16)
        r2 = (sRO, sRO.n)
        waitp("scalar", r2)
        wgc = run("scalar", lambda e, o=pb[0:8, PB_STG:PB_STG + D],
                  i_=pf[0:8, PF_S:PF_S + D]:
                  e.activation(o, i_, AF.Identity), inc=sS)
        waitp("tensor", wgc)
        for dc in range(DC):
            par = dc % 2
            waitp("tensor", bank_free[6 + par])
            r = run("tensor", lambda e, o=psT[par][:, 0:8],
                    i_=pb[0:8, PB_STG + dc * 128:PB_STG + (dc + 1) * 128],
                    d=idb[0:8, 0:8]: e.transpose(o, i_, d), inc=sT)
            bank_ready[6 + par] = r
            waitp("vector", r)
            bank_free[6 + par] = run(
                "vector", lambda e, o=wgT[:, dc * 8:(dc + 1) * 8],
                i_=psT[par][:, 0:8]: e.tensor_copy(o, i_), inc=sV)
        rows_done = (sV, sV.n)
        wg_t_done = (sT, sT.n)

        wu_ring = [None] * 2
        wu_counter = [0]

        def emit_wu_load(kind, e_i, u):
            slot = wu_counter[0] % 2
            wu_counter[0] += 1
            if kind == 1:
                src = w1_h[e_i, u * 128:(u + 1) * 128, :]
            elif kind == 2:
                src = w2_h[e_i, u * 128:(u + 1) * 128, :]
            else:
                dch, fcol = u // 2, u % 2
                src = w3_h[e_i, dch * 128:(dch + 1) * 128,
                           fcol * 1024:(fcol + 1) * 1024]
            if wu_ring[slot] is not None:
                waitp("gpsimd", wu_ring[slot])
            ld = run("gpsimd", lambda e, o=wstg[:, slot * 1024:(slot + 1) * 1024],
                     s=src: e.dma_start(o, s), inc=sG, k=16)
            return slot, ld

        def emit_wu_xpose(kind, e_i, u, slot, ld, region_gate, par):
            waitp("tensor", ld)
            if kind == 1:
                dst = pb[:, PB_W1:PB_W1 + 16384].rearrange(
                    "p (dc f) -> p dc f", f=F)[:, :, u * 128:(u + 1) * 128]
            elif kind == 2:
                dst = pb[:, PB_W2:PB_W2 + 16384].rearrange(
                    "p (dc f) -> p dc f", f=F)[:, :, u * 128:(u + 1) * 128]
            else:
                dch, fcol = u // 2, u % 2
                dst = pb[:, PB_W3:PB_W3 + 16384].rearrange(
                    "p (fc d) -> p fc d", d=D)[
                    :, fcol * 8:(fcol + 1) * 8, dch * 128:(dch + 1) * 128]
            waitp("scalar", region_gate)
            ev = emit_t8(wstg[:, slot * 1024:(slot + 1) * 1024], dst, par,
                         evac_eng="scalar")
            wu_ring[slot] = ev
            return ev

        w1u = [[None] * FC for _ in range(NE)]
        w2u = [[None] * FC for _ in range(NE)]
        w3done = [None] * NE

        slot_free = [None] * 34
        x_loaded = [None] * 32

        def emit_xload(i):
            slot = i % 2
            t = i % NTILE
            src = x_h[t * TT:(t + 1) * TT, :].rearrange("(tc p) d -> p tc d",
                                                        p=128)
            dst = pf[:, PF_SLOT + slot * TC * D:PF_SLOT + (slot + 1) * TC * D] \
                .rearrange("p (tc d) -> p tc d", d=D)
            if i >= 2:
                assert slot_free[i] is not None
                waitp("sync", slot_free[i])
            x_loaded[i] = run("sync", lambda e, o=dst, s=src: e.dma_start(o, s),
                              inc=sX, k=16)

        emit_xload(0)
        emit_xload(1)

        kv_ready = [None] * BC
        k_first = True
        gse_prev = None
        stg_state = {0: [wg_t_done], 1: []}
        xnt_store_gate = None
        wu_sched_A = []
        for u in range(FC):
            wu_sched_A.append((1, 0, u))
        for u in range(FC):
            wu_sched_A.append((2, 0, u))
        wuA_loaded = []

        for t in range(NTILE):
            b = t // 4
            slot = t % 2
            li = t
            waitp("scalar", x_loaded[li])
            waitp("vector", x_loaded[li])
            if t == 0:
                waitp("scalar", idb_done)
            for tc in range(TC):
                ln_done = emit_ln1(slot, tc, True, t * 4 + tc)
                cpair, stg = emit_cast(slot, tc, ln_done, stg_state[tc % 2])
                waitp("sync", cpair)
                run("sync", lambda e,
                    o=xnbf_d[t * TT + tc * 128:t * TT + (tc + 1) * 128, :],
                    i_=stg: e.dma_start(o, i_), inc=sXB, k=16)
                waitp("tensor", cpair)
                dst3 = xnT.rearrange("p (dc tt) -> p dc tt", tt=TT)[
                    :, :, tc * 128:(tc + 1) * 128]
                ev = emit_t8(stg, dst3, tc % 2, evac_eng="vector",
                             evac_gates=(xnt_store_gate,)
                             if xnt_store_gate else ())
                stg_state[tc % 2] = [(sT, sT.n), (sXB, sXB.n)]
                xnT_rdy = ev
                waitp("tensor", xnT_rdy)
                tcol = tc * 128
                if k_first:
                    waitp("tensor", wab_phA)
                    k_first = False
                for (bank0, wsb, brow_i) in ((0, wA, 0), (2, wB, 1)):
                    for jh in range(2):
                        bank = bank0 + jh
                        waitp("tensor", bank_free[bank])
                        for dc in range(DC):
                            run("tensor", lambda e, o=bank_ap(bank),
                                l=xnT[:, dc * TT + tcol: dc * TT + tcol + 128],
                                rr=wsb[:, dc * D + jh * 512:
                                       dc * D + (jh + 1) * 512],
                                stf=(dc == 0):
                                e.matmul(o, l, rr, start=stf, stop=False))
                        r = run("tensor", lambda e, o=bank_ap(bank),
                                l=ones_bf[0:1, 0:128],
                                rr=brows[0:1, brow_i * D + jh * 512:
                                         brow_i * D + (jh + 1) * 512]:
                                e.matmul(o, l, rr, start=False, stop=True),
                                inc=sT)
                        bank_ready[bank] = r
                f1 = pf[:, PF_S:PF_S + D]
                f2 = pf[:, PF_S + D:PF_S + 2 * D]
                v1 = None
                for jh in range(2):
                    waitp("vector", bank_ready[jh])
                    v1 = run("vector", lambda e, o=f2[:, jh * 512:(jh + 1) * 512],
                             i_=bank_ap(jh):
                             e.tensor_scalar(o, i_, 0.0, None, op0=ALU.min),
                             inc=sV if jh == 1 else None)
                waitp("scalar", v1)
                s1 = run("scalar", lambda e, o=f1, i_=f2:
                         e.activation(o, i_, AF.Exp), inc=sS)
                waitp("vector", s1)
                kd = None
                for jh in range(2):
                    kd = run("vector", lambda e, o=f1[:, jh * 512:(jh + 1) * 512],
                             i_=bank_ap(jh), j=f1[:, jh * 512:(jh + 1) * 512]:
                             e.scalar_tensor_tensor(o, i_, 0.0, j, op0=ALU.max,
                                                    op1=ALU.add), inc=sV)
                waitp("tensor", kd)
                rsc = None
                for dc in range(DC):
                    rsc = run("tensor", lambda e, o=bank_ap(0, 0, 8),
                              l=xnT[:, dc * TT + tcol: dc * TT + tcol + 128],
                              rr=wgT[:, dc * 8:(dc + 1) * 8],
                              stf=(dc == 0), spf=(dc == DC - 1):
                              e.matmul(o, l, rr, start=stf, stop=spf,
                                       skip_group_check=True),
                              inc=sT if dc == DC - 1 else None)
                waitp("vector", rsc)
                sev = runv(lambda e, o=gsc[:, tc * 8:(tc + 1) * 8],
                           i_=bank_ap(0, 0, 8): e.tensor_copy(o, i_))
                bank_free[0] = sev
                bank_free[1] = kd
                wait("vector", sV, kd[1])
                runv(lambda e, o=f2, i_=f1, j=f1:
                     e.tensor_tensor(o, i_, j, op=ALU.mult))
                runv(lambda e, o=kss[:],
                     i_=f2.rearrange("p (h dd) -> p h dd", dd=64):
                     e.tensor_reduce(o, i_, axis=AX.X, op=ALU.add))
                v2 = runv(lambda e, o=kss[:], i_=kss[:]:
                          e.tensor_scalar(o, i_, EPS2, None, op0=ALU.add))
                waitp("scalar", v2)
                runsc(lambda e, o=kls[:], i_=kss[:]:
                      e.activation(o, i_, AF.Ln))
                s2 = runsc(lambda e, o=krs[:], i_=kls[:]:
                           e.activation(o, i_, AF.Exp, scale=-0.5))
                waitp("vector", s2)
                kb = None
                for h in range(16):
                    kb = run("vector", lambda e,
                             o=pb[:, PB_HID + tc * 1024 + h * 64:
                                PB_HID + tc * 1024 + (h + 1) * 64],
                             i_=f1[:, h * 64:(h + 1) * 64], sc=krs[:, h:h + 1]:
                             e.tensor_scalar(o, i_, sc, None, op0=ALU.mult),
                             inc=sV if h == 15 else None)
                vb = None
                for jh in range(2):
                    waitp("vector", bank_ready[2 + jh])
                    vb = run("vector", lambda e,
                             o=pb[:, PB_HID + 4096 + tc * 1024 + jh * 512:
                                  PB_HID + 4096 + tc * 1024 + (jh + 1) * 512],
                             i_=bank_ap(2 + jh): e.tensor_copy(o, i_), inc=sV)
                bank_free[2] = bank_free[3] = vb
                waitp("tensor", kb)
                waitp("tensor", vb)
                first = (tc == 0 and t % 4 == 0)
                last = (tc == TC - 1 and t % 4 == 3)
                if first:
                    waitp("tensor", bank_free[4])
                    waitp("tensor", bank_free[5])
                    if b == 0:
                        waitp("vector", z_done)
                r = None
                for pbk in range(8):
                    st_f = first and (pbk % 4 == 0)
                    sp_f = last and (pbk % 4 == 3)
                    r = run("tensor", lambda e,
                            o=bank_ap(4 + pbk // 4, (pbk % 4) * 128, 128),
                            l=pb[:, PB_HID + tc * 1024 + pbk * 128:
                                 PB_HID + tc * 1024 + (pbk + 1) * 128],
                            rr=pb[:, PB_HID + 4096 + tc * 1024 + pbk * 128:
                                  PB_HID + 4096 + tc * 1024 + (pbk + 1) * 128],
                            stf=st_f, spf=sp_f:
                            e.matmul(o, l, rr, start=stf, stop=spf,
                                     skip_group_check=True),
                            inc=sT if pbk == 7 else None)
                bank_ready[4] = bank_ready[5] = r
                if tc == TC - 1:
                    slot_free[li + 2] = (sT, sT.n)
                    waitp("sync", xnT_rdy)
                    run("sync", lambda e, o=xnt_d[t], i_=xnT:
                        e.dma_start(o, i_), inc=sXT, k=16)
                    xnt_store_gate = (sXT, sXT.n)
            if gse_prev is not None:
                waitp("vector", gse_prev)
            gevac = runv(lambda e, o=gm1[:],
                         i_=gsc[:].rearrange("p (tc k) -> p tc k", k=8):
                         e.tensor_reduce(o, i_, axis=AX.X, op=ALU.max))
            for tc in range(TC):
                sl = gsc[:, tc * 8:(tc + 1) * 8]
                m1 = gm1[:, tc:tc + 1]
                runv(lambda e, o=gtt[:, tc * 8:(tc + 1) * 8], i_=sl,
                     m=m1: e.tensor_scalar(o, i_, m, None, op0=ALU.subtract))
                runv(lambda e, o=gmk[:], i_=sl, m=m1:
                     e.tensor_scalar(o, i_, m, None, op0=ALU.is_ge))
                runv(lambda e, o=gs2[:, tc * 8:(tc + 1) * 8], i_=gmk[:],
                     j=sl: e.scalar_tensor_tensor(o, i_, -BIG, j, op0=ALU.mult,
                                                  op1=ALU.add))
            gv = runv(lambda e, o=gm2[:],
                      i_=gs2[:].rearrange("p (tc k) -> p tc k", k=8):
                      e.tensor_reduce(o, i_, axis=AX.X, op=ALU.max))
            waitp("scalar", gv)
            gse = runsc(lambda e: e.activation(gex[:], gtt[:], AF.Exp))
            gse_prev = gse
            waitp("vector", gse)
            for tc in range(TC):
                sl = gsc[:, tc * 8:(tc + 1) * 8]
                runv(lambda e, o=gmk[:], i_=sl, m=gm2[:, tc:tc + 1]:
                     e.tensor_scalar(o, i_, m, None, op0=ALU.is_ge))
                runv(lambda e, o=mask_all[:, t * 32 + tc * 8:t * 32 + (tc + 1) * 8],
                     i_=gmk[:]: e.tensor_copy(o, i_))
                runv(lambda e, o=gtt[:, tc * 8:(tc + 1) * 8],
                     i_=gex[:, tc * 8:(tc + 1) * 8], j=gmk[:]:
                     e.tensor_tensor(o, i_, j, op=ALU.mult))
            runv(lambda e, o=gden[:],
                 i_=gtt[:].rearrange("p (tc k) -> p tc k", k=8):
                 e.tensor_reduce(o, i_, axis=AX.X, op=ALU.add))
            runv(lambda e: e.reciprocal(grec[:], gden[:]))
            for tc in range(TC):
                runv(lambda e, o=gat_all[:, t * 32 + tc * 8:t * 32 + (tc + 1) * 8],
                     i_=gtt[:, tc * 8:(tc + 1) * 8],
                     m=grec[:, tc:tc + 1]:
                     e.tensor_scalar(o, i_, m, None, op0=ALU.mult))
            if t % 4 == 3:
                waitp("vector", bank_ready[4])
                kvd = None
                for pbk in range(8):
                    base = 2048 + pbk * 128
                    kvo = b * 1024 + pbk * 128
                    run("vector", lambda e, o=kv[0:64, kvo:kvo + 64],
                        i_=psA[0:64, base:base + 64]: e.tensor_copy(o, i_))
                    kvd = run("vector",
                              lambda e, o=kv[64:128, kvo + 64:kvo + 128],
                              i_=psA[64:128, base + 64:base + 128]:
                              e.tensor_copy(o, i_),
                              inc=sV if pbk == 7 else None)
                bank_free[4] = bank_free[5] = kvd
                kv_ready[b] = kvd
            for uu in range(2):
                idx = t * 2 + uu
                if idx < len(wu_sched_A):
                    kind, e_i, u = wu_sched_A[idx]
                    slot, ld = emit_wu_load(kind, e_i, u)
                    ev = emit_wu_xpose(kind, e_i, u, slot, ld, None, idx % 2)
                    if kind == 1:
                        w1u[e_i][u] = ev
                    else:
                        w2u[e_i][u] = ev
            if li + 1 < NTILE and x_loaded[li + 1] is None:
                emit_xload(li + 1)
            if li + 2 < NTILE and slot_free[li + 2] is not None \
                    and x_loaded[li + 2] is None:
                emit_xload(li + 2)

        A_end_T = (sT, sT.n)
        A_end_V = (sV, sV.n)
        A_end_S = (sS, sS.n)
        A_end_XB = (sXB, sXB.n)

        waitp("vector", A_end_S)
        tokf = pf[:, PF_TOKF:PF_TOKF + 64]
        runv(lambda e, o=tokf, i_=tokid[:]: e.tensor_copy(o, i_))
        ecapv = pf[:, PF_ECAP:PF_ECAP + 512].rearrange("p (c e) -> p e c", e=8)
        for e_i in range(NE):
            runv(lambda e, o=ecapv[:, e_i], v=float(e_i * CAP):
                 e.memset(o, v))
        wrun = pf[:, PF_WRUN:PF_WRUN + 64]
        runv(lambda e, o=wrun: e.memset(o, 0.0))
        whichv = pf[:, PF_WHICH:PF_WHICH + 512].rearrange("p (c e) -> p e c", e=8)
        maskv = mask_all[:].rearrange("p (c e) -> p e c", e=8)
        gatv = gat_all[:].rearrange("p (c e) -> p e c", e=8)
        metab = pf[:, PF_META:PF_META + 1024].rearrange("p (c f) -> p f c", f=16)
        for e_i in range(NE):
            runv(lambda e, o=whichv[:, e_i], i_=wrun: e.tensor_copy(o, i_))
            runv(lambda e, o=metab[:, 8 + e_i], i_=tokf, j=wrun:
                 e.scalar_tensor_tensor(o, i_, 2.0, j, op0=ALU.mult,
                                        op1=ALU.add))
            runv(lambda e, o=metab[:, e_i], i_=gatv[:, e_i]:
                 e.tensor_copy(o, i_))
            if e_i < NE - 1:
                runv(lambda e, o=wrun, i_=wrun, j=maskv[:, e_i]:
                     e.tensor_tensor(o, i_, j, op=ALU.add))
        meta_built = (sV, sV.n)
        runv(lambda e: e.memset(sent[:, 0:8], 0.0))
        sb = runv(lambda e: e.memset(sent[:, 8:16], float(2 * NTOK)))
        waitp("sync", meta_built)
        waitp("sync", sb)
        run("sync", lambda e, o=meta_d[0:NTOK, :]
            .rearrange("(c p) f -> p c f", p=128),
            i_=pf[:, PF_META:PF_META + 1024].rearrange(
                "p (c f) -> p c f", f=16): e.dma_start(o, i_), inc=sME, k=16)
        run("sync", lambda e, o=meta_d[NTOK:NTOK + 1, :], i_=sent[:]:
            e.dma_start(o, i_), inc=sME, k=16)
        meta_stored = (sME, sME.n)

        slotav = pf[:, PF_SLOTA:PF_SLOTA + 512].rearrange("p (c e) -> p e c", e=8)
        for e_i in range(NE):
            mcp = runv(lambda e, o=mecb[:], i_=maskv[:, e_i]: e.tensor_copy(o, i_))
            cur, nxt = mecb, mcumA
            for sh in (1, 2, 4, 8, 16, 32):
                runv(lambda e, o=nxt[:, 0:sh], i_=cur[:, 0:sh]:
                     e.tensor_copy(o, i_))
                runv(lambda e, o=nxt[:, sh:64], i_=cur[:, sh:64],
                     j=cur[:, 0:64 - sh]: e.tensor_tensor(o, i_, j, op=ALU.add))
                cur, nxt = (mcumA, mcumB) if cur is mecb else (
                    (mcumB, mcumA) if cur is mcumA else (mcumA, mcumB))
            runv(lambda e, o=mshift[:, 0:1]: e.memset(o, 0.0))
            msf = runv(lambda e, o=mshift[:, 1:64], i_=cur[:, 0:63]:
                       e.tensor_copy(o, i_))
            waitp("tensor", mcp)
            waitp("tensor", msf)
            waitp("tensor", bank_free[0])
            run("tensor", lambda e, o=bank_ap(0, 0, 64), l=ltb[:], rr=mecb[:]:
                e.matmul(o, l, rr, start=True, stop=False))
            mm = run("tensor", lambda e, o=bank_ap(0, 0, 64), l=oneb[:],
                     rr=mshift[:]: e.matmul(o, l, rr, start=False, stop=True),
                     inc=sT)
            bank_ready[0] = mm
            waitp("vector", mm)
            bank_free[0] = runv(lambda e, o=slotav[:, e_i], i_=bank_ap(0, 0, 64):
                                e.tensor_copy(o, i_))
        slota = pf[:, PF_SLOTA:PF_SLOTA + 512]
        basea = pf[:, PF_BASE:PF_BASE + 512]
        tmpa = pf[:, PF_TMP:PF_TMP + 512]
        runv(lambda e, o=basea, i_=slota, j=pf[:, PF_ECAP:PF_ECAP + 512]:
             e.tensor_tensor(o, i_, j, op=ALU.add))
        runv(lambda e, o=tmpa, i_=slota:
             e.tensor_scalar(o, i_, float(CAP), None, op0=ALU.is_ge))
        runv(lambda e, o=basea, i_=tmpa, j=basea:
             e.scalar_tensor_tensor(o, i_, BIGI, j, op0=ALU.mult, op1=ALU.add))
        whicha = pf[:, PF_WHICH:PF_WHICH + 512]
        for k in range(2):
            runv(lambda e, o=tmpa, i_=whicha, v=float(k):
                 e.tensor_scalar(o, i_, v, None, op0=ALU.is_equal))
            runv(lambda e, o=tmpa, i_=tmpa, j=mask_all[:]:
                 e.tensor_tensor(o, i_, j, op=ALU.mult))
            runv(lambda e, o=tmpa, i_=tmpa, j=basea:
                 e.tensor_tensor(o, i_, j, op=ALU.mult))
            runv(lambda e, o=pf[:, PF_D12 + k * 64:PF_D12 + (k + 1) * 64],
                 i_=tmpa.rearrange("p (c e) -> p c e", e=8):
                 e.tensor_reduce(o, i_, axis=AX.X, op=ALU.add))
        runv(lambda e, o=pf[:, PF_D12:PF_D12 + 128],
             i_=pf[:, PF_D12:PF_D12 + 128]:
             e.tensor_scalar(o, i_, float(NE * CAP), None, op0=ALU.min))
        d12c = runv(lambda e, o=d12i[:], i_=pf[:, PF_D12:PF_D12 + 128]:
                    e.tensor_copy(o, i_))

        waitp("gpsimd", ixinit)
        waitp("gpsimd", d12c)
        for c in range(NCOL):
            for k in range(2):
                run("gpsimd", lambda e, c=c, k=k: e.indirect_dma_start(
                    idx_d,
                    IndirectOffsetOnAxis(ap=d12i[:, k * 64 + c:k * 64 + c + 1],
                                         axis=0),
                    tokid[:, c:c + 1], None),
                    inc=sG, k=16)
        scat_done = (sG, sG.n)
        wait("gpsimd", sG, sG.n)
        idx_loaded = [None] * NE
        for e_i in range(NE):
            idx_loaded[e_i] = run(
                "gpsimd", lambda e, e_i=e_i:
                e.dma_start(idxsb[:, e_i * NCH:(e_i + 1) * NCH],
                            idx_d[e_i * CAP:(e_i + 1) * CAP].rearrange(
                                "(p c) one -> p (c one)", p=128)),
                inc=sG, k=16)
        R_end_V = (sV, sV.n)

        w1sb = pb[:, PB_W1:PB_W1 + 16384]
        w2sb = pb[:, PB_W2:PB_W2 + 16384]
        w3sb = pb[:, PB_W3:PB_W3 + 16384]
        xeT = pb[:, PB_XT:PB_XT + 4096]
        hid = pb[:, PB_HID:PB_HID + 8192]
        eobs = [pb[:, PB_STG:PB_STG + 1024],
                pb[:, PB_STG + 1024:PB_STG + 2048]]

        xg_gate = [None, None]
        eob_gate = [None, None]
        last_h1 = [None] * NE
        last_h2 = [None] * NE
        last_eo = [None] * NE
        metag_gate = None
        par_ctr = [0]

        def next_par():
            par_ctr[0] ^= 1
            return par_ctr[0]

        w3_sched = [(0, u, A_end_T) for u in range(FC)]

        prev_xeT_gate = None

        for e_i in range(NE):
            for ch in range(CHK):
                if metag_gate is not None:
                    waitp("gpsimd", metag_gate)
                waitp("gpsimd", idx_loaded[e_i])
                waitp("gpsimd", meta_stored)
                gath = [None] * 4
                mg = [None] * 4
                for g in range(4):
                    col = e_i * NCH + ch * 4 + g
                    if xg_gate[g % 2] is not None:
                        waitp("gpsimd", xg_gate[g % 2])
                    gath[g] = run("gpsimd", lambda e, col=col, g=g:
                                  e.indirect_dma_start(
                                      xg[:, (g % 2) * 1024:(g % 2) * 1024 + 1024],
                                      None, xnbf_d,
                                      IndirectOffsetOnAxis(
                                          ap=idxsb[:, col:col + 1], axis=0)),
                                  inc=sG, k=16)
                    mg[g] = run("gpsimd", lambda e, col=col, g=g:
                                e.indirect_dma_start(
                                    metag[:, g * 16:(g + 1) * 16], None, meta_d,
                                    IndirectOffsetOnAxis(
                                        ap=idxsb[:, col:col + 1], axis=0)),
                                inc=sG, k=16)
                    waitp("tensor", gath[g])
                    if prev_xeT_gate is not None:
                        waitp("tensor", prev_xeT_gate)
                    dst3 = xeT.rearrange("p (dc tt) -> p dc tt", tt=512)[
                        :, :, g * 128:(g + 1) * 128]
                    ev = emit_t8(xg[:, (g % 2) * 1024:(g % 2) * 1024 + 1024],
                                 dst3, next_par(), evac_eng="vector")
                    xg_gate[g % 2] = ev
                prev_xeT_gate = None
                xeT_rdy = (sV, sV.n)
                mgall = mg[3]
                mult_last = None
                for fc in range(FC):
                    bank1 = fc % 2
                    bank2 = 2 + fc % 2
                    waitp("tensor", w1u[e_i][fc])
                    waitp("tensor", bank_free[bank1])
                    if fc == 0:
                        waitp("tensor", xeT_rdy)
                    r1_ = None
                    for dc in range(DC):
                        r1_ = run("tensor", lambda e, o=bank_ap(bank1),
                                  l=w1sb[:, dc * F + fc * 128:
                                         dc * F + fc * 128 + 128],
                                  rr=xeT[:, dc * 512:(dc + 1) * 512],
                                  stf=(dc == 0), spf=(dc == DC - 1):
                                  e.matmul(o, l, rr, start=stf, stop=spf),
                                  inc=sT if dc == DC - 1 else None)
                    bank_ready[bank1] = r1_
                    waitp("tensor", w2u[e_i][fc])
                    waitp("tensor", bank_free[bank2])
                    r2_ = None
                    for dc in range(DC):
                        r2_ = run("tensor", lambda e, o=bank_ap(bank2),
                                  l=w2sb[:, dc * F + fc * 128:
                                         dc * F + fc * 128 + 128],
                                  rr=xeT[:, dc * 512:(dc + 1) * 512],
                                  stf=(dc == 0), spf=(dc == DC - 1):
                                  e.matmul(o, l, rr, start=stf, stop=spf),
                                  inc=sT if dc == DC - 1 else None)
                    bank_ready[bank2] = r2_
                    if fc == FC - 1:
                        if ch == CHK - 1:
                            last_h1[e_i] = r1_
                            last_h2[e_i] = r2_
                        prev_xeT_gate = r2_
                    waitp("scalar", bank_ready[bank1])
                    hsl = hid[:, fc * 512:(fc + 1) * 512]
                    silu_l = run("scalar", lambda e, o=hsl, i_=bank_ap(bank1):
                                 e.activation(o, i_, AF.Silu), inc=sS)
                    bank_free[bank1] = silu_l
                    waitp("vector", silu_l)
                    waitp("vector", bank_ready[bank2])
                    mult_last = run("vector", lambda e, o=hsl, i_=hsl,
                                    j=bank_ap(bank2):
                                    e.tensor_tensor(o, i_, j, op=ALU.mult),
                                    inc=sV)
                    bank_free[bank2] = mult_last
                    if ch == 0 and w3_sched:
                        ee, u, gate = w3_sched.pop(0)
                        slot3, ld3 = emit_wu_load(3, ee, u)
                        emit_wu_xpose(3, ee, u, slot3, ld3, gate, next_par())
                        if not w3_sched:
                            w3done[ee] = (sS, sS.n)
                    if ch == CHK - 1 and e_i + 1 < NE:
                        slot1, ld1 = emit_wu_load(1, e_i + 1, fc)
                        w1u[e_i + 1][fc] = emit_wu_xpose(
                            1, e_i + 1, fc, slot1, ld1, r1_, next_par())
                        slot2, ld2 = emit_wu_load(2, e_i + 1, fc)
                        w2u[e_i + 1][fc] = emit_wu_xpose(
                            2, e_i + 1, fc, slot2, ld2, r2_, next_par())
                waitp("tensor", mult_last)
                if ch == 0:
                    waitp("tensor", w3done[e_i])
                waitp("vector", mgall)
                eo_evac_last = None
                for g in range(4):
                    for dh in range(2):
                        bank = 4 + dh
                        waitp("tensor", bank_free[bank])
                        r = None
                        for fc in range(FC):
                            r = run("tensor", lambda e, o=bank_ap(bank),
                                    l=hid[:, fc * 512 + g * 128:
                                         fc * 512 + g * 128 + 128],
                                    rr=w3sb[:, fc * D + dh * 512:
                                           fc * D + (dh + 1) * 512],
                                    stf=(fc == 0), spf=(fc == FC - 1):
                                    e.matmul(o, l, rr, start=stf, stop=spf),
                                    inc=sT if fc == FC - 1 else None)
                        bank_ready[bank] = r
                        if e_i == NE - 1 and ch == CHK - 1 and g == 3 and dh == 1:
                            last_eo[e_i] = r
                    es = g % 2
                    if eob_gate[es] is not None:
                        waitp("vector", eob_gate[es])
                    gcol = metag[:, g * 16 + e_i:g * 16 + e_i + 1]
                    ev = None
                    for dh in range(2):
                        waitp("vector", bank_ready[4 + dh])
                        ev = run("vector", lambda e,
                                 o=eobs[es][:, dh * 512:(dh + 1) * 512],
                                 i_=bank_ap(4 + dh), m=gcol:
                                 e.tensor_scalar(o, i_, m, None, op0=ALU.mult),
                                 inc=sV)
                        bank_free[4 + dh] = ev
                    dcast = runv(lambda e, o=d2i[:, g:g + 1],
                                 i_=metag[:, g * 16 + 8 + e_i:
                                          g * 16 + 8 + e_i + 1]:
                                 e.tensor_copy(o, i_))
                    eo_evac_last = dcast
                    waitp("gpsimd", ev)
                    waitp("gpsimd", dcast)
                    sc = run("gpsimd", lambda e, es=es, g=g:
                             e.indirect_dma_start(
                                 out2_d,
                                 IndirectOffsetOnAxis(ap=d2i[:, g:g + 1], axis=0),
                                 eobs[es][:], None),
                             inc=sG, k=16)
                    eob_gate[es] = sc
                metag_gate = eo_evac_last
                if e_i == NE - 1 and ch == CHK - 1:
                    last_eo[e_i] = bank_ready[5]
            if e_i + 1 < NE:
                gate = bank_ready[5]
                w3_sched = [(e_i + 1, u, gate) for u in range(FC)]

        C_end_eo = last_eo[NE - 1]
        C_scat_all = (sG, sG.n)

        waitp("gpsimd", C_end_eo)
        for (w_src, w_dst) in ((wq_h, wA), (wo_h, wB)):
            run("gpsimd", lambda e, s=w_src, d=w_dst:
                e.dma_start(d.rearrange("p (dc j) -> p dc j", j=D),
                            s.rearrange("(dc p) j -> p dc j", p=128)),
                inc=sWAB, k=16)
        wab_phB = (sWAB, sWAB.n)

        run("sync", lambda e: e.dma_start(pf[0:1, PF_S:PF_S + D], gam_h),
            inc=sRO, k=16)
        run("sync", lambda e: e.dma_start(pf[0:1, PF_S + D:PF_S + 2 * D], bet_h),
            inc=sRO, k=16)
        r3 = (sRO, sRO.n)
        waitp("tensor", r3)
        for j, dst in ((0, g_bc), (1, b_bc)):
            for dh in range(2):
                bank = 2 + dh
                waitp("tensor", bank_free[bank])
                r = run("tensor", lambda e, o=bank_ap(bank), l=ones32[0:1, :],
                        rr=pf[0:1, PF_S + j * D + dh * 512:
                              PF_S + j * D + dh * 512 + 512]:
                        e.matmul(o, l, rr, start=True, stop=True), inc=sT)
                bank_ready[bank] = r
                waitp("vector", r)
                waitp("vector", last_h1[NE - 1])
                bank_free[bank] = run(
                    "vector", lambda e, o=dst[:, dh * 512:(dh + 1) * 512],
                    i_=bank_ap(bank): e.tensor_copy(o, i_), inc=sV)
        gb_done = (sV, sV.n)

        for i, bh in enumerate((bq_h, bo_h)):
            run("sync", lambda e, s=bh, o=pf[0:1, PF_S + 2048 + i * D:
                                            PF_S + 2048 + (i + 1) * D]:
                e.dma_start(o, s), inc=sRO, k=16)
        rb = (sRO, sRO.n)
        waitp("vector", rb)
        waitp("vector", A_end_T)
        brB = run("vector", lambda e: e.tensor_copy(
            brows[:], pf[0:1, PF_S + 2048:PF_S + 4096]), inc=sV)

        slot_free[16] = R_end_V
        slot_free[17] = R_end_V
        waitp("sync", meta_stored)
        waitp("sync", R_end_V)
        emit_xload(16)
        emit_xload(17)

        waitp("vector", C_end_eo)
        waitp("scalar", eob_gate[0])
        waitp("scalar", eob_gate[1])
        waitp("sync", C_scat_all)
        waitp("scalar", C_scat_all)

        qT = pb[:, PB_HID:PB_HID + 4096]
        gbf = pb[:, PB_HID + 4096:PB_HID + 8192]
        ssb = pf[:, PF_S:PF_S + 4096]
        scr = pf[:, PF_SCR:PF_SCR + 1024]
        o2_gate = [None, None]

        xnt_loaded = [None] * NTILE
        xnt_used = [None] * NTILE

        XNT_OFF = (PB_XT, PB_W2)

        def emit_xnt_load(tt_):
            dst = pb[:, XNT_OFF[tt_ % 2]:XNT_OFF[tt_ % 2] + 4096]
            if tt_ >= 2:
                waitp("sync", xnt_used[tt_ - 2])
            xnt_loaded[tt_] = run(
                "sync", lambda e, o=dst, s=xnt_d[tt_]: e.dma_start(o, s),
                inc=sXT, k=16)

        emit_xnt_load(0)
        emit_xnt_load(1)
        for t in range(NTILE):
            b = t // 4
            slot = t % 2
            li = 16 + t
            xnT = pb[:, XNT_OFF[t % 2]:XNT_OFF[t % 2] + 4096]
            waitp("scalar", x_loaded[li])
            waitp("vector", x_loaded[li])
            if t > 0:
                wait("vector", sO, 16 * 4 * t)
                wait("scalar", sO, 16 * 4 * t)
            xn_norm = [None] * TC
            for tc in range(TC):
                xoff = PF_SLOT + slot * TC * D + tc * D
                xn_norm[tc] = run(
                    "vector", lambda e, o=pf[:, xoff:xoff + D],
                    i_=pf[:, xoff:xoff + D],
                    m=mcol[:, t * 4 + tc:t * 4 + tc + 1],
                    r=rstdc[:, t * 4 + tc:t * 4 + tc + 1]:
                    e.tensor_scalar(o, i_, m, r, op0=ALU.subtract,
                                    op1=ALU.mult), inc=sV)
            waitp("tensor", xnt_loaded[t])
            if t == 0:
                waitp("tensor", wab_phB)
                waitp("tensor", brB)
            qT_w = [None] * (JC // 2)
            t1 = scr[:, 0:1024]
            for jp in range(JC // 2):
                b0 = (jp % 2) * 2
                for jj in range(2):
                    jc = jp * 2 + jj
                    bank = b0 + jj
                    waitp("tensor", bank_free[bank])
                    for dc in range(DC):
                        run("tensor", lambda e, o=bank_ap(bank),
                            l=wA[:, dc * D + jc * 128: dc * D + jc * 128 + 128],
                            rr=xnT[:, dc * TT:(dc + 1) * TT], stf=(dc == 0):
                            e.matmul(o, l, rr, start=stf, stop=False))
                    r = run("tensor", lambda e, o=bank_ap(bank),
                            l=brows[0:1, jc * 128:(jc + 1) * 128],
                            rr=ones_bf[0:1, 0:512]:
                            e.matmul(o, l, rr, start=False, stop=True), inc=sT)
                    bank_ready[bank] = r
                pw = bank_ap(b0, 0, 1024)
                waitp("vector", bank_ready[b0 + 1])
                v1 = run("vector", lambda e, o=t1, i_=pw:
                         e.tensor_scalar(o, i_, 0.0, None, op0=ALU.min), inc=sV)
                waitp("scalar", v1)
                s1 = run("scalar", lambda e, o=ssb[:, 3072:4096], i_=t1:
                         e.activation(o, i_, AF.Exp), inc=sS)
                waitp("vector", s1)
                qT_w[jp] = run(
                    "vector", lambda e, o=qT[:, jp * 1024:(jp + 1) * 1024],
                    i_=pw, j=ssb[:, 3072:4096]:
                    e.scalar_tensor_tensor(o, i_, 0.0, j, op0=ALU.max,
                                           op1=ALU.add), inc=sV)
                bank_free[b0] = bank_free[b0 + 1] = qT_w[jp]
            xnt_used[t] = (sT, sT.n)
            if t + 2 < NTILE:
                emit_xnt_load(t + 2)
            waitp("tensor", kv_ready[b])
            for jc in range(JC):
                bank = 2 + jc % 2
                waitp("tensor", bank_free[bank])
                waitp("tensor", qT_w[jc // 2])
                r = run("tensor", lambda e, o=bank_ap(bank),
                        l=kv[:, b * 1024 + jc * 128: b * 1024 + (jc + 1) * 128],
                        rr=pb[:, PB_HID + jc * TT:PB_HID + (jc + 1) * TT]:
                        e.matmul(o, l, rr, start=True, stop=True), inc=sT)
                bank_ready[bank] = r
                waitp("vector", r)
                bank_free[bank] = run(
                    "vector", lambda e,
                    o=pb[:, PB_HID + jc * TT:PB_HID + (jc + 1) * TT],
                    i_=bank_ap(bank): e.tensor_copy(o, i_), inc=sV)
            attnT_ready = (sV, sV.n)
            a_last = None
            for tc in range(TC):
                for dh in range(2):
                    bank = 4 + dh
                    waitp("tensor", bank_free[bank])
                    if tc == 0 and dh == 0:
                        waitp("tensor", attnT_ready)
                    for jc in range(JC):
                        run("tensor", lambda e, o=bank_ap(bank),
                            l=pb[:, PB_HID + jc * TT + tc * 128:
                                 PB_HID + jc * TT + tc * 128 + 128],
                            rr=wB[:, jc * D + dh * 512: jc * D + (dh + 1) * 512],
                            stf=(jc == 0):
                            e.matmul(o, l, rr, start=stf, stop=False))
                    r = run("tensor", lambda e, o=bank_ap(bank),
                            l=ones_bf[0:1, 0:128],
                            rr=brows[0:1, D + dh * 512:
                                     D + (dh + 1) * 512]:
                            e.matmul(o, l, rr, start=False, stop=True), inc=sT)
                    bank_ready[bank] = r
                xoff = PF_SLOT + slot * TC * D + tc * D
                ssl = ssb[:, tc * D:(tc + 1) * D]
                yv = None
                waitp("vector", xn_norm[tc])
                for dh in range(2):
                    waitp("vector", bank_ready[4 + dh])
                    yv = run("vector", lambda e,
                             o=ssl[:, dh * 512:(dh + 1) * 512],
                             i_=pf[:, xoff + dh * 512: xoff + (dh + 1) * 512],
                             j=bank_ap(4 + dh):
                             e.tensor_tensor(o, i_, j, op=ALU.add), inc=sV)
                    bank_free[4 + dh] = yv
                run("vector", lambda e, o=pf[:, xoff:xoff + D],
                    i_=pf[:, xoff:xoff + D],
                    m=rinvc[:, t * 4 + tc: t * 4 + tc + 1],
                    mm=mcol[:, t * 4 + tc: t * 4 + tc + 1]:
                    e.tensor_scalar(o, i_, m, mm, op0=ALU.mult, op1=ALU.add))
                waitp("scalar", yv)
                run("scalar", lambda e, o=scr[:], i_=ssl, a=kss[:, tc:tc + 1]:
                    e.activation(o, i_, AF.Identity, accum_out=a))
                a_last = run("scalar", lambda e, o=scr[:], i_=ssl,
                             a=kss[:, 4 + tc:5 + tc]:
                             e.activation(o, i_, AF.Square, accum_out=a),
                             inc=sS)
            waitp("vector", a_last)
            runv(lambda e, o=kls[:, 0:8], i_=kss[:, 0:8]:
                 e.tensor_scalar(o, i_, 1.0 / D, None, op0=ALU.mult))
            runv(lambda e, o=krs[:, 0:4], i_=kls[:, 0:4], j=kls[:, 0:4]:
                 e.tensor_tensor(o, i_, j, op=ALU.mult))
            runv(lambda e, o=krs[:, 4:8], i_=kls[:, 4:8], j=krs[:, 0:4]:
                 e.tensor_tensor(o, i_, j, op=ALU.subtract))
            v2 = runv(lambda e, o=krs[:, 4:8], i_=krs[:, 4:8]:
                      e.tensor_scalar(o, i_, EPS2, None, op0=ALU.add))
            waitp("scalar", v2)
            runsc(lambda e, o=krs[:, 8:12], i_=krs[:, 4:8]:
                  e.activation(o, i_, AF.Ln))
            s2 = runsc(lambda e, o=krs[:, 12:16], i_=krs[:, 8:12]:
                       e.activation(o, i_, AF.Exp, scale=-0.5))
            waitp("vector", s2)
            if t == 0:
                waitp("vector", gb_done)
            for tc in range(TC):
                ssl = ssb[:, tc * D:(tc + 1) * D]
                run("vector", lambda e, o=ssl, i_=ssl, m=kls[:, tc:tc + 1],
                    r_=krs[:, 12 + tc:13 + tc]:
                    e.tensor_scalar(o, i_, m, r_, op0=ALU.subtract,
                                    op1=ALU.mult))
                run("vector", lambda e, o=ssl, i_=ssl, j=g_bc[:]:
                    e.tensor_tensor(o, i_, j, op=ALU.mult))
                run("vector", lambda e, o=ssl, i_=ssl, j=b_bc[:]:
                    e.tensor_tensor(o, i_, j, op=ALU.add))
            waitp("vector", (sS, sS.n))

            def o2load(tc):
                es = tc % 2
                if o2_gate[es] is not None:
                    waitp("sync", o2_gate[es])
                r0 = t * TT + tc * 128
                buf = wstg[:, 0:2048] if es == 0 else xg[:, 0:2048]
                return run("sync", lambda e, o=buf,
                           s=out2_d[2 * r0:2 * (r0 + 128), :].rearrange(
                               "(p r) d -> p (r d)", p=128):
                           e.dma_start(o, s), inc=sO2, k=16), buf

            o2p = {0: o2load(0), 1: o2load(1)}
            ep_v = None
            gs_prev = None
            for tc in range(TC):
                es = tc % 2
                o2ld, o2buf = o2p[tc]
                waitp("vector", o2ld)
                waitp("vector", gs_prev)
                ofn = runv(lambda e, o=scr[:], i_=o2buf[:, 0:1024],
                           j=o2buf[:, 1024:2048]:
                           e.tensor_tensor(o, i_, j, op=ALU.add))
                o2_gate[es] = ofn
                if tc + 2 < TC:
                    o2p[tc + 2] = o2load(tc + 2)
                waitp("scalar", ofn)
                gsl = gbf[:, tc * 1024:(tc + 1) * 1024]
                gs_ = run("scalar", lambda e, o=gsl, i_=scr[:]:
                          e.activation(o, i_, AF.Silu), inc=sS)
                gs_prev = gs_
                waitp("vector", gs_)
                ssl = ssb[:, tc * D:(tc + 1) * D]
                xoff = PF_SLOT + slot * TC * D + tc * D
                xsl = pf[:, xoff: xoff + D]
                run("vector", lambda e, o=ssl, i_=ssl, j=gsl:
                    e.tensor_tensor(o, i_, j, op=ALU.mult))
                ep_v = run("vector", lambda e, o=ssl, i_=ssl, j=xsl:
                           e.tensor_tensor(o, i_, j, op=ALU.add), inc=sV)
                waitp("sync", ep_v)
                run("sync", lambda e,
                    o=out_h[t * TT + tc * 128: t * TT + (tc + 1) * 128, :],
                    s=ssb[:, tc * D:(tc + 1) * D]: e.dma_start(o, s),
                    inc=sO, k=16)
            slot_free[li + 2] = ep_v
            if li + 2 < 32 and x_loaded[li + 2] is None:
                emit_xload(li + 2)

        wait("sync", sO, sO.n)
        assert sO.n == 16 * NTILE * TC, sO.n

        with nc.Block() as block:
            @block.sync
            def _(e):
                for f_ in ops["sync"]:
                    f_(e)

            @block.scalar
            def _(e):
                for f_ in ops["scalar"]:
                    f_(e)

            @block.vector
            def _(e):
                for f_ in ops["vector"]:
                    f_(e)

            @block.tensor
            def _(e):
                for f_ in ops["tensor"]:
                    f_(e)

            @block.gpsimd
            def _(e):
                for f_ in ops["gpsimd"]:
                    f_(e)

    return nc


def kernel(**inputs):
    x = np.ascontiguousarray(inputs["L0_attn_x"], dtype=np.float32)
    base = {
        "Wq": np.ascontiguousarray(inputs["L0_attn_Wq"], np.float32),
        "Wk": np.ascontiguousarray(inputs["L0_attn_Wk"], np.float32),
        "Wv": np.ascontiguousarray(inputs["L0_attn_Wv"], np.float32),
        "Wo": np.ascontiguousarray(inputs["L0_attn_Wo"], np.float32),
        "bq": np.ascontiguousarray(inputs["L0_attn_bq"], np.float32).reshape(1, D),
        "bk": np.ascontiguousarray(inputs["L0_attn_bk"], np.float32).reshape(1, D),
        "bv": np.ascontiguousarray(inputs["L0_attn_bv"], np.float32).reshape(1, D),
        "bo": np.ascontiguousarray(inputs["L0_attn_bo"], np.float32).reshape(1, D),
        "gamma": np.ascontiguousarray(
            inputs["L0_attn_gamma"], np.float32).reshape(1, D),
        "beta": np.ascontiguousarray(
            inputs["L0_attn_beta"], np.float32).reshape(1, D),
        "Wg": np.ascontiguousarray(inputs["L0_ffn_Wg"], np.float32),
        "W1": np.ascontiguousarray(inputs["L0_ffn_W1"], np.float32),
        "W2": np.ascontiguousarray(inputs["L0_ffn_W2"], np.float32),
        "W3": np.ascontiguousarray(inputs["L0_ffn_W3"], np.float32),
    }
    in_maps = []
    for c in range(NC):
        m = dict(base)
        m["x"] = np.ascontiguousarray(x[c * BC:(c + 1) * BC].reshape(NTOK, D))
        in_maps.append(m)
    nc = build_nc()
    res = run_bass_kernel_spmd(nc, in_maps, core_ids=list(range(NC)))
    out = np.concatenate(
        [np.asarray(r["out"], dtype=np.float32).reshape(BC, S, D)
         for r in res.results], axis=0)
    return out


# version 09
# speedup vs baseline: 1.9747x; 1.0225x over previous
import numpy as np
from contextlib import ExitStack

import concourse.bass as bass
import concourse.mybir as mybir
from concourse.bass_utils import run_bass_kernel_spmd
from concourse.bass import IndirectOffsetOnAxis

F32 = mybir.dt.float32
BF16 = mybir.dt.bfloat16
I32 = mybir.dt.int32
AF = mybir.ActivationFunctionType
ALU = mybir.AluOpType
AX = mybir.AxisListType

NC = 8
BC = 4
S = 2048
D = 1024
NTOK = BC * S
TT = 512
NTILE = NTOK // TT
TC = TT // 128
DC = D // 128
JC = 8
F = 2048
FC = F // 128
NE = 8
CAP = 3072
NCH = CAP // 128
CHK = CAP // 512
NCOL = NTOK // 128
EPS1 = 1e-5
EPS2 = 1e-6
BIG = 1.0e30
BIGI = float(2 ** 20)

ENGS = ("sync", "scalar", "vector", "tensor", "gpsimd")

PB_W3 = 0
PB_W1 = 16384
PB_W2 = 32768
PB_XT = 49152
PB_STG = 53248
PB_HID = 55296
PB_TOT = 63488

PF_SLOT = 0
PF_S = 8192
PF_SCR = 12288
PF_TOT = 13312
PF_META = 0
PF_WRUN = 1024
PF_TOKF = 1088
PF_SLOTA = 1152
PF_WHICH = 1664
PF_BASE = 2176
PF_TMP = 2688
PF_D12 = 3200
PF_ECAP = 3328


def build_nc():
    nc = bass.Bass("TRN2", target_bir_lowering=False)

    x_h = nc.dram_tensor("x", [NTOK, D], F32, kind="ExternalInput").ap()
    out_h = nc.dram_tensor("out", [NTOK, D], F32, kind="ExternalOutput").ap()
    wq_h = nc.dram_tensor("Wq", [D, D], F32, kind="ExternalInput").ap()
    wk_h = nc.dram_tensor("Wk", [D, D], F32, kind="ExternalInput").ap()
    wv_h = nc.dram_tensor("Wv", [D, D], F32, kind="ExternalInput").ap()
    wo_h = nc.dram_tensor("Wo", [D, D], F32, kind="ExternalInput").ap()
    bq_h = nc.dram_tensor("bq", [1, D], F32, kind="ExternalInput").ap()
    bk_h = nc.dram_tensor("bk", [1, D], F32, kind="ExternalInput").ap()
    bv_h = nc.dram_tensor("bv", [1, D], F32, kind="ExternalInput").ap()
    bo_h = nc.dram_tensor("bo", [1, D], F32, kind="ExternalInput").ap()
    gam_h = nc.dram_tensor("gamma", [1, D], F32, kind="ExternalInput").ap()
    bet_h = nc.dram_tensor("beta", [1, D], F32, kind="ExternalInput").ap()
    wg_h = nc.dram_tensor("Wg", [NE, D], F32, kind="ExternalInput").ap()
    w1_h = nc.dram_tensor("W1", [NE, F, D], F32, kind="ExternalInput").ap()
    w2_h = nc.dram_tensor("W2", [NE, F, D], F32, kind="ExternalInput").ap()
    w3_h = nc.dram_tensor("W3", [NE, D, F], F32, kind="ExternalInput").ap()
    xnbf_d = nc.dram_tensor("xnbf", [NTOK + 1, D], BF16, kind="Internal").ap()
    xnt_d = nc.dram_tensor("xnt", [NTILE, 128, 4096], BF16, kind="Internal").ap()
    meta_d = nc.dram_tensor("meta", [NTOK + 1, 16], F32, kind="Internal").ap()
    idx_d = nc.dram_tensor("idxd", [NE * CAP + 1, 1], I32, kind="Internal").ap()
    out2_d = nc.dram_tensor("out2", [2 * NTOK + 1, D], BF16, kind="Internal").ap()

    ctx = ExitStack()
    with ctx:
        E = ctx.enter_context
        pf = E(nc.sbuf_tensor("pf", [128, PF_TOT], F32))
        pb = E(nc.sbuf_tensor("pb", [128, PB_TOT], BF16))
        kv = E(nc.sbuf_tensor("kv", [128, BC * 1024], BF16))
        wstg = E(nc.sbuf_tensor("wstg", [128, 2048], BF16))
        xg = E(nc.sbuf_tensor("xg", [128, 2048], BF16))
        idb = E(nc.sbuf_tensor("idb", [128, 128], BF16))
        ltb = E(nc.sbuf_tensor("ltb", [128, 128], BF16))
        oneb = E(nc.sbuf_tensor("oneb", [128, 128], BF16))
        ones_bf = E(nc.sbuf_tensor("ones_bf", [1, 512], BF16))
        ones32 = E(nc.sbuf_tensor("ones32", [1, 128], F32))
        brows = E(nc.sbuf_tensor("brows", [1, 2048], BF16))
        wgT = E(nc.sbuf_tensor("wgT", [128, 64], BF16))
        gsc = E(nc.sbuf_tensor("gsc", [128, 32], F32))
        gat_all = E(nc.sbuf_tensor("gat_all", [128, NCOL * 8], F32))
        mask_all = E(nc.sbuf_tensor("mask_all", [128, NCOL * 8], BF16))
        mecb = E(nc.sbuf_tensor("mecb", [128, 64], BF16))
        mcumA = E(nc.sbuf_tensor("mcumA", [128, 64], BF16))
        mcumB = E(nc.sbuf_tensor("mcumB", [128, 64], BF16))
        mshift = E(nc.sbuf_tensor("mshift", [128, 64], BF16))
        tokid = E(nc.sbuf_tensor("tokid", [128, NCOL], I32))
        d12i = E(nc.sbuf_tensor("d12i", [128, 128], I32))
        idxsb = E(nc.sbuf_tensor("idxsb", [128, NE * NCH], I32))
        metag = E(nc.sbuf_tensor("metag", [128, 64], F32))
        d2i = E(nc.sbuf_tensor("d2i", [128, 4], I32))
        sent = E(nc.sbuf_tensor("sent", [1, 16], F32))
        st2 = E(nc.sbuf_tensor("st2", [128, 8], F32))
        vt = E(nc.sbuf_tensor("vt", [128, 4], F32))
        mcol = E(nc.sbuf_tensor("mcol", [128, 64], F32))
        rstdc = E(nc.sbuf_tensor("rstdc", [128, 64], F32))
        rinvc = E(nc.sbuf_tensor("rinvc", [128, 64], F32))
        gtt = E(nc.sbuf_tensor("gtt", [128, 32], F32))
        gs2 = E(nc.sbuf_tensor("gs2", [128, 32], F32))
        gex = E(nc.sbuf_tensor("gex", [128, 32], F32))
        gm1 = E(nc.sbuf_tensor("gm1", [128, 4], F32))
        gm2 = E(nc.sbuf_tensor("gm2", [128, 4], F32))
        gden = E(nc.sbuf_tensor("gden", [128, 4], F32))
        grec = E(nc.sbuf_tensor("grec", [128, 4], F32))
        gmk = E(nc.sbuf_tensor("gmk", [128, 8], F32))
        kss = E(nc.sbuf_tensor("kss", [128, 16], F32))
        kls = E(nc.sbuf_tensor("kls", [128, 16], F32))
        krs = E(nc.sbuf_tensor("krs", [128, 16], F32))

        psA = E(nc.psum_tensor("psA", [128, 6 * 512], F32))
        psT = [E(nc.psum_tensor("psT1", [128, 1024], BF16)),
               E(nc.psum_tensor("psT2", [128, 1024], BF16))]

        itmp = pf[:, PF_SCR:PF_SCR + 128]
        wA = pb[:, 0:8192]
        wB = pb[:, 8192:16384]
        g_bc = pb[:, PB_W1:PB_W1 + 1024]
        b_bc = pb[:, PB_W1 + 1024:PB_W1 + 2048]
        xnT = pb[:, PB_XT:PB_XT + 4096]

        class Ctr:
            __slots__ = ("h", "n", "name")

            def __init__(self, h, name):
                self.h = h
                self.n = 0
                self.name = name

        def sem(name):
            return Ctr(E(nc.semaphore(name)), name)

        sT = sem("sT")
        sV = sem("sV")
        sS = sem("sS")
        sG = sem("sG")
        sX = sem("sX")
        sWAB = sem("sWAB")
        sRO = sem("sRO")
        sO = sem("sO")
        sXB = sem("sXB")
        sZ = sem("sZ")
        sME = sem("sME")
        sO2 = sem("sO2")
        sXT = sem("sXT")

        ops = {k: [] for k in ENGS}

        def emit(eng, f):
            ops[eng].append(f)

        def wait(eng, ctr, v):
            assert v <= ctr.n, f"wait on future {ctr.name}: {v} > {ctr.n}"
            if v <= 0:
                return
            emit(eng, lambda e, h=ctr.h, vv=v: e.wait_ge(h, vv))

        def waitp(eng, pair):
            if pair is not None:
                wait(eng, pair[0], pair[1])

        def run(eng, maker, inc=None, k=1):
            if inc is None:
                emit(eng, maker)
                return None
            inc.n += k
            emit(eng, lambda e, m=maker, h=inc.h, kk=k: m(e).then_inc(h, kk))
            return (inc, inc.n)

        def runv(maker):
            p = run("vector", maker, inc=sV)
            wait("vector", sV, p[1])
            return p

        def runsc(maker):
            p = run("scalar", maker, inc=sS)
            wait("scalar", sS, p[1])
            return p

        bank_free = [None] * 8
        bank_ready = [None] * 8

        def bank_ap(b, off=0, width=512):
            return psA[:, b * 512 + off: b * 512 + off + width]

        def emit_t8(src_ap, dst_ap3, par, evac_eng="vector", evac_gates=()):
            waitp("tensor", bank_free[6 + par])
            r = None
            for i in range(8):
                r = run("tensor",
                        lambda e, o=psT[par][:, i * 128:(i + 1) * 128],
                        i_=src_ap[:, i * 128:(i + 1) * 128], d=idb[:]:
                        e.transpose(o, i_, d),
                        inc=sT if i == 7 else None)
            bank_ready[6 + par] = r
            waitp(evac_eng, r)
            for gp in evac_gates:
                waitp(evac_eng, gp)
            in_ap = psT[par][:].rearrange("p (i t) -> p i t", t=128)
            if evac_eng == "vector":
                ev = run("vector", lambda e, o=dst_ap3, i_=in_ap:
                         e.tensor_copy(o, i_), inc=sV)
            else:
                ev = run("scalar", lambda e, o=dst_ap3, i_=in_ap:
                         e.activation(o, i_, AF.Identity), inc=sS)
            bank_free[6 + par] = ev
            return ev

        def emit_ln1(slot, tc, need_rinv, scol):
            xoff = PF_SLOT + slot * TC * D + tc * D
            xi = pf[:, xoff: xoff + D]
            dump = pf[:, PF_SCR:PF_SCR + D]
            run("scalar", lambda e, o=dump, i_=xi, a=st2[:, 0:1]:
                e.activation(o, i_, AF.Identity, accum_out=a))
            a1 = run("scalar", lambda e, o=dump, i_=xi, a=st2[:, 1:2]:
                     e.activation(o, i_, AF.Square, accum_out=a), inc=sS)
            waitp("vector", a1)
            mc = mcol[:, scol:scol + 1]
            runv(lambda e, o=mc, i_=st2[:, 0:1]:
                 e.tensor_scalar(o, i_, 1.0 / D, None, op0=ALU.mult))
            runv(lambda e, o=vt[:, 0:1], i_=st2[:, 1:2]:
                 e.tensor_scalar(o, i_, 1.0 / D, None, op0=ALU.mult))
            runv(lambda e, o=vt[:, 1:2], i_=mc, j=mc:
                 e.tensor_tensor(o, i_, j, op=ALU.mult))
            runv(lambda e, o=vt[:, 2:3], i_=vt[:, 0:1], j=vt[:, 1:2]:
                 e.tensor_tensor(o, i_, j, op=ALU.subtract))
            v1 = runv(lambda e, o=vt[:, 2:3], i_=vt[:, 2:3]:
                      e.tensor_scalar(o, i_, EPS1, None, op0=ALU.add))
            waitp("scalar", v1)
            runsc(lambda e, o=vt[:, 3:4], i_=vt[:, 2:3]:
                  e.activation(o, i_, AF.Ln))
            rc = rstdc[:, scol:scol + 1]
            s1 = runsc(lambda e, o=rc, i_=vt[:, 3:4]:
                       e.activation(o, i_, AF.Exp, scale=-0.5))
            if need_rinv:
                ric = rinvc[:, scol:scol + 1]
                s1 = runsc(lambda e, o=ric, i_=vt[:, 3:4]:
                           e.activation(o, i_, AF.Exp, scale=0.5))
            waitp("vector", s1)
            return run("vector", lambda e, o=xi, i_=xi, m=mc, r=rc:
                       e.tensor_scalar(o, i_, m, r, op0=ALU.subtract,
                                       op1=ALU.mult), inc=sV)

        def emit_cast(slot, tc, ln_done, stg_gates):
            xoff = PF_SLOT + slot * TC * D + tc * D
            stg = pb[:, PB_STG + (tc % 2) * 1024: PB_STG + (tc % 2) * 1024 + 1024]
            waitp("scalar", ln_done)
            for gp in stg_gates:
                waitp("scalar", gp)
            c = run("scalar", lambda e, o=stg, i_=pf[:, xoff:xoff + D]:
                    e.activation(o, i_, AF.Identity), inc=sS)
            return c, stg

        run("gpsimd", lambda e: e.iota(itmp[:], [[1, 128]], channel_multiplier=-1,
                                       allow_small_or_imprecise_dtypes=True),
            inc=sG)
        g2 = run("gpsimd", lambda e: e.iota(tokid[:], [[128, NCOL]],
                                            channel_multiplier=1), inc=sG)
        wait("vector", sG, sG.n)
        runv(lambda e, o=ltb[:], i_=itmp[:]:
             e.tensor_scalar(o, i_, 0.0, None, op0=ALU.is_gt))
        runv(lambda e, o=itmp[:], i_=itmp[:]:
             e.tensor_scalar(o, i_, 0.0, None, op0=ALU.is_equal))
        idb_done = runv(lambda e: e.tensor_copy(idb[:], itmp[:]))
        runv(lambda e: e.memset(oneb[:], 1.0))
        runv(lambda e: e.memset(ones_bf[:], 1.0))
        runv(lambda e: e.memset(ones32[:], 1.0))
        runv(lambda e: e.memset(xg[:], 0.0))
        ixm = runv(lambda e: e.memset(idxsb[:], NTOK))
        kvz = runv(lambda e: e.memset(kv[:], 0.0))

        waitp("scalar", kvz)
        for z in range(32):
            run("scalar", lambda e, z=z, o=out2_d[z * 512:(z + 1) * 512, :]
                .rearrange("(p r) d -> p (r d)", p=128), i_=kv[:]:
                e.dma_start(o, i_), inc=sZ, k=16)
        run("scalar", lambda e, o=xnbf_d[NTOK:NTOK + 1, :], i_=kv[0:1, 0:1024]:
            e.dma_start(o, i_), inc=sZ, k=16)
        z_done = (sZ, sZ.n)

        waitp("gpsimd", ixm)
        run("gpsimd", lambda e, o=idx_d[0:NE * CAP].rearrange(
            "(p c) one -> p (c one)", p=128),
            i_=idxsb[:]: e.dma_start(o, i_), inc=sG, k=16)
        ixinit = (sG, sG.n)

        for (w_src, w_dst) in ((wk_h, wA), (wv_h, wB)):
            run("gpsimd", lambda e, s=w_src, d=w_dst:
                e.dma_start(d.rearrange("p (dc j) -> p dc j", j=D),
                            s.rearrange("(dc p) j -> p dc j", p=128)),
                inc=sWAB, k=16)
        wab_phA = (sWAB, sWAB.n)

        for i, bh in enumerate((bk_h, bv_h)):
            run("sync", lambda e, s=bh, o=pf[0:1, PF_S + i * D:PF_S + (i + 1) * D]:
                e.dma_start(o, s), inc=sRO, k=16)
        r1 = (sRO, sRO.n)
        waitp("vector", r1)
        bro = run("vector", lambda e: e.tensor_copy(
            brows[:], pf[0:1, PF_S:PF_S + 2048]), inc=sV)
        waitp("sync", bro)
        run("sync", lambda e: e.dma_start(pf[0:8, PF_S:PF_S + D], wg_h),
            inc=sRO, k=16)
        r2 = (sRO, sRO.n)
        waitp("scalar", r2)
        wgc = run("scalar", lambda e, o=pb[0:8, PB_STG:PB_STG + D],
                  i_=pf[0:8, PF_S:PF_S + D]:
                  e.activation(o, i_, AF.Identity), inc=sS)
        waitp("tensor", wgc)
        for dc in range(DC):
            par = dc % 2
            waitp("tensor", bank_free[6 + par])
            r = run("tensor", lambda e, o=psT[par][:, 0:8],
                    i_=pb[0:8, PB_STG + dc * 128:PB_STG + (dc + 1) * 128],
                    d=idb[0:8, 0:8]: e.transpose(o, i_, d), inc=sT)
            bank_ready[6 + par] = r
            waitp("vector", r)
            bank_free[6 + par] = run(
                "vector", lambda e, o=wgT[:, dc * 8:(dc + 1) * 8],
                i_=psT[par][:, 0:8]: e.tensor_copy(o, i_), inc=sV)
        rows_done = (sV, sV.n)
        wg_t_done = (sT, sT.n)

        wu_ring = [None] * 2
        wu_counter = [0]

        def emit_wu_load(kind, e_i, u):
            slot = wu_counter[0] % 2
            wu_counter[0] += 1
            if kind == 1:
                src = w1_h[e_i, u * 128:(u + 1) * 128, :]
            elif kind == 2:
                src = w2_h[e_i, u * 128:(u + 1) * 128, :]
            else:
                dch, fcol = u // 2, u % 2
                src = w3_h[e_i, dch * 128:(dch + 1) * 128,
                           fcol * 1024:(fcol + 1) * 1024]
            if wu_ring[slot] is not None:
                waitp("gpsimd", wu_ring[slot])
            ld = run("gpsimd", lambda e, o=wstg[:, slot * 1024:(slot + 1) * 1024],
                     s=src: e.dma_start(o, s), inc=sG, k=16)
            return slot, ld

        def emit_wu_xpose(kind, e_i, u, slot, ld, region_gate, par):
            waitp("tensor", ld)
            if kind == 1:
                dst = pb[:, PB_W1:PB_W1 + 16384].rearrange(
                    "p (dc f) -> p dc f", f=F)[:, :, u * 128:(u + 1) * 128]
            elif kind == 2:
                dst = pb[:, PB_W2:PB_W2 + 16384].rearrange(
                    "p (dc f) -> p dc f", f=F)[:, :, u * 128:(u + 1) * 128]
            else:
                dch, fcol = u // 2, u % 2
                dst = pb[:, PB_W3:PB_W3 + 16384].rearrange(
                    "p (fc d) -> p fc d", d=D)[
                    :, fcol * 8:(fcol + 1) * 8, dch * 128:(dch + 1) * 128]
            waitp("scalar", region_gate)
            ev = emit_t8(wstg[:, slot * 1024:(slot + 1) * 1024], dst, par,
                         evac_eng="scalar")
            wu_ring[slot] = ev
            return ev

        w1u = [[None] * FC for _ in range(NE)]
        w2u = [[None] * FC for _ in range(NE)]
        w3done = [None] * NE

        slot_free = [None] * 34
        x_loaded = [None] * 32

        def emit_xload(i):
            slot = i % 2
            t = i % NTILE
            src = x_h[t * TT:(t + 1) * TT, :].rearrange("(tc p) d -> p tc d",
                                                        p=128)
            dst = pf[:, PF_SLOT + slot * TC * D:PF_SLOT + (slot + 1) * TC * D] \
                .rearrange("p (tc d) -> p tc d", d=D)
            if i >= 2:
                assert slot_free[i] is not None
                waitp("sync", slot_free[i])
            x_loaded[i] = run("sync", lambda e, o=dst, s=src: e.dma_start(o, s),
                              inc=sX, k=16)

        emit_xload(0)
        emit_xload(1)

        kv_ready = [None] * BC
        k_first = True
        gse_prev = None
        stg_state = {0: [wg_t_done], 1: []}
        xnt_store_gate = None
        wu_sched_A = []
        for u in range(FC):
            wu_sched_A.append((1, 0, u))
        for u in range(FC):
            wu_sched_A.append((2, 0, u))
        wuA_loaded = []

        for t in range(NTILE):
            b = t // 4
            slot = t % 2
            li = t
            waitp("scalar", x_loaded[li])
            waitp("vector", x_loaded[li])
            if t == 0:
                waitp("scalar", idb_done)
            a1 = None
            for tc in range(TC):
                xoff = PF_SLOT + slot * TC * D + tc * D
                run("scalar", lambda e, o=pf[:, PF_SCR:PF_SCR + D],
                    i_=pf[:, xoff:xoff + D], a=st2[:, tc:tc + 1]:
                    e.activation(o, i_, AF.Identity, accum_out=a))
                a1 = run("scalar", lambda e, o=pf[:, PF_SCR:PF_SCR + D],
                         i_=pf[:, xoff:xoff + D], a=st2[:, 4 + tc:5 + tc]:
                         e.activation(o, i_, AF.Square, accum_out=a), inc=sS)
            waitp("vector", a1)
            mc4 = mcol[:, t * 4:t * 4 + 4]
            runv(lambda e, o=mc4, i_=st2[:, 0:4]:
                 e.tensor_scalar(o, i_, 1.0 / D, None, op0=ALU.mult))
            runv(lambda e, o=vt[:, 0:4], i_=st2[:, 4:8]:
                 e.tensor_scalar(o, i_, 1.0 / D, None, op0=ALU.mult))
            runv(lambda e, o=st2[:, 0:4], i_=mc4, j=mc4:
                 e.tensor_tensor(o, i_, j, op=ALU.mult))
            runv(lambda e, o=vt[:, 0:4], i_=vt[:, 0:4], j=st2[:, 0:4]:
                 e.tensor_tensor(o, i_, j, op=ALU.subtract))
            v1a = runv(lambda e, o=vt[:, 0:4], i_=vt[:, 0:4]:
                       e.tensor_scalar(o, i_, EPS1, None, op0=ALU.add))
            waitp("scalar", v1a)
            runsc(lambda e, o=st2[:, 4:8], i_=vt[:, 0:4]:
                  e.activation(o, i_, AF.Ln))
            runsc(lambda e, o=rstdc[:, t * 4:t * 4 + 4], i_=st2[:, 4:8]:
                  e.activation(o, i_, AF.Exp, scale=-0.5))
            s1a = runsc(lambda e, o=rinvc[:, t * 4:t * 4 + 4], i_=st2[:, 4:8]:
                        e.activation(o, i_, AF.Exp, scale=0.5))
            waitp("vector", s1a)
            ln_dones = [None] * TC
            for tc in range(TC):
                xoff = PF_SLOT + slot * TC * D + tc * D
                ln_dones[tc] = run(
                    "vector", lambda e, o=pf[:, xoff:xoff + D],
                    i_=pf[:, xoff:xoff + D],
                    m=mcol[:, t * 4 + tc:t * 4 + tc + 1],
                    r=rstdc[:, t * 4 + tc:t * 4 + tc + 1]:
                    e.tensor_scalar(o, i_, m, r, op0=ALU.subtract,
                                    op1=ALU.mult), inc=sV)
            for tc in range(TC):
                ln_done = ln_dones[tc]
                cpair, stg = emit_cast(slot, tc, ln_done, stg_state[tc % 2])
                waitp("sync", cpair)
                run("sync", lambda e,
                    o=xnbf_d[t * TT + tc * 128:t * TT + (tc + 1) * 128, :],
                    i_=stg: e.dma_start(o, i_), inc=sXB, k=16)
                waitp("tensor", cpair)
                dst3 = xnT.rearrange("p (dc tt) -> p dc tt", tt=TT)[
                    :, :, tc * 128:(tc + 1) * 128]
                ev = emit_t8(stg, dst3, tc % 2, evac_eng="vector",
                             evac_gates=(xnt_store_gate,)
                             if xnt_store_gate else ())
                stg_state[tc % 2] = [(sT, sT.n), (sXB, sXB.n)]
                xnT_rdy = ev
                waitp("tensor", xnT_rdy)
                tcol = tc * 128
                if k_first:
                    waitp("tensor", wab_phA)
                    k_first = False
                for (bank0, wsb, brow_i) in ((0, wA, 0), (2, wB, 1)):
                    for jh in range(2):
                        bank = bank0 + jh
                        waitp("tensor", bank_free[bank])
                        for dc in range(DC):
                            run("tensor", lambda e, o=bank_ap(bank),
                                l=xnT[:, dc * TT + tcol: dc * TT + tcol + 128],
                                rr=wsb[:, dc * D + jh * 512:
                                       dc * D + (jh + 1) * 512],
                                stf=(dc == 0):
                                e.matmul(o, l, rr, start=stf, stop=False))
                        r = run("tensor", lambda e, o=bank_ap(bank),
                                l=ones_bf[0:1, 0:128],
                                rr=brows[0:1, brow_i * D + jh * 512:
                                         brow_i * D + (jh + 1) * 512]:
                                e.matmul(o, l, rr, start=False, stop=True),
                                inc=sT)
                        bank_ready[bank] = r
                f1 = pf[:, PF_S:PF_S + D]
                f2 = pf[:, PF_S + D:PF_S + 2 * D]
                v1 = None
                for jh in range(2):
                    waitp("vector", bank_ready[jh])
                    v1 = run("vector", lambda e, o=f2[:, jh * 512:(jh + 1) * 512],
                             i_=bank_ap(jh):
                             e.tensor_scalar(o, i_, 0.0, None, op0=ALU.min),
                             inc=sV if jh == 1 else None)
                waitp("scalar", v1)
                s1 = run("scalar", lambda e, o=f1, i_=f2:
                         e.activation(o, i_, AF.Exp), inc=sS)
                waitp("vector", s1)
                kd = None
                for jh in range(2):
                    kd = run("vector", lambda e, o=f1[:, jh * 512:(jh + 1) * 512],
                             i_=bank_ap(jh), j=f1[:, jh * 512:(jh + 1) * 512]:
                             e.scalar_tensor_tensor(o, i_, 0.0, j, op0=ALU.max,
                                                    op1=ALU.add), inc=sV)
                waitp("tensor", kd)
                rsc = None
                for dc in range(DC):
                    rsc = run("tensor", lambda e, o=bank_ap(0, 0, 8),
                              l=xnT[:, dc * TT + tcol: dc * TT + tcol + 128],
                              rr=wgT[:, dc * 8:(dc + 1) * 8],
                              stf=(dc == 0), spf=(dc == DC - 1):
                              e.matmul(o, l, rr, start=stf, stop=spf,
                                       skip_group_check=True),
                              inc=sT if dc == DC - 1 else None)
                waitp("vector", rsc)
                sev = runv(lambda e, o=gsc[:, tc * 8:(tc + 1) * 8],
                           i_=bank_ap(0, 0, 8): e.tensor_copy(o, i_))
                bank_free[0] = sev
                bank_free[1] = kd
                wait("vector", sV, kd[1])
                runv(lambda e, o=f2, i_=f1, j=f1:
                     e.tensor_tensor(o, i_, j, op=ALU.mult))
                runv(lambda e, o=kss[:],
                     i_=f2.rearrange("p (h dd) -> p h dd", dd=64):
                     e.tensor_reduce(o, i_, axis=AX.X, op=ALU.add))
                v2 = runv(lambda e, o=kss[:], i_=kss[:]:
                          e.tensor_scalar(o, i_, EPS2, None, op0=ALU.add))
                waitp("scalar", v2)
                runsc(lambda e, o=kls[:], i_=kss[:]:
                      e.activation(o, i_, AF.Ln))
                s2 = runsc(lambda e, o=krs[:], i_=kls[:]:
                           e.activation(o, i_, AF.Exp, scale=-0.5))
                waitp("vector", s2)
                kb = run("vector", lambda e,
                         o=pb[:, PB_HID + tc * 1024:
                              PB_HID + (tc + 1) * 1024].rearrange(
                             "p (h d) -> p h d", d=64),
                         i_=f1[:].rearrange("p (h d) -> p h d", d=64),
                         j=krs[:, :, None].broadcast_to([128, 16, 64]):
                         e.tensor_tensor(o, i_, j, op=ALU.mult), inc=sV)
                vb = None
                for jh in range(2):
                    waitp("vector", bank_ready[2 + jh])
                    vb = run("vector", lambda e,
                             o=pb[:, PB_HID + 4096 + tc * 1024 + jh * 512:
                                  PB_HID + 4096 + tc * 1024 + (jh + 1) * 512],
                             i_=bank_ap(2 + jh): e.tensor_copy(o, i_), inc=sV)
                bank_free[2] = bank_free[3] = vb
                waitp("tensor", kb)
                waitp("tensor", vb)
                first = (tc == 0 and t % 4 == 0)
                last = (tc == TC - 1 and t % 4 == 3)
                if first:
                    waitp("tensor", bank_free[4])
                    waitp("tensor", bank_free[5])
                    if b == 0:
                        waitp("vector", z_done)
                r = None
                for pbk in range(8):
                    st_f = first and (pbk % 4 == 0)
                    sp_f = last and (pbk % 4 == 3)
                    r = run("tensor", lambda e,
                            o=bank_ap(4 + pbk // 4, (pbk % 4) * 128, 128),
                            l=pb[:, PB_HID + tc * 1024 + pbk * 128:
                                 PB_HID + tc * 1024 + (pbk + 1) * 128],
                            rr=pb[:, PB_HID + 4096 + tc * 1024 + pbk * 128:
                                  PB_HID + 4096 + tc * 1024 + (pbk + 1) * 128],
                            stf=st_f, spf=sp_f:
                            e.matmul(o, l, rr, start=stf, stop=spf,
                                     skip_group_check=True),
                            inc=sT if pbk == 7 else None)
                bank_ready[4] = bank_ready[5] = r
                if tc == TC - 1:
                    slot_free[li + 2] = (sT, sT.n)
                    waitp("sync", xnT_rdy)
                    run("sync", lambda e, o=xnt_d[t], i_=xnT:
                        e.dma_start(o, i_), inc=sXT, k=16)
                    xnt_store_gate = (sXT, sXT.n)
            if gse_prev is not None:
                waitp("vector", gse_prev)
            g3 = lambda a: a[:].rearrange("p (tc k) -> p tc k", k=8)
            gm1b = gm1[:, :, None].broadcast_to([128, 4, 8])
            gm2b = gm2[:, :, None].broadcast_to([128, 4, 8])
            grecb = grec[:, :, None].broadcast_to([128, 4, 8])
            runv(lambda e, o=gm1[:], i_=g3(gsc):
                 e.tensor_reduce(o, i_, axis=AX.X, op=ALU.max))
            runv(lambda e, o=g3(gtt), i_=g3(gsc), j=gm1b:
                 e.tensor_tensor(o, i_, j, op=ALU.subtract))
            runv(lambda e, o=g3(gs2), i_=g3(gsc), j=gm1b:
                 e.tensor_tensor(o, i_, j, op=ALU.is_ge))
            runv(lambda e, o=gs2[:], i_=gs2[:], j=gsc[:]:
                 e.scalar_tensor_tensor(o, i_, -BIG, j, op0=ALU.mult,
                                        op1=ALU.add))
            gv = runv(lambda e, o=gm2[:], i_=g3(gs2):
                      e.tensor_reduce(o, i_, axis=AX.X, op=ALU.max))
            waitp("scalar", gv)
            gse = runsc(lambda e: e.activation(gex[:], gtt[:], AF.Exp))
            gse_prev = gse
            waitp("vector", gse)
            runv(lambda e, o=g3(gtt), i_=g3(gsc), j=gm2b:
                 e.tensor_tensor(o, i_, j, op=ALU.is_ge))
            runv(lambda e, o=mask_all[:, t * 32:(t + 1) * 32], i_=gtt[:]:
                 e.tensor_copy(o, i_))
            runv(lambda e, o=gtt[:], i_=gex[:], j=gtt[:]:
                 e.tensor_tensor(o, i_, j, op=ALU.mult))
            runv(lambda e, o=gden[:], i_=g3(gtt):
                 e.tensor_reduce(o, i_, axis=AX.X, op=ALU.add))
            runv(lambda e: e.reciprocal(grec[:], gden[:]))
            runv(lambda e, o=gat_all[:, t * 32:(t + 1) * 32].rearrange(
                "p (tc k) -> p tc k", k=8), i_=g3(gtt),
                 j=grecb: e.tensor_tensor(o, i_, j, op=ALU.mult))
            if t % 4 == 3:
                waitp("vector", bank_ready[4])
                run("vector", lambda e,
                    o=kv[0:64, b * 1024:(b + 1) * 1024].rearrange(
                        "p (q c) -> p q c", c=128)[:, :, 0:64],
                    i_=psA[0:64, 2048:3072].rearrange(
                        "p (q c) -> p q c", c=128)[:, :, 0:64]:
                    e.tensor_copy(o, i_))
                kvd = run("vector", lambda e,
                          o=kv[64:128, b * 1024:(b + 1) * 1024].rearrange(
                              "p (q c) -> p q c", c=128)[:, :, 64:128],
                          i_=psA[64:128, 2048:3072].rearrange(
                              "p (q c) -> p q c", c=128)[:, :, 64:128]:
                          e.tensor_copy(o, i_), inc=sV)
                bank_free[4] = bank_free[5] = kvd
                kv_ready[b] = kvd
            for uu in range(2):
                idx = t * 2 + uu
                if idx < len(wu_sched_A):
                    kind, e_i, u = wu_sched_A[idx]
                    slot, ld = emit_wu_load(kind, e_i, u)
                    ev = emit_wu_xpose(kind, e_i, u, slot, ld, None, idx % 2)
                    if kind == 1:
                        w1u[e_i][u] = ev
                    else:
                        w2u[e_i][u] = ev
            if li + 1 < NTILE and x_loaded[li + 1] is None:
                emit_xload(li + 1)
            if li + 2 < NTILE and slot_free[li + 2] is not None \
                    and x_loaded[li + 2] is None:
                emit_xload(li + 2)

        A_end_T = (sT, sT.n)
        A_end_V = (sV, sV.n)
        A_end_S = (sS, sS.n)
        A_end_XB = (sXB, sXB.n)

        waitp("vector", A_end_S)
        tokf = pf[:, PF_TOKF:PF_TOKF + 64]
        runv(lambda e, o=tokf, i_=tokid[:]: e.tensor_copy(o, i_))
        ecapv = pf[:, PF_ECAP:PF_ECAP + 512].rearrange("p (c e) -> p e c", e=8)
        for e_i in range(NE):
            runv(lambda e, o=ecapv[:, e_i], v=float(e_i * CAP):
                 e.memset(o, v))
        wrun = pf[:, PF_WRUN:PF_WRUN + 64]
        runv(lambda e, o=wrun: e.memset(o, 0.0))
        whichv = pf[:, PF_WHICH:PF_WHICH + 512].rearrange("p (c e) -> p e c", e=8)
        maskv = mask_all[:].rearrange("p (c e) -> p e c", e=8)
        gatv = gat_all[:].rearrange("p (c e) -> p e c", e=8)
        metab = pf[:, PF_META:PF_META + 1024].rearrange("p (c f) -> p f c", f=16)
        for e_i in range(NE):
            runv(lambda e, o=whichv[:, e_i], i_=wrun: e.tensor_copy(o, i_))
            runv(lambda e, o=metab[:, 8 + e_i], i_=tokf, j=wrun:
                 e.scalar_tensor_tensor(o, i_, 2.0, j, op0=ALU.mult,
                                        op1=ALU.add))
            runv(lambda e, o=metab[:, e_i], i_=gatv[:, e_i]:
                 e.tensor_copy(o, i_))
            if e_i < NE - 1:
                runv(lambda e, o=wrun, i_=wrun, j=maskv[:, e_i]:
                     e.tensor_tensor(o, i_, j, op=ALU.add))
        meta_built = (sV, sV.n)
        runv(lambda e: e.memset(sent[:, 0:8], 0.0))
        sb = runv(lambda e: e.memset(sent[:, 8:16], float(2 * NTOK)))
        waitp("sync", meta_built)
        waitp("sync", sb)
        run("sync", lambda e, o=meta_d[0:NTOK, :]
            .rearrange("(c p) f -> p c f", p=128),
            i_=pf[:, PF_META:PF_META + 1024].rearrange(
                "p (c f) -> p c f", f=16): e.dma_start(o, i_), inc=sME, k=16)
        run("sync", lambda e, o=meta_d[NTOK:NTOK + 1, :], i_=sent[:]:
            e.dma_start(o, i_), inc=sME, k=16)
        meta_stored = (sME, sME.n)

        slotav = pf[:, PF_SLOTA:PF_SLOTA + 512].rearrange("p (c e) -> p e c", e=8)
        for e_i in range(NE):
            mcp = runv(lambda e, o=mecb[:], i_=maskv[:, e_i]: e.tensor_copy(o, i_))
            cur, nxt = mecb, mcumA
            for sh in (1, 2, 4, 8, 16, 32):
                runv(lambda e, o=nxt[:, 0:sh], i_=cur[:, 0:sh]:
                     e.tensor_copy(o, i_))
                runv(lambda e, o=nxt[:, sh:64], i_=cur[:, sh:64],
                     j=cur[:, 0:64 - sh]: e.tensor_tensor(o, i_, j, op=ALU.add))
                cur, nxt = (mcumA, mcumB) if cur is mecb else (
                    (mcumB, mcumA) if cur is mcumA else (mcumA, mcumB))
            runv(lambda e, o=mshift[:, 0:1]: e.memset(o, 0.0))
            msf = runv(lambda e, o=mshift[:, 1:64], i_=cur[:, 0:63]:
                       e.tensor_copy(o, i_))
            waitp("tensor", mcp)
            waitp("tensor", msf)
            waitp("tensor", bank_free[0])
            run("tensor", lambda e, o=bank_ap(0, 0, 64), l=ltb[:], rr=mecb[:]:
                e.matmul(o, l, rr, start=True, stop=False))
            mm = run("tensor", lambda e, o=bank_ap(0, 0, 64), l=oneb[:],
                     rr=mshift[:]: e.matmul(o, l, rr, start=False, stop=True),
                     inc=sT)
            bank_ready[0] = mm
            waitp("vector", mm)
            bank_free[0] = runv(lambda e, o=slotav[:, e_i], i_=bank_ap(0, 0, 64):
                                e.tensor_copy(o, i_))
        slota = pf[:, PF_SLOTA:PF_SLOTA + 512]
        basea = pf[:, PF_BASE:PF_BASE + 512]
        tmpa = pf[:, PF_TMP:PF_TMP + 512]
        runv(lambda e, o=basea, i_=slota, j=pf[:, PF_ECAP:PF_ECAP + 512]:
             e.tensor_tensor(o, i_, j, op=ALU.add))
        runv(lambda e, o=tmpa, i_=slota:
             e.tensor_scalar(o, i_, float(CAP), None, op0=ALU.is_ge))
        runv(lambda e, o=basea, i_=tmpa, j=basea:
             e.scalar_tensor_tensor(o, i_, BIGI, j, op0=ALU.mult, op1=ALU.add))
        whicha = pf[:, PF_WHICH:PF_WHICH + 512]
        for k in range(2):
            runv(lambda e, o=tmpa, i_=whicha, v=float(k):
                 e.tensor_scalar(o, i_, v, None, op0=ALU.is_equal))
            runv(lambda e, o=tmpa, i_=tmpa, j=mask_all[:]:
                 e.tensor_tensor(o, i_, j, op=ALU.mult))
            runv(lambda e, o=tmpa, i_=tmpa, j=basea:
                 e.tensor_tensor(o, i_, j, op=ALU.mult))
            runv(lambda e, o=pf[:, PF_D12 + k * 64:PF_D12 + (k + 1) * 64],
                 i_=tmpa.rearrange("p (c e) -> p c e", e=8):
                 e.tensor_reduce(o, i_, axis=AX.X, op=ALU.add))
        runv(lambda e, o=pf[:, PF_D12:PF_D12 + 128],
             i_=pf[:, PF_D12:PF_D12 + 128]:
             e.tensor_scalar(o, i_, float(NE * CAP), None, op0=ALU.min))
        d12c = runv(lambda e, o=d12i[:], i_=pf[:, PF_D12:PF_D12 + 128]:
                    e.tensor_copy(o, i_))

        waitp("gpsimd", ixinit)
        waitp("gpsimd", d12c)
        for c in range(NCOL):
            for k in range(2):
                run("gpsimd", lambda e, c=c, k=k: e.indirect_dma_start(
                    idx_d,
                    IndirectOffsetOnAxis(ap=d12i[:, k * 64 + c:k * 64 + c + 1],
                                         axis=0),
                    tokid[:, c:c + 1], None),
                    inc=sG, k=16)
        scat_done = (sG, sG.n)
        wait("gpsimd", sG, sG.n)
        idx_loaded = [None] * NE
        for e_i in range(NE):
            idx_loaded[e_i] = run(
                "gpsimd", lambda e, e_i=e_i:
                e.dma_start(idxsb[:, e_i * NCH:(e_i + 1) * NCH],
                            idx_d[e_i * CAP:(e_i + 1) * CAP].rearrange(
                                "(p c) one -> p (c one)", p=128)),
                inc=sG, k=16)
        R_end_V = (sV, sV.n)

        w1sb = pb[:, PB_W1:PB_W1 + 16384]
        w2sb = pb[:, PB_W2:PB_W2 + 16384]
        w3sb = pb[:, PB_W3:PB_W3 + 16384]
        xeT = pb[:, PB_XT:PB_XT + 4096]
        hid = pb[:, PB_HID:PB_HID + 8192]
        eobs = [pb[:, PB_STG:PB_STG + 1024],
                pb[:, PB_STG + 1024:PB_STG + 2048]]

        xg_gate = [None, None]
        eob_gate = [None, None]
        last_h1 = [None] * NE
        last_h2 = [None] * NE
        last_eo = [None] * NE
        metag_gate = None
        par_ctr = [0]

        def next_par():
            par_ctr[0] ^= 1
            return par_ctr[0]

        w3_sched = [(0, u, A_end_T) for u in range(FC)]

        prev_xeT_gate = None

        for e_i in range(NE):
            for ch in range(CHK):
                if metag_gate is not None:
                    waitp("gpsimd", metag_gate)
                waitp("gpsimd", idx_loaded[e_i])
                waitp("gpsimd", meta_stored)
                gath = [None] * 4
                mg = [None] * 4
                for g in range(4):
                    col = e_i * NCH + ch * 4 + g
                    if xg_gate[g % 2] is not None:
                        waitp("gpsimd", xg_gate[g % 2])
                    gath[g] = run("gpsimd", lambda e, col=col, g=g:
                                  e.indirect_dma_start(
                                      xg[:, (g % 2) * 1024:(g % 2) * 1024 + 1024],
                                      None, xnbf_d,
                                      IndirectOffsetOnAxis(
                                          ap=idxsb[:, col:col + 1], axis=0)),
                                  inc=sG, k=16)
                    mg[g] = run("gpsimd", lambda e, col=col, g=g:
                                e.indirect_dma_start(
                                    metag[:, g * 16:(g + 1) * 16], None, meta_d,
                                    IndirectOffsetOnAxis(
                                        ap=idxsb[:, col:col + 1], axis=0)),
                                inc=sG, k=16)
                    waitp("tensor", gath[g])
                    if prev_xeT_gate is not None:
                        waitp("tensor", prev_xeT_gate)
                    dst3 = xeT.rearrange("p (dc tt) -> p dc tt", tt=512)[
                        :, :, g * 128:(g + 1) * 128]
                    ev = emit_t8(xg[:, (g % 2) * 1024:(g % 2) * 1024 + 1024],
                                 dst3, next_par(), evac_eng="vector")
                    xg_gate[g % 2] = ev
                prev_xeT_gate = None
                xeT_rdy = (sV, sV.n)
                mgall = mg[3]
                mult_last = None
                for fc in range(FC):
                    bank1 = fc % 2
                    bank2 = 2 + fc % 2
                    waitp("tensor", w1u[e_i][fc])
                    waitp("tensor", bank_free[bank1])
                    if fc == 0:
                        waitp("tensor", xeT_rdy)
                    r1_ = None
                    for dc in range(DC):
                        r1_ = run("tensor", lambda e, o=bank_ap(bank1),
                                  l=w1sb[:, dc * F + fc * 128:
                                         dc * F + fc * 128 + 128],
                                  rr=xeT[:, dc * 512:(dc + 1) * 512],
                                  stf=(dc == 0), spf=(dc == DC - 1):
                                  e.matmul(o, l, rr, start=stf, stop=spf),
                                  inc=sT if dc == DC - 1 else None)
                    bank_ready[bank1] = r1_
                    waitp("tensor", w2u[e_i][fc])
                    waitp("tensor", bank_free[bank2])
                    r2_ = None
                    for dc in range(DC):
                        r2_ = run("tensor", lambda e, o=bank_ap(bank2),
                                  l=w2sb[:, dc * F + fc * 128:
                                         dc * F + fc * 128 + 128],
                                  rr=xeT[:, dc * 512:(dc + 1) * 512],
                                  stf=(dc == 0), spf=(dc == DC - 1):
                                  e.matmul(o, l, rr, start=stf, stop=spf),
                                  inc=sT if dc == DC - 1 else None)
                    bank_ready[bank2] = r2_
                    if fc == FC - 1:
                        if ch == CHK - 1:
                            last_h1[e_i] = r1_
                            last_h2[e_i] = r2_
                        prev_xeT_gate = r2_
                    waitp("scalar", bank_ready[bank1])
                    hsl = hid[:, fc * 512:(fc + 1) * 512]
                    silu_l = run("scalar", lambda e, o=hsl, i_=bank_ap(bank1):
                                 e.activation(o, i_, AF.Silu), inc=sS)
                    bank_free[bank1] = silu_l
                    waitp("vector", silu_l)
                    waitp("vector", bank_ready[bank2])
                    mult_last = run("vector", lambda e, o=hsl, i_=hsl,
                                    j=bank_ap(bank2):
                                    e.tensor_tensor(o, i_, j, op=ALU.mult),
                                    inc=sV)
                    bank_free[bank2] = mult_last
                    if ch == 0 and w3_sched:
                        ee, u, gate = w3_sched.pop(0)
                        slot3, ld3 = emit_wu_load(3, ee, u)
                        emit_wu_xpose(3, ee, u, slot3, ld3, gate, next_par())
                        if not w3_sched:
                            w3done[ee] = (sS, sS.n)
                    if ch == CHK - 1 and e_i + 1 < NE:
                        slot1, ld1 = emit_wu_load(1, e_i + 1, fc)
                        w1u[e_i + 1][fc] = emit_wu_xpose(
                            1, e_i + 1, fc, slot1, ld1, r1_, next_par())
                        slot2, ld2 = emit_wu_load(2, e_i + 1, fc)
                        w2u[e_i + 1][fc] = emit_wu_xpose(
                            2, e_i + 1, fc, slot2, ld2, r2_, next_par())
                waitp("tensor", mult_last)
                if ch == 0:
                    waitp("tensor", w3done[e_i])
                waitp("vector", mgall)
                eo_evac_last = None
                for g in range(4):
                    for dh in range(2):
                        bank = 4 + dh
                        waitp("tensor", bank_free[bank])
                        r = None
                        for fc in range(FC):
                            r = run("tensor", lambda e, o=bank_ap(bank),
                                    l=hid[:, fc * 512 + g * 128:
                                         fc * 512 + g * 128 + 128],
                                    rr=w3sb[:, fc * D + dh * 512:
                                           fc * D + (dh + 1) * 512],
                                    stf=(fc == 0), spf=(fc == FC - 1):
                                    e.matmul(o, l, rr, start=stf, stop=spf),
                                    inc=sT if fc == FC - 1 else None)
                        bank_ready[bank] = r
                        if e_i == NE - 1 and ch == CHK - 1 and g == 3 and dh == 1:
                            last_eo[e_i] = r
                    es = g % 2
                    if eob_gate[es] is not None:
                        waitp("vector", eob_gate[es])
                    gcol = metag[:, g * 16 + e_i:g * 16 + e_i + 1]
                    ev = None
                    for dh in range(2):
                        waitp("vector", bank_ready[4 + dh])
                        ev = run("vector", lambda e,
                                 o=eobs[es][:, dh * 512:(dh + 1) * 512],
                                 i_=bank_ap(4 + dh), m=gcol:
                                 e.tensor_scalar(o, i_, m, None, op0=ALU.mult),
                                 inc=sV)
                        bank_free[4 + dh] = ev
                    dcast = runv(lambda e, o=d2i[:, g:g + 1],
                                 i_=metag[:, g * 16 + 8 + e_i:
                                          g * 16 + 8 + e_i + 1]:
                                 e.tensor_copy(o, i_))
                    eo_evac_last = dcast
                    waitp("gpsimd", ev)
                    waitp("gpsimd", dcast)
                    sc = run("gpsimd", lambda e, es=es, g=g:
                             e.indirect_dma_start(
                                 out2_d,
                                 IndirectOffsetOnAxis(ap=d2i[:, g:g + 1], axis=0),
                                 eobs[es][:], None),
                             inc=sG, k=16)
                    eob_gate[es] = sc
                metag_gate = eo_evac_last
                if e_i == NE - 1 and ch == CHK - 1:
                    last_eo[e_i] = bank_ready[5]
            if e_i + 1 < NE:
                gate = bank_ready[5]
                w3_sched = [(e_i + 1, u, gate) for u in range(FC)]

        C_end_eo = last_eo[NE - 1]
        C_scat_all = (sG, sG.n)

        waitp("gpsimd", C_end_eo)
        for (w_src, w_dst) in ((wq_h, wA), (wo_h, wB)):
            run("gpsimd", lambda e, s=w_src, d=w_dst:
                e.dma_start(d.rearrange("p (dc j) -> p dc j", j=D),
                            s.rearrange("(dc p) j -> p dc j", p=128)),
                inc=sWAB, k=16)
        wab_phB = (sWAB, sWAB.n)

        run("sync", lambda e: e.dma_start(pf[0:1, PF_S:PF_S + D], gam_h),
            inc=sRO, k=16)
        run("sync", lambda e: e.dma_start(pf[0:1, PF_S + D:PF_S + 2 * D], bet_h),
            inc=sRO, k=16)
        r3 = (sRO, sRO.n)
        waitp("tensor", r3)
        for j, dst in ((0, g_bc), (1, b_bc)):
            for dh in range(2):
                bank = 2 + dh
                waitp("tensor", bank_free[bank])
                r = run("tensor", lambda e, o=bank_ap(bank), l=ones32[0:1, :],
                        rr=pf[0:1, PF_S + j * D + dh * 512:
                              PF_S + j * D + dh * 512 + 512]:
                        e.matmul(o, l, rr, start=True, stop=True), inc=sT)
                bank_ready[bank] = r
                waitp("vector", r)
                waitp("vector", last_h1[NE - 1])
                bank_free[bank] = run(
                    "vector", lambda e, o=dst[:, dh * 512:(dh + 1) * 512],
                    i_=bank_ap(bank): e.tensor_copy(o, i_), inc=sV)
        gb_done = (sV, sV.n)

        for i, bh in enumerate((bq_h, bo_h)):
            run("sync", lambda e, s=bh, o=pf[0:1, PF_S + 2048 + i * D:
                                            PF_S + 2048 + (i + 1) * D]:
                e.dma_start(o, s), inc=sRO, k=16)
        rb = (sRO, sRO.n)
        waitp("vector", rb)
        waitp("vector", A_end_T)
        brB = run("vector", lambda e: e.tensor_copy(
            brows[:], pf[0:1, PF_S + 2048:PF_S + 4096]), inc=sV)

        slot_free[16] = R_end_V
        slot_free[17] = R_end_V
        waitp("sync", meta_stored)
        waitp("sync", R_end_V)
        emit_xload(16)
        emit_xload(17)

        waitp("vector", C_end_eo)
        waitp("scalar", eob_gate[0])
        waitp("scalar", eob_gate[1])
        waitp("sync", C_scat_all)
        waitp("scalar", C_scat_all)

        qT = pb[:, PB_HID:PB_HID + 4096]
        gbf = pb[:, PB_HID + 4096:PB_HID + 8192]
        ssb = pf[:, PF_S:PF_S + 4096]
        scr = pf[:, PF_SCR:PF_SCR + 1024]
        o2_gate = [None, None]

        xnt_loaded = [None] * NTILE
        xnt_used = [None] * NTILE

        XNT_OFF = (PB_XT, PB_W2)

        def emit_xnt_load(tt_):
            dst = pb[:, XNT_OFF[tt_ % 2]:XNT_OFF[tt_ % 2] + 4096]
            if tt_ >= 2:
                waitp("sync", xnt_used[tt_ - 2])
            xnt_loaded[tt_] = run(
                "sync", lambda e, o=dst, s=xnt_d[tt_]: e.dma_start(o, s),
                inc=sXT, k=16)

        emit_xnt_load(0)
        emit_xnt_load(1)
        for t in range(NTILE):
            b = t // 4
            slot = t % 2
            li = 16 + t
            xnT = pb[:, XNT_OFF[t % 2]:XNT_OFF[t % 2] + 4096]
            waitp("scalar", x_loaded[li])
            waitp("vector", x_loaded[li])
            if t > 0:
                wait("vector", sO, 16 * 4 * t)
                wait("scalar", sO, 16 * 4 * t)
            xn_norm = [None] * TC
            for tc in range(TC):
                xoff = PF_SLOT + slot * TC * D + tc * D
                xn_norm[tc] = run(
                    "vector", lambda e, o=pf[:, xoff:xoff + D],
                    i_=pf[:, xoff:xoff + D],
                    m=mcol[:, t * 4 + tc:t * 4 + tc + 1],
                    r=rstdc[:, t * 4 + tc:t * 4 + tc + 1]:
                    e.tensor_scalar(o, i_, m, r, op0=ALU.subtract,
                                    op1=ALU.mult), inc=sV)
            waitp("tensor", xnt_loaded[t])
            if t == 0:
                waitp("tensor", wab_phB)
                waitp("tensor", brB)
            qT_w = [None] * (JC // 2)
            t1 = scr[:, 0:1024]
            for jp in range(JC // 2):
                b0 = (jp % 2) * 2
                for jj in range(2):
                    jc = jp * 2 + jj
                    bank = b0 + jj
                    waitp("tensor", bank_free[bank])
                    for dc in range(DC):
                        run("tensor", lambda e, o=bank_ap(bank),
                            l=wA[:, dc * D + jc * 128: dc * D + jc * 128 + 128],
                            rr=xnT[:, dc * TT:(dc + 1) * TT], stf=(dc == 0):
                            e.matmul(o, l, rr, start=stf, stop=False))
                    r = run("tensor", lambda e, o=bank_ap(bank),
                            l=brows[0:1, jc * 128:(jc + 1) * 128],
                            rr=ones_bf[0:1, 0:512]:
                            e.matmul(o, l, rr, start=False, stop=True), inc=sT)
                    bank_ready[bank] = r
                pw = bank_ap(b0, 0, 1024)
                waitp("vector", bank_ready[b0 + 1])
                v1 = run("vector", lambda e, o=t1, i_=pw:
                         e.tensor_scalar(o, i_, 0.0, None, op0=ALU.min), inc=sV)
                waitp("scalar", v1)
                s1 = run("scalar", lambda e, o=ssb[:, 3072:4096], i_=t1:
                         e.activation(o, i_, AF.Exp), inc=sS)
                waitp("vector", s1)
                qT_w[jp] = run(
                    "vector", lambda e, o=qT[:, jp * 1024:(jp + 1) * 1024],
                    i_=pw, j=ssb[:, 3072:4096]:
                    e.scalar_tensor_tensor(o, i_, 0.0, j, op0=ALU.max,
                                           op1=ALU.add), inc=sV)
                bank_free[b0] = bank_free[b0 + 1] = qT_w[jp]
            xnt_used[t] = (sT, sT.n)
            if t + 2 < NTILE:
                emit_xnt_load(t + 2)
            waitp("tensor", kv_ready[b])
            for jc in range(JC):
                bank = 2 + jc % 2
                waitp("tensor", bank_free[bank])
                waitp("tensor", qT_w[jc // 2])
                r = run("tensor", lambda e, o=bank_ap(bank),
                        l=kv[:, b * 1024 + jc * 128: b * 1024 + (jc + 1) * 128],
                        rr=pb[:, PB_HID + jc * TT:PB_HID + (jc + 1) * TT]:
                        e.matmul(o, l, rr, start=True, stop=True), inc=sT)
                bank_ready[bank] = r
                waitp("vector", r)
                bank_free[bank] = run(
                    "vector", lambda e,
                    o=pb[:, PB_HID + jc * TT:PB_HID + (jc + 1) * TT],
                    i_=bank_ap(bank): e.tensor_copy(o, i_), inc=sV)
            attnT_ready = (sV, sV.n)
            a_last = None
            recon = [None] * TC
            for tc in range(TC):
                for dh in range(2):
                    bank = 4 + dh
                    waitp("tensor", bank_free[bank])
                    if tc == 0 and dh == 0:
                        waitp("tensor", attnT_ready)
                    for jc in range(JC):
                        run("tensor", lambda e, o=bank_ap(bank),
                            l=pb[:, PB_HID + jc * TT + tc * 128:
                                 PB_HID + jc * TT + tc * 128 + 128],
                            rr=wB[:, jc * D + dh * 512: jc * D + (dh + 1) * 512],
                            stf=(jc == 0):
                            e.matmul(o, l, rr, start=stf, stop=False))
                    r = run("tensor", lambda e, o=bank_ap(bank),
                            l=ones_bf[0:1, 0:128],
                            rr=brows[0:1, D + dh * 512:
                                     D + (dh + 1) * 512]:
                            e.matmul(o, l, rr, start=False, stop=True), inc=sT)
                    bank_ready[bank] = r
                xoff = PF_SLOT + slot * TC * D + tc * D
                ssl = ssb[:, tc * D:(tc + 1) * D]
                yv = None
                waitp("vector", xn_norm[tc])
                for dh in range(2):
                    waitp("vector", bank_ready[4 + dh])
                    yv = run("vector", lambda e,
                             o=ssl[:, dh * 512:(dh + 1) * 512],
                             i_=pf[:, xoff + dh * 512: xoff + (dh + 1) * 512],
                             j=bank_ap(4 + dh):
                             e.tensor_tensor(o, i_, j, op=ALU.add), inc=sV)
                    bank_free[4 + dh] = yv
                waitp("gpsimd", yv)
                recon[tc] = run("gpsimd", lambda e, o=pf[:, xoff:xoff + D],
                                i_=pf[:, xoff:xoff + D],
                                m=rinvc[:, t * 4 + tc: t * 4 + tc + 1],
                                mm=mcol[:, t * 4 + tc: t * 4 + tc + 1]:
                                e.tensor_scalar(o, i_, m, mm, op0=ALU.mult,
                                                op1=ALU.add), inc=sG)
                waitp("scalar", yv)
                run("scalar", lambda e, o=scr[:], i_=ssl, a=kss[:, tc:tc + 1]:
                    e.activation(o, i_, AF.Identity, accum_out=a))
                a_last = run("scalar", lambda e, o=scr[:], i_=ssl,
                             a=kss[:, 4 + tc:5 + tc]:
                             e.activation(o, i_, AF.Square, accum_out=a),
                             inc=sS)
            waitp("vector", a_last)
            runv(lambda e, o=kls[:, 0:8], i_=kss[:, 0:8]:
                 e.tensor_scalar(o, i_, 1.0 / D, None, op0=ALU.mult))
            runv(lambda e, o=krs[:, 0:4], i_=kls[:, 0:4], j=kls[:, 0:4]:
                 e.tensor_tensor(o, i_, j, op=ALU.mult))
            runv(lambda e, o=krs[:, 4:8], i_=kls[:, 4:8], j=krs[:, 0:4]:
                 e.tensor_tensor(o, i_, j, op=ALU.subtract))
            v2 = runv(lambda e, o=krs[:, 4:8], i_=krs[:, 4:8]:
                      e.tensor_scalar(o, i_, EPS2, None, op0=ALU.add))
            waitp("scalar", v2)
            runsc(lambda e, o=krs[:, 8:12], i_=krs[:, 4:8]:
                  e.activation(o, i_, AF.Ln))
            s2 = runsc(lambda e, o=krs[:, 12:16], i_=krs[:, 8:12]:
                       e.activation(o, i_, AF.Exp, scale=-0.5))
            waitp("vector", s2)
            if t == 0:
                waitp("vector", gb_done)
            for tc in range(TC):
                ssl = ssb[:, tc * D:(tc + 1) * D]
                run("vector", lambda e, o=ssl, i_=ssl, m=kls[:, tc:tc + 1],
                    r_=krs[:, 12 + tc:13 + tc]:
                    e.tensor_scalar(o, i_, m, r_, op0=ALU.subtract,
                                    op1=ALU.mult))
                run("vector", lambda e, o=ssl, i_=ssl, j=g_bc[:]:
                    e.tensor_tensor(o, i_, j, op=ALU.mult))
                run("vector", lambda e, o=ssl, i_=ssl, j=b_bc[:]:
                    e.tensor_tensor(o, i_, j, op=ALU.add))
            waitp("vector", (sS, sS.n))

            def o2load(tc):
                es = tc % 2
                if o2_gate[es] is not None:
                    waitp("sync", o2_gate[es])
                r0 = t * TT + tc * 128
                buf = wstg[:, 0:2048] if es == 0 else xg[:, 0:2048]
                return run("sync", lambda e, o=buf,
                           s=out2_d[2 * r0:2 * (r0 + 128), :].rearrange(
                               "(p r) d -> p (r d)", p=128):
                           e.dma_start(o, s), inc=sO2, k=16), buf

            o2p = {0: o2load(0), 1: o2load(1)}
            ep_v = None
            gs_prev = None
            for tc in range(TC):
                es = tc % 2
                o2ld, o2buf = o2p[tc]
                waitp("vector", o2ld)
                waitp("vector", gs_prev)
                ofn = runv(lambda e, o=scr[:], i_=o2buf[:, 0:1024],
                           j=o2buf[:, 1024:2048]:
                           e.tensor_tensor(o, i_, j, op=ALU.add))
                o2_gate[es] = ofn
                if tc + 2 < TC:
                    o2p[tc + 2] = o2load(tc + 2)
                waitp("scalar", ofn)
                gsl = gbf[:, tc * 1024:(tc + 1) * 1024]
                gs_ = run("scalar", lambda e, o=gsl, i_=scr[:]:
                          e.activation(o, i_, AF.Silu), inc=sS)
                gs_prev = gs_
                waitp("vector", gs_)
                ssl = ssb[:, tc * D:(tc + 1) * D]
                xoff = PF_SLOT + slot * TC * D + tc * D
                xsl = pf[:, xoff: xoff + D]
                run("vector", lambda e, o=ssl, i_=ssl, j=gsl:
                    e.tensor_tensor(o, i_, j, op=ALU.mult))
                waitp("vector", recon[tc])
                ep_v = run("vector", lambda e, o=ssl, i_=ssl, j=xsl:
                           e.tensor_tensor(o, i_, j, op=ALU.add), inc=sV)
                waitp("sync", ep_v)
                run("sync", lambda e,
                    o=out_h[t * TT + tc * 128: t * TT + (tc + 1) * 128, :],
                    s=ssb[:, tc * D:(tc + 1) * D]: e.dma_start(o, s),
                    inc=sO, k=16)
            slot_free[li + 2] = ep_v
            if li + 2 < 32 and x_loaded[li + 2] is None:
                emit_xload(li + 2)

        wait("sync", sO, sO.n)
        assert sO.n == 16 * NTILE * TC, sO.n

        with nc.Block() as block:
            @block.sync
            def _(e):
                for f_ in ops["sync"]:
                    f_(e)

            @block.scalar
            def _(e):
                for f_ in ops["scalar"]:
                    f_(e)

            @block.vector
            def _(e):
                for f_ in ops["vector"]:
                    f_(e)

            @block.tensor
            def _(e):
                for f_ in ops["tensor"]:
                    f_(e)

            @block.gpsimd
            def _(e):
                for f_ in ops["gpsimd"]:
                    f_(e)

    return nc


def kernel(**inputs):
    x = np.ascontiguousarray(inputs["L0_attn_x"], dtype=np.float32)
    base = {
        "Wq": np.ascontiguousarray(inputs["L0_attn_Wq"], np.float32),
        "Wk": np.ascontiguousarray(inputs["L0_attn_Wk"], np.float32),
        "Wv": np.ascontiguousarray(inputs["L0_attn_Wv"], np.float32),
        "Wo": np.ascontiguousarray(inputs["L0_attn_Wo"], np.float32),
        "bq": np.ascontiguousarray(inputs["L0_attn_bq"], np.float32).reshape(1, D),
        "bk": np.ascontiguousarray(inputs["L0_attn_bk"], np.float32).reshape(1, D),
        "bv": np.ascontiguousarray(inputs["L0_attn_bv"], np.float32).reshape(1, D),
        "bo": np.ascontiguousarray(inputs["L0_attn_bo"], np.float32).reshape(1, D),
        "gamma": np.ascontiguousarray(
            inputs["L0_attn_gamma"], np.float32).reshape(1, D),
        "beta": np.ascontiguousarray(
            inputs["L0_attn_beta"], np.float32).reshape(1, D),
        "Wg": np.ascontiguousarray(inputs["L0_ffn_Wg"], np.float32),
        "W1": np.ascontiguousarray(inputs["L0_ffn_W1"], np.float32),
        "W2": np.ascontiguousarray(inputs["L0_ffn_W2"], np.float32),
        "W3": np.ascontiguousarray(inputs["L0_ffn_W3"], np.float32),
    }
    in_maps = []
    for c in range(NC):
        m = dict(base)
        m["x"] = np.ascontiguousarray(x[c * BC:(c + 1) * BC].reshape(NTOK, D))
        in_maps.append(m)
    nc = build_nc()
    res = run_bass_kernel_spmd(nc, in_maps, core_ids=list(range(NC)))
    out = np.concatenate(
        [np.asarray(r["out"], dtype=np.float32).reshape(BC, S, D)
         for r in res.results], axis=0)
    return out


# version 10
# speedup vs baseline: 2.2340x; 1.1313x over previous
import numpy as np
from contextlib import ExitStack

import concourse.bass as bass
import concourse.mybir as mybir
from concourse.bass_utils import run_bass_kernel_spmd
from concourse.bass import IndirectOffsetOnAxis

F32 = mybir.dt.float32
BF16 = mybir.dt.bfloat16
I32 = mybir.dt.int32
AF = mybir.ActivationFunctionType
ALU = mybir.AluOpType
AX = mybir.AxisListType

NC = 8
BC = 4
S = 2048
D = 1024
NTOK = BC * S
TT = 512
NTILE = NTOK // TT
TC = TT // 128
DC = D // 128
JC = 8
F = 2048
FC = F // 128
NE = 8
CAP = 2560
NCH = CAP // 128
CHK = CAP // 512
NCOL = NTOK // 128
EPS1 = 1e-5
EPS2 = 1e-6
BIG = 1.0e30
BIGI = float(2 ** 20)

ENGS = ("sync", "scalar", "vector", "tensor", "gpsimd")

PB_W3 = 0
PB_W1 = 16384
PB_W2 = 32768
PB_XT = 49152
PB_STG = 53248
PB_HID = 55296
PB_TOT = 63488

PF_SLOT = 0
PF_S = 8192
PF_SCR = 12288
PF_TOT = 13312
PF_META = 0
PF_WRUN = 1024
PF_TOKF = 1088
PF_SLOTA = 1152
PF_WHICH = 1664
PF_BASE = 2176
PF_TMP = 2688
PF_D12 = 3200
PF_ECAP = 3328


def build_nc():
    nc = bass.Bass("TRN2", target_bir_lowering=False)

    x_h = nc.dram_tensor("x", [NTOK, D], F32, kind="ExternalInput").ap()
    out_h = nc.dram_tensor("out", [NTOK, D], F32, kind="ExternalOutput").ap()
    wq_h = nc.dram_tensor("Wq", [D, D], F32, kind="ExternalInput").ap()
    wk_h = nc.dram_tensor("Wk", [D, D], F32, kind="ExternalInput").ap()
    wv_h = nc.dram_tensor("Wv", [D, D], F32, kind="ExternalInput").ap()
    wo_h = nc.dram_tensor("Wo", [D, D], F32, kind="ExternalInput").ap()
    bq_h = nc.dram_tensor("bq", [1, D], F32, kind="ExternalInput").ap()
    bk_h = nc.dram_tensor("bk", [1, D], F32, kind="ExternalInput").ap()
    bv_h = nc.dram_tensor("bv", [1, D], F32, kind="ExternalInput").ap()
    bo_h = nc.dram_tensor("bo", [1, D], F32, kind="ExternalInput").ap()
    gam_h = nc.dram_tensor("gamma", [1, D], F32, kind="ExternalInput").ap()
    bet_h = nc.dram_tensor("beta", [1, D], F32, kind="ExternalInput").ap()
    wg_h = nc.dram_tensor("Wg", [NE, D], F32, kind="ExternalInput").ap()
    w1_h = nc.dram_tensor("W1", [NE, F, D], F32, kind="ExternalInput").ap()
    w2_h = nc.dram_tensor("W2", [NE, F, D], F32, kind="ExternalInput").ap()
    w3_h = nc.dram_tensor("W3", [NE, D, F], F32, kind="ExternalInput").ap()
    xnbf_d = nc.dram_tensor("xnbf", [NTOK + 1, D], BF16, kind="Internal").ap()
    xnt_d = nc.dram_tensor("xnt", [NTILE, 128, 4096], BF16, kind="Internal").ap()
    meta_d = nc.dram_tensor("meta", [NTOK + 1, 16], F32, kind="Internal").ap()
    idx_d = nc.dram_tensor("idxd", [NE * CAP + 1, 1], I32, kind="Internal").ap()
    out2_d = nc.dram_tensor("out2", [2 * NTOK + 1, D], BF16, kind="Internal").ap()

    ctx = ExitStack()
    with ctx:
        E = ctx.enter_context
        pf = E(nc.sbuf_tensor("pf", [128, PF_TOT], F32))
        pb = E(nc.sbuf_tensor("pb", [128, PB_TOT], BF16))
        kv = E(nc.sbuf_tensor("kv", [128, BC * 1024], BF16))
        wstg = E(nc.sbuf_tensor("wstg", [128, 2048], BF16))
        xg = E(nc.sbuf_tensor("xg", [128, 2048], BF16))
        idb = E(nc.sbuf_tensor("idb", [128, 128], BF16))
        ltb = E(nc.sbuf_tensor("ltb", [128, 128], BF16))
        oneb = E(nc.sbuf_tensor("oneb", [128, 128], BF16))
        ones_bf = E(nc.sbuf_tensor("ones_bf", [1, 512], BF16))
        ones32 = E(nc.sbuf_tensor("ones32", [1, 128], F32))
        brows = E(nc.sbuf_tensor("brows", [1, 2048], BF16))
        wgT = E(nc.sbuf_tensor("wgT", [128, 64], BF16))
        gsc = E(nc.sbuf_tensor("gsc", [128, 32], F32))
        gat_all = E(nc.sbuf_tensor("gat_all", [128, NCOL * 8], F32))
        mask_all = E(nc.sbuf_tensor("mask_all", [128, NCOL * 8], BF16))
        mecb = E(nc.sbuf_tensor("mecb", [128, 64], BF16))
        mcumA = E(nc.sbuf_tensor("mcumA", [128, 64], BF16))
        mcumB = E(nc.sbuf_tensor("mcumB", [128, 64], BF16))
        mshift = E(nc.sbuf_tensor("mshift", [128, 64], BF16))
        tokid = E(nc.sbuf_tensor("tokid", [128, NCOL], I32))
        d12i = E(nc.sbuf_tensor("d12i", [128, 128], I32))
        idxsb = E(nc.sbuf_tensor("idxsb", [128, NE * NCH], I32))
        metag = E(nc.sbuf_tensor("metag", [128, 64], F32))
        d2i = E(nc.sbuf_tensor("d2i", [128, 4], I32))
        sent = E(nc.sbuf_tensor("sent", [1, 16], F32))
        st2 = E(nc.sbuf_tensor("st2", [128, 8], F32))
        vt = E(nc.sbuf_tensor("vt", [128, 4], F32))
        mcol = E(nc.sbuf_tensor("mcol", [128, 64], F32))
        rstdc = E(nc.sbuf_tensor("rstdc", [128, 64], F32))
        rinvc = E(nc.sbuf_tensor("rinvc", [128, 64], F32))
        gtt = E(nc.sbuf_tensor("gtt", [128, 32], F32))
        gs2 = E(nc.sbuf_tensor("gs2", [128, 32], F32))
        gex = E(nc.sbuf_tensor("gex", [128, 32], F32))
        gm1 = E(nc.sbuf_tensor("gm1", [128, 4], F32))
        gm2 = E(nc.sbuf_tensor("gm2", [128, 4], F32))
        gden = E(nc.sbuf_tensor("gden", [128, 4], F32))
        grec = E(nc.sbuf_tensor("grec", [128, 4], F32))
        gmk = E(nc.sbuf_tensor("gmk", [128, 8], F32))
        kss = E(nc.sbuf_tensor("kss", [128, 16], F32))
        kls = E(nc.sbuf_tensor("kls", [128, 16], F32))
        krs = E(nc.sbuf_tensor("krs", [128, 16], F32))

        psA = E(nc.psum_tensor("psA", [128, 6 * 512], F32))
        psT = [E(nc.psum_tensor("psT1", [128, 1024], BF16)),
               E(nc.psum_tensor("psT2", [128, 1024], BF16))]

        itmp = pf[:, PF_SCR:PF_SCR + 128]
        wA = pb[:, 0:8192]
        wB = pb[:, 8192:16384]
        g_bc = pb[:, PB_W1:PB_W1 + 1024]
        b_bc = pb[:, PB_W1 + 1024:PB_W1 + 2048]
        xnT = pb[:, PB_XT:PB_XT + 4096]

        class Ctr:
            __slots__ = ("h", "n", "name")

            def __init__(self, h, name):
                self.h = h
                self.n = 0
                self.name = name

        def sem(name):
            return Ctr(E(nc.semaphore(name)), name)

        sT = sem("sT")
        sV = sem("sV")
        sS = sem("sS")
        sG = sem("sG")
        sX = sem("sX")
        sWAB = sem("sWAB")
        sRO = sem("sRO")
        sO = sem("sO")
        sXB = sem("sXB")
        sZ = sem("sZ")
        sME = sem("sME")
        sO2 = sem("sO2")
        sXT = sem("sXT")

        ops = {k: [] for k in ENGS}

        def emit(eng, f):
            ops[eng].append(f)

        def wait(eng, ctr, v):
            assert v <= ctr.n, f"wait on future {ctr.name}: {v} > {ctr.n}"
            if v <= 0:
                return
            emit(eng, lambda e, h=ctr.h, vv=v: e.wait_ge(h, vv))

        def waitp(eng, pair):
            if pair is not None:
                wait(eng, pair[0], pair[1])

        def run(eng, maker, inc=None, k=1):
            if inc is None:
                emit(eng, maker)
                return None
            inc.n += k
            emit(eng, lambda e, m=maker, h=inc.h, kk=k: m(e).then_inc(h, kk))
            return (inc, inc.n)

        def runv(maker):
            p = run("vector", maker, inc=sV)
            wait("vector", sV, p[1])
            return p

        def runsc(maker):
            p = run("scalar", maker, inc=sS)
            wait("scalar", sS, p[1])
            return p

        bank_free = [None] * 8
        bank_ready = [None] * 8

        def bank_ap(b, off=0, width=512):
            return psA[:, b * 512 + off: b * 512 + off + width]

        def emit_t8(src_ap, dst_ap3, par, evac_eng="vector", evac_gates=()):
            waitp("tensor", bank_free[6 + par])
            r = None
            for i in range(8):
                r = run("tensor",
                        lambda e, o=psT[par][:, i * 128:(i + 1) * 128],
                        i_=src_ap[:, i * 128:(i + 1) * 128], d=idb[:]:
                        e.transpose(o, i_, d),
                        inc=sT if i == 7 else None)
            bank_ready[6 + par] = r
            waitp(evac_eng, r)
            for gp in evac_gates:
                waitp(evac_eng, gp)
            in_ap = psT[par][:].rearrange("p (i t) -> p i t", t=128)
            if evac_eng == "vector":
                ev = run("vector", lambda e, o=dst_ap3, i_=in_ap:
                         e.tensor_copy(o, i_), inc=sV)
            else:
                ev = run("scalar", lambda e, o=dst_ap3, i_=in_ap:
                         e.activation(o, i_, AF.Identity), inc=sS)
            bank_free[6 + par] = ev
            return ev

        def emit_ln1(slot, tc, need_rinv, scol):
            xoff = PF_SLOT + slot * TC * D + tc * D
            xi = pf[:, xoff: xoff + D]
            dump = pf[:, PF_SCR:PF_SCR + D]
            run("scalar", lambda e, o=dump, i_=xi, a=st2[:, 0:1]:
                e.activation(o, i_, AF.Identity, accum_out=a))
            a1 = run("scalar", lambda e, o=dump, i_=xi, a=st2[:, 1:2]:
                     e.activation(o, i_, AF.Square, accum_out=a), inc=sS)
            waitp("vector", a1)
            mc = mcol[:, scol:scol + 1]
            runv(lambda e, o=mc, i_=st2[:, 0:1]:
                 e.tensor_scalar(o, i_, 1.0 / D, None, op0=ALU.mult))
            runv(lambda e, o=vt[:, 0:1], i_=st2[:, 1:2]:
                 e.tensor_scalar(o, i_, 1.0 / D, None, op0=ALU.mult))
            runv(lambda e, o=vt[:, 1:2], i_=mc, j=mc:
                 e.tensor_tensor(o, i_, j, op=ALU.mult))
            runv(lambda e, o=vt[:, 2:3], i_=vt[:, 0:1], j=vt[:, 1:2]:
                 e.tensor_tensor(o, i_, j, op=ALU.subtract))
            v1 = runv(lambda e, o=vt[:, 2:3], i_=vt[:, 2:3]:
                      e.tensor_scalar(o, i_, EPS1, None, op0=ALU.add))
            waitp("scalar", v1)
            runsc(lambda e, o=vt[:, 3:4], i_=vt[:, 2:3]:
                  e.activation(o, i_, AF.Ln))
            rc = rstdc[:, scol:scol + 1]
            s1 = runsc(lambda e, o=rc, i_=vt[:, 3:4]:
                       e.activation(o, i_, AF.Exp, scale=-0.5))
            if need_rinv:
                ric = rinvc[:, scol:scol + 1]
                s1 = runsc(lambda e, o=ric, i_=vt[:, 3:4]:
                           e.activation(o, i_, AF.Exp, scale=0.5))
            waitp("vector", s1)
            return run("vector", lambda e, o=xi, i_=xi, m=mc, r=rc:
                       e.tensor_scalar(o, i_, m, r, op0=ALU.subtract,
                                       op1=ALU.mult), inc=sV)

        def emit_cast(slot, tc, ln_done, stg_gates):
            xoff = PF_SLOT + slot * TC * D + tc * D
            stg = pb[:, PB_STG + (tc % 2) * 1024: PB_STG + (tc % 2) * 1024 + 1024]
            waitp("scalar", ln_done)
            for gp in stg_gates:
                waitp("scalar", gp)
            c = run("scalar", lambda e, o=stg, i_=pf[:, xoff:xoff + D]:
                    e.activation(o, i_, AF.Identity), inc=sS)
            return c, stg

        run("gpsimd", lambda e: e.iota(itmp[:], [[1, 128]], channel_multiplier=-1,
                                       allow_small_or_imprecise_dtypes=True),
            inc=sG)
        g2 = run("gpsimd", lambda e: e.iota(tokid[:], [[128, NCOL]],
                                            channel_multiplier=1), inc=sG)
        wait("vector", sG, sG.n)
        runv(lambda e, o=ltb[:], i_=itmp[:]:
             e.tensor_scalar(o, i_, 0.0, None, op0=ALU.is_gt))
        runv(lambda e, o=itmp[:], i_=itmp[:]:
             e.tensor_scalar(o, i_, 0.0, None, op0=ALU.is_equal))
        idb_done = runv(lambda e: e.tensor_copy(idb[:], itmp[:]))
        runv(lambda e: e.memset(oneb[:], 1.0))
        runv(lambda e: e.memset(ones_bf[:], 1.0))
        runv(lambda e: e.memset(ones32[:], 1.0))
        runv(lambda e: e.memset(xg[:], 0.0))
        ixm = runv(lambda e: e.memset(idxsb[:], NTOK))
        kvz = runv(lambda e: e.memset(kv[:], 0.0))

        waitp("scalar", kvz)
        for z in range(32):
            run("scalar", lambda e, z=z, o=out2_d[z * 512:(z + 1) * 512, :]
                .rearrange("(p r) d -> p (r d)", p=128), i_=kv[:]:
                e.dma_start(o, i_), inc=sZ, k=16)
        run("scalar", lambda e, o=xnbf_d[NTOK:NTOK + 1, :], i_=kv[0:1, 0:1024]:
            e.dma_start(o, i_), inc=sZ, k=16)
        z_done = (sZ, sZ.n)

        waitp("gpsimd", ixm)
        run("gpsimd", lambda e, o=idx_d[0:NE * CAP].rearrange(
            "(p c) one -> p (c one)", p=128),
            i_=idxsb[:]: e.dma_start(o, i_), inc=sG, k=16)
        ixinit = (sG, sG.n)

        for (w_src, w_dst) in ((wk_h, wA), (wv_h, wB)):
            run("gpsimd", lambda e, s=w_src, d=w_dst:
                e.dma_start(d.rearrange("p (dc j) -> p dc j", j=D),
                            s.rearrange("(dc p) j -> p dc j", p=128)),
                inc=sWAB, k=16)
        wab_phA = (sWAB, sWAB.n)

        for i, bh in enumerate((bk_h, bv_h)):
            run("sync", lambda e, s=bh, o=pf[0:1, PF_S + i * D:PF_S + (i + 1) * D]:
                e.dma_start(o, s), inc=sRO, k=16)
        r1 = (sRO, sRO.n)
        waitp("vector", r1)
        bro = run("vector", lambda e: e.tensor_copy(
            brows[:], pf[0:1, PF_S:PF_S + 2048]), inc=sV)
        waitp("sync", bro)
        run("sync", lambda e: e.dma_start(pf[0:8, PF_S:PF_S + D], wg_h),
            inc=sRO, k=16)
        r2 = (sRO, sRO.n)
        waitp("scalar", r2)
        wgc = run("scalar", lambda e, o=pb[0:8, PB_STG:PB_STG + D],
                  i_=pf[0:8, PF_S:PF_S + D]:
                  e.activation(o, i_, AF.Identity), inc=sS)
        waitp("tensor", wgc)
        for dc in range(DC):
            par = dc % 2
            waitp("tensor", bank_free[6 + par])
            r = run("tensor", lambda e, o=psT[par][:, 0:8],
                    i_=pb[0:8, PB_STG + dc * 128:PB_STG + (dc + 1) * 128],
                    d=idb[0:8, 0:8]: e.transpose(o, i_, d), inc=sT)
            bank_ready[6 + par] = r
            waitp("vector", r)
            bank_free[6 + par] = run(
                "vector", lambda e, o=wgT[:, dc * 8:(dc + 1) * 8],
                i_=psT[par][:, 0:8]: e.tensor_copy(o, i_), inc=sV)
        rows_done = (sV, sV.n)
        wg_t_done = (sT, sT.n)

        wu_ring = [None] * 2
        wu_counter = [0]

        def emit_wu_load(kind, e_i, u):
            slot = wu_counter[0] % 2
            wu_counter[0] += 1
            if kind == 1:
                src = w1_h[e_i, u * 128:(u + 1) * 128, :]
            elif kind == 2:
                src = w2_h[e_i, u * 128:(u + 1) * 128, :]
            else:
                dch, fcol = u // 2, u % 2
                src = w3_h[e_i, dch * 128:(dch + 1) * 128,
                           fcol * 1024:(fcol + 1) * 1024]
            if wu_ring[slot] is not None:
                waitp("gpsimd", wu_ring[slot])
            ld = run("gpsimd", lambda e, o=wstg[:, slot * 1024:(slot + 1) * 1024],
                     s=src: e.dma_start(o, s), inc=sG, k=16)
            return slot, ld

        def emit_wu_xpose(kind, e_i, u, slot, ld, region_gate, par):
            waitp("tensor", ld)
            if kind == 1:
                dst = pb[:, PB_W1:PB_W1 + 16384].rearrange(
                    "p (dc f) -> p dc f", f=F)[:, :, u * 128:(u + 1) * 128]
            elif kind == 2:
                dst = pb[:, PB_W2:PB_W2 + 16384].rearrange(
                    "p (dc f) -> p dc f", f=F)[:, :, u * 128:(u + 1) * 128]
            else:
                dch, fcol = u // 2, u % 2
                dst = pb[:, PB_W3:PB_W3 + 16384].rearrange(
                    "p (fc d) -> p fc d", d=D)[
                    :, fcol * 8:(fcol + 1) * 8, dch * 128:(dch + 1) * 128]
            waitp("scalar", region_gate)
            ev = emit_t8(wstg[:, slot * 1024:(slot + 1) * 1024], dst, par,
                         evac_eng="scalar")
            wu_ring[slot] = ev
            return ev

        w1u = [[None] * FC for _ in range(NE)]
        w2u = [[None] * FC for _ in range(NE)]
        w3done = [None] * NE

        slot_free = [None] * 34
        x_loaded = [None] * 32

        def emit_xload(i):
            slot = i % 2
            t = i % NTILE
            src = x_h[t * TT:(t + 1) * TT, :].rearrange("(tc p) d -> p tc d",
                                                        p=128)
            dst = pf[:, PF_SLOT + slot * TC * D:PF_SLOT + (slot + 1) * TC * D] \
                .rearrange("p (tc d) -> p tc d", d=D)
            if i >= 2:
                assert slot_free[i] is not None
                waitp("sync", slot_free[i])
            x_loaded[i] = run("sync", lambda e, o=dst, s=src: e.dma_start(o, s),
                              inc=sX, k=16)

        emit_xload(0)
        emit_xload(1)

        kv_ready = [None] * BC
        k_first = True
        gse_prev = None
        stg_state = {0: [wg_t_done], 1: []}
        xnt_store_gate = None
        wu_sched_A = []
        for u in range(FC):
            wu_sched_A.append((1, 0, u))
        for u in range(FC):
            wu_sched_A.append((2, 0, u))
        wuA_loaded = []

        for t in range(NTILE):
            b = t // 4
            slot = t % 2
            li = t
            waitp("scalar", x_loaded[li])
            waitp("vector", x_loaded[li])
            if t == 0:
                waitp("scalar", idb_done)
            a1 = None
            for tc in range(TC):
                xoff = PF_SLOT + slot * TC * D + tc * D
                run("scalar", lambda e, o=pf[:, PF_SCR:PF_SCR + D],
                    i_=pf[:, xoff:xoff + D], a=st2[:, tc:tc + 1]:
                    e.activation(o, i_, AF.Identity, accum_out=a))
                a1 = run("scalar", lambda e, o=pf[:, PF_SCR:PF_SCR + D],
                         i_=pf[:, xoff:xoff + D], a=st2[:, 4 + tc:5 + tc]:
                         e.activation(o, i_, AF.Square, accum_out=a), inc=sS)
            waitp("vector", a1)
            mc4 = mcol[:, t * 4:t * 4 + 4]
            runv(lambda e, o=mc4, i_=st2[:, 0:4]:
                 e.tensor_scalar(o, i_, 1.0 / D, None, op0=ALU.mult))
            runv(lambda e, o=vt[:, 0:4], i_=st2[:, 4:8]:
                 e.tensor_scalar(o, i_, 1.0 / D, None, op0=ALU.mult))
            runv(lambda e, o=st2[:, 0:4], i_=mc4, j=mc4:
                 e.tensor_tensor(o, i_, j, op=ALU.mult))
            runv(lambda e, o=vt[:, 0:4], i_=vt[:, 0:4], j=st2[:, 0:4]:
                 e.tensor_tensor(o, i_, j, op=ALU.subtract))
            v1a = runv(lambda e, o=vt[:, 0:4], i_=vt[:, 0:4]:
                       e.tensor_scalar(o, i_, EPS1, None, op0=ALU.add))
            waitp("scalar", v1a)
            runsc(lambda e, o=st2[:, 4:8], i_=vt[:, 0:4]:
                  e.activation(o, i_, AF.Ln))
            runsc(lambda e, o=rstdc[:, t * 4:t * 4 + 4], i_=st2[:, 4:8]:
                  e.activation(o, i_, AF.Exp, scale=-0.5))
            s1a = runsc(lambda e, o=rinvc[:, t * 4:t * 4 + 4], i_=st2[:, 4:8]:
                        e.activation(o, i_, AF.Exp, scale=0.5))
            waitp("vector", s1a)
            ln_dones = [None] * TC
            for tc in range(TC):
                xoff = PF_SLOT + slot * TC * D + tc * D
                ln_dones[tc] = run(
                    "vector", lambda e, o=pf[:, xoff:xoff + D],
                    i_=pf[:, xoff:xoff + D],
                    m=mcol[:, t * 4 + tc:t * 4 + tc + 1],
                    r=rstdc[:, t * 4 + tc:t * 4 + tc + 1]:
                    e.tensor_scalar(o, i_, m, r, op0=ALU.subtract,
                                    op1=ALU.mult), inc=sV)
            for tc in range(TC):
                ln_done = ln_dones[tc]
                cpair, stg = emit_cast(slot, tc, ln_done, stg_state[tc % 2])
                waitp("sync", cpair)
                run("sync", lambda e,
                    o=xnbf_d[t * TT + tc * 128:t * TT + (tc + 1) * 128, :],
                    i_=stg: e.dma_start(o, i_), inc=sXB, k=16)
                waitp("tensor", cpair)
                dst3 = xnT.rearrange("p (dc tt) -> p dc tt", tt=TT)[
                    :, :, tc * 128:(tc + 1) * 128]
                ev = emit_t8(stg, dst3, tc % 2, evac_eng="vector",
                             evac_gates=(xnt_store_gate,)
                             if xnt_store_gate else ())
                stg_state[tc % 2] = [(sT, sT.n), (sXB, sXB.n)]
                xnT_rdy = ev
                waitp("tensor", xnT_rdy)
                tcol = tc * 128
                if k_first:
                    waitp("tensor", wab_phA)
                    k_first = False
                for (bank0, wsb, brow_i) in ((0, wA, 0), (2, wB, 1)):
                    for jh in range(2):
                        bank = bank0 + jh
                        waitp("tensor", bank_free[bank])
                        for dc in range(DC):
                            run("tensor", lambda e, o=bank_ap(bank),
                                l=xnT[:, dc * TT + tcol: dc * TT + tcol + 128],
                                rr=wsb[:, dc * D + jh * 512:
                                       dc * D + (jh + 1) * 512],
                                stf=(dc == 0):
                                e.matmul(o, l, rr, start=stf, stop=False))
                        r = run("tensor", lambda e, o=bank_ap(bank),
                                l=ones_bf[0:1, 0:128],
                                rr=brows[0:1, brow_i * D + jh * 512:
                                         brow_i * D + (jh + 1) * 512]:
                                e.matmul(o, l, rr, start=False, stop=True),
                                inc=sT)
                        bank_ready[bank] = r
                f1 = pf[:, PF_S:PF_S + D]
                f2 = pf[:, PF_S + D:PF_S + 2 * D]
                v1 = None
                for jh in range(2):
                    waitp("vector", bank_ready[jh])
                    v1 = run("vector", lambda e, o=f2[:, jh * 512:(jh + 1) * 512],
                             i_=bank_ap(jh):
                             e.tensor_scalar(o, i_, 0.0, None, op0=ALU.min),
                             inc=sV if jh == 1 else None)
                waitp("scalar", v1)
                s1 = run("scalar", lambda e, o=f1, i_=f2:
                         e.activation(o, i_, AF.Exp), inc=sS)
                waitp("vector", s1)
                kd = None
                for jh in range(2):
                    kd = run("vector", lambda e, o=f1[:, jh * 512:(jh + 1) * 512],
                             i_=bank_ap(jh), j=f1[:, jh * 512:(jh + 1) * 512]:
                             e.scalar_tensor_tensor(o, i_, 0.0, j, op0=ALU.max,
                                                    op1=ALU.add), inc=sV)
                waitp("tensor", kd)
                rsc = None
                for dc in range(DC):
                    rsc = run("tensor", lambda e, o=bank_ap(0, 0, 8),
                              l=xnT[:, dc * TT + tcol: dc * TT + tcol + 128],
                              rr=wgT[:, dc * 8:(dc + 1) * 8],
                              stf=(dc == 0), spf=(dc == DC - 1):
                              e.matmul(o, l, rr, start=stf, stop=spf,
                                       skip_group_check=True),
                              inc=sT if dc == DC - 1 else None)
                waitp("vector", rsc)
                sev = runv(lambda e, o=gsc[:, tc * 8:(tc + 1) * 8],
                           i_=bank_ap(0, 0, 8): e.tensor_copy(o, i_))
                bank_free[0] = sev
                bank_free[1] = kd
                wait("vector", sV, kd[1])
                runv(lambda e, o=f2, i_=f1, j=f1:
                     e.tensor_tensor(o, i_, j, op=ALU.mult))
                runv(lambda e, o=kss[:],
                     i_=f2.rearrange("p (h dd) -> p h dd", dd=64):
                     e.tensor_reduce(o, i_, axis=AX.X, op=ALU.add))
                v2 = runv(lambda e, o=kss[:], i_=kss[:]:
                          e.tensor_scalar(o, i_, EPS2, None, op0=ALU.add))
                waitp("scalar", v2)
                runsc(lambda e, o=kls[:], i_=kss[:]:
                      e.activation(o, i_, AF.Ln))
                s2 = runsc(lambda e, o=krs[:], i_=kls[:]:
                           e.activation(o, i_, AF.Exp, scale=-0.5))
                waitp("vector", s2)
                kb = run("vector", lambda e,
                         o=pb[:, PB_HID + tc * 1024:
                              PB_HID + (tc + 1) * 1024].rearrange(
                             "p (h d) -> p h d", d=64),
                         i_=f1[:].rearrange("p (h d) -> p h d", d=64),
                         j=krs[:, :, None].broadcast_to([128, 16, 64]):
                         e.tensor_tensor(o, i_, j, op=ALU.mult), inc=sV)
                vb = None
                for jh in range(2):
                    waitp("vector", bank_ready[2 + jh])
                    vb = run("vector", lambda e,
                             o=pb[:, PB_HID + 4096 + tc * 1024 + jh * 512:
                                  PB_HID + 4096 + tc * 1024 + (jh + 1) * 512],
                             i_=bank_ap(2 + jh): e.tensor_copy(o, i_), inc=sV)
                bank_free[2] = bank_free[3] = vb
                waitp("tensor", kb)
                waitp("tensor", vb)
                first = (tc == 0 and t % 4 == 0)
                last = (tc == TC - 1 and t % 4 == 3)
                if first:
                    waitp("tensor", bank_free[4])
                    waitp("tensor", bank_free[5])
                    if b == 0:
                        waitp("vector", z_done)
                r = None
                for pbk in range(8):
                    st_f = first and (pbk % 4 == 0)
                    sp_f = last and (pbk % 4 == 3)
                    r = run("tensor", lambda e,
                            o=bank_ap(4 + pbk // 4, (pbk % 4) * 128, 128),
                            l=pb[:, PB_HID + tc * 1024 + pbk * 128:
                                 PB_HID + tc * 1024 + (pbk + 1) * 128],
                            rr=pb[:, PB_HID + 4096 + tc * 1024 + pbk * 128:
                                  PB_HID + 4096 + tc * 1024 + (pbk + 1) * 128],
                            stf=st_f, spf=sp_f:
                            e.matmul(o, l, rr, start=stf, stop=spf,
                                     skip_group_check=True),
                            inc=sT if pbk == 7 else None)
                bank_ready[4] = bank_ready[5] = r
                if tc == TC - 1:
                    slot_free[li + 2] = (sT, sT.n)
                    waitp("sync", xnT_rdy)
                    run("sync", lambda e, o=xnt_d[t], i_=xnT:
                        e.dma_start(o, i_), inc=sXT, k=16)
                    xnt_store_gate = (sXT, sXT.n)
            if gse_prev is not None:
                waitp("vector", gse_prev)
            g3 = lambda a: a[:].rearrange("p (tc k) -> p tc k", k=8)
            gm1b = gm1[:, :, None].broadcast_to([128, 4, 8])
            gm2b = gm2[:, :, None].broadcast_to([128, 4, 8])
            grecb = grec[:, :, None].broadcast_to([128, 4, 8])
            runv(lambda e, o=gm1[:], i_=g3(gsc):
                 e.tensor_reduce(o, i_, axis=AX.X, op=ALU.max))
            runv(lambda e, o=g3(gtt), i_=g3(gsc), j=gm1b:
                 e.tensor_tensor(o, i_, j, op=ALU.subtract))
            runv(lambda e, o=g3(gs2), i_=g3(gsc), j=gm1b:
                 e.tensor_tensor(o, i_, j, op=ALU.is_ge))
            runv(lambda e, o=gs2[:], i_=gs2[:], j=gsc[:]:
                 e.scalar_tensor_tensor(o, i_, -BIG, j, op0=ALU.mult,
                                        op1=ALU.add))
            gv = runv(lambda e, o=gm2[:], i_=g3(gs2):
                      e.tensor_reduce(o, i_, axis=AX.X, op=ALU.max))
            waitp("scalar", gv)
            gse = runsc(lambda e: e.activation(gex[:], gtt[:], AF.Exp))
            gse_prev = gse
            waitp("vector", gse)
            runv(lambda e, o=g3(gtt), i_=g3(gsc), j=gm2b:
                 e.tensor_tensor(o, i_, j, op=ALU.is_ge))
            runv(lambda e, o=mask_all[:, t * 32:(t + 1) * 32], i_=gtt[:]:
                 e.tensor_copy(o, i_))
            runv(lambda e, o=gtt[:], i_=gex[:], j=gtt[:]:
                 e.tensor_tensor(o, i_, j, op=ALU.mult))
            runv(lambda e, o=gden[:], i_=g3(gtt):
                 e.tensor_reduce(o, i_, axis=AX.X, op=ALU.add))
            runv(lambda e: e.reciprocal(grec[:], gden[:]))
            runv(lambda e, o=gat_all[:, t * 32:(t + 1) * 32].rearrange(
                "p (tc k) -> p tc k", k=8), i_=g3(gtt),
                 j=grecb: e.tensor_tensor(o, i_, j, op=ALU.mult))
            if t % 4 == 3:
                waitp("vector", bank_ready[4])
                run("vector", lambda e,
                    o=kv[0:64, b * 1024:(b + 1) * 1024].rearrange(
                        "p (q c) -> p q c", c=128)[:, :, 0:64],
                    i_=psA[0:64, 2048:3072].rearrange(
                        "p (q c) -> p q c", c=128)[:, :, 0:64]:
                    e.tensor_copy(o, i_))
                kvd = run("vector", lambda e,
                          o=kv[64:128, b * 1024:(b + 1) * 1024].rearrange(
                              "p (q c) -> p q c", c=128)[:, :, 64:128],
                          i_=psA[64:128, 2048:3072].rearrange(
                              "p (q c) -> p q c", c=128)[:, :, 64:128]:
                          e.tensor_copy(o, i_), inc=sV)
                bank_free[4] = bank_free[5] = kvd
                kv_ready[b] = kvd
            for uu in range(2):
                idx = t * 2 + uu
                if idx < len(wu_sched_A):
                    kind, e_i, u = wu_sched_A[idx]
                    slot, ld = emit_wu_load(kind, e_i, u)
                    ev = emit_wu_xpose(kind, e_i, u, slot, ld, None, idx % 2)
                    if kind == 1:
                        w1u[e_i][u] = ev
                    else:
                        w2u[e_i][u] = ev
            if li + 1 < NTILE and x_loaded[li + 1] is None:
                emit_xload(li + 1)
            if li + 2 < NTILE and slot_free[li + 2] is not None \
                    and x_loaded[li + 2] is None:
                emit_xload(li + 2)

        A_end_T = (sT, sT.n)
        A_end_V = (sV, sV.n)
        A_end_S = (sS, sS.n)
        A_end_XB = (sXB, sXB.n)

        waitp("vector", A_end_S)
        tokf = pf[:, PF_TOKF:PF_TOKF + 64]
        runv(lambda e, o=tokf, i_=tokid[:]: e.tensor_copy(o, i_))
        ecapv = pf[:, PF_ECAP:PF_ECAP + 512].rearrange("p (c e) -> p e c", e=8)
        for e_i in range(NE):
            runv(lambda e, o=ecapv[:, e_i], v=float(e_i * CAP):
                 e.memset(o, v))
        wrun = pf[:, PF_WRUN:PF_WRUN + 64]
        runv(lambda e, o=wrun: e.memset(o, 0.0))
        whichv = pf[:, PF_WHICH:PF_WHICH + 512].rearrange("p (c e) -> p e c", e=8)
        maskv = mask_all[:].rearrange("p (c e) -> p e c", e=8)
        gatv = gat_all[:].rearrange("p (c e) -> p e c", e=8)
        metab = pf[:, PF_META:PF_META + 1024].rearrange("p (c f) -> p f c", f=16)
        for e_i in range(NE):
            runv(lambda e, o=whichv[:, e_i], i_=wrun: e.tensor_copy(o, i_))
            runv(lambda e, o=metab[:, 8 + e_i], i_=tokf, j=wrun:
                 e.scalar_tensor_tensor(o, i_, 2.0, j, op0=ALU.mult,
                                        op1=ALU.add))
            runv(lambda e, o=metab[:, e_i], i_=gatv[:, e_i]:
                 e.tensor_copy(o, i_))
            if e_i < NE - 1:
                runv(lambda e, o=wrun, i_=wrun, j=maskv[:, e_i]:
                     e.tensor_tensor(o, i_, j, op=ALU.add))
        meta_built = (sV, sV.n)
        runv(lambda e: e.memset(sent[:, 0:8], 0.0))
        sb = runv(lambda e: e.memset(sent[:, 8:16], float(2 * NTOK)))
        waitp("sync", meta_built)
        waitp("sync", sb)
        run("sync", lambda e, o=meta_d[0:NTOK, :]
            .rearrange("(c p) f -> p c f", p=128),
            i_=pf[:, PF_META:PF_META + 1024].rearrange(
                "p (c f) -> p c f", f=16): e.dma_start(o, i_), inc=sME, k=16)
        run("sync", lambda e, o=meta_d[NTOK:NTOK + 1, :], i_=sent[:]:
            e.dma_start(o, i_), inc=sME, k=16)
        meta_stored = (sME, sME.n)

        slotav = pf[:, PF_SLOTA:PF_SLOTA + 512].rearrange("p (c e) -> p e c", e=8)
        for e_i in range(NE):
            mcp = runv(lambda e, o=mecb[:], i_=maskv[:, e_i]: e.tensor_copy(o, i_))
            cur, nxt = mecb, mcumA
            for sh in (1, 2, 4, 8, 16, 32):
                runv(lambda e, o=nxt[:, 0:sh], i_=cur[:, 0:sh]:
                     e.tensor_copy(o, i_))
                runv(lambda e, o=nxt[:, sh:64], i_=cur[:, sh:64],
                     j=cur[:, 0:64 - sh]: e.tensor_tensor(o, i_, j, op=ALU.add))
                cur, nxt = (mcumA, mcumB) if cur is mecb else (
                    (mcumB, mcumA) if cur is mcumA else (mcumA, mcumB))
            runv(lambda e, o=mshift[:, 0:1]: e.memset(o, 0.0))
            msf = runv(lambda e, o=mshift[:, 1:64], i_=cur[:, 0:63]:
                       e.tensor_copy(o, i_))
            waitp("tensor", mcp)
            waitp("tensor", msf)
            waitp("tensor", bank_free[0])
            run("tensor", lambda e, o=bank_ap(0, 0, 64), l=ltb[:], rr=mecb[:]:
                e.matmul(o, l, rr, start=True, stop=False))
            mm = run("tensor", lambda e, o=bank_ap(0, 0, 64), l=oneb[:],
                     rr=mshift[:]: e.matmul(o, l, rr, start=False, stop=True),
                     inc=sT)
            bank_ready[0] = mm
            waitp("vector", mm)
            bank_free[0] = runv(lambda e, o=slotav[:, e_i], i_=bank_ap(0, 0, 64):
                                e.tensor_copy(o, i_))
        slota = pf[:, PF_SLOTA:PF_SLOTA + 512]
        basea = pf[:, PF_BASE:PF_BASE + 512]
        tmpa = pf[:, PF_TMP:PF_TMP + 512]
        runv(lambda e, o=basea, i_=slota, j=pf[:, PF_ECAP:PF_ECAP + 512]:
             e.tensor_tensor(o, i_, j, op=ALU.add))
        runv(lambda e, o=tmpa, i_=slota:
             e.tensor_scalar(o, i_, float(CAP), None, op0=ALU.is_ge))
        runv(lambda e, o=basea, i_=tmpa, j=basea:
             e.scalar_tensor_tensor(o, i_, BIGI, j, op0=ALU.mult, op1=ALU.add))
        whicha = pf[:, PF_WHICH:PF_WHICH + 512]
        for k in range(2):
            runv(lambda e, o=tmpa, i_=whicha, v=float(k):
                 e.tensor_scalar(o, i_, v, None, op0=ALU.is_equal))
            runv(lambda e, o=tmpa, i_=tmpa, j=mask_all[:]:
                 e.tensor_tensor(o, i_, j, op=ALU.mult))
            runv(lambda e, o=tmpa, i_=tmpa, j=basea:
                 e.tensor_tensor(o, i_, j, op=ALU.mult))
            runv(lambda e, o=pf[:, PF_D12 + k * 64:PF_D12 + (k + 1) * 64],
                 i_=tmpa.rearrange("p (c e) -> p c e", e=8):
                 e.tensor_reduce(o, i_, axis=AX.X, op=ALU.add))
        runv(lambda e, o=pf[:, PF_D12:PF_D12 + 128],
             i_=pf[:, PF_D12:PF_D12 + 128]:
             e.tensor_scalar(o, i_, float(NE * CAP), None, op0=ALU.min))
        d12c = runv(lambda e, o=d12i[:], i_=pf[:, PF_D12:PF_D12 + 128]:
                    e.tensor_copy(o, i_))

        waitp("gpsimd", ixinit)
        waitp("gpsimd", d12c)
        for c in range(NCOL):
            for k in range(2):
                run("gpsimd", lambda e, c=c, k=k: e.indirect_dma_start(
                    idx_d,
                    IndirectOffsetOnAxis(ap=d12i[:, k * 64 + c:k * 64 + c + 1],
                                         axis=0),
                    tokid[:, c:c + 1], None),
                    inc=sG, k=16)
        scat_done = (sG, sG.n)
        wait("gpsimd", sG, sG.n)
        idx_loaded = [None] * NE
        for e_i in range(NE):
            idx_loaded[e_i] = run(
                "gpsimd", lambda e, e_i=e_i:
                e.dma_start(idxsb[:, e_i * NCH:(e_i + 1) * NCH],
                            idx_d[e_i * CAP:(e_i + 1) * CAP].rearrange(
                                "(p c) one -> p (c one)", p=128)),
                inc=sG, k=16)
        R_end_V = (sV, sV.n)

        w1sb = pb[:, PB_W1:PB_W1 + 16384]
        w2sb = pb[:, PB_W2:PB_W2 + 16384]
        w3sb = pb[:, PB_W3:PB_W3 + 16384]
        xeT = pb[:, PB_XT:PB_XT + 4096]
        hid = pb[:, PB_HID:PB_HID + 8192]
        eobs = [pb[:, PB_STG:PB_STG + 1024],
                pb[:, PB_STG + 1024:PB_STG + 2048]]

        xg_gate = [None, None]
        eob_gate = [None, None]
        last_h1 = [None] * NE
        last_h2 = [None] * NE
        last_eo = [None] * NE
        metag_gate = None
        par_ctr = [0]

        def next_par():
            par_ctr[0] ^= 1
            return par_ctr[0]

        w3_sched = [(0, u, A_end_T) for u in range(FC)]

        prev_xeT_gate = None

        for e_i in range(NE):
            for ch in range(CHK):
                if metag_gate is not None:
                    waitp("gpsimd", metag_gate)
                waitp("gpsimd", idx_loaded[e_i])
                waitp("gpsimd", meta_stored)
                gath = [None] * 4
                mg = [None] * 4
                for g in range(4):
                    col = e_i * NCH + ch * 4 + g
                    if xg_gate[g % 2] is not None:
                        waitp("gpsimd", xg_gate[g % 2])
                    gath[g] = run("gpsimd", lambda e, col=col, g=g:
                                  e.indirect_dma_start(
                                      xg[:, (g % 2) * 1024:(g % 2) * 1024 + 1024],
                                      None, xnbf_d,
                                      IndirectOffsetOnAxis(
                                          ap=idxsb[:, col:col + 1], axis=0)),
                                  inc=sG, k=16)
                    mg[g] = run("gpsimd", lambda e, col=col, g=g:
                                e.indirect_dma_start(
                                    metag[:, g * 16:(g + 1) * 16], None, meta_d,
                                    IndirectOffsetOnAxis(
                                        ap=idxsb[:, col:col + 1], axis=0)),
                                inc=sG, k=16)
                    waitp("tensor", gath[g])
                    if prev_xeT_gate is not None:
                        waitp("tensor", prev_xeT_gate)
                    dst3 = xeT.rearrange("p (dc tt) -> p dc tt", tt=512)[
                        :, :, g * 128:(g + 1) * 128]
                    ev = emit_t8(xg[:, (g % 2) * 1024:(g % 2) * 1024 + 1024],
                                 dst3, next_par(), evac_eng="vector")
                    xg_gate[g % 2] = ev
                prev_xeT_gate = None
                xeT_rdy = (sV, sV.n)
                mgall = mg[3]
                mult_last = None
                for fc in range(FC):
                    bank1 = fc % 2
                    bank2 = 2 + fc % 2
                    waitp("tensor", w1u[e_i][fc])
                    waitp("tensor", bank_free[bank1])
                    if fc == 0:
                        waitp("tensor", xeT_rdy)
                    r1_ = None
                    for dc in range(DC):
                        r1_ = run("tensor", lambda e, o=bank_ap(bank1),
                                  l=w1sb[:, dc * F + fc * 128:
                                         dc * F + fc * 128 + 128],
                                  rr=xeT[:, dc * 512:(dc + 1) * 512],
                                  stf=(dc == 0), spf=(dc == DC - 1):
                                  e.matmul(o, l, rr, start=stf, stop=spf),
                                  inc=sT if dc == DC - 1 else None)
                    bank_ready[bank1] = r1_
                    waitp("tensor", w2u[e_i][fc])
                    waitp("tensor", bank_free[bank2])
                    r2_ = None
                    for dc in range(DC):
                        r2_ = run("tensor", lambda e, o=bank_ap(bank2),
                                  l=w2sb[:, dc * F + fc * 128:
                                         dc * F + fc * 128 + 128],
                                  rr=xeT[:, dc * 512:(dc + 1) * 512],
                                  stf=(dc == 0), spf=(dc == DC - 1):
                                  e.matmul(o, l, rr, start=stf, stop=spf),
                                  inc=sT if dc == DC - 1 else None)
                    bank_ready[bank2] = r2_
                    if fc == FC - 1:
                        if ch == CHK - 1:
                            last_h1[e_i] = r1_
                            last_h2[e_i] = r2_
                        prev_xeT_gate = r2_
                    waitp("scalar", bank_ready[bank1])
                    hsl = hid[:, fc * 512:(fc + 1) * 512]
                    silu_l = run("scalar", lambda e, o=hsl, i_=bank_ap(bank1):
                                 e.activation(o, i_, AF.Silu), inc=sS)
                    bank_free[bank1] = silu_l
                    waitp("vector", silu_l)
                    waitp("vector", bank_ready[bank2])
                    mult_last = run("vector", lambda e, o=hsl, i_=hsl,
                                    j=bank_ap(bank2):
                                    e.tensor_tensor(o, i_, j, op=ALU.mult),
                                    inc=sV)
                    bank_free[bank2] = mult_last
                    if ch == 0 and w3_sched:
                        ee, u, gate = w3_sched.pop(0)
                        slot3, ld3 = emit_wu_load(3, ee, u)
                        emit_wu_xpose(3, ee, u, slot3, ld3, gate, next_par())
                        if not w3_sched:
                            w3done[ee] = (sS, sS.n)
                    if ch == CHK - 1 and e_i + 1 < NE:
                        slot1, ld1 = emit_wu_load(1, e_i + 1, fc)
                        w1u[e_i + 1][fc] = emit_wu_xpose(
                            1, e_i + 1, fc, slot1, ld1, r1_, next_par())
                        slot2, ld2 = emit_wu_load(2, e_i + 1, fc)
                        w2u[e_i + 1][fc] = emit_wu_xpose(
                            2, e_i + 1, fc, slot2, ld2, r2_, next_par())
                waitp("tensor", mult_last)
                if ch == 0:
                    waitp("tensor", w3done[e_i])
                waitp("vector", mgall)
                eo_evac_last = None
                for g in range(4):
                    for dh in range(2):
                        bank = 4 + dh
                        waitp("tensor", bank_free[bank])
                        r = None
                        for fc in range(FC):
                            r = run("tensor", lambda e, o=bank_ap(bank),
                                    l=hid[:, fc * 512 + g * 128:
                                         fc * 512 + g * 128 + 128],
                                    rr=w3sb[:, fc * D + dh * 512:
                                           fc * D + (dh + 1) * 512],
                                    stf=(fc == 0), spf=(fc == FC - 1):
                                    e.matmul(o, l, rr, start=stf, stop=spf),
                                    inc=sT if fc == FC - 1 else None)
                        bank_ready[bank] = r
                        if e_i == NE - 1 and ch == CHK - 1 and g == 3 and dh == 1:
                            last_eo[e_i] = r
                    es = g % 2
                    if eob_gate[es] is not None:
                        waitp("vector", eob_gate[es])
                    gcol = metag[:, g * 16 + e_i:g * 16 + e_i + 1]
                    ev = None
                    for dh in range(2):
                        waitp("vector", bank_ready[4 + dh])
                        ev = run("vector", lambda e,
                                 o=eobs[es][:, dh * 512:(dh + 1) * 512],
                                 i_=bank_ap(4 + dh), m=gcol:
                                 e.tensor_scalar(o, i_, m, None, op0=ALU.mult),
                                 inc=sV)
                        bank_free[4 + dh] = ev
                    dcast = runv(lambda e, o=d2i[:, g:g + 1],
                                 i_=metag[:, g * 16 + 8 + e_i:
                                          g * 16 + 8 + e_i + 1]:
                                 e.tensor_copy(o, i_))
                    eo_evac_last = dcast
                    waitp("gpsimd", ev)
                    waitp("gpsimd", dcast)
                    sc = run("gpsimd", lambda e, es=es, g=g:
                             e.indirect_dma_start(
                                 out2_d,
                                 IndirectOffsetOnAxis(ap=d2i[:, g:g + 1], axis=0),
                                 eobs[es][:], None),
                             inc=sG, k=16)
                    eob_gate[es] = sc
                metag_gate = eo_evac_last
                if e_i == NE - 1 and ch == CHK - 1:
                    last_eo[e_i] = bank_ready[5]
            if e_i + 1 < NE:
                gate = bank_ready[5]
                w3_sched = [(e_i + 1, u, gate) for u in range(FC)]

        C_end_eo = last_eo[NE - 1]
        C_scat_all = (sG, sG.n)

        waitp("gpsimd", C_end_eo)
        for (w_src, w_dst) in ((wq_h, wA), (wo_h, wB)):
            run("gpsimd", lambda e, s=w_src, d=w_dst:
                e.dma_start(d.rearrange("p (dc j) -> p dc j", j=D),
                            s.rearrange("(dc p) j -> p dc j", p=128)),
                inc=sWAB, k=16)
        wab_phB = (sWAB, sWAB.n)

        run("sync", lambda e: e.dma_start(pf[0:1, PF_S:PF_S + D], gam_h),
            inc=sRO, k=16)
        run("sync", lambda e: e.dma_start(pf[0:1, PF_S + D:PF_S + 2 * D], bet_h),
            inc=sRO, k=16)
        r3 = (sRO, sRO.n)
        waitp("tensor", r3)
        for j, dst in ((0, g_bc), (1, b_bc)):
            for dh in range(2):
                bank = 2 + dh
                waitp("tensor", bank_free[bank])
                r = run("tensor", lambda e, o=bank_ap(bank), l=ones32[0:1, :],
                        rr=pf[0:1, PF_S + j * D + dh * 512:
                              PF_S + j * D + dh * 512 + 512]:
                        e.matmul(o, l, rr, start=True, stop=True), inc=sT)
                bank_ready[bank] = r
                waitp("vector", r)
                waitp("vector", last_h1[NE - 1])
                bank_free[bank] = run(
                    "vector", lambda e, o=dst[:, dh * 512:(dh + 1) * 512],
                    i_=bank_ap(bank): e.tensor_copy(o, i_), inc=sV)
        gb_done = (sV, sV.n)

        for i, bh in enumerate((bq_h, bo_h)):
            run("sync", lambda e, s=bh, o=pf[0:1, PF_S + 2048 + i * D:
                                            PF_S + 2048 + (i + 1) * D]:
                e.dma_start(o, s), inc=sRO, k=16)
        rb = (sRO, sRO.n)
        waitp("vector", rb)
        waitp("vector", A_end_T)
        brB = run("vector", lambda e: e.tensor_copy(
            brows[:], pf[0:1, PF_S + 2048:PF_S + 4096]), inc=sV)

        slot_free[16] = R_end_V
        slot_free[17] = R_end_V
        waitp("sync", meta_stored)
        waitp("sync", R_end_V)
        emit_xload(16)
        emit_xload(17)

        waitp("vector", C_end_eo)
        waitp("scalar", eob_gate[0])
        waitp("scalar", eob_gate[1])
        waitp("sync", C_scat_all)
        waitp("scalar", C_scat_all)

        qT = pb[:, PB_HID:PB_HID + 4096]
        gbf = pb[:, PB_HID + 4096:PB_HID + 8192]
        ssb = pf[:, PF_S:PF_S + 4096]
        scr = pf[:, PF_SCR:PF_SCR + 1024]
        o2_gate = [None, None]
        prev_a_last = None

        xnt_loaded = [None] * NTILE
        xnt_used = [None] * NTILE

        XNT_OFF = (PB_XT, PB_W2)

        def emit_xnt_load(tt_):
            dst = pb[:, XNT_OFF[tt_ % 2]:XNT_OFF[tt_ % 2] + 4096]
            if tt_ >= 2:
                waitp("sync", xnt_used[tt_ - 2])
            xnt_loaded[tt_] = run(
                "sync", lambda e, o=dst, s=xnt_d[tt_]: e.dma_start(o, s),
                inc=sXT, k=16)

        emit_xnt_load(0)
        emit_xnt_load(1)
        for t in range(NTILE):
            b = t // 4
            slot = t % 2
            li = 16 + t
            xnT = pb[:, XNT_OFF[t % 2]:XNT_OFF[t % 2] + 4096]
            waitp("scalar", x_loaded[li])
            waitp("vector", x_loaded[li])
            xn_norm = [None] * TC
            for tc in range(TC):
                xoff = PF_SLOT + slot * TC * D + tc * D
                xn_norm[tc] = run(
                    "vector", lambda e, o=pf[:, xoff:xoff + D],
                    i_=pf[:, xoff:xoff + D],
                    m=mcol[:, t * 4 + tc:t * 4 + tc + 1],
                    r=rstdc[:, t * 4 + tc:t * 4 + tc + 1]:
                    e.tensor_scalar(o, i_, m, r, op0=ALU.subtract,
                                    op1=ALU.mult), inc=sV)
            waitp("tensor", xnt_loaded[t])
            if t == 0:
                waitp("tensor", wab_phB)
                waitp("tensor", brB)
            qT_w = [None] * (JC // 2)
            t1 = scr[:, 0:1024]
            if prev_a_last is not None:
                waitp("vector", prev_a_last)
            for jp in range(JC // 2):
                b0 = (jp % 2) * 2
                for jj in range(2):
                    jc = jp * 2 + jj
                    bank = b0 + jj
                    waitp("tensor", bank_free[bank])
                    for dc in range(DC):
                        run("tensor", lambda e, o=bank_ap(bank),
                            l=wA[:, dc * D + jc * 128: dc * D + jc * 128 + 128],
                            rr=xnT[:, dc * TT:(dc + 1) * TT], stf=(dc == 0):
                            e.matmul(o, l, rr, start=stf, stop=False))
                    r = run("tensor", lambda e, o=bank_ap(bank),
                            l=brows[0:1, jc * 128:(jc + 1) * 128],
                            rr=ones_bf[0:1, 0:512]:
                            e.matmul(o, l, rr, start=False, stop=True), inc=sT)
                    bank_ready[bank] = r
                pw = bank_ap(b0, 0, 1024)
                waitp("vector", bank_ready[b0 + 1])
                v1 = run("vector", lambda e, o=t1, i_=pw:
                         e.tensor_scalar(o, i_, 0.0, None, op0=ALU.min), inc=sV)
                waitp("scalar", v1)
                s1 = run("scalar", lambda e, o=pb[:, 36864:37888], i_=t1:
                         e.activation(o, i_, AF.Exp), inc=sS)
                waitp("vector", s1)
                qT_w[jp] = run(
                    "vector", lambda e, o=qT[:, jp * 1024:(jp + 1) * 1024],
                    i_=pw, j=pb[:, 36864:37888]:
                    e.scalar_tensor_tensor(o, i_, 0.0, j, op0=ALU.max,
                                           op1=ALU.add), inc=sV)
                bank_free[b0] = bank_free[b0 + 1] = qT_w[jp]
            xnt_used[t] = (sT, sT.n)
            if t + 2 < NTILE:
                emit_xnt_load(t + 2)
            waitp("tensor", kv_ready[b])
            for jc in range(JC):
                bank = 2 + jc % 2
                waitp("tensor", bank_free[bank])
                waitp("tensor", qT_w[jc // 2])
                r = run("tensor", lambda e, o=bank_ap(bank),
                        l=kv[:, b * 1024 + jc * 128: b * 1024 + (jc + 1) * 128],
                        rr=pb[:, PB_HID + jc * TT:PB_HID + (jc + 1) * TT]:
                        e.matmul(o, l, rr, start=True, stop=True), inc=sT)
                bank_ready[bank] = r
                waitp("scalar", r)
                bank_free[bank] = run(
                    "scalar", lambda e,
                    o=pb[:, PB_HID + jc * TT:PB_HID + (jc + 1) * TT],
                    i_=bank_ap(bank): e.activation(o, i_, AF.Identity), inc=sS)
            attnT_ready = (sS, sS.n)
            a_last = None
            recon = [None] * TC
            for tc in range(TC):
                for dh in range(2):
                    bank = 4 + dh
                    waitp("tensor", bank_free[bank])
                    if tc == 0 and dh == 0:
                        waitp("tensor", attnT_ready)
                    for jc in range(JC):
                        run("tensor", lambda e, o=bank_ap(bank),
                            l=pb[:, PB_HID + jc * TT + tc * 128:
                                 PB_HID + jc * TT + tc * 128 + 128],
                            rr=wB[:, jc * D + dh * 512: jc * D + (dh + 1) * 512],
                            stf=(jc == 0):
                            e.matmul(o, l, rr, start=stf, stop=False))
                    r = run("tensor", lambda e, o=bank_ap(bank),
                            l=ones_bf[0:1, 0:128],
                            rr=brows[0:1, D + dh * 512:
                                     D + (dh + 1) * 512]:
                            e.matmul(o, l, rr, start=False, stop=True), inc=sT)
                    bank_ready[bank] = r
                xoff = PF_SLOT + slot * TC * D + tc * D
                ssl = ssb[:, tc * D:(tc + 1) * D]
                yv = None
                waitp("vector", xn_norm[tc])
                if t > 0 and tc == 0:
                    wait("vector", sO, 16 * 4 * t)
                for dh in range(2):
                    waitp("vector", bank_ready[4 + dh])
                    yv = run("vector", lambda e,
                             o=ssl[:, dh * 512:(dh + 1) * 512],
                             i_=pf[:, xoff + dh * 512: xoff + (dh + 1) * 512],
                             j=bank_ap(4 + dh):
                             e.tensor_tensor(o, i_, j, op=ALU.add), inc=sV)
                    bank_free[4 + dh] = yv
                waitp("gpsimd", yv)
                recon[tc] = run("gpsimd", lambda e, o=pf[:, xoff:xoff + D],
                                i_=pf[:, xoff:xoff + D],
                                m=rinvc[:, t * 4 + tc: t * 4 + tc + 1],
                                mm=mcol[:, t * 4 + tc: t * 4 + tc + 1]:
                                e.tensor_scalar(o, i_, m, mm, op0=ALU.mult,
                                                op1=ALU.add), inc=sG)
                waitp("scalar", yv)
                run("scalar", lambda e, o=scr[:], i_=ssl, a=kss[:, tc:tc + 1]:
                    e.activation(o, i_, AF.Identity, accum_out=a))
                a_last = run("scalar", lambda e, o=scr[:], i_=ssl,
                             a=kss[:, 4 + tc:5 + tc]:
                             e.activation(o, i_, AF.Square, accum_out=a),
                             inc=sS)
            waitp("vector", a_last)
            runv(lambda e, o=kls[:, 0:8], i_=kss[:, 0:8]:
                 e.tensor_scalar(o, i_, 1.0 / D, None, op0=ALU.mult))
            runv(lambda e, o=krs[:, 0:4], i_=kls[:, 0:4], j=kls[:, 0:4]:
                 e.tensor_tensor(o, i_, j, op=ALU.mult))
            runv(lambda e, o=krs[:, 4:8], i_=kls[:, 4:8], j=krs[:, 0:4]:
                 e.tensor_tensor(o, i_, j, op=ALU.subtract))
            v2 = runv(lambda e, o=krs[:, 4:8], i_=krs[:, 4:8]:
                      e.tensor_scalar(o, i_, EPS2, None, op0=ALU.add))
            waitp("scalar", v2)
            runsc(lambda e, o=krs[:, 8:12], i_=krs[:, 4:8]:
                  e.activation(o, i_, AF.Ln))
            s2 = runsc(lambda e, o=krs[:, 12:16], i_=krs[:, 8:12]:
                       e.activation(o, i_, AF.Exp, scale=-0.5))
            waitp("vector", s2)
            if t == 0:
                waitp("vector", gb_done)
            for tc in range(TC):
                ssl = ssb[:, tc * D:(tc + 1) * D]
                run("vector", lambda e, o=ssl, i_=ssl, m=kls[:, tc:tc + 1],
                    r_=krs[:, 12 + tc:13 + tc]:
                    e.tensor_scalar(o, i_, m, r_, op0=ALU.subtract,
                                    op1=ALU.mult))
                run("vector", lambda e, o=ssl, i_=ssl, j=g_bc[:]:
                    e.tensor_tensor(o, i_, j, op=ALU.mult))
                run("vector", lambda e, o=ssl, i_=ssl, j=b_bc[:]:
                    e.tensor_tensor(o, i_, j, op=ALU.add))
            s_done = (sV, sV.n)
            prev_a_last = a_last
            if t > 0:
                wait("scalar", sO, 16 * 4 * t)

            def o2load(tc):
                es = tc % 2
                if o2_gate[es] is not None:
                    waitp("sync", o2_gate[es])
                r0 = t * TT + tc * 128
                buf = wstg[:, 0:2048] if es == 0 else xg[:, 0:2048]
                return run("sync", lambda e, o=buf,
                           s=out2_d[2 * r0:2 * (r0 + 128), :].rearrange(
                               "(p r) d -> p (r d)", p=128):
                           e.dma_start(o, s), inc=sO2, k=16), buf

            o2p = {0: o2load(0), 1: o2load(1)}
            ep_v = None
            gs_prev = None
            for tc in range(TC):
                es = tc % 2
                o2ld, o2buf = o2p[tc]
                waitp("gpsimd", o2ld)
                waitp("gpsimd", gs_prev)
                ofn = run("gpsimd", lambda e, o=o2buf[:, 0:1024],
                          i_=o2buf[:, 0:1024], j=o2buf[:, 1024:2048]:
                          e.tensor_tensor(o, i_, j, op=ALU.add), inc=sG)
                waitp("scalar", ofn)
                gsl = gbf[:, tc * 1024:(tc + 1) * 1024]
                gs_ = run("scalar", lambda e, o=gsl, i_=o2buf[:, 0:1024]:
                          e.activation(o, i_, AF.Silu), inc=sS)
                gs_prev = gs_
                o2_gate[es] = gs_
                if tc + 2 < TC:
                    o2p[tc + 2] = o2load(tc + 2)
                ssl = ssb[:, tc * D:(tc + 1) * D]
                xoff = PF_SLOT + slot * TC * D + tc * D
                xsl = pf[:, xoff: xoff + D]
                waitp("gpsimd", gs_)
                waitp("gpsimd", s_done)
                sgm = run("gpsimd", lambda e, o=ssl, i_=ssl, j=gsl:
                          e.tensor_tensor(o, i_, j, op=ALU.mult), inc=sG)
                waitp("gpsimd", sgm)
                waitp("gpsimd", recon[tc])
                ep_v = run("gpsimd", lambda e, o=ssl, i_=ssl, j=xsl:
                           e.tensor_tensor(o, i_, j, op=ALU.add), inc=sG)
                waitp("sync", ep_v)
                run("sync", lambda e,
                    o=out_h[t * TT + tc * 128: t * TT + (tc + 1) * 128, :],
                    s=ssb[:, tc * D:(tc + 1) * D]: e.dma_start(o, s),
                    inc=sO, k=16)
            slot_free[li + 2] = ep_v
            if li + 2 < 32 and x_loaded[li + 2] is None:
                emit_xload(li + 2)

        wait("sync", sO, sO.n)
        assert sO.n == 16 * NTILE * TC, sO.n

        with nc.Block() as block:
            @block.sync
            def _(e):
                for f_ in ops["sync"]:
                    f_(e)

            @block.scalar
            def _(e):
                for f_ in ops["scalar"]:
                    f_(e)

            @block.vector
            def _(e):
                for f_ in ops["vector"]:
                    f_(e)

            @block.tensor
            def _(e):
                for f_ in ops["tensor"]:
                    f_(e)

            @block.gpsimd
            def _(e):
                for f_ in ops["gpsimd"]:
                    f_(e)

    return nc


def kernel(**inputs):
    x = np.ascontiguousarray(inputs["L0_attn_x"], dtype=np.float32)
    base = {
        "Wq": np.ascontiguousarray(inputs["L0_attn_Wq"], np.float32),
        "Wk": np.ascontiguousarray(inputs["L0_attn_Wk"], np.float32),
        "Wv": np.ascontiguousarray(inputs["L0_attn_Wv"], np.float32),
        "Wo": np.ascontiguousarray(inputs["L0_attn_Wo"], np.float32),
        "bq": np.ascontiguousarray(inputs["L0_attn_bq"], np.float32).reshape(1, D),
        "bk": np.ascontiguousarray(inputs["L0_attn_bk"], np.float32).reshape(1, D),
        "bv": np.ascontiguousarray(inputs["L0_attn_bv"], np.float32).reshape(1, D),
        "bo": np.ascontiguousarray(inputs["L0_attn_bo"], np.float32).reshape(1, D),
        "gamma": np.ascontiguousarray(
            inputs["L0_attn_gamma"], np.float32).reshape(1, D),
        "beta": np.ascontiguousarray(
            inputs["L0_attn_beta"], np.float32).reshape(1, D),
        "Wg": np.ascontiguousarray(inputs["L0_ffn_Wg"], np.float32),
        "W1": np.ascontiguousarray(inputs["L0_ffn_W1"], np.float32),
        "W2": np.ascontiguousarray(inputs["L0_ffn_W2"], np.float32),
        "W3": np.ascontiguousarray(inputs["L0_ffn_W3"], np.float32),
    }
    in_maps = []
    for c in range(NC):
        m = dict(base)
        m["x"] = np.ascontiguousarray(x[c * BC:(c + 1) * BC].reshape(NTOK, D))
        in_maps.append(m)
    nc = build_nc()
    res = run_bass_kernel_spmd(nc, in_maps, core_ids=list(range(NC)))
    out = np.concatenate(
        [np.asarray(r["out"], dtype=np.float32).reshape(BC, S, D)
         for r in res.results], axis=0)
    return out


# version 11
# speedup vs baseline: 2.2386x; 1.0021x over previous
import numpy as np
from contextlib import ExitStack

import concourse.bass as bass
import concourse.mybir as mybir
from concourse.bass_utils import run_bass_kernel_spmd
from concourse.bass import IndirectOffsetOnAxis

F32 = mybir.dt.float32
BF16 = mybir.dt.bfloat16
I32 = mybir.dt.int32
AF = mybir.ActivationFunctionType
ALU = mybir.AluOpType
AX = mybir.AxisListType

NC = 8
BC = 4
S = 2048
D = 1024
NTOK = BC * S
TT = 512
NTILE = NTOK // TT
TC = TT // 128
DC = D // 128
JC = 8
F = 2048
FC = F // 128
NE = 8
CAP = 2560
NCH = CAP // 128
CHK = CAP // 512
NCOL = NTOK // 128
EPS1 = 1e-5
EPS2 = 1e-6
BIG = 1.0e30
BIGI = float(2 ** 20)

ENGS = ("sync", "scalar", "vector", "tensor", "gpsimd")

PB_W3 = 0
PB_W1 = 16384
PB_W2 = 32768
PB_XT = 49152
PB_STG = 53248
PB_HID = 55296
PB_TOT = 63488

PF_SLOT = 0
PF_S = 8192
PF_SCR = 12288
PF_TOT = 13312
PF_META = 0
PF_WRUN = 1024
PF_TOKF = 1088
PF_SLOTA = 1152
PF_WHICH = 1664
PF_BASE = 2176
PF_TMP = 2688
PF_D12 = 3200
PF_ECAP = 3328


def build_nc():
    nc = bass.Bass("TRN2", target_bir_lowering=False)

    x_h = nc.dram_tensor("x", [NTOK, D], F32, kind="ExternalInput").ap()
    out_h = nc.dram_tensor("out", [NTOK, D], F32, kind="ExternalOutput").ap()
    wq_h = nc.dram_tensor("Wq", [D, D], F32, kind="ExternalInput").ap()
    wk_h = nc.dram_tensor("Wk", [D, D], F32, kind="ExternalInput").ap()
    wv_h = nc.dram_tensor("Wv", [D, D], F32, kind="ExternalInput").ap()
    wo_h = nc.dram_tensor("Wo", [D, D], F32, kind="ExternalInput").ap()
    bq_h = nc.dram_tensor("bq", [1, D], F32, kind="ExternalInput").ap()
    bk_h = nc.dram_tensor("bk", [1, D], F32, kind="ExternalInput").ap()
    bv_h = nc.dram_tensor("bv", [1, D], F32, kind="ExternalInput").ap()
    bo_h = nc.dram_tensor("bo", [1, D], F32, kind="ExternalInput").ap()
    gam_h = nc.dram_tensor("gamma", [1, D], F32, kind="ExternalInput").ap()
    bet_h = nc.dram_tensor("beta", [1, D], F32, kind="ExternalInput").ap()
    wg_h = nc.dram_tensor("Wg", [NE, D], F32, kind="ExternalInput").ap()
    w1_h = nc.dram_tensor("W1", [NE, F, D], F32, kind="ExternalInput").ap()
    w2_h = nc.dram_tensor("W2", [NE, F, D], F32, kind="ExternalInput").ap()
    w3_h = nc.dram_tensor("W3", [NE, D, F], F32, kind="ExternalInput").ap()
    xnbf_d = nc.dram_tensor("xnbf", [NTOK + 1, D], BF16, kind="Internal").ap()
    xnt_d = nc.dram_tensor("xnt", [NTILE, 128, 4096], BF16, kind="Internal").ap()
    meta_d = nc.dram_tensor("meta", [NTOK + 1, 16], F32, kind="Internal").ap()
    idx_d = nc.dram_tensor("idxd", [NE * CAP + 1, 1], I32, kind="Internal").ap()
    out2_d = nc.dram_tensor("out2", [2 * NTOK + 1, D], BF16, kind="Internal").ap()

    ctx = ExitStack()
    with ctx:
        E = ctx.enter_context
        pf = E(nc.sbuf_tensor("pf", [128, PF_TOT], F32))
        pb = E(nc.sbuf_tensor("pb", [128, PB_TOT], BF16))
        kv = E(nc.sbuf_tensor("kv", [128, BC * 1024], BF16))
        wstg = E(nc.sbuf_tensor("wstg", [128, 2048], BF16))
        xg = E(nc.sbuf_tensor("xg", [128, 2048], BF16))
        idb = E(nc.sbuf_tensor("idb", [128, 128], BF16))
        ltb = E(nc.sbuf_tensor("ltb", [128, 128], BF16))
        oneb = E(nc.sbuf_tensor("oneb", [128, 128], BF16))
        ones_bf = E(nc.sbuf_tensor("ones_bf", [1, 512], BF16))
        ones32 = E(nc.sbuf_tensor("ones32", [1, 128], F32))
        brows = E(nc.sbuf_tensor("brows", [1, 2048], BF16))
        wgT = E(nc.sbuf_tensor("wgT", [128, 64], BF16))
        gsc = E(nc.sbuf_tensor("gsc", [128, 32], F32))
        gat_all = E(nc.sbuf_tensor("gat_all", [128, NCOL * 8], F32))
        mask_all = E(nc.sbuf_tensor("mask_all", [128, NCOL * 8], BF16))
        mecb = E(nc.sbuf_tensor("mecb", [128, 64], BF16))
        mcumA = E(nc.sbuf_tensor("mcumA", [128, 64], BF16))
        mcumB = E(nc.sbuf_tensor("mcumB", [128, 64], BF16))
        mshift = E(nc.sbuf_tensor("mshift", [128, 64], BF16))
        tokid = E(nc.sbuf_tensor("tokid", [128, NCOL], I32))
        d12i = E(nc.sbuf_tensor("d12i", [128, 128], I32))
        idxsb = E(nc.sbuf_tensor("idxsb", [128, NE * NCH], I32))
        metag = E(nc.sbuf_tensor("metag", [128, 64], F32))
        d2i = E(nc.sbuf_tensor("d2i", [128, 4], I32))
        sent = E(nc.sbuf_tensor("sent", [1, 16], F32))
        st2 = E(nc.sbuf_tensor("st2", [128, 8], F32))
        vt = E(nc.sbuf_tensor("vt", [128, 4], F32))
        mcol = E(nc.sbuf_tensor("mcol", [128, 64], F32))
        rstdc = E(nc.sbuf_tensor("rstdc", [128, 64], F32))
        rinvc = E(nc.sbuf_tensor("rinvc", [128, 64], F32))
        gtt = E(nc.sbuf_tensor("gtt", [128, 32], F32))
        gs2 = E(nc.sbuf_tensor("gs2", [128, 32], F32))
        gex = E(nc.sbuf_tensor("gex", [128, 32], F32))
        gm1 = E(nc.sbuf_tensor("gm1", [128, 4], F32))
        gm2 = E(nc.sbuf_tensor("gm2", [128, 4], F32))
        gden = E(nc.sbuf_tensor("gden", [128, 4], F32))
        grec = E(nc.sbuf_tensor("grec", [128, 4], F32))
        gmk = E(nc.sbuf_tensor("gmk", [128, 8], F32))
        kss = E(nc.sbuf_tensor("kss", [128, 16], F32))
        kls = E(nc.sbuf_tensor("kls", [128, 16], F32))
        krs = E(nc.sbuf_tensor("krs", [128, 16], F32))

        psA = E(nc.psum_tensor("psA", [128, 6 * 512], F32))
        psT = [E(nc.psum_tensor("psT1", [128, 1024], BF16)),
               E(nc.psum_tensor("psT2", [128, 1024], BF16))]

        itmp = pf[:, PF_SCR:PF_SCR + 128]
        wA = pb[:, 0:8192]
        wB = pb[:, 8192:16384]
        g_bc = pb[:, PB_W1:PB_W1 + 1024]
        b_bc = pb[:, PB_W1 + 1024:PB_W1 + 2048]
        xnT = pb[:, PB_XT:PB_XT + 4096]

        class Ctr:
            __slots__ = ("h", "n", "name")

            def __init__(self, h, name):
                self.h = h
                self.n = 0
                self.name = name

        def sem(name):
            return Ctr(E(nc.semaphore(name)), name)

        sT = sem("sT")
        sV = sem("sV")
        sS = sem("sS")
        sG = sem("sG")
        sX = sem("sX")
        sWAB = sem("sWAB")
        sRO = sem("sRO")
        sO = sem("sO")
        sXB = sem("sXB")
        sZ = sem("sZ")
        sME = sem("sME")
        sO2 = sem("sO2")
        sXT = sem("sXT")

        ops = {k: [] for k in ENGS}

        def emit(eng, f):
            ops[eng].append(f)

        def wait(eng, ctr, v):
            assert v <= ctr.n, f"wait on future {ctr.name}: {v} > {ctr.n}"
            if v <= 0:
                return
            emit(eng, lambda e, h=ctr.h, vv=v: e.wait_ge(h, vv))

        def waitp(eng, pair):
            if pair is not None:
                wait(eng, pair[0], pair[1])

        def run(eng, maker, inc=None, k=1):
            if inc is None:
                emit(eng, maker)
                return None
            inc.n += k
            emit(eng, lambda e, m=maker, h=inc.h, kk=k: m(e).then_inc(h, kk))
            return (inc, inc.n)

        def runv(maker):
            p = run("vector", maker, inc=sV)
            wait("vector", sV, p[1])
            return p

        def runsc(maker):
            p = run("scalar", maker, inc=sS)
            wait("scalar", sS, p[1])
            return p

        bank_free = [None] * 8
        bank_ready = [None] * 8

        def bank_ap(b, off=0, width=512):
            return psA[:, b * 512 + off: b * 512 + off + width]

        def emit_t8(src_ap, dst_ap3, par, evac_eng="vector", evac_gates=()):
            waitp("tensor", bank_free[6 + par])
            r = None
            for i in range(8):
                r = run("tensor",
                        lambda e, o=psT[par][:, i * 128:(i + 1) * 128],
                        i_=src_ap[:, i * 128:(i + 1) * 128], d=idb[:]:
                        e.transpose(o, i_, d),
                        inc=sT if i == 7 else None)
            bank_ready[6 + par] = r
            waitp(evac_eng, r)
            for gp in evac_gates:
                waitp(evac_eng, gp)
            in_ap = psT[par][:].rearrange("p (i t) -> p i t", t=128)
            if evac_eng == "vector":
                ev = run("vector", lambda e, o=dst_ap3, i_=in_ap:
                         e.tensor_copy(o, i_), inc=sV)
            else:
                ev = run("scalar", lambda e, o=dst_ap3, i_=in_ap:
                         e.activation(o, i_, AF.Identity), inc=sS)
            bank_free[6 + par] = ev
            return ev

        def emit_ln1(slot, tc, need_rinv, scol):
            xoff = PF_SLOT + slot * TC * D + tc * D
            xi = pf[:, xoff: xoff + D]
            dump = pf[:, PF_SCR:PF_SCR + D]
            run("scalar", lambda e, o=dump, i_=xi, a=st2[:, 0:1]:
                e.activation(o, i_, AF.Identity, accum_out=a))
            a1 = run("scalar", lambda e, o=dump, i_=xi, a=st2[:, 1:2]:
                     e.activation(o, i_, AF.Square, accum_out=a), inc=sS)
            waitp("vector", a1)
            mc = mcol[:, scol:scol + 1]
            runv(lambda e, o=mc, i_=st2[:, 0:1]:
                 e.tensor_scalar(o, i_, 1.0 / D, None, op0=ALU.mult))
            runv(lambda e, o=vt[:, 0:1], i_=st2[:, 1:2]:
                 e.tensor_scalar(o, i_, 1.0 / D, None, op0=ALU.mult))
            runv(lambda e, o=vt[:, 1:2], i_=mc, j=mc:
                 e.tensor_tensor(o, i_, j, op=ALU.mult))
            runv(lambda e, o=vt[:, 2:3], i_=vt[:, 0:1], j=vt[:, 1:2]:
                 e.tensor_tensor(o, i_, j, op=ALU.subtract))
            v1 = runv(lambda e, o=vt[:, 2:3], i_=vt[:, 2:3]:
                      e.tensor_scalar(o, i_, EPS1, None, op0=ALU.add))
            waitp("scalar", v1)
            runsc(lambda e, o=vt[:, 3:4], i_=vt[:, 2:3]:
                  e.activation(o, i_, AF.Ln))
            rc = rstdc[:, scol:scol + 1]
            s1 = runsc(lambda e, o=rc, i_=vt[:, 3:4]:
                       e.activation(o, i_, AF.Exp, scale=-0.5))
            if need_rinv:
                ric = rinvc[:, scol:scol + 1]
                s1 = runsc(lambda e, o=ric, i_=vt[:, 3:4]:
                           e.activation(o, i_, AF.Exp, scale=0.5))
            waitp("vector", s1)
            return run("vector", lambda e, o=xi, i_=xi, m=mc, r=rc:
                       e.tensor_scalar(o, i_, m, r, op0=ALU.subtract,
                                       op1=ALU.mult), inc=sV)

        def emit_cast(slot, tc, ln_done, stg_gates):
            xoff = PF_SLOT + slot * TC * D + tc * D
            stg = pb[:, PB_STG + (tc % 2) * 1024: PB_STG + (tc % 2) * 1024 + 1024]
            waitp("scalar", ln_done)
            for gp in stg_gates:
                waitp("scalar", gp)
            c = run("scalar", lambda e, o=stg, i_=pf[:, xoff:xoff + D]:
                    e.activation(o, i_, AF.Identity), inc=sS)
            return c, stg

        run("gpsimd", lambda e: e.iota(itmp[:], [[1, 128]], channel_multiplier=-1,
                                       allow_small_or_imprecise_dtypes=True),
            inc=sG)
        g2 = run("gpsimd", lambda e: e.iota(tokid[:], [[128, NCOL]],
                                            channel_multiplier=1), inc=sG)
        wait("vector", sG, sG.n)
        runv(lambda e, o=ltb[:], i_=itmp[:]:
             e.tensor_scalar(o, i_, 0.0, None, op0=ALU.is_gt))
        runv(lambda e, o=itmp[:], i_=itmp[:]:
             e.tensor_scalar(o, i_, 0.0, None, op0=ALU.is_equal))
        idb_done = runv(lambda e: e.tensor_copy(idb[:], itmp[:]))
        runv(lambda e: e.memset(oneb[:], 1.0))
        runv(lambda e: e.memset(ones_bf[:], 1.0))
        runv(lambda e: e.memset(ones32[:], 1.0))
        runv(lambda e: e.memset(xg[:], 0.0))
        ixm = runv(lambda e: e.memset(idxsb[:], NTOK))
        kvz = runv(lambda e: e.memset(kv[:], 0.0))

        waitp("scalar", kvz)
        for z in range(32):
            run("scalar", lambda e, z=z, o=out2_d[z * 512:(z + 1) * 512, :]
                .rearrange("(p r) d -> p (r d)", p=128), i_=kv[:]:
                e.dma_start(o, i_), inc=sZ, k=16)
        run("scalar", lambda e, o=xnbf_d[NTOK:NTOK + 1, :], i_=kv[0:1, 0:1024]:
            e.dma_start(o, i_), inc=sZ, k=16)
        z_done = (sZ, sZ.n)

        waitp("gpsimd", ixm)
        run("gpsimd", lambda e, o=idx_d[0:NE * CAP].rearrange(
            "(p c) one -> p (c one)", p=128),
            i_=idxsb[:]: e.dma_start(o, i_), inc=sG, k=16)
        ixinit = (sG, sG.n)

        for (w_src, w_dst) in ((wk_h, wA), (wv_h, wB)):
            run("gpsimd", lambda e, s=w_src, d=w_dst:
                e.dma_start(d.rearrange("p (dc j) -> p dc j", j=D),
                            s.rearrange("(dc p) j -> p dc j", p=128)),
                inc=sWAB, k=16)
        wab_phA = (sWAB, sWAB.n)

        for i, bh in enumerate((bk_h, bv_h)):
            run("sync", lambda e, s=bh, o=pf[0:1, PF_S + i * D:PF_S + (i + 1) * D]:
                e.dma_start(o, s), inc=sRO, k=16)
        r1 = (sRO, sRO.n)
        waitp("vector", r1)
        bro = run("vector", lambda e: e.tensor_copy(
            brows[:], pf[0:1, PF_S:PF_S + 2048]), inc=sV)
        waitp("sync", bro)
        run("sync", lambda e: e.dma_start(pf[0:8, PF_S:PF_S + D], wg_h),
            inc=sRO, k=16)
        r2 = (sRO, sRO.n)
        waitp("scalar", r2)
        wgc = run("scalar", lambda e, o=pb[0:8, PB_STG:PB_STG + D],
                  i_=pf[0:8, PF_S:PF_S + D]:
                  e.activation(o, i_, AF.Identity), inc=sS)
        waitp("tensor", wgc)
        for dc in range(DC):
            par = dc % 2
            waitp("tensor", bank_free[6 + par])
            r = run("tensor", lambda e, o=psT[par][:, 0:8],
                    i_=pb[0:8, PB_STG + dc * 128:PB_STG + (dc + 1) * 128],
                    d=idb[0:8, 0:8]: e.transpose(o, i_, d), inc=sT)
            bank_ready[6 + par] = r
            waitp("vector", r)
            bank_free[6 + par] = run(
                "vector", lambda e, o=wgT[:, dc * 8:(dc + 1) * 8],
                i_=psT[par][:, 0:8]: e.tensor_copy(o, i_), inc=sV)
        rows_done = (sV, sV.n)
        wg_t_done = (sT, sT.n)

        wu_ring = [None] * 2
        wu_counter = [0]

        def emit_wu_load(kind, e_i, u):
            slot = wu_counter[0] % 2
            wu_counter[0] += 1
            if kind == 1:
                src = w1_h[e_i, u * 128:(u + 1) * 128, :]
            elif kind == 2:
                src = w2_h[e_i, u * 128:(u + 1) * 128, :]
            else:
                dch, fcol = u // 2, u % 2
                src = w3_h[e_i, dch * 128:(dch + 1) * 128,
                           fcol * 1024:(fcol + 1) * 1024]
            if wu_ring[slot] is not None:
                waitp("gpsimd", wu_ring[slot])
            ld = run("gpsimd", lambda e, o=wstg[:, slot * 1024:(slot + 1) * 1024],
                     s=src: e.dma_start(o, s), inc=sG, k=16)
            return slot, ld

        def emit_wu_xpose(kind, e_i, u, slot, ld, region_gate, par):
            waitp("tensor", ld)
            if kind == 1:
                dst = pb[:, PB_W1:PB_W1 + 16384].rearrange(
                    "p (dc f) -> p dc f", f=F)[:, :, u * 128:(u + 1) * 128]
            elif kind == 2:
                dst = pb[:, PB_W2:PB_W2 + 16384].rearrange(
                    "p (dc f) -> p dc f", f=F)[:, :, u * 128:(u + 1) * 128]
            else:
                dch, fcol = u // 2, u % 2
                dst = pb[:, PB_W3:PB_W3 + 16384].rearrange(
                    "p (fc d) -> p fc d", d=D)[
                    :, fcol * 8:(fcol + 1) * 8, dch * 128:(dch + 1) * 128]
            waitp("scalar", region_gate)
            ev = emit_t8(wstg[:, slot * 1024:(slot + 1) * 1024], dst, par,
                         evac_eng="scalar")
            wu_ring[slot] = ev
            return ev

        w1u = [[None] * FC for _ in range(NE)]
        w2u = [[None] * FC for _ in range(NE)]
        w3done = [None] * NE

        slot_free = [None] * 34
        x_loaded = [None] * 32

        def emit_xload(i):
            slot = i % 2
            t = i % NTILE
            src = x_h[t * TT:(t + 1) * TT, :].rearrange("(tc p) d -> p tc d",
                                                        p=128)
            dst = pf[:, PF_SLOT + slot * TC * D:PF_SLOT + (slot + 1) * TC * D] \
                .rearrange("p (tc d) -> p tc d", d=D)
            if i >= 2:
                assert slot_free[i] is not None
                waitp("sync", slot_free[i])
            x_loaded[i] = run("sync", lambda e, o=dst, s=src: e.dma_start(o, s),
                              inc=sX, k=16)

        emit_xload(0)
        emit_xload(1)

        kv_ready = [None] * BC
        k_first = True
        gse_prev = None
        stg_state = {0: [wg_t_done], 1: []}
        xnt_store_gate = None
        wu_sched_A = []
        for u in range(FC):
            wu_sched_A.append((1, 0, u))
        for u in range(FC):
            wu_sched_A.append((2, 0, u))
        wuA_loaded = []

        for t in range(NTILE):
            b = t // 4
            slot = t % 2
            li = t
            waitp("scalar", x_loaded[li])
            waitp("vector", x_loaded[li])
            if t == 0:
                waitp("scalar", idb_done)
            a1 = None
            for tc in range(TC):
                xoff = PF_SLOT + slot * TC * D + tc * D
                run("scalar", lambda e, o=pf[:, PF_SCR:PF_SCR + D],
                    i_=pf[:, xoff:xoff + D], a=st2[:, tc:tc + 1]:
                    e.activation(o, i_, AF.Identity, accum_out=a))
                a1 = run("scalar", lambda e, o=pf[:, PF_SCR:PF_SCR + D],
                         i_=pf[:, xoff:xoff + D], a=st2[:, 4 + tc:5 + tc]:
                         e.activation(o, i_, AF.Square, accum_out=a), inc=sS)
            waitp("vector", a1)
            mc4 = mcol[:, t * 4:t * 4 + 4]
            runv(lambda e, o=mc4, i_=st2[:, 0:4]:
                 e.tensor_scalar(o, i_, 1.0 / D, None, op0=ALU.mult))
            runv(lambda e, o=vt[:, 0:4], i_=st2[:, 4:8]:
                 e.tensor_scalar(o, i_, 1.0 / D, None, op0=ALU.mult))
            runv(lambda e, o=st2[:, 0:4], i_=mc4, j=mc4:
                 e.tensor_tensor(o, i_, j, op=ALU.mult))
            runv(lambda e, o=vt[:, 0:4], i_=vt[:, 0:4], j=st2[:, 0:4]:
                 e.tensor_tensor(o, i_, j, op=ALU.subtract))
            v1a = runv(lambda e, o=vt[:, 0:4], i_=vt[:, 0:4]:
                       e.tensor_scalar(o, i_, EPS1, None, op0=ALU.add))
            waitp("scalar", v1a)
            runsc(lambda e, o=st2[:, 4:8], i_=vt[:, 0:4]:
                  e.activation(o, i_, AF.Ln))
            runsc(lambda e, o=rstdc[:, t * 4:t * 4 + 4], i_=st2[:, 4:8]:
                  e.activation(o, i_, AF.Exp, scale=-0.5))
            s1a = runsc(lambda e, o=rinvc[:, t * 4:t * 4 + 4], i_=st2[:, 4:8]:
                        e.activation(o, i_, AF.Exp, scale=0.5))
            waitp("vector", s1a)
            ln_dones = [None] * TC
            for tc in range(TC):
                xoff = PF_SLOT + slot * TC * D + tc * D
                ln_dones[tc] = run(
                    "vector", lambda e, o=pf[:, xoff:xoff + D],
                    i_=pf[:, xoff:xoff + D],
                    m=mcol[:, t * 4 + tc:t * 4 + tc + 1],
                    r=rstdc[:, t * 4 + tc:t * 4 + tc + 1]:
                    e.tensor_scalar(o, i_, m, r, op0=ALU.subtract,
                                    op1=ALU.mult), inc=sV)
            xnT_rdys = [None] * TC
            for tc in range(TC):
                ln_done = ln_dones[tc]
                cpair, stg = emit_cast(slot, tc, ln_done, stg_state[tc % 2])
                waitp("sync", cpair)
                run("sync", lambda e,
                    o=xnbf_d[t * TT + tc * 128:t * TT + (tc + 1) * 128, :],
                    i_=stg: e.dma_start(o, i_), inc=sXB, k=16)
                waitp("tensor", cpair)
                dst3 = xnT.rearrange("p (dc tt) -> p dc tt", tt=TT)[
                    :, :, tc * 128:(tc + 1) * 128]
                ev = emit_t8(stg, dst3, tc % 2, evac_eng="vector",
                             evac_gates=(xnt_store_gate,)
                             if xnt_store_gate else ())
                stg_state[tc % 2] = [(sT, sT.n), (sXB, sXB.n)]
                xnT_rdys[tc] = ev
            for tc in range(TC):
                xnT_rdy = xnT_rdys[tc]
                waitp("tensor", xnT_rdy)
                tcol = tc * 128
                if k_first:
                    waitp("tensor", wab_phA)
                    k_first = False
                kb0 = 0 if tc % 2 == 0 else 2
                vb0 = 2 - kb0 + 2 if False else (2 if tc % 2 == 0 else 0)
                for (bank0, wsb, brow_i) in ((kb0, wA, 0), (vb0, wB, 1)):
                    for jh in range(2):
                        bank = bank0 + jh
                        waitp("tensor", bank_free[bank])
                        for dc in range(DC):
                            run("tensor", lambda e, o=bank_ap(bank),
                                l=xnT[:, dc * TT + tcol: dc * TT + tcol + 128],
                                rr=wsb[:, dc * D + jh * 512:
                                       dc * D + (jh + 1) * 512],
                                stf=(dc == 0):
                                e.matmul(o, l, rr, start=stf, stop=False))
                        r = run("tensor", lambda e, o=bank_ap(bank),
                                l=ones_bf[0:1, 0:128],
                                rr=brows[0:1, brow_i * D + jh * 512:
                                         brow_i * D + (jh + 1) * 512]:
                                e.matmul(o, l, rr, start=False, stop=True),
                                inc=sT)
                        bank_ready[bank] = r
                f1 = pf[:, PF_S:PF_S + D]
                f2 = pf[:, PF_S + D:PF_S + 2 * D]
                waitp("vector", bank_ready[kb0 + 1])
                v1 = run("vector", lambda e, o=f2,
                         i_=bank_ap(kb0, 0, 1024):
                         e.tensor_scalar(o, i_, 0.0, None, op0=ALU.min),
                         inc=sV)
                waitp("scalar", v1)
                s1 = run("scalar", lambda e, o=f1, i_=f2:
                         e.activation(o, i_, AF.Exp), inc=sS)
                waitp("vector", s1)
                kd = run("vector", lambda e, o=f1, i_=bank_ap(kb0, 0, 1024),
                         j=f1:
                         e.scalar_tensor_tensor(o, i_, 0.0, j, op0=ALU.max,
                                                op1=ALU.add), inc=sV)
                waitp("tensor", kd)
                rsc = None
                for dc in range(DC):
                    rsc = run("tensor", lambda e, o=bank_ap(kb0, 0, 8),
                              l=xnT[:, dc * TT + tcol: dc * TT + tcol + 128],
                              rr=wgT[:, dc * 8:(dc + 1) * 8],
                              stf=(dc == 0), spf=(dc == DC - 1):
                              e.matmul(o, l, rr, start=stf, stop=spf,
                                       skip_group_check=True),
                              inc=sT if dc == DC - 1 else None)
                waitp("vector", rsc)
                sev = runv(lambda e, o=gsc[:, tc * 8:(tc + 1) * 8],
                           i_=bank_ap(kb0, 0, 8): e.tensor_copy(o, i_))
                bank_free[kb0] = sev
                bank_free[kb0 + 1] = kd
                wait("vector", sV, kd[1])
                runv(lambda e, o=f2, i_=f1, j=f1:
                     e.tensor_tensor(o, i_, j, op=ALU.mult))
                runv(lambda e, o=kss[:],
                     i_=f2.rearrange("p (h dd) -> p h dd", dd=64):
                     e.tensor_reduce(o, i_, axis=AX.X, op=ALU.add))
                v2 = runv(lambda e, o=kss[:], i_=kss[:]:
                          e.tensor_scalar(o, i_, EPS2, None, op0=ALU.add))
                waitp("scalar", v2)
                runsc(lambda e, o=kls[:], i_=kss[:]:
                      e.activation(o, i_, AF.Ln))
                s2 = runsc(lambda e, o=krs[:], i_=kls[:]:
                           e.activation(o, i_, AF.Exp, scale=-0.5))
                waitp("vector", s2)
                kb = run("vector", lambda e,
                         o=pb[:, PB_HID + tc * 1024:
                              PB_HID + (tc + 1) * 1024].rearrange(
                             "p (h d) -> p h d", d=64),
                         i_=f1[:].rearrange("p (h d) -> p h d", d=64),
                         j=krs[:, :, None].broadcast_to([128, 16, 64]):
                         e.tensor_tensor(o, i_, j, op=ALU.mult), inc=sV)
                waitp("vector", bank_ready[vb0 + 1])
                vb = run("vector", lambda e,
                         o=pb[:, PB_HID + 4096 + tc * 1024:
                              PB_HID + 4096 + (tc + 1) * 1024],
                         i_=bank_ap(vb0, 0, 1024): e.tensor_copy(o, i_), inc=sV)
                bank_free[vb0] = bank_free[vb0 + 1] = vb
                waitp("tensor", kb)
                waitp("tensor", vb)
                first = (tc == 0 and t % 4 == 0)
                last = (tc == TC - 1 and t % 4 == 3)
                if first:
                    waitp("tensor", bank_free[4])
                    waitp("tensor", bank_free[5])
                    if b == 0:
                        waitp("vector", z_done)
                r = None
                for pbk in range(8):
                    st_f = first and (pbk % 4 == 0)
                    sp_f = last and (pbk % 4 == 3)
                    r = run("tensor", lambda e,
                            o=bank_ap(4 + pbk // 4, (pbk % 4) * 128, 128),
                            l=pb[:, PB_HID + tc * 1024 + pbk * 128:
                                 PB_HID + tc * 1024 + (pbk + 1) * 128],
                            rr=pb[:, PB_HID + 4096 + tc * 1024 + pbk * 128:
                                  PB_HID + 4096 + tc * 1024 + (pbk + 1) * 128],
                            stf=st_f, spf=sp_f:
                            e.matmul(o, l, rr, start=stf, stop=spf,
                                     skip_group_check=True),
                            inc=sT if pbk == 7 else None)
                bank_ready[4] = bank_ready[5] = r
                if tc == TC - 1:
                    slot_free[li + 2] = (sT, sT.n)
                    waitp("sync", xnT_rdy)
                    run("sync", lambda e, o=xnt_d[t], i_=xnT:
                        e.dma_start(o, i_), inc=sXT, k=16)
                    xnt_store_gate = (sXT, sXT.n)
            if gse_prev is not None:
                waitp("vector", gse_prev)
            g3 = lambda a: a[:].rearrange("p (tc k) -> p tc k", k=8)
            gm1b = gm1[:, :, None].broadcast_to([128, 4, 8])
            gm2b = gm2[:, :, None].broadcast_to([128, 4, 8])
            grecb = grec[:, :, None].broadcast_to([128, 4, 8])
            runv(lambda e, o=gm1[:], i_=g3(gsc):
                 e.tensor_reduce(o, i_, axis=AX.X, op=ALU.max))
            runv(lambda e, o=g3(gtt), i_=g3(gsc), j=gm1b:
                 e.tensor_tensor(o, i_, j, op=ALU.subtract))
            runv(lambda e, o=g3(gs2), i_=g3(gsc), j=gm1b:
                 e.tensor_tensor(o, i_, j, op=ALU.is_ge))
            runv(lambda e, o=gs2[:], i_=gs2[:], j=gsc[:]:
                 e.scalar_tensor_tensor(o, i_, -BIG, j, op0=ALU.mult,
                                        op1=ALU.add))
            gv = runv(lambda e, o=gm2[:], i_=g3(gs2):
                      e.tensor_reduce(o, i_, axis=AX.X, op=ALU.max))
            waitp("scalar", gv)
            gse = runsc(lambda e: e.activation(gex[:], gtt[:], AF.Exp))
            gse_prev = gse
            waitp("vector", gse)
            runv(lambda e, o=g3(gtt), i_=g3(gsc), j=gm2b:
                 e.tensor_tensor(o, i_, j, op=ALU.is_ge))
            runv(lambda e, o=mask_all[:, t * 32:(t + 1) * 32], i_=gtt[:]:
                 e.tensor_copy(o, i_))
            runv(lambda e, o=gtt[:], i_=gex[:], j=gtt[:]:
                 e.tensor_tensor(o, i_, j, op=ALU.mult))
            runv(lambda e, o=gden[:], i_=g3(gtt):
                 e.tensor_reduce(o, i_, axis=AX.X, op=ALU.add))
            runv(lambda e: e.reciprocal(grec[:], gden[:]))
            runv(lambda e, o=gat_all[:, t * 32:(t + 1) * 32].rearrange(
                "p (tc k) -> p tc k", k=8), i_=g3(gtt),
                 j=grecb: e.tensor_tensor(o, i_, j, op=ALU.mult))
            if t % 4 == 3:
                waitp("vector", bank_ready[4])
                run("vector", lambda e,
                    o=kv[0:64, b * 1024:(b + 1) * 1024].rearrange(
                        "p (q c) -> p q c", c=128)[:, :, 0:64],
                    i_=psA[0:64, 2048:3072].rearrange(
                        "p (q c) -> p q c", c=128)[:, :, 0:64]:
                    e.tensor_copy(o, i_))
                kvd = run("vector", lambda e,
                          o=kv[64:128, b * 1024:(b + 1) * 1024].rearrange(
                              "p (q c) -> p q c", c=128)[:, :, 64:128],
                          i_=psA[64:128, 2048:3072].rearrange(
                              "p (q c) -> p q c", c=128)[:, :, 64:128]:
                          e.tensor_copy(o, i_), inc=sV)
                bank_free[4] = bank_free[5] = kvd
                kv_ready[b] = kvd
            for uu in range(2):
                idx = t * 2 + uu
                if idx < len(wu_sched_A):
                    kind, e_i, u = wu_sched_A[idx]
                    slot, ld = emit_wu_load(kind, e_i, u)
                    ev = emit_wu_xpose(kind, e_i, u, slot, ld, None, idx % 2)
                    if kind == 1:
                        w1u[e_i][u] = ev
                    else:
                        w2u[e_i][u] = ev
            if li + 1 < NTILE and x_loaded[li + 1] is None:
                emit_xload(li + 1)
            if li + 2 < NTILE and slot_free[li + 2] is not None \
                    and x_loaded[li + 2] is None:
                emit_xload(li + 2)

        A_end_T = (sT, sT.n)
        A_end_V = (sV, sV.n)
        A_end_S = (sS, sS.n)
        A_end_XB = (sXB, sXB.n)

        waitp("vector", A_end_S)
        tokf = pf[:, PF_TOKF:PF_TOKF + 64]
        runv(lambda e, o=tokf, i_=tokid[:]: e.tensor_copy(o, i_))
        ecapv = pf[:, PF_ECAP:PF_ECAP + 512].rearrange("p (c e) -> p e c", e=8)
        for e_i in range(NE):
            runv(lambda e, o=ecapv[:, e_i], v=float(e_i * CAP):
                 e.memset(o, v))
        wrun = pf[:, PF_WRUN:PF_WRUN + 64]
        runv(lambda e, o=wrun: e.memset(o, 0.0))
        whichv = pf[:, PF_WHICH:PF_WHICH + 512].rearrange("p (c e) -> p e c", e=8)
        maskv = mask_all[:].rearrange("p (c e) -> p e c", e=8)
        gatv = gat_all[:].rearrange("p (c e) -> p e c", e=8)
        metab = pf[:, PF_META:PF_META + 1024].rearrange("p (c f) -> p f c", f=16)
        for e_i in range(NE):
            runv(lambda e, o=whichv[:, e_i], i_=wrun: e.tensor_copy(o, i_))
            runv(lambda e, o=metab[:, 8 + e_i], i_=tokf, j=wrun:
                 e.scalar_tensor_tensor(o, i_, 2.0, j, op0=ALU.mult,
                                        op1=ALU.add))
            runv(lambda e, o=metab[:, e_i], i_=gatv[:, e_i]:
                 e.tensor_copy(o, i_))
            if e_i < NE - 1:
                runv(lambda e, o=wrun, i_=wrun, j=maskv[:, e_i]:
                     e.tensor_tensor(o, i_, j, op=ALU.add))
        meta_built = (sV, sV.n)
        runv(lambda e: e.memset(sent[:, 0:8], 0.0))
        sb = runv(lambda e: e.memset(sent[:, 8:16], float(2 * NTOK)))
        waitp("sync", meta_built)
        waitp("sync", sb)
        run("sync", lambda e, o=meta_d[0:NTOK, :]
            .rearrange("(c p) f -> p c f", p=128),
            i_=pf[:, PF_META:PF_META + 1024].rearrange(
                "p (c f) -> p c f", f=16): e.dma_start(o, i_), inc=sME, k=16)
        run("sync", lambda e, o=meta_d[NTOK:NTOK + 1, :], i_=sent[:]:
            e.dma_start(o, i_), inc=sME, k=16)
        meta_stored = (sME, sME.n)

        slotav = pf[:, PF_SLOTA:PF_SLOTA + 512].rearrange("p (c e) -> p e c", e=8)
        for e_i in range(NE):
            mcp = runv(lambda e, o=mecb[:], i_=maskv[:, e_i]: e.tensor_copy(o, i_))
            cur, nxt = mecb, mcumA
            for sh in (1, 2, 4, 8, 16, 32):
                runv(lambda e, o=nxt[:, 0:sh], i_=cur[:, 0:sh]:
                     e.tensor_copy(o, i_))
                runv(lambda e, o=nxt[:, sh:64], i_=cur[:, sh:64],
                     j=cur[:, 0:64 - sh]: e.tensor_tensor(o, i_, j, op=ALU.add))
                cur, nxt = (mcumA, mcumB) if cur is mecb else (
                    (mcumB, mcumA) if cur is mcumA else (mcumA, mcumB))
            runv(lambda e, o=mshift[:, 0:1]: e.memset(o, 0.0))
            msf = runv(lambda e, o=mshift[:, 1:64], i_=cur[:, 0:63]:
                       e.tensor_copy(o, i_))
            waitp("tensor", mcp)
            waitp("tensor", msf)
            waitp("tensor", bank_free[0])
            run("tensor", lambda e, o=bank_ap(0, 0, 64), l=ltb[:], rr=mecb[:]:
                e.matmul(o, l, rr, start=True, stop=False))
            mm = run("tensor", lambda e, o=bank_ap(0, 0, 64), l=oneb[:],
                     rr=mshift[:]: e.matmul(o, l, rr, start=False, stop=True),
                     inc=sT)
            bank_ready[0] = mm
            waitp("vector", mm)
            bank_free[0] = runv(lambda e, o=slotav[:, e_i], i_=bank_ap(0, 0, 64):
                                e.tensor_copy(o, i_))
        slota = pf[:, PF_SLOTA:PF_SLOTA + 512]
        basea = pf[:, PF_BASE:PF_BASE + 512]
        tmpa = pf[:, PF_TMP:PF_TMP + 512]
        runv(lambda e, o=basea, i_=slota, j=pf[:, PF_ECAP:PF_ECAP + 512]:
             e.tensor_tensor(o, i_, j, op=ALU.add))
        runv(lambda e, o=tmpa, i_=slota:
             e.tensor_scalar(o, i_, float(CAP), None, op0=ALU.is_ge))
        runv(lambda e, o=basea, i_=tmpa, j=basea:
             e.scalar_tensor_tensor(o, i_, BIGI, j, op0=ALU.mult, op1=ALU.add))
        whicha = pf[:, PF_WHICH:PF_WHICH + 512]
        for k in range(2):
            runv(lambda e, o=tmpa, i_=whicha, v=float(k):
                 e.tensor_scalar(o, i_, v, None, op0=ALU.is_equal))
            runv(lambda e, o=tmpa, i_=tmpa, j=mask_all[:]:
                 e.tensor_tensor(o, i_, j, op=ALU.mult))
            runv(lambda e, o=tmpa, i_=tmpa, j=basea:
                 e.tensor_tensor(o, i_, j, op=ALU.mult))
            runv(lambda e, o=pf[:, PF_D12 + k * 64:PF_D12 + (k + 1) * 64],
                 i_=tmpa.rearrange("p (c e) -> p c e", e=8):
                 e.tensor_reduce(o, i_, axis=AX.X, op=ALU.add))
        runv(lambda e, o=pf[:, PF_D12:PF_D12 + 128],
             i_=pf[:, PF_D12:PF_D12 + 128]:
             e.tensor_scalar(o, i_, float(NE * CAP), None, op0=ALU.min))
        d12c = runv(lambda e, o=d12i[:], i_=pf[:, PF_D12:PF_D12 + 128]:
                    e.tensor_copy(o, i_))

        waitp("gpsimd", ixinit)
        waitp("gpsimd", d12c)
        for c in range(NCOL):
            for k in range(2):
                run("gpsimd", lambda e, c=c, k=k: e.indirect_dma_start(
                    idx_d,
                    IndirectOffsetOnAxis(ap=d12i[:, k * 64 + c:k * 64 + c + 1],
                                         axis=0),
                    tokid[:, c:c + 1], None),
                    inc=sG, k=16)
        scat_done = (sG, sG.n)
        wait("gpsimd", sG, sG.n)
        idx_loaded = [None] * NE
        for e_i in range(NE):
            idx_loaded[e_i] = run(
                "gpsimd", lambda e, e_i=e_i:
                e.dma_start(idxsb[:, e_i * NCH:(e_i + 1) * NCH],
                            idx_d[e_i * CAP:(e_i + 1) * CAP].rearrange(
                                "(p c) one -> p (c one)", p=128)),
                inc=sG, k=16)
        R_end_V = (sV, sV.n)

        w1sb = pb[:, PB_W1:PB_W1 + 16384]
        w2sb = pb[:, PB_W2:PB_W2 + 16384]
        w3sb = pb[:, PB_W3:PB_W3 + 16384]
        xeT = pb[:, PB_XT:PB_XT + 4096]
        hid = pb[:, PB_HID:PB_HID + 8192]
        eobs = [pb[:, PB_STG:PB_STG + 1024],
                pb[:, PB_STG + 1024:PB_STG + 2048]]

        xg_gate = [None, None]
        eob_gate = [None, None]
        last_h1 = [None] * NE
        last_h2 = [None] * NE
        last_eo = [None] * NE
        metag_gate = None
        par_ctr = [0]

        def next_par():
            par_ctr[0] ^= 1
            return par_ctr[0]

        w3_sched = [(0, u, A_end_T) for u in range(FC)]

        prev_xeT_gate = None

        for e_i in range(NE):
            for ch in range(CHK):
                if metag_gate is not None:
                    waitp("gpsimd", metag_gate)
                waitp("gpsimd", idx_loaded[e_i])
                waitp("gpsimd", meta_stored)
                gath = [None] * 4
                mg = [None] * 4
                for g in range(4):
                    col = e_i * NCH + ch * 4 + g
                    if xg_gate[g % 2] is not None:
                        waitp("gpsimd", xg_gate[g % 2])
                    gath[g] = run("gpsimd", lambda e, col=col, g=g:
                                  e.indirect_dma_start(
                                      xg[:, (g % 2) * 1024:(g % 2) * 1024 + 1024],
                                      None, xnbf_d,
                                      IndirectOffsetOnAxis(
                                          ap=idxsb[:, col:col + 1], axis=0)),
                                  inc=sG, k=16)
                    mg[g] = run("gpsimd", lambda e, col=col, g=g:
                                e.indirect_dma_start(
                                    metag[:, g * 16:(g + 1) * 16], None, meta_d,
                                    IndirectOffsetOnAxis(
                                        ap=idxsb[:, col:col + 1], axis=0)),
                                inc=sG, k=16)
                    waitp("tensor", gath[g])
                    if prev_xeT_gate is not None:
                        waitp("tensor", prev_xeT_gate)
                    dst3 = xeT.rearrange("p (dc tt) -> p dc tt", tt=512)[
                        :, :, g * 128:(g + 1) * 128]
                    ev = emit_t8(xg[:, (g % 2) * 1024:(g % 2) * 1024 + 1024],
                                 dst3, next_par(), evac_eng="vector")
                    xg_gate[g % 2] = ev
                prev_xeT_gate = None
                xeT_rdy = (sV, sV.n)
                mgall = mg[3]
                mult_last = None
                for fc in range(FC):
                    bank1 = fc % 2
                    bank2 = 2 + fc % 2
                    waitp("tensor", w1u[e_i][fc])
                    waitp("tensor", bank_free[bank1])
                    if fc == 0:
                        waitp("tensor", xeT_rdy)
                    r1_ = None
                    for dc in range(DC):
                        r1_ = run("tensor", lambda e, o=bank_ap(bank1),
                                  l=w1sb[:, dc * F + fc * 128:
                                         dc * F + fc * 128 + 128],
                                  rr=xeT[:, dc * 512:(dc + 1) * 512],
                                  stf=(dc == 0), spf=(dc == DC - 1):
                                  e.matmul(o, l, rr, start=stf, stop=spf),
                                  inc=sT if dc == DC - 1 else None)
                    bank_ready[bank1] = r1_
                    waitp("tensor", w2u[e_i][fc])
                    waitp("tensor", bank_free[bank2])
                    r2_ = None
                    for dc in range(DC):
                        r2_ = run("tensor", lambda e, o=bank_ap(bank2),
                                  l=w2sb[:, dc * F + fc * 128:
                                         dc * F + fc * 128 + 128],
                                  rr=xeT[:, dc * 512:(dc + 1) * 512],
                                  stf=(dc == 0), spf=(dc == DC - 1):
                                  e.matmul(o, l, rr, start=stf, stop=spf),
                                  inc=sT if dc == DC - 1 else None)
                    bank_ready[bank2] = r2_
                    if fc == FC - 1:
                        if ch == CHK - 1:
                            last_h1[e_i] = r1_
                            last_h2[e_i] = r2_
                        prev_xeT_gate = r2_
                    waitp("scalar", bank_ready[bank1])
                    hsl = hid[:, fc * 512:(fc + 1) * 512]
                    silu_l = run("scalar", lambda e, o=hsl, i_=bank_ap(bank1):
                                 e.activation(o, i_, AF.Silu), inc=sS)
                    bank_free[bank1] = silu_l
                    waitp("vector", silu_l)
                    waitp("vector", bank_ready[bank2])
                    mult_last = run("vector", lambda e, o=hsl, i_=hsl,
                                    j=bank_ap(bank2):
                                    e.tensor_tensor(o, i_, j, op=ALU.mult),
                                    inc=sV)
                    bank_free[bank2] = mult_last
                    if ch == 0 and w3_sched:
                        ee, u, gate = w3_sched.pop(0)
                        slot3, ld3 = emit_wu_load(3, ee, u)
                        emit_wu_xpose(3, ee, u, slot3, ld3, gate, next_par())
                        if not w3_sched:
                            w3done[ee] = (sS, sS.n)
                    if ch == CHK - 1 and e_i + 1 < NE:
                        slot1, ld1 = emit_wu_load(1, e_i + 1, fc)
                        w1u[e_i + 1][fc] = emit_wu_xpose(
                            1, e_i + 1, fc, slot1, ld1, r1_, next_par())
                        slot2, ld2 = emit_wu_load(2, e_i + 1, fc)
                        w2u[e_i + 1][fc] = emit_wu_xpose(
                            2, e_i + 1, fc, slot2, ld2, r2_, next_par())
                waitp("tensor", mult_last)
                if ch == 0:
                    waitp("tensor", w3done[e_i])
                waitp("vector", mgall)
                eo_evac_last = None
                for g in range(4):
                    for dh in range(2):
                        bank = 4 + dh
                        waitp("tensor", bank_free[bank])
                        r = None
                        for fc in range(FC):
                            r = run("tensor", lambda e, o=bank_ap(bank),
                                    l=hid[:, fc * 512 + g * 128:
                                         fc * 512 + g * 128 + 128],
                                    rr=w3sb[:, fc * D + dh * 512:
                                           fc * D + (dh + 1) * 512],
                                    stf=(fc == 0), spf=(fc == FC - 1):
                                    e.matmul(o, l, rr, start=stf, stop=spf),
                                    inc=sT if fc == FC - 1 else None)
                        bank_ready[bank] = r
                        if e_i == NE - 1 and ch == CHK - 1 and g == 3 and dh == 1:
                            last_eo[e_i] = r
                    es = g % 2
                    if eob_gate[es] is not None:
                        waitp("vector", eob_gate[es])
                    gcol = metag[:, g * 16 + e_i:g * 16 + e_i + 1]
                    ev = None
                    for dh in range(2):
                        waitp("vector", bank_ready[4 + dh])
                        ev = run("vector", lambda e,
                                 o=eobs[es][:, dh * 512:(dh + 1) * 512],
                                 i_=bank_ap(4 + dh), m=gcol:
                                 e.tensor_scalar(o, i_, m, None, op0=ALU.mult),
                                 inc=sV)
                        bank_free[4 + dh] = ev
                    dcast = runv(lambda e, o=d2i[:, g:g + 1],
                                 i_=metag[:, g * 16 + 8 + e_i:
                                          g * 16 + 8 + e_i + 1]:
                                 e.tensor_copy(o, i_))
                    eo_evac_last = dcast
                    waitp("gpsimd", ev)
                    waitp("gpsimd", dcast)
                    sc = run("gpsimd", lambda e, es=es, g=g:
                             e.indirect_dma_start(
                                 out2_d,
                                 IndirectOffsetOnAxis(ap=d2i[:, g:g + 1], axis=0),
                                 eobs[es][:], None),
                             inc=sG, k=16)
                    eob_gate[es] = sc
                metag_gate = eo_evac_last
                if e_i == NE - 1 and ch == CHK - 1:
                    last_eo[e_i] = bank_ready[5]
            if e_i + 1 < NE:
                gate = bank_ready[5]
                w3_sched = [(e_i + 1, u, gate) for u in range(FC)]

        C_end_eo = last_eo[NE - 1]
        C_scat_all = (sG, sG.n)

        waitp("gpsimd", C_end_eo)
        for (w_src, w_dst) in ((wq_h, wA), (wo_h, wB)):
            run("gpsimd", lambda e, s=w_src, d=w_dst:
                e.dma_start(d.rearrange("p (dc j) -> p dc j", j=D),
                            s.rearrange("(dc p) j -> p dc j", p=128)),
                inc=sWAB, k=16)
        wab_phB = (sWAB, sWAB.n)

        run("sync", lambda e: e.dma_start(pf[0:1, PF_S:PF_S + D], gam_h),
            inc=sRO, k=16)
        run("sync", lambda e: e.dma_start(pf[0:1, PF_S + D:PF_S + 2 * D], bet_h),
            inc=sRO, k=16)
        r3 = (sRO, sRO.n)
        waitp("tensor", r3)
        for j, dst in ((0, g_bc), (1, b_bc)):
            for dh in range(2):
                bank = 2 + dh
                waitp("tensor", bank_free[bank])
                r = run("tensor", lambda e, o=bank_ap(bank), l=ones32[0:1, :],
                        rr=pf[0:1, PF_S + j * D + dh * 512:
                              PF_S + j * D + dh * 512 + 512]:
                        e.matmul(o, l, rr, start=True, stop=True), inc=sT)
                bank_ready[bank] = r
                waitp("vector", r)
                waitp("vector", last_h1[NE - 1])
                bank_free[bank] = run(
                    "vector", lambda e, o=dst[:, dh * 512:(dh + 1) * 512],
                    i_=bank_ap(bank): e.tensor_copy(o, i_), inc=sV)
        gb_done = (sV, sV.n)

        for i, bh in enumerate((bq_h, bo_h)):
            run("sync", lambda e, s=bh, o=pf[0:1, PF_S + 2048 + i * D:
                                            PF_S + 2048 + (i + 1) * D]:
                e.dma_start(o, s), inc=sRO, k=16)
        rb = (sRO, sRO.n)
        waitp("vector", rb)
        waitp("vector", A_end_T)
        brB = run("vector", lambda e: e.tensor_copy(
            brows[:], pf[0:1, PF_S + 2048:PF_S + 4096]), inc=sV)

        slot_free[16] = R_end_V
        slot_free[17] = R_end_V
        waitp("sync", meta_stored)
        waitp("sync", R_end_V)
        emit_xload(16)
        emit_xload(17)

        waitp("vector", C_end_eo)
        waitp("scalar", eob_gate[0])
        waitp("scalar", eob_gate[1])
        waitp("sync", C_scat_all)
        waitp("scalar", C_scat_all)

        qT = pb[:, PB_HID:PB_HID + 4096]
        gbf = pb[:, PB_HID + 4096:PB_HID + 8192]
        ssb = pf[:, PF_S:PF_S + 4096]
        scr = pf[:, PF_SCR:PF_SCR + 1024]
        o2_gate = [None, None]
        prev_a_last = None

        xnt_loaded = [None] * NTILE
        xnt_used = [None] * NTILE

        XNT_OFF = (PB_XT, PB_W2)

        def emit_xnt_load(tt_):
            dst = pb[:, XNT_OFF[tt_ % 2]:XNT_OFF[tt_ % 2] + 4096]
            if tt_ >= 2:
                waitp("sync", xnt_used[tt_ - 2])
            xnt_loaded[tt_] = run(
                "sync", lambda e, o=dst, s=xnt_d[tt_]: e.dma_start(o, s),
                inc=sXT, k=16)

        emit_xnt_load(0)
        emit_xnt_load(1)
        for t in range(NTILE):
            b = t // 4
            slot = t % 2
            li = 16 + t
            xnT = pb[:, XNT_OFF[t % 2]:XNT_OFF[t % 2] + 4096]
            waitp("scalar", x_loaded[li])
            waitp("vector", x_loaded[li])
            xn_norm = [None] * TC
            for tc in range(TC):
                xoff = PF_SLOT + slot * TC * D + tc * D
                xn_norm[tc] = run(
                    "vector", lambda e, o=pf[:, xoff:xoff + D],
                    i_=pf[:, xoff:xoff + D],
                    m=mcol[:, t * 4 + tc:t * 4 + tc + 1],
                    r=rstdc[:, t * 4 + tc:t * 4 + tc + 1]:
                    e.tensor_scalar(o, i_, m, r, op0=ALU.subtract,
                                    op1=ALU.mult), inc=sV)
            waitp("tensor", xnt_loaded[t])
            if t == 0:
                waitp("tensor", wab_phB)
                waitp("tensor", brB)
            qT_w = [None] * (JC // 2)
            t1 = scr[:, 0:1024]
            for jp in range(JC // 2):
                b0 = (jp % 2) * 2
                for jj in range(2):
                    jc = jp * 2 + jj
                    bank = b0 + jj
                    waitp("tensor", bank_free[bank])
                    for dc in range(DC):
                        run("tensor", lambda e, o=bank_ap(bank),
                            l=wA[:, dc * D + jc * 128: dc * D + jc * 128 + 128],
                            rr=xnT[:, dc * TT:(dc + 1) * TT], stf=(dc == 0):
                            e.matmul(o, l, rr, start=stf, stop=False))
                    r = run("tensor", lambda e, o=bank_ap(bank),
                            l=brows[0:1, jc * 128:(jc + 1) * 128],
                            rr=ones_bf[0:1, 0:512]:
                            e.matmul(o, l, rr, start=False, stop=True), inc=sT)
                    bank_ready[bank] = r
                pw = bank_ap(b0, 0, 1024)
                waitp("vector", bank_ready[b0 + 1])
                v1 = run("vector", lambda e, o=t1, i_=pw:
                         e.tensor_scalar(o, i_, 0.0, None, op0=ALU.min), inc=sV)
                waitp("scalar", v1)
                s1 = run("scalar", lambda e, o=pb[:, 36864:37888], i_=t1:
                         e.activation(o, i_, AF.Exp), inc=sS)
                waitp("vector", s1)
                qT_w[jp] = run(
                    "vector", lambda e, o=qT[:, jp * 1024:(jp + 1) * 1024],
                    i_=pw, j=pb[:, 36864:37888]:
                    e.scalar_tensor_tensor(o, i_, 0.0, j, op0=ALU.max,
                                           op1=ALU.add), inc=sV)
                bank_free[b0] = bank_free[b0 + 1] = qT_w[jp]
            xnt_used[t] = (sT, sT.n)
            if t + 2 < NTILE:
                emit_xnt_load(t + 2)
            waitp("tensor", kv_ready[b])
            for jc in range(JC):
                bank = 2 + jc % 2
                waitp("tensor", bank_free[bank])
                waitp("tensor", qT_w[jc // 2])
                r = run("tensor", lambda e, o=bank_ap(bank),
                        l=kv[:, b * 1024 + jc * 128: b * 1024 + (jc + 1) * 128],
                        rr=pb[:, PB_HID + jc * TT:PB_HID + (jc + 1) * TT]:
                        e.matmul(o, l, rr, start=True, stop=True), inc=sT)
                bank_ready[bank] = r
                waitp("scalar", r)
                bank_free[bank] = run(
                    "scalar", lambda e,
                    o=pb[:, PB_HID + jc * TT:PB_HID + (jc + 1) * TT],
                    i_=bank_ap(bank): e.activation(o, i_, AF.Identity), inc=sS)
            attnT_ready = (sS, sS.n)
            a_last = None
            recon = [None] * TC
            for tc in range(TC):
                for dh in range(2):
                    bank = 4 + dh
                    waitp("tensor", bank_free[bank])
                    if tc == 0 and dh == 0:
                        waitp("tensor", attnT_ready)
                    for jc in range(JC):
                        run("tensor", lambda e, o=bank_ap(bank),
                            l=pb[:, PB_HID + jc * TT + tc * 128:
                                 PB_HID + jc * TT + tc * 128 + 128],
                            rr=wB[:, jc * D + dh * 512: jc * D + (dh + 1) * 512],
                            stf=(jc == 0):
                            e.matmul(o, l, rr, start=stf, stop=False))
                    r = run("tensor", lambda e, o=bank_ap(bank),
                            l=ones_bf[0:1, 0:128],
                            rr=brows[0:1, D + dh * 512:
                                     D + (dh + 1) * 512]:
                            e.matmul(o, l, rr, start=False, stop=True), inc=sT)
                    bank_ready[bank] = r
                xoff = PF_SLOT + slot * TC * D + tc * D
                ssl = ssb[:, tc * D:(tc + 1) * D]
                yv = None
                waitp("vector", xn_norm[tc])
                if t > 0 and tc == 0:
                    wait("vector", sO, 16 * 4 * t)
                for dh in range(2):
                    waitp("vector", bank_ready[4 + dh])
                    yv = run("vector", lambda e,
                             o=ssl[:, dh * 512:(dh + 1) * 512],
                             i_=pf[:, xoff + dh * 512: xoff + (dh + 1) * 512],
                             j=bank_ap(4 + dh):
                             e.tensor_tensor(o, i_, j, op=ALU.add), inc=sV)
                    bank_free[4 + dh] = yv
                waitp("gpsimd", yv)
                recon[tc] = run("gpsimd", lambda e, o=pf[:, xoff:xoff + D],
                                i_=pf[:, xoff:xoff + D],
                                m=rinvc[:, t * 4 + tc: t * 4 + tc + 1],
                                mm=mcol[:, t * 4 + tc: t * 4 + tc + 1]:
                                e.tensor_scalar(o, i_, m, mm, op0=ALU.mult,
                                                op1=ALU.add), inc=sG)
                waitp("scalar", yv)
                run("scalar", lambda e, o=pb[:, 37888:38912], i_=ssl,
                    a=kss[:, tc:tc + 1]:
                    e.activation(o, i_, AF.Identity, accum_out=a))
                a_last = run("scalar", lambda e, o=pb[:, 37888:38912], i_=ssl,
                             a=kss[:, 4 + tc:5 + tc]:
                             e.activation(o, i_, AF.Square, accum_out=a),
                             inc=sS)
            waitp("vector", a_last)
            runv(lambda e, o=kls[:, 0:8], i_=kss[:, 0:8]:
                 e.tensor_scalar(o, i_, 1.0 / D, None, op0=ALU.mult))
            runv(lambda e, o=krs[:, 0:4], i_=kls[:, 0:4], j=kls[:, 0:4]:
                 e.tensor_tensor(o, i_, j, op=ALU.mult))
            runv(lambda e, o=krs[:, 4:8], i_=kls[:, 4:8], j=krs[:, 0:4]:
                 e.tensor_tensor(o, i_, j, op=ALU.subtract))
            v2 = runv(lambda e, o=krs[:, 4:8], i_=krs[:, 4:8]:
                      e.tensor_scalar(o, i_, EPS2, None, op0=ALU.add))
            waitp("scalar", v2)
            runsc(lambda e, o=krs[:, 8:12], i_=krs[:, 4:8]:
                  e.activation(o, i_, AF.Ln))
            s2 = runsc(lambda e, o=krs[:, 12:16], i_=krs[:, 8:12]:
                       e.activation(o, i_, AF.Exp, scale=-0.5))
            waitp("vector", s2)
            if t == 0:
                waitp("vector", gb_done)
            for tc in range(TC):
                ssl = ssb[:, tc * D:(tc + 1) * D]
                run("vector", lambda e, o=ssl, i_=ssl, m=kls[:, tc:tc + 1],
                    r_=krs[:, 12 + tc:13 + tc]:
                    e.tensor_scalar(o, i_, m, r_, op0=ALU.subtract,
                                    op1=ALU.mult))
                run("vector", lambda e, o=ssl, i_=ssl, j=g_bc[:]:
                    e.tensor_tensor(o, i_, j, op=ALU.mult))
                run("vector", lambda e, o=ssl, i_=ssl, j=b_bc[:]:
                    e.tensor_tensor(o, i_, j, op=ALU.add))
            s_done = (sV, sV.n)
            prev_a_last = a_last
            if t > 0:
                wait("scalar", sO, 16 * 4 * t)

            def o2load(tc):
                es = tc % 2
                if o2_gate[es] is not None:
                    waitp("sync", o2_gate[es])
                r0 = t * TT + tc * 128
                buf = wstg[:, 0:2048] if es == 0 else xg[:, 0:2048]
                return run("sync", lambda e, o=buf,
                           s=out2_d[2 * r0:2 * (r0 + 128), :].rearrange(
                               "(p r) d -> p (r d)", p=128):
                           e.dma_start(o, s), inc=sO2, k=16), buf

            o2p = {0: o2load(0), 1: o2load(1)}
            ep_v = None
            gs_prev = None
            for tc in range(TC):
                es = tc % 2
                o2ld, o2buf = o2p[tc]
                waitp("gpsimd", o2ld)
                waitp("gpsimd", gs_prev)
                ofn = run("gpsimd", lambda e, o=o2buf[:, 0:1024],
                          i_=o2buf[:, 0:1024], j=o2buf[:, 1024:2048]:
                          e.tensor_tensor(o, i_, j, op=ALU.add), inc=sG)
                waitp("scalar", ofn)
                gsl = gbf[:, tc * 1024:(tc + 1) * 1024]
                gs_ = run("scalar", lambda e, o=gsl, i_=o2buf[:, 0:1024]:
                          e.activation(o, i_, AF.Silu), inc=sS)
                gs_prev = gs_
                o2_gate[es] = gs_
                if tc + 2 < TC:
                    o2p[tc + 2] = o2load(tc + 2)
                ssl = ssb[:, tc * D:(tc + 1) * D]
                xoff = PF_SLOT + slot * TC * D + tc * D
                xsl = pf[:, xoff: xoff + D]
                waitp("gpsimd", gs_)
                waitp("gpsimd", s_done)
                sgm = run("gpsimd", lambda e, o=ssl, i_=ssl, j=gsl:
                          e.tensor_tensor(o, i_, j, op=ALU.mult), inc=sG)
                waitp("gpsimd", sgm)
                waitp("gpsimd", recon[tc])
                ep_v = run("gpsimd", lambda e, o=ssl, i_=ssl, j=xsl:
                           e.tensor_tensor(o, i_, j, op=ALU.add), inc=sG)
                waitp("sync", ep_v)
                run("sync", lambda e,
                    o=out_h[t * TT + tc * 128: t * TT + (tc + 1) * 128, :],
                    s=ssb[:, tc * D:(tc + 1) * D]: e.dma_start(o, s),
                    inc=sO, k=16)
            slot_free[li + 2] = ep_v
            if li + 2 < 32 and x_loaded[li + 2] is None:
                emit_xload(li + 2)

        wait("sync", sO, sO.n)
        assert sO.n == 16 * NTILE * TC, sO.n

        with nc.Block() as block:
            @block.sync
            def _(e):
                for f_ in ops["sync"]:
                    f_(e)

            @block.scalar
            def _(e):
                for f_ in ops["scalar"]:
                    f_(e)

            @block.vector
            def _(e):
                for f_ in ops["vector"]:
                    f_(e)

            @block.tensor
            def _(e):
                for f_ in ops["tensor"]:
                    f_(e)

            @block.gpsimd
            def _(e):
                for f_ in ops["gpsimd"]:
                    f_(e)

    return nc


def kernel(**inputs):
    x = np.ascontiguousarray(inputs["L0_attn_x"], dtype=np.float32)
    base = {
        "Wq": np.ascontiguousarray(inputs["L0_attn_Wq"], np.float32),
        "Wk": np.ascontiguousarray(inputs["L0_attn_Wk"], np.float32),
        "Wv": np.ascontiguousarray(inputs["L0_attn_Wv"], np.float32),
        "Wo": np.ascontiguousarray(inputs["L0_attn_Wo"], np.float32),
        "bq": np.ascontiguousarray(inputs["L0_attn_bq"], np.float32).reshape(1, D),
        "bk": np.ascontiguousarray(inputs["L0_attn_bk"], np.float32).reshape(1, D),
        "bv": np.ascontiguousarray(inputs["L0_attn_bv"], np.float32).reshape(1, D),
        "bo": np.ascontiguousarray(inputs["L0_attn_bo"], np.float32).reshape(1, D),
        "gamma": np.ascontiguousarray(
            inputs["L0_attn_gamma"], np.float32).reshape(1, D),
        "beta": np.ascontiguousarray(
            inputs["L0_attn_beta"], np.float32).reshape(1, D),
        "Wg": np.ascontiguousarray(inputs["L0_ffn_Wg"], np.float32),
        "W1": np.ascontiguousarray(inputs["L0_ffn_W1"], np.float32),
        "W2": np.ascontiguousarray(inputs["L0_ffn_W2"], np.float32),
        "W3": np.ascontiguousarray(inputs["L0_ffn_W3"], np.float32),
    }
    in_maps = []
    for c in range(NC):
        m = dict(base)
        m["x"] = np.ascontiguousarray(x[c * BC:(c + 1) * BC].reshape(NTOK, D))
        in_maps.append(m)
    nc = build_nc()
    res = run_bass_kernel_spmd(nc, in_maps, core_ids=list(range(NC)))
    out = np.concatenate(
        [np.asarray(r["out"], dtype=np.float32).reshape(BC, S, D)
         for r in res.results], axis=0)
    return out
